# version 24
# baseline (speedup 1.0000x reference)
import bisect
import math
from contextlib import ExitStack

import numpy as np

import concourse.bass as bass
import concourse.mybir as mybir
from concourse.bass_utils import run_bass_kernel_spmd

F32 = mybir.dt.float32
BF16 = mybir.dt.bfloat16
AF = mybir.ActivationFunctionType
ALU = mybir.AluOpType
AX = mybir.AxisListType

TWO_PI = 2.0 * math.pi
C1 = float(np.float32(TWO_PI))
C2 = float(np.float32(TWO_PI - C1))
C3 = TWO_PI - C1 - C2
NEG_BIG = -8.0e10
LN_EPS = 1e-5
P = 128


def full_cfg():
    return dict(T=4096, H=1024, NH=16, HID=4096, QA_T=19, BCH=4, FCH=3,
                NCORES=8, ROLE_SPLIT=4, MAXW=512)


def mini_cfg():
    return dict(T=512, H=256, NH=4, HID=512, QA_T=2, BCH=2, FCH=1,
                NCORES=2, ROLE_SPLIT=1)


def med_cfg():
    return dict(T=1024, H=256, NH=4, HID=2048, QA_T=3, BCH=2, FCH=2,
                NCORES=2, ROLE_SPLIT=1, MAXW=512)


def med2_cfg():
    return dict(T=1024, H=1024, NH=16, HID=2048, QA_T=5, BCH=4, FCH=2,
                NCORES=2, ROLE_SPLIT=1, MAXW=1024)



class _Eng:
    def __init__(self, name, eng, semname):
        self.name = name
        self.eng = eng
        self.semname = semname
        self.pos = 0
        self.inc_pos = []
        self.inc_cnt = []
        self.count = 0
        self.handles = {}
        self.waited = {}


class Kb:

    def __init__(self, nc, sems, streams):
        self.nc = nc
        self.sems = sems
        self.streams = streams
        self.E = {
            "T": _Eng("T", nc.tensor, "et"),
            "V": _Eng("V", nc.vector, "ev"),
            "S": _Eng("S", nc.scalar, "es"),
            "G": _Eng("G", nc.gpsimd, "eg"),
            "Q": _Eng("Q", nc.sync, "eq"),
        }
        self.dmacnt = {}
        self.res = {}

    def clone_counters_from(self, other):
        for n, e in self.E.items():
            oe = other.E[n]
            e.pos = oe.pos
            e.inc_pos = list(oe.inc_pos)
            e.inc_cnt = list(oe.inc_cnt)
            e.count = oe.count
            e.handles = dict(oe.handles)
            e.waited = dict(oe.waited)
        self.dmacnt = dict(other.dmacnt)
        self.res = {k: {"w": v["w"], "r": dict(v["r"])}
                    for k, v in other.res.items()}

    def _resolve(self, tok):
        if tok[0] == "D":
            st = tok[1]
            return (self.streams[st], self.dmacnt[st])
        _, en, pos = tok
        e = self.E[en]
        i = bisect.bisect_left(e.inc_pos, pos)
        if i < len(e.inc_pos):
            return (e.semname, e.inc_cnt[i])
        ins = e.handles.get(pos)
        assert ins is not None, f"no handle for {en}@{pos}"
        e.count += 1
        ins.then_inc(self.sems[e.semname], 1)
        e.inc_pos.append(pos)
        e.inc_cnt.append(e.count)
        return (e.semname, e.count)

    def _wait_deps(self, e, reads, writes):
        toks = []
        for r in reads:
            ent = self.res.get(r)
            if ent is not None and ent["w"] is not None:
                toks.append(ent["w"])
        for w in writes:
            ent = self.res.get(w)
            if ent is not None:
                if ent["w"] is not None:
                    toks.append(ent["w"])
                toks.extend(ent["r"].values())
        need = {}
        for t in toks:
            s, v = self._resolve(t)
            if need.get(s, 0) < v:
                need[s] = v
        for s, v in need.items():
            if e.waited.get(s, 0) < v:
                e.eng.wait_ge(self.sems[s], v)
                e.waited[s] = v

    def _record(self, tok, reads, writes):
        for w in writes:
            self.res[w] = {"w": tok, "r": {}}
        rk = tok[1]
        for r in reads:
            ent = self.res.setdefault(r, {"w": None, "r": {}})
            ent["r"][rk] = tok

    def op(self, en, method, *args, reads=(), writes=(), **kw):
        e = self.E[en]
        self._wait_deps(e, reads, writes)
        ins = getattr(e.eng, method)(*args, **kw)
        e.pos += 1
        e.handles[e.pos] = ins
        self._record(("E", en, e.pos), reads, writes)
        return ins

    def mm(self, out, lhsT, rhs, *, start, stop, psum, reads=()):
        e = self.E["T"]
        self._wait_deps(e, reads, (psum,) if start else ())
        ins = e.eng.matmul(out, lhsT, rhs, start=start, stop=stop)
        e.pos += 1
        e.handles[e.pos] = ins
        tok = ("E", "T", e.pos)
        for r in reads:
            ent = self.res.setdefault(r, {"w": None, "r": {}})
            ent["r"]["T"] = tok
        if start:
            self.res[psum] = {"w": tok, "r": {}}
        else:
            self.res[psum]["w"] = tok
        return ins

    def dma(self, qn, out, in_, *, stream, reads=(), writes=(),
            transpose=False, slow=False):
        e = self.E[qn]
        if stream not in self.streams:
            self.streams[stream] = f"d{len(self.streams)}"
        semn = self.streams[stream]
        self._wait_deps(e, reads, writes)
        if transpose:
            ins = e.eng.dma_start(out=out, in_=in_, transpose=True)
        else:
            ins = e.eng.dma_start(out=out, in_=in_,
                                  allow_slow_non_contiguous=slow)
        self.dmacnt[stream] = self.dmacnt.get(stream, 0) + 16
        ins.then_inc(self.sems[semn], 16)
        self._record(("D", stream, self.dmacnt[stream]), reads, writes)

    def mark(self, key, stream):
        self.res[key] = {"w": ("D", stream, self.dmacnt.get(stream, 0)),
                         "r": {}}

    def barrier_reset(self):
        for ent in self.res.values():
            if ent["w"] is not None and ent["w"][0] == "E":
                ent["w"] = None
            ent["r"] = {k: v for k, v in ent["r"].items() if v[0] == "D"}

    def fence_dma(self, en="G"):
        e = self.E[en]
        for st, v in self.dmacnt.items():
            semn = self.streams[st]
            if v and e.waited.get(semn, 0) < v:
                e.eng.wait_ge(self.sems[semn], v)
                e.waited[semn] = v


def _bank_pieces(a, b):
    out = []
    while a < b:
        nxt = min(b, (a // 512 + 1) * 512)
        out.append((a, nxt))
        a = nxt
    return out



def build_program(cfg):
    TD = "S" if cfg.get("TDUAL", False) else "Q"
    TD2 = "Q"
    T, H, NH, HID = cfg["T"], cfg["H"], cfg["NH"], cfg["HID"]
    QA_T, BCH, FCH = cfg["QA_T"], cfg["BCH"], cfg["FCH"]
    NT = T // P
    HC = H // P
    HIDC = HID // P
    HS = H // NH
    R = HS // 2
    assert HS == 64 and R == 32 and H == NH * 64
    QA = QA_T * P
    MAXOWN = max(QA_T, NT - QA_T)
    SL = min(1024, HID)
    NSL = HID // SL
    SEG1 = min(512, HID)
    SEG2 = min(512, H)
    BNW = min(512, H)
    BNW2 = min(512, HID)
    SCALE = 1.0 / math.sqrt(HS)
    NHP = NH // 2
    assert NHP == HC
    MAXW = cfg.get("MAXW", 2048)
    GRP = 4
    W_B = BCH * P
    assert W_B <= 512

    nc = bass.Bass("TRN2", target_bir_lowering=False)

    dt = lambda n, s, d, k: nc.dram_tensor(n, s, d, kind=k).ap()
    xs = dt("xs", [T, H], F32, "ExternalInput")
    WSRC = {w: dt("W" + w, [H, H], F32, "ExternalInput") for w in "qkvo"}
    BSRC = {w: dt("b" + w, [H], F32, "ExternalInput") for w in "qkvo"}
    g1 = dt("g1", [H], F32, "ExternalInput")
    b1ln = dt("b1ln", [H], F32, "ExternalInput")
    w1 = dt("w1", [HID, H], F32, "ExternalInput")
    b1 = dt("b1", [HID], F32, "ExternalInput")
    g2 = dt("g2", [HID], F32, "ExternalInput")
    b2ln = dt("b2ln", [HID], F32, "ExternalInput")
    w2 = dt("w2", [H, HID], F32, "ExternalInput")
    b2 = dt("b2", [H], F32, "ExternalInput")
    ntab1 = dt("ntab1", [T, R], F32, "ExternalInput")
    ntab2 = dt("ntab2", [T, R], F32, "ExternalInput")
    out = dt("out", [MAXOWN * P, H], F32, "ExternalOutput")

    xhsc = dt("xhsc", [T, H], BF16, "Internal")
    krsc = dt("krsc", [T, NH * 64], BF16, "Internal")
    qrsc = dt("qrsc", [MAXOWN * P, NH * 64], BF16, "Internal")
    vsc = dt("vsc", [NH * P, NT * 65], BF16, "Internal")
    x2s = dt("x2s", [MAXOWN * P, H], F32, "Internal")
    xfsc = dt("xfsc", [MAXOWN * P, H], BF16, "Internal")
    hsc = dt("hsc", [MAXOWN * P, HID], BF16, "Internal")
    t1sc = dt("t1sc", [HID, H], BF16, "Internal")
    t2sc = dt("t2sc", [H, HID], BF16, "Internal")
    t1Ts = dt("t1Ts", [P, H // P * HID], BF16, "Internal")
    t2Ts = dt("t2Ts", [P, HID // P * H], BF16, "Internal")
    mkdr = dt("mkdr", [16, T], BF16, "Internal")
    wqsc = dt("wqsc", [4 * H, H], BF16, "Internal")

    LN1E4_OVER_R = math.log(1e-4) / R
    WOFF = {"q": 0, "k": H, "v": 2 * H, "o": 3 * H}

    with ExitStack() as gst:
        GE = gst.enter_context
        sems = {n: GE(nc.semaphore(n))
                for n in ("et", "ev", "es", "eg", "eq")}
        for i in range(88):
            sems[f"d{i}"] = GE(nc.semaphore(f"d{i}"))
        streams = {}
        pid = nc.partition_id(list(mybir.ALL_ENGINES))

        cidx = GE(nc.sbuf_tensor("cidx", [P, P], F32))
        pidx = GE(nc.sbuf_tensor("pidx", [P, 1], F32))
        tri01 = GE(nc.sbuf_tensor("tri01", [P, P], BF16))
        identf = GE(nc.sbuf_tensor("identf", [P, P], F32))
        onesf = GE(nc.sbuf_tensor("onesf", [16, P], F32))
        onescol = GE(nc.sbuf_tensor("onescol", [P, 1], F32))
        sc128 = GE(nc.sbuf_tensor("sc128", [P, 2], F32))
        rc128 = GE(nc.sbuf_tensor("rc128", [P, 2], F32))
        qtmp = GE(nc.sbuf_tensor("qtmp", [1, 8], F32))
        qacc = GE(nc.sbuf_tensor("qacc", [P, 3], F32))
        g1c = GE(nc.sbuf_tensor("g1c", [P, HC], F32))
        g2c = GE(nc.sbuf_tensor("g2c", [P, HIDC], F32))
        b1lz = GE(nc.sbuf_tensor("b1lz", [P, 32 + HC], BF16))
        b2lz = GE(nc.sbuf_tensor("b2lz", [P, 32 + HIDC], BF16))
        onesK = GE(nc.sbuf_tensor("onesK", [33, 512], BF16))
        onesZ = GE(nc.sbuf_tensor("onesZ", [P, 33], BF16))
        c2pre = GE(nc.sbuf_tensor("c2pre", [1, H], F32))
        mu1 = GE(nc.sbuf_tensor("mu1", [P, MAXOWN], F32))
        rr1 = GE(nc.sbuf_tensor("rr1", [P, MAXOWN], F32))
        brows = {w: GE(nc.sbuf_tensor(f"br_{w}", [33, H], BF16))
                 for w in "qkv"}
        boB = GE(nc.sbuf_tensor("boB", [P, H], BF16))
        constt = GE(nc.sbuf_tensor("constt", [P, 2], F32))
        nc.const_aps.register(F32, LN_EPS, constt[:, 0:1])
        nc.const_aps.register(F32, 1e-30, constt[:, 1:2])

        kbA = Kb(nc, sems, streams)
        kbB = Kb(nc, sems, streams)

        def rsqrt_sc(kb, dst, dst_key, src_ap, src_keys):
            kb.op("S", "activation", dst[:, 1:2], src_ap, AF.Ln,
                  bias=LN_EPS, reads=src_keys, writes=(dst_key,))
            kb.op("S", "activation", dst[:, 0:1], dst[:, 1:2], AF.Exp,
                  scale=-0.5, reads=(dst_key,), writes=(dst_key,))

        stTab = ExitStack()
        TE = stTab.enter_context
        cosT = TE(nc.sbuf_tensor("cosT", [P, R * NT], BF16, side="right"))
        sinT = TE(nc.sbuf_tensor("sinT", [P, R * NT], BF16, side="right"))
        wTq = [TE(nc.sbuf_tensor(f"wqT{hc}", [P, H], BF16, side="right"))
               for hc in range(HC)]
        wTk = [TE(nc.sbuf_tensor(f"wkT{hc}", [P, H], BF16, side="right"))
               for hc in range(HC)]
        wTv = [TE(nc.sbuf_tensor(f"wvT{hc}", [P, H], BF16, side="right"))
               for hc in range(HC)]
        wT = {"q": wTq, "k": wTk, "v": wTv}

        stLN = ExitStack()
        EL = stLN.enter_context
        xt = [EL(nc.sbuf_tensor(f"A_xt{i}", [P, H], F32, side="right"))
              for i in range(2)]
        bns = EL(nc.sbuf_tensor("A_bns", [P, 6 * (H // BNW)], F32,
                                side="right"))
        mv = EL(nc.sbuf_tensor("A_mv", [P, 2], F32, side="right"))
        sr = EL(nc.sbuf_tensor("A_sr", [P, 2], F32, side="right"))
        xh = EL(nc.sbuf_tensor("A_xh", [P, H], BF16, side="right"))

        def emit_ln_tile(kb, j):
            jb = j % 2
            xkey = f"A_xt{jb}"
            kb.dma("Q", xt[jb][:, :], xs[j * P:(j + 1) * P, :],
                   stream=xkey, reads=(xkey,), writes=(xkey,))
            for c in range(H // BNW):
                kb.op("V", "bn_stats", bns[:, c * 6:(c + 1) * 6],
                      xt[jb][:, c * BNW:(c + 1) * BNW],
                      reads=(xkey,), writes=("A_bns",))
            kb.op("V", "bn_aggr", mv[:, :], bns[:, :],
                  reads=("A_bns",), writes=("A_mv",))
            rsqrt_sc(kb, sr, "A_sr", mv[:, 1:2], ("A_mv",))
            kb.op("V", "tensor_scalar", xh[:, :], xt[jb][:, :],
                  mv[:, 0:1], sr[:, 0:1], ALU.subtract, ALU.mult,
                  reads=(xkey, "A_mv", "A_sr"), writes=("A_xh",))
            kb.dma("G", xhsc[j * P:(j + 1) * P, :], xh[:, :],
                   stream="xhsc", reads=("A_xh",),
                   writes=(f"xhsc_f{j}",))


        with ExitStack() as st0:
            E0 = st0.enter_context
            theta = E0(nc.sbuf_tensor("theta", [1, R], F32))
            irow = E0(nc.sbuf_tensor("irow", [1, R], F32))
            thetab = E0(nc.sbuf_tensor("thetab", [P, R], F32))
            tcol = E0(nc.sbuf_tensor("tcol", [P, 1], F32))
            angt = E0(nc.sbuf_tensor("angt", [P, R], F32))
            angc = E0(nc.sbuf_tensor("angc", [P, R], F32))
            n1t = E0(nc.sbuf_tensor("n1t", [P, R], F32))
            n2t = E0(nc.sbuf_tensor("n2t", [P, R], F32))
            wstg = [E0(nc.sbuf_tensor(f"wstg{i}", [P, H], BF16))
                    for i in range(2)]
            brow = E0(nc.sbuf_tensor("brow", [1, H], F32))
            lrow = E0(nc.sbuf_tensor("lrow", [P, max(HC, HIDC)], F32))
            pp0 = E0(nc.psum_tensor("pp0", [P, 512], F32))

            kb = kbA
            kb.op("G", "iota", cidx[:, :], [[1, P]], channel_multiplier=0,
                  allow_small_or_imprecise_dtypes=True, writes=("cidx",))
            kb.op("G", "iota", pidx[:, :], [[1, 1]], channel_multiplier=1,
                  allow_small_or_imprecise_dtypes=True, writes=("pidx",))
            kb.op("G", "iota", irow[0:1, :], [[1, R]], channel_multiplier=0,
                  allow_small_or_imprecise_dtypes=True, writes=("irow",))
            kb.op("V", "tensor_scalar", identf[:, :], cidx[:, :], pidx[:, :],
                  None, ALU.is_equal, reads=("cidx", "pidx"),
                  writes=("identf",))
            kb.op("V", "tensor_scalar", tri01[:, :], cidx[:, :], pidx[:, :],
                  None, ALU.is_ge,
                  reads=("cidx", "pidx"), writes=("tri01",))
            kb.op("V", "memset", onesf[:, :], 1.0, writes=("onesf",))
            kb.op("V", "memset", onesK[:, :], 0.0, writes=("onesK",))
            kb.op("V", "memset", onesK[0:1, :], 1.0, reads=("onesK",),
                  writes=("onesK",))
            kb.op("V", "memset", onesK[32:33, :], 1.0, reads=("onesK",),
                  writes=("onesK",))
            kb.op("V", "memset", onesZ[:, :], 0.0, writes=("onesZ",))
            kb.op("V", "memset", onesZ[:, 0:1], 1.0, reads=("onesZ",),
                  writes=("onesZ",))
            kb.op("V", "memset", onescol[:, :], 1.0, writes=("onescol",))
            kb.op("V", "memset", qacc[:, :], 0.0, writes=("qaccs",))
            kb.op("V", "memset", constt[:, 0:1], LN_EPS, writes=("constt",))
            kb.op("V", "memset", constt[:, 1:2], 1e-30, reads=("constt",),
                  writes=("constt",))
            kb.op("S", "activation", theta[0:1, :], irow[0:1, :], AF.Exp,
                  scale=LN1E4_OVER_R, reads=("irow",), writes=("theta",))
            kb.mm(pp0[0:P, 0:R], onesf[0:1, 0:P], theta[0:1, :],
                  start=True, stop=True, psum="pp0", reads=("onesf", "theta"))
            kb.op("V", "tensor_copy", out=thetab[:, :], in_=pp0[0:P, 0:R],
                  reads=("pp0",), writes=("thetab",))

            for j in range(NT):
                cs = slice(j * R, (j + 1) * R)
                kb.dma("Q", n1t[:, :], ntab1[j * P:(j + 1) * P, :],
                       stream="n1t", reads=("n1t",), writes=("n1t",))
                kb.dma("Q", n2t[:, :], ntab2[j * P:(j + 1) * P, :],
                       stream="n2t", reads=("n2t",), writes=("n2t",))
                kb.op("V", "tensor_scalar", tcol[:, :], pidx[:, :],
                      float(j * P + 1), None, ALU.add,
                      reads=("pidx",), writes=("tcol",))
                kb.op("V", "tensor_scalar", angt[:, :], thetab[:, :],
                      tcol[:, :], None, ALU.mult,
                      reads=("thetab", "tcol"), writes=("angt",))
                kb.op("V", "tensor_scalar", angc[:, :], angt[:, :],
                      math.pi / 2.0, None, ALU.add,
                      reads=("angt",), writes=("angc",))
                for c in (C1, C2, C3):
                    kb.op("V", "scalar_tensor_tensor", angt[:, :], n1t[:, :],
                          -c, angt[:, :], ALU.mult, ALU.add,
                          reads=("n1t", "angt"), writes=("angt",))
                    kb.op("V", "scalar_tensor_tensor", angc[:, :], n2t[:, :],
                          -c, angc[:, :], ALU.mult, ALU.add,
                          reads=("n2t", "angc"), writes=("angc",))
                kb.op("S", "activation", sinT[:, cs], angt[:, :], AF.Sin,
                      reads=("angt",), writes=(f"sinT{j}",))
                kb.op("S", "activation", cosT[:, cs], angc[:, :], AF.Sin,
                      reads=("angc",), writes=(f"cosT{j}",))

            for w in "qkvo":
                for oc in range(HC):
                    sg = wstg[oc % 2]
                    sk = f"wstg{oc % 2}"
                    kb.dma("G", sg[:, :], WSRC[w][oc * P:(oc + 1) * P, :],
                           stream=sk, reads=(sk,), writes=(sk,))
                    kb.dma("G",
                           wqsc[WOFF[w] + oc * P:WOFF[w] + (oc + 1) * P, :],
                           sg[:, :], stream="wqsc", reads=(sk,))
            kb.mark("wqsc_f", "wqsc")
            for w in "qkv":
                for hc in range(HC):
                    kb.dma("Q", wT[w][hc][:, :],
                           wqsc[WOFF[w]:WOFF[w] + H, hc * P:(hc + 1) * P],
                           stream=f"wT{hc % 4}", transpose=True,
                           reads=(f"w{w}T{hc}", "wqsc_f"),
                           writes=(f"w{w}T{hc}",))
            for w in "qkv":
                kb.op("V", "memset", brows[w][:, :], 0.0,
                      writes=(f"br_{w}",))
                kb.dma("Q", brow[0:1, :], BSRC[w][None, :],
                       stream="brow", reads=("brow",), writes=("brow",))
                kb.op("V", "tensor_copy", out=brows[w][0:1, :],
                      in_=brow[0:1, :], reads=("brow", f"br_{w}"),
                      writes=(f"br_{w}",))
                kb.op("V", "tensor_tensor", out=brows[w][32:33, :],
                      in0=brow[0:1, :], in1=brows[w][0:1, :],
                      op=ALU.subtract, reads=("brow", f"br_{w}"),
                      writes=(f"br_{w}",))
            for w, bt, bkey in (("o", boB, "boB"),):
                kb.dma("Q", brow[0:1, :], BSRC[w][None, :],
                       stream="brow2", reads=("brow",), writes=("brow",))
                for a, b in _bank_pieces(0, H):
                    kb.mm(pp0[0:P, 0:b - a], onesf[0:1, 0:P],
                          brow[0:1, a:b], start=True, stop=True,
                          psum="pp0", reads=("brow", "onesf"))
                    kb.op("V", "tensor_copy", out=bt[:, a:b],
                          in_=pp0[0:P, 0:b - a], reads=("pp0",),
                          writes=(bkey,))

            kb.dma("Q", g1c[:, :], g1.rearrange("(c p) -> p c", p=P),
                   stream="g1c", writes=("g1c",), slow=True)
            kb.dma("Q", g2c[:, :], g2.rearrange("(c p) -> p c", p=P),
                   stream="g2c", writes=("g2c",), slow=True)
            kb.dma("Q", lrow[:, 0:HC], b1ln.rearrange("(c p) -> p c", p=P),
                   stream="lrow", reads=("lrow",), writes=("lrow",),
                   slow=True)
            kb.op("V", "memset", b1lz[:, :], 0.0, writes=("b1lz",))
            kb.op("V", "tensor_copy", out=b1lz[:, 0:HC], in_=lrow[:, 0:HC],
                  reads=("lrow", "b1lz"), writes=("b1lz",))
            kb.op("V", "tensor_tensor", out=b1lz[:, 32:32 + HC],
                  in0=lrow[:, 0:HC],
                  in1=b1lz[:, 0:HC], op=ALU.subtract,
                  reads=("lrow", "b1lz"), writes=("b1lz",))
            kb.dma("Q", lrow[:, 0:HIDC], b2ln.rearrange("(c p) -> p c", p=P),
                   stream="lrow", reads=("lrow",), writes=("lrow",),
                   slow=True)
            kb.op("V", "memset", b2lz[:, :], 0.0, writes=("b2lz",))
            kb.op("V", "tensor_copy", out=b2lz[:, 0:HIDC],
                  in_=lrow[:, 0:HIDC],
                  reads=("lrow", "b2lz"), writes=("b2lz",))
            kb.op("V", "tensor_tensor", out=b2lz[:, 32:32 + HIDC],
                  in0=lrow[:, 0:HIDC],
                  in1=b2lz[:, 0:HIDC], op=ALU.subtract,
                  reads=("lrow", "b2lz"), writes=("b2lz",))
            for j in range(QA_T):
                emit_ln_tile(kb, j)
            kb.fence_dma("V")
            kb.fence_dma("S")
            kb.fence_dma("G")

        nc.all_engine_barrier()
        kbA.barrier_reset()
        kbB.clone_counters_from(kbA)

        stAB = ExitStack()
        AE = stAB.enter_context
        krt = [AE(nc.sbuf_tensor(f"krt{c}", [P, T], BF16)) for c in range(NHP)]
        qrt = [AE(nc.sbuf_tensor(f"qrt{c}", [P, MAXOWN * P], BF16))
               for c in range(NHP)]
        mkacc = AE(nc.sbuf_tensor("mkacc", [P, NH], F32))

        stAL = ExitStack()
        EA = stAL.enter_context
        xhTg = [[EA(nc.sbuf_tensor(f"A_xhT{g}_{hc}", [P, GRP * P], BF16,
                                   side="right")) for hc in range(HC)]
                for g in range(2)]
        rt1 = EA(nc.sbuf_tensor("A_rt1", [P, H // 2], F32, side="right"))
        rt2 = EA(nc.sbuf_tensor("A_rt2", [P, H // 2], F32, side="right"))
        krb = EA(nc.sbuf_tensor("A_krb", [P, NH * 64], BF16, side="right"))
        sq16 = EA(nc.sbuf_tensor("A_sq16", [P, NH], F32, side="right"))
        sqQ = EA(nc.sbuf_tensor("A_sqQ", [P, NH], F32, side="right"))
        sq2 = EA(nc.sbuf_tensor("A_sq2", [P, NH], F32, side="right"))
        vst = [EA(nc.sbuf_tensor(f"A_vst{i}", [P, NH * 65], BF16,
                                 side="right")) for i in range(2)]
        ppr = [EA(nc.psum_tensor(f"A_ppr{i}", [P, 512], F32)) for i in range(4)]

        def phase_a(kb, role):
            own_t0 = 0 if role == 0 else QA_T
            own_nt = QA_T if role == 0 else NT - QA_T
            kv_nt = QA_T if role == 0 else NT

            kb.op("V", "memset", mkacc[:, :], 0.0, writes=("mkacc",))
            kb.op("G", "memset", krb[:, :], 0.0, writes=("A_krb",))
            for i in range(2):
                kb.op("G", "memset",
                      vst[i].rearrange("p (h c) -> p h c", h=NH)[:, :, 64:65],
                      1.0, writes=(f"A_vst{i}_ones",))

            def proj(w, gslot, jj, psum0):
                for a, b in _bank_pieces(0, H):
                    i = psum0 + (a // 512) % 2
                    pk = f"A_ppr{i}"
                    pr = ppr[i]
                    for hc in range(HC):
                        kb.mm(pr[0:P, 0:b - a],
                              xhTg[gslot][hc][:, jj * P:(jj + 1) * P],
                              wT[w][hc][:, a:b],
                              start=(hc == 0), stop=False, psum=pk,
                              reads=(f"A_xhTg{gslot}_{hc}", f"w{w}T{hc}"))
                    kb.mm(pr[0:P, 0:b - a], onesK[0:33, 0:P],
                          brows[w][0:33, a:b], start=False, stop=True,
                          psum=pk, reads=("onesK", f"br_{w}"))

            def rope_psum(kb, psum0, dst3, dstkey, j):
                if cfg.get("NOROPE", False):
                    for a, b in _bank_pieces(0, H):
                        i = psum0 + (a // 512) % 2
                        pk = f"A_ppr{i}"
                        kb.op("V", "tensor_copy",
                              out=rt1[:, a // 2:a // 2 + (b - a) // 2],
                              in_=ppr[i][0:P, 0:(b - a) // 2],
                              reads=(pk,), writes=("A_rt1",))
                    kb.op("V", "tensor_copy", out=dst3,
                          in_=rt1.rearrange("p (h r) -> p h r", h=NH),
                          reads=("A_rt1",), writes=(dstkey,))
                    return
                cs = slice(j * R, (j + 1) * R)
                cb = cosT[:, cs].rearrange("p (o r) -> p o r", o=1)\
                    .broadcast_to((P, NH, R))
                sb = sinT[:, cs].rearrange("p (o r) -> p o r", o=1)\
                    .broadcast_to((P, NH, R))
                for a, b in _bank_pieces(0, H):
                    i = psum0 + (a // 512) % 2
                    pk = f"A_ppr{i}"
                    hper = (b - a) // 64
                    h0 = a // 64
                    se = ppr[i][0:P, 0:b - a].rearrange(
                        "p (h r two) -> p h r two", h=hper, two=2)
                    hs_ = slice(h0, h0 + hper)
                    kb.op("V", "tensor_tensor",
                          out=rt1.rearrange("p (h r) -> p h r", h=NH)[:, hs_],
                          in0=se[:, :, :, 0], in1=cb[:, hs_], op=ALU.mult,
                          reads=(pk, f"cosT{j}"), writes=("A_rt1",))
                    kb.op("V", "tensor_tensor",
                          out=rt2.rearrange("p (h r) -> p h r", h=NH)[:, hs_],
                          in0=se[:, :, :, 1], in1=sb[:, hs_], op=ALU.mult,
                          reads=(pk, f"sinT{j}"), writes=("A_rt2",))
                kb.op("V", "tensor_tensor", out=dst3,
                      in0=rt1.rearrange("p (h r) -> p h r", h=NH),
                      in1=rt2.rearrange("p (h r) -> p h r", h=NH),
                      op=ALU.subtract,
                      reads=("A_rt1", "A_rt2"), writes=(dstkey,))

            groups = [(g0, min(GRP, kv_nt - g0))
                      for g0 in range(0, kv_nt, GRP)]
            if cfg.get("AGRP1", False):
                groups = groups[:1]
            for gi, (g0, gn) in enumerate(groups):
                gslot = gi % 2
                for jj in range(gn):
                    j = g0 + jj
                    if j >= QA_T:
                        emit_ln_tile(kb, j)
                for hc in range(HC):
                    kb.dma(TD2 if hc % 2 == 0 else TD,
                           xhTg[gslot][hc][:, 0:gn * P],
                           xhsc[g0 * P:(g0 + gn) * P, hc * P:(hc + 1) * P],
                           stream=f"xhT{gslot}_{hc % 2}", transpose=True,
                           reads=tuple([f"A_xhTg{gslot}_{hc}"] +
                                       [f"xhsc_f{g0 + t}" for t in range(gn)]),
                           writes=(f"A_xhTg{gslot}_{hc}",))
                if cfg.get("NOPROJ", False):
                    continue
                for jj in range(gn):
                    j = g0 + jj
                    jb = j % 2
                    proj("k", gslot, jj, 0)
                    krz = krb.rearrange("p (h z) -> p h z", z=64)
                    rope_psum(kb, 0, krz[:, :, 0:R], "A_krb", j)
                    kb.op("V", "tensor_tensor",
                          out=rt2.rearrange("p (h r) -> p h r", h=NH),
                          in0=krz[:, :, 0:R],
                          in1=krz[:, :, 0:R], op=ALU.mult,
                          reads=("A_krb",), writes=("A_rt2",))
                    kb.op("V", "tensor_reduce", sq16[:, :],
                          rt2.rearrange("p (h r) -> p h r", h=NH),
                          axis=AX.X, op=ALU.add,
                          reads=("A_rt2",), writes=("A_sq16",))
                    kb.op("V", "tensor_tensor", out=mkacc[:, :],
                          in0=mkacc[:, :], in1=sq16[:, :], op=ALU.max,
                          reads=("mkacc", "A_sq16"), writes=("mkacc",))
                    kb.dma("G", krsc[j * P:(j + 1) * P, :], krb[:, :],
                           stream="krsc",
                           reads=("A_krb", f"krsc_f{j}"),
                           writes=(f"krsc_f{j}",))
                    if cfg.get("ONLYK", False):
                        continue
                    vb = vst[jb]
                    for a, b in _bank_pieces(0, H):
                        i = 2 + (a // 512) % 2
                        pk = f"A_ppr{i}"
                        pr = ppr[i]
                        for hc in range(HC):
                            kb.mm(pr[0:P, 0:b - a],
                                  xhTg[gslot][hc][:, jj * P:(jj + 1) * P],
                                  wT["v"][hc][:, a:b],
                                  start=(hc == 0), stop=False,
                                  psum=pk,
                                  reads=(f"A_xhTg{gslot}_{hc}", f"wvT{hc}"))
                        kb.mm(pr[0:P, 0:b - a], onesK[0:33, 0:P],
                              brows["v"][0:33, a:b], start=False, stop=True,
                              psum=pk, reads=("onesK", "br_v"))
                        h0 = a // 64
                        nh_ = (b - a) // 64
                        kb.op("V", "tensor_copy",
                              out=vb.rearrange("p (h c) -> p h c", h=NH)[
                                  :, h0:h0 + nh_, 0:64],
                              in_=pr[0:P, 0:b - a].rearrange(
                                  "p (h c) -> p h c", c=64),
                              reads=(pk,), writes=(f"A_vst{jb}",))
                    if not cfg.get("NOV", False):
                        kb.dma("G",
                               vsc.rearrange("(h p) m -> p h m", p=P)[
                                   :, :, j * 65:(j + 1) * 65],
                               vb.rearrange("p (h c) -> p h c", h=NH),
                               stream=f"vsc{jb}",
                               reads=(f"A_vst{jb}", f"A_vst{jb}_ones"),
                               writes=(f"vsc_f{j}",))
                    if own_t0 <= j < own_t0 + own_nt:
                        jo = j - own_t0
                        proj("q", gslot, jj, 0)
                        krz = krb.rearrange("p (h z) -> p h z", z=64)
                        rope_psum(kb, 0, krz[:, :, 0:R], "A_krb", j)
                        kb.op("V", "tensor_tensor",
                              out=rt2.rearrange("p (h r) -> p h r", h=NH),
                              in0=krz[:, :, 0:R],
                              in1=krz[:, :, 0:R], op=ALU.mult,
                              reads=("A_krb",), writes=("A_rt2",))
                        kb.op("V", "tensor_reduce", sqQ[:, :],
                              rt2.rearrange("p (h r) -> p h r", h=NH),
                              axis=AX.X, op=ALU.add,
                              reads=("A_rt2",), writes=("A_sqQ",))
                        kb.op("S", "activation", sq2[:, :], sqQ[:, :],
                              AF.Ln, bias=1e-30,
                              reads=("A_sqQ",), writes=("A_sq2",))
                        with nc.allow_low_precision(
                                reason="softmax shift bound; slack is huge"):
                            kb.op("S", "activation", krz[:, :, 32:33],
                                  sq2.rearrange("p (h o) -> p h o", o=1),
                                  AF.Exp, scale=0.5,
                                  reads=("A_sq2",), writes=("A_krb",))
                        kb.dma("G", qrsc[jo * P:(jo + 1) * P, :], krb[:, :],
                               stream="qrsc",
                               reads=("A_krb", f"qrsc_f{jo}"),
                               writes=(f"qrsc_f{jo}",))
            kb.fence_dma("V")
            kb.fence_dma("S")

        if cfg.get("STOP", "") == "0":
            def phase_a(kb, role):
                kb.fence_dma("G")
        with nc.If(pid < cfg["ROLE_SPLIT"]):
            phase_a(kbA, 0)
        with nc.Else():
            phase_a(kbB, 1)
        nc.all_engine_barrier()
        kbA.barrier_reset()
        kbB.barrier_reset()
        stAL.close()

        stA2 = ExitStack()
        EA2 = stA2.enter_context
        mkt = EA2(nc.sbuf_tensor("A2_mkt", [16, P], F32, side="right"))
        mk1 = EA2(nc.sbuf_tensor("A2_mk1", [16, 4], F32, side="right"))
        mkd = EA2(nc.sbuf_tensor("A2_mkd", [16, 16], F32, side="right"))
        mkb = EA2(nc.sbuf_tensor("A2_mkb", [P, NH], F32, side="right"))
        mks = EA2(nc.sbuf_tensor("A2_mks", [16, T], BF16, side="right"))
        pps2 = EA2(nc.psum_tensor("A2_pps", [P, 512], F32))

        def phase_a2(kb, role):
            own_t0 = 0 if role == 0 else QA_T
            own_nt = QA_T if role == 0 else NT - QA_T
            kv_nt = QA_T if role == 0 else NT
            pps = pps2
            kb.op("T", "transpose", pps[0:NH, 0:P], mkacc[:, :], identf[:, :],
                  reads=("mkacc", "identf"), writes=("A2_pps",))
            kb.op("V", "tensor_copy", out=mkt[0:NH, :], in_=pps[0:NH, 0:P],
                  reads=("A2_pps",), writes=("A2_mkt",))
            kb.op("V", "tensor_reduce", mk1[0:NH, 0:1], mkt[0:NH, :],
                  axis=AX.X, op=ALU.max, reads=("A2_mkt",), writes=("A2_mk1",))
            kb.op("S", "activation", mk1[0:NH, 1:2], mk1[0:NH, 0:1],
                  AF.Ln, bias=1e-30, reads=("A2_mk1",), writes=("A2_l",))
            kb.op("S", "activation", mk1[0:NH, 2:3], mk1[0:NH, 1:2],
                  AF.Exp, scale=0.5, reads=("A2_l",), writes=("A2_s",))
            kb.op("V", "tensor_scalar", mk1[0:NH, 3:4], mk1[0:NH, 2:3],
                  -1.0, None, ALU.mult, reads=("A2_s",), writes=("A2_n",))
            kb.op("V", "tensor_scalar", mkd[0:NH, 0:NH],
                  identf[0:NH, 0:NH], mk1[0:NH, 3:4], None, ALU.mult,
                  reads=("identf", "A2_n"), writes=("A2_mkd",))
            kb.mm(pps[0:P, 0:NH], onesf[0:NH, :], mkd[0:NH, 0:NH],
                  start=True, stop=True, psum="A2_pps",
                  reads=("onesf", "A2_mkd"))
            kb.op("V", "tensor_copy", out=mkb[:, :], in_=pps[0:P, 0:NH],
                  reads=("A2_pps",), writes=("A2_mkb",))
            kb.op("V", "memset", mks[0:NH, 0:kv_nt * P], 1.0,
                  writes=("A2_mks",))
            kb.op("V", "tensor_scalar", mks[0:NH, 0:kv_nt * P],
                  mks[0:NH, 0:kv_nt * P], mk1[0:NH, 3:4], None, ALU.mult,
                  reads=("A2_mks", "A2_n"), writes=("A2_mks",))
            kb.dma("G", mkdr[0:NH, 0:kv_nt * P], mks[0:NH, 0:kv_nt * P],
                   stream="mkdr", reads=("A2_mks",))
            kb.mark("mkdr_f", "mkdr")
            for c in range(NHP):
                kb.dma(TD2 if c % 2 == 0 else TD,
                       krt[c][0:P, 0:kv_nt * P],
                       krsc[0:kv_nt * P, c * P:(c + 1) * P],
                       stream=f"krtL{c % 4}", transpose=True,
                       reads=tuple([f"krt_{c}"] +
                                   [f"krsc_f{j}" for j in range(kv_nt)]),
                       writes=(f"krt_{c}",))
                kb.dma(TD if c % 2 == 0 else TD2,
                       qrt[c][0:P, 0:own_nt * P],
                       qrsc[0:own_nt * P, c * P:(c + 1) * P],
                       stream=f"qrtL{c % 4}", transpose=True,
                       reads=tuple([f"qrt_{c}"] +
                                   [f"qrsc_f{j}" for j in range(own_nt)]),
                       writes=(f"qrt_{c}",))
            for c in range(NHP):
                for b0, h in ((32, 2 * c), (96, 2 * c + 1)):
                    kb.dma("Q", krt[c][b0:b0 + 1, 0:kv_nt * P],
                           mkdr[h:h + 1, 0:kv_nt * P],
                           stream=f"mkl{c % 4}",
                           reads=(f"krt_{c}", "mkdr_f"),
                           writes=(f"krt_{c}",))
            kb.fence_dma("Q")
            kb.fence_dma("S")

        if cfg.get("STOP", "") in ("0", "A"):
            def phase_a2(kb, role):
                kb.fence_dma("G")
        with nc.If(pid < cfg["ROLE_SPLIT"]):
            phase_a2(kbA, 0)
        with nc.Else():
            phase_a2(kbB, 1)
        nc.all_engine_barrier()
        kbA.barrier_reset()
        kbB.barrier_reset()
        stA2.close()
        stLN.close()
        stTab.close()

        stB = ExitStack()
        EB = stB.enter_context
        vbuf = [EB(nc.sbuf_tensor(f"vbuf{i}", [P, NT * 65], BF16,
                                  side="right")) for i in range(2)]
        ptp = [EB(nc.sbuf_tensor(f"ptp{i}", [P, 2 * W_B], BF16, side="right"))
               for i in range(2)]
        yts = [EB(nc.sbuf_tensor(f"yts{c}", [P, W_B], BF16, side="right"))
               for c in range(NHP)]
        rrow = EB(nc.sbuf_tensor("rrow", [1, max(W_B, H)], F32, side="right"))
        rsb = EB(nc.sbuf_tensor("rsb", [64, W_B], F32, side="right"))
        atp = EB(nc.sbuf_tensor("atp", [P, H], F32, side="right"))
        x2t = EB(nc.sbuf_tensor("x2t", [P, H], F32, side="right"))
        xre = EB(nc.sbuf_tensor("xre", [P, H], F32, side="right"))
        bnsB = EB(nc.sbuf_tensor("bnsB", [P, 6 * (H // BNW)], F32,
                                 side="right"))
        mvB = EB(nc.sbuf_tensor("mvB", [P, 2], F32, side="right"))
        srB = EB(nc.sbuf_tensor("srB", [P, 2], F32, side="right"))
        mvC = EB(nc.sbuf_tensor("mvC", [P, 6], F32, side="right"))
        qst = [EB(nc.sbuf_tensor(f"qst{i}", [P, MAXW], F32, side="right"))
               for i in range(3)]
        qg = EB(nc.sbuf_tensor("qg", [P, MAXW], BF16, side="right"))
        qt16 = EB(nc.sbuf_tensor("qt16", [P, MAXW], BF16, side="right"))
        tstg = EB(nc.sbuf_tensor("tstg", [P, max(H, HID)], BF16,
                                 side="right"))
        woT = [EB(nc.sbuf_tensor(f"woT{hc}", [P, H], BF16, side="right"))
               for hc in range(HC)]
        psc = [EB(nc.psum_tensor(f"psc{i}", [P, 2 * W_B], F32))
               for i in range(2)]
        pyt = [EB(nc.psum_tensor(f"pyt{i}", [P, W_B], F32))
               for i in range(2)]
        c2preB = EB(nc.sbuf_tensor("c2preB", [33, H], F32, side="right"))
        pmo = EB(nc.psum_tensor("pmo", [P, 512], F32))
        wT["o"] = woT

        def phase_b(kb, role):
            own_t0 = 0 if role == 0 else QA_T
            own_nt = QA_T if role == 0 else NT - QA_T
            kb.op("V", "memset", c2preB[:, :], 0.0, writes=("c2preB",))

            for hc in range(HC):
                kb.dma("Q", woT[hc][:, :],
                       wqsc[WOFF["o"]:WOFF["o"] + H, hc * P:(hc + 1) * P],
                       stream=f"wT{hc % 4}", transpose=True,
                       reads=(f"woT{hc}", "wqsc_f"), writes=(f"woT{hc}",))

            mats = [(0, w1, HID // P, H, max(1, H // MAXW), t1sc),
                    (1, w2, H // P, HID, max(1, HID // MAXW), t2sc)]
            qsteps = []

            def q_ld(src, jt, cc, cw, slot):
                kb.dma("Q", qst[slot][:, 0:cw],
                       src[jt * P:(jt + 1) * P, cc * cw:(cc + 1) * cw],
                       stream=f"qst{slot}", reads=(f"qst{slot}",),
                       writes=(f"qst{slot}",))

            def q_reduce(mi, cw, slot):
                kb.op("V", "tensor_reduce", qacc[:, 0:1],
                      qst[slot][:, 0:cw],
                      axis=AX.X, op=ALU.add, apply_absolute_value=True,
                      reads=(f"qst{slot}",), writes=("qacc1",))
                kb.op("V", "tensor_tensor", out=qacc[:, 1 + mi:2 + mi],
                      in0=qacc[:, 1 + mi:2 + mi], in1=qacc[:, 0:1],
                      op=ALU.add, reads=("qacc1", "qaccs"), writes=("qaccs",))

            def q_fin():
                for mi, src, ntile, width, ncc, dstT in mats:
                    n_el = float(ntile * P * width)
                    kb.mm(pmo[0:1, 0:1], qacc[:, 1 + mi:2 + mi],
                          onescol[:, 0:1], start=True, stop=True, psum="pmo",
                          reads=("qaccs", "onescol"))
                    kb.op("V", "tensor_scalar", qtmp[0:1, 4 * mi:4 * mi + 1],
                          pmo[0:1, 0:1], 1.0 / n_el, 1e-5, ALU.mult, ALU.max,
                          reads=("pmo",), writes=(f"qtmp_m{mi}",))
                    kb.op("V", "reciprocal",
                          out=qtmp[0:1, 4 * mi + 1:4 * mi + 2],
                          in_=qtmp[0:1, 4 * mi:4 * mi + 1],
                          reads=(f"qtmp_m{mi}",), writes=(f"qtmp_s{mi}",))
                    kb.op("V", "reciprocal",
                          out=qtmp[0:1, 4 * mi + 2:4 * mi + 3],
                          in_=qtmp[0:1, 4 * mi + 1:4 * mi + 2],
                          reads=(f"qtmp_s{mi}",), writes=(f"qtmp_r{mi}",))
                    kb.mm(pmo[0:P, 0:1], onesf[0:1, 0:P],
                          qtmp[0:1, 4 * mi + 1:4 * mi + 2],
                          start=True, stop=True, psum="pmo",
                          reads=(f"qtmp_s{mi}", "onesf"))
                    kb.op("V", "tensor_copy", out=sc128[:, mi:mi + 1],
                          in_=pmo[0:P, 0:1], reads=("pmo",), writes=("sc128",))
                    kb.mm(pmo[0:P, 0:1], onesf[0:1, 0:P],
                          qtmp[0:1, 4 * mi + 2:4 * mi + 3],
                          start=True, stop=True, psum="pmo",
                          reads=(f"qtmp_r{mi}", "onesf"))
                    kb.op("V", "tensor_copy", out=rc128[:, mi:mi + 1],
                          in_=pmo[0:P, 0:1], reads=("pmo",), writes=("rc128",))

            def q_tern(mi, jt, cc, cw, dstT, slot):
                kb.op("V", "tensor_scalar", qg[:, 0:cw],
                      qst[slot][:, 0:cw],
                      sc128[:, mi:mi + 1], 0.5, ALU.mult, ALU.is_gt,
                      reads=(f"qst{slot}", "sc128"), writes=("qg",))
                kb.op("V", "tensor_scalar", qt16[:, 0:cw],
                      qst[slot][:, 0:cw],
                      sc128[:, mi:mi + 1], -0.5, ALU.mult, ALU.is_lt,
                      reads=(f"qst{slot}", "sc128"), writes=("qt16",))
                kb.op("V", "tensor_tensor", out=qt16[:, 0:cw],
                      in0=qg[:, 0:cw], in1=qt16[:, 0:cw], op=ALU.subtract,
                      reads=("qg", "qt16"), writes=("qt16",))
                kb.dma("G", dstT[jt * P:(jt + 1) * P,
                               cc * cw:(cc + 1) * cw],
                       qt16[:, 0:cw], stream=f"tsc{mi}", reads=("qt16",))

            def q_stage(mi, c):
                if mi == 0:
                    kb.dma("Q", tstg[:, 0:HID],
                           t1sc[0:HID, c * P:(c + 1) * P],
                           stream="tstg", transpose=True,
                           reads=("tstg", "tsc0_f"), writes=("tstg",))
                    kb.dma("G",
                           t1Ts.rearrange("p (s c m) -> p s c m",
                                          c=HC, m=SEG1)[:, :, c, :],
                           tstg[:, 0:HID].rearrange("p (s m) -> p s m",
                                                    m=SEG1),
                           stream="t1Ts", reads=("tstg",))
                else:
                    kb.dma("Q", tstg[:, 0:H],
                           t2sc[0:H, c * P:(c + 1) * P],
                           stream="tstg", transpose=True,
                           reads=("tstg", "tsc1_f"), writes=("tstg",))
                    for pidx, (a2, b2_) in enumerate(_bank_pieces(0, H)):
                        kb.mm(pmo[0:33, 0:b2_ - a2],
                              b2lz[:, c:c + 33], tstg[:, a2:b2_],
                              start=True, stop=True, psum="pmo",
                              reads=("b2lz", "tstg"))
                        kb.op("V", "tensor_tensor",
                              out=c2preB[0:33, a2:b2_],
                              in0=pmo[0:33, 0:b2_ - a2],
                              in1=c2preB[0:33, a2:b2_], op=ALU.add,
                              reads=("pmo", "c2preB"),
                              writes=("c2preB",))
                    kb.op("V", "tensor_scalar", tstg[:, 0:H],
                          tstg[:, 0:H], g2c[:, c:c + 1], None, ALU.mult,
                          reads=("tstg", "g2c"), writes=("tstg",))
                    kb.dma("G",
                           t2Ts.rearrange("p (s c m) -> p s c m",
                                          c=HIDC, m=SEG2)[:, :, c, :],
                           tstg[:, 0:H].rearrange("p (s m) -> p s m",
                                                  m=SEG2),
                           stream="t2Ts", reads=("tstg",))

            def c2pre_fin():
                for a2, b2_ in _bank_pieces(0, H):
                    kb.op("V", "tensor_copy", out=rrow[0:1, 0:b2_ - a2],
                          in_=c2preB[32:33, a2:b2_],
                          reads=("c2preB",), writes=("rrow",))
                    kb.op("V", "tensor_tensor", out=c2pre[0:1, a2:b2_],
                          in0=c2preB[0:1, a2:b2_],
                          in1=rrow[0:1, 0:b2_ - a2], op=ALU.add,
                          reads=("c2preB", "rrow"),
                          writes=("c2pre",))

            nsl_ = [0]

            def mkld(src, jt, cc, cw):
                slot = nsl_[0] % 3
                nsl_[0] += 1
                return (lambda: q_ld(src, jt, cc, cw, slot)), slot

            for mi, src, ntile, width, ncc, dstT in mats:
                for jt in range(ntile):
                    for cc in range(ncc):
                        ld, slot = mkld(src, jt, cc, width // ncc)
                        qsteps.append(
                            (ld, lambda mi=mi, cw=width // ncc,
                             slot=slot: q_reduce(mi, cw, slot)))
            qsteps.append((None, lambda: q_fin()))
            for mi, src, ntile, width, ncc, dstT in mats:
                for jt in range(ntile):
                    for cc in range(ncc):
                        ld, slot = mkld(src, jt, cc, width // ncc)
                        qsteps.append(
                            (ld, lambda mi=mi, jt=jt, cc=cc,
                             cw=width // ncc, dstT=dstT, slot=slot:
                             q_tern(mi, jt, cc, cw, dstT, slot)))
            qsteps.append((None, lambda: kb.mark("tsc0_f", "tsc0")))
            qsteps.append((None, lambda: kb.mark("tsc1_f", "tsc1")))
            for c in range(HC):
                qsteps.append((None, lambda c=c: q_stage(0, c)))
            for c in range(HIDC):
                qsteps.append((None, lambda c=c: q_stage(1, c)))
            qsteps.append((None, lambda: c2pre_fin()))
            qsteps.append((None, lambda: kb.mark("t1Ts_f", "t1Ts")))
            qsteps.append((None, lambda: kb.mark("t2Ts_f", "t2Ts")))
            qi = [0]
            li = [0]

            def run_q(n):
                while n > 0 and qi[0] < len(qsteps):
                    while li[0] < len(qsteps) and li[0] <= qi[0] + 2:
                        ld = qsteps[li[0]][0]
                        if ld is not None:
                            ld()
                        li[0] += 1
                    qsteps[qi[0]][1]()
                    qi[0] += 1
                    n -= 1

            def emit_avs(pend, h, Wq_, ktmax):
                pairs, tslot = pend
                for (kt, lt0, half) in pairs:
                    col0 = lt0 * P
                    kb.mm(pyt[h % 2][0:65, col0:Wq_],
                          vbuf[h % 2][:, kt * 65:(kt + 1) * 65],
                          ptp[tslot][:, half * W_B + col0:half * W_B + Wq_],
                          start=(kt == 0), stop=(kt == ktmax),
                          psum=f"pyt{h % 2}",
                          reads=(f"vbuf{h % 2}", f"ptp{tslot}"))

            chunks = []
            t0 = 0
            while t0 < own_nt:
                w = min(BCH, own_nt - t0)
                chunks.append((t0, w))
                t0 += w

            pend_norm = [None]
            nq_total = None
            for (ct0, Wt) in chunks:
                glob0 = own_t0 + ct0
                ktmax = glob0 + Wt - 1
                nkt = ktmax + 1
                Wq_ = Wt * P
                kb.dma("Q", vbuf[0][:, 0:nkt * 65],
                       vsc[0:P, 0:nkt * 65], stream="vbuf0",
                       reads=tuple(["vbuf0"] +
                                   [f"vsc_f{j}" for j in range(nkt)]),
                       writes=("vbuf0",))
                for h in range(NH):
                    if h + 1 < NH:
                        kb.dma("Q", vbuf[(h + 1) % 2][:, 0:nkt * 65],
                               vsc[(h + 1) * P:(h + 2) * P, 0:nkt * 65],
                               stream=f"vbuf{(h + 1) % 2}",
                               reads=tuple([f"vbuf{(h + 1) % 2}"] +
                                           [f"vsc_f{j}" for j in range(nkt)]),
                               writes=(f"vbuf{(h + 1) % 2}",))
                    c = h // 2
                    b0 = 64 * (h % 2)
                    kkeys = (f"krt_{c}", f"qrt_{c}")
                    pend = None
                    fl = 0
                    kt = 0
                    while kt < nkt:
                        lt0 = max(0, kt - glob0)
                        pairable = (kt + 1 < nkt and kt + 1 <= glob0
                                    and Wq_ == W_B)
                        group = [(kt, lt0, 0)]
                        if pairable:
                            group.append((kt + 1, 0, 1))
                        tslot = fl % 2
                        pk = f"psc{tslot}"
                        for (k2, l2, half) in group:
                            col0 = l2 * P
                            kb.mm(psc[tslot][0:P,
                                             half * W_B + col0:
                                             half * W_B + Wq_],
                                  krt[c][b0:b0 + 33, k2 * P:(k2 + 1) * P],
                                  qrt[c][b0:b0 + 33,
                                         ct0 * P + col0:ct0 * P + Wq_],
                                  start=True, stop=True, psum=pk,
                                  reads=kkeys)
                        lo = group[0][1] * P
                        hi = (len(group) - 1) * W_B + Wq_
                        kb.op("S", "activation",
                              ptp[tslot][:, lo:hi], psc[tslot][0:P, lo:hi],
                              AF.Exp, scale=SCALE,
                              reads=(pk,), writes=(f"ptp{tslot}",))
                        for (k2, l2, half) in group:
                            if k2 >= glob0:
                                col0 = l2 * P
                                kb.op("V", "tensor_tensor",
                                      out=ptp[tslot][:,
                                                     half * W_B + col0:
                                                     half * W_B + col0 + P],
                                      in0=ptp[tslot][:,
                                                     half * W_B + col0:
                                                     half * W_B + col0 + P],
                                      in1=tri01[:, :], op=ALU.mult,
                                      reads=(f"ptp{tslot}", "tri01"),
                                      writes=(f"ptp{tslot}",))
                        if pend is not None:
                            emit_avs(pend, h, Wq_, ktmax)
                        if fl == 0 and pend_norm[0] is not None:
                            pend_norm[0]()
                            pend_norm[0] = None
                        pend = (group, tslot)
                        fl += 1
                        kt += len(group)
                    if pend is not None:
                        emit_avs(pend, h, Wq_, ktmax)

                    def mknorm(h=h, c=c, b0=b0, Wq_=Wq_):
                        def donorm():
                            kb.op("S", "activation", rrow[0:1, 0:Wq_],
                                  pyt[h % 2][64:65, 0:Wq_], AF.Ln,
                                  reads=(f"pyt{h % 2}",), writes=("rrow",))
                            kb.op("S", "activation", rrow[0:1, 0:Wq_],
                                  rrow[0:1, 0:Wq_], AF.Exp, scale=-1.0,
                                  reads=("rrow",), writes=("rrow",))
                            kb.mm(pmo[0:64, 0:Wq_], onesf[0:1, 0:64],
                                  rrow[0:1, 0:Wq_], start=True, stop=True,
                                  psum="pmo", reads=("rrow", "onesf"))
                            kb.op("V", "tensor_copy", out=rsb[0:64, 0:Wq_],
                                  in_=pmo[0:64, 0:Wq_], reads=("pmo",),
                                  writes=("rsb",))
                            kb.op("V", "tensor_tensor",
                                  out=yts[c][b0:b0 + 64, 0:Wq_],
                                  in0=pyt[h % 2][0:64, 0:Wq_],
                                  in1=rsb[0:64, 0:Wq_],
                                  op=ALU.mult,
                                  reads=(f"pyt{h % 2}", "rsb"),
                                  writes=(f"yts{c}",))
                        return donorm
                    pend_norm[0] = mknorm()
                    run_q(2)
                if pend_norm[0] is not None:
                    pend_norm[0]()
                    pend_norm[0] = None
                run_q(2)
                for lt in range(Wt):
                    jo = ct0 + lt
                    for a, b in _bank_pieces(0, H):
                        for hc in range(HC):
                            kb.mm(pmo[0:P, 0:b - a],
                                  yts[hc][0:P, lt * P:(lt + 1) * P],
                                  wT["o"][hc][:, a:b],
                                  start=(hc == 0), stop=(hc == HC - 1),
                                  psum="pmo",
                                  reads=(f"yts{hc}", f"woT{hc}"))
                        kb.op("V", "tensor_tensor", out=atp[:, a:b],
                              in0=pmo[0:P, 0:b - a], in1=boB[:, a:b],
                              op=ALU.add,
                              reads=("pmo", "boB"), writes=("atp",))
                    for cb in range(H // BNW):
                        kb.op("V", "bn_stats", bnsB[:, cb * 6:(cb + 1) * 6],
                              atp[:, cb * BNW:(cb + 1) * BNW],
                              reads=("atp",), writes=("bnsB",))
                    kb.op("V", "bn_aggr", mvB[:, :], bnsB[:, :],
                          reads=("bnsB",), writes=("mvB",))
                    rsqrt_sc(kb, srB, "srB", mvB[:, 1:2], ("mvB",))
                    kb.dma("Q", xre[:, :],
                           xs[(own_t0 + jo) * P:(own_t0 + jo + 1) * P, :],
                           stream="xre", reads=("xre",), writes=("xre",))
                    kb.op("V", "tensor_scalar", atp[:, :], atp[:, :],
                          mvB[:, 0:1], srB[:, 0:1], ALU.subtract, ALU.mult,
                          reads=("atp", "mvB", "srB"), writes=("atp",))
                    kb.op("V", "tensor_tensor", out=x2t[:, :], in0=atp[:, :],
                          in1=xre[:, :], op=ALU.add,
                          reads=("atp", "xre"), writes=("x2t",))
                    kb.dma("G", x2s[jo * P:(jo + 1) * P, :], x2t[:, :],
                           stream=f"x2s{jo % 2}", reads=("x2t",),
                           writes=(f"x2s_f{jo}",))
                    for cb in range(H // BNW):
                        kb.op("V", "bn_stats", bnsB[:, cb * 6:(cb + 1) * 6],
                              x2t[:, cb * BNW:(cb + 1) * BNW],
                              reads=("x2t",), writes=("bnsB",))
                    kb.op("V", "bn_aggr", mvC[:, 0:2], bnsB[:, :],
                          reads=("bnsB",), writes=("mvC",))
                    kb.op("S", "activation", mvC[:, 2:3], mvC[:, 1:2], AF.Ln,
                          bias=LN_EPS, reads=("mvC",), writes=("mvC_l",))
                    kb.op("S", "activation", mvC[:, 3:4], mvC[:, 2:3],
                          AF.Exp, scale=-0.5, reads=("mvC_l",),
                          writes=("mvC_r1",))
                    kb.op("V", "tensor_tensor", out=mvC[:, 4:5],
                          in0=mvC[:, 3:4], in1=mvC[:, 3:4], op=ALU.mult,
                          reads=("mvC_r1",), writes=("mvC_v2",))
                    kb.op("V", "tensor_tensor", out=mvC[:, 4:5],
                          in0=mvC[:, 4:5], in1=mvC[:, 1:2], op=ALU.mult,
                          reads=("mvC_v2", "mvC"), writes=("mvC_v2",))
                    kb.op("S", "activation", mvC[:, 5:6], mvC[:, 4:5], AF.Ln,
                          bias=LN_EPS, reads=("mvC_v2",), writes=("mvC_l2",))
                    kb.op("S", "activation", mvC[:, 5:6], mvC[:, 5:6],
                          AF.Exp, scale=-0.5, reads=("mvC_l2",),
                          writes=("mvC_r2",))
                    kb.op("V", "tensor_copy", out=mu1[:, jo:jo + 1],
                          in_=mvC[:, 0:1], reads=("mvC",), writes=("mu1",))
                    kb.op("V", "tensor_tensor", out=rr1[:, jo:jo + 1],
                          in0=mvC[:, 3:4], in1=mvC[:, 5:6], op=ALU.mult,
                          reads=("mvC_r1", "mvC_r2"), writes=("rr1",))
                    run_q(3)
            run_q(10 ** 9)
            kb.fence_dma("V")
            kb.fence_dma("S")

        if cfg.get("STOP", "") in ("0", "A", "A2"):
            def phase_b(kb, role):
                kb.fence_dma("G")
        with nc.If(pid < cfg["ROLE_SPLIT"]):
            phase_b(kbA, 0)
        with nc.Else():
            phase_b(kbB, 1)
        nc.all_engine_barrier()
        kbA.barrier_reset()
        kbB.barrier_reset()
        stB.close()
        stAB.close()

        stC = ExitStack()
        EC = stC.enter_context
        x2r = [EC(nc.sbuf_tensor(f"x2r{i}", [P, H], F32)) for i in range(2)]
        x2e = [EC(nc.sbuf_tensor(f"x2e{i}", [P, H], F32)) for i in range(2)]
        xfT = [[EC(nc.sbuf_tensor(f"xfT{tb}_{hc}", [P, FCH * P], BF16))
                for hc in range(HC)] for tb in range(2)]
        w1tb = [EC(nc.sbuf_tensor(f"w1tb{i}", [P, HC * 512], BF16))
                for i in range(2)]
        w2tb = EC(nc.sbuf_tensor("w2tb", [P, HIDC * 512], BF16))
        hTt = EC(nc.sbuf_tensor("hTt", [P, HIDC * FCH * P], BF16))
        sqr = [EC(nc.sbuf_tensor(f"sqr{i}", [P, FCH * P], BF16))
               for i in range(3)]
        varrow = EC(nc.sbuf_tensor("varrow", [1, FCH * P], F32))
        mu2row = EC(nc.sbuf_tensor("mu2row", [1, FCH * P], F32))
        c1row = EC(nc.sbuf_tensor("c1row", [33, HID], BF16))
        c2b = EC(nc.sbuf_tensor("c2b", [P, H], F32))
        c2g = EC(nc.sbuf_tensor("c2g", [1, H], F32))
        crow = EC(nc.sbuf_tensor("crow", [1, 512], F32))
        bhrow = EC(nc.sbuf_tensor("bhrow", [1, 512], F32))
        bnsC = EC(nc.sbuf_tensor("bnsC", [P, FCH * 6 * (HID // BNW2)],
                                  F32))
        mvD = EC(nc.sbuf_tensor("mvD", [P, 2], F32))
        srD = EC(nc.sbuf_tensor("srD", [P, 2], F32))
        rsD = EC(nc.sbuf_tensor("rsD", [P, FCH], F32))
        murow = EC(nc.sbuf_tensor("murow", [1, FCH * P], F32))
        xfC = EC(nc.sbuf_tensor("xfC", [P, H], BF16))
        ffo = [EC(nc.sbuf_tensor(f"ffo{i}", [P, H], F32)) for i in range(FCH)]
        oto = EC(nc.sbuf_tensor("oto", [P, H], F32))
        pf = [EC(nc.psum_tensor(f"pf{i}", [P, 512], F32)) for i in range(3)]
        pst = [EC(nc.psum_tensor(f"pst{i}", [P, 512], F32))
               for i in range(2)]
        pc2 = [EC(nc.psum_tensor(f"pc2{i}", [P, 512], F32)) for i in range(2)]
        pmu = EC(nc.psum_tensor("pmu", [P, 512], F32))

        def phase_c(kb, role):
            own_t0 = 0 if role == 0 else QA_T
            own_nt = QA_T if role == 0 else NT - QA_T
            kb.op("V", "memset", c1row[:, :], 0.0, writes=("c1row",))
            first = True

            chunks = []
            t0 = 0
            while t0 < own_nt:
                w = min(FCH, own_nt - t0)
                chunks.append((t0, w))
                t0 += w
            def stage_chunk(ct0, Wt, tb):
                for i in range(Wt):
                    jo = ct0 + i
                    jb = i % 2
                    kb.dma("Q", x2r[jb][:, :], x2s[jo * P:(jo + 1) * P, :],
                           stream=f"x2r{jb}",
                           reads=(f"x2r{jb}", f"x2s_f{jo}"),
                           writes=(f"x2r{jb}",))
                    kb.op("V", "tensor_scalar", xfC[:, 0:H],
                          x2r[jb][:, :],
                          mu1[:, jo:jo + 1], rr1[:, jo:jo + 1],
                          ALU.subtract, ALU.mult,
                          reads=(f"x2r{jb}", "mu1", "rr1"),
                          writes=("xfC",))
                    kb.dma("G", xfsc[jo * P:(jo + 1) * P, :], xfC[:, 0:H],
                           stream="xfsc", reads=("xfC",),
                           writes=(f"xfsc_f{jo}",))
                for hc in range(HC):
                    kb.dma("Q",
                           xfT[tb][hc][:, 0:Wt * P],
                           xfsc[ct0 * P:(ct0 + Wt) * P, hc * P:(hc + 1) * P],
                           stream=f"xfT{hc % 4}", transpose=True,
                           reads=tuple([f"xfTg{tb}_{hc}"] +
                                       [f"xfsc_f{ct0 + i}"
                                        for i in range(Wt)]),
                           writes=(f"xfTg{tb}_{hc}",))
                for hc in range(HC):
                    kb.op("V", "tensor_scalar", xfT[tb][hc][:, 0:Wt * P],
                          xfT[tb][hc][:, 0:Wt * P], g1c[:, hc:hc + 1],
                          None, ALU.mult,
                          reads=(f"xfTg{tb}_{hc}", "g1c"),
                          writes=(f"xfTg{tb}_{hc}",))

            stage_chunk(chunks[0][0], chunks[0][1], 0)
            for ci, (ct0, Wt) in enumerate(chunks):
                tb = ci % 2
                if ci + 1 < len(chunks):
                    stage_chunk(chunks[ci + 1][0], chunks[ci + 1][1],
                                (ci + 1) % 2)
                hpieces = []
                for hs in range(NSL):
                    for a, b in _bank_pieces(0, SL):
                        hpieces.append((hs, a, b))
                pend_stats = []

                def w1load(pi2):
                    hs2, a2, b2_ = hpieces[pi2]
                    ga2 = hs2 * SL + a2
                    slab = (ga2 // SEG1) * (HC * SEG1)
                    wk2 = f"w1tb{pi2 % 2}"
                    kb.dma("Q", w1tb[pi2 % 2][:, 0:HC * (b2_ - a2)],
                           t1Ts[:, slab:slab + HC * (b2_ - a2)],
                           stream=wk2, reads=(wk2, "t1Ts_f"),
                           writes=(wk2,))
                w1load(0)
                if len(hpieces) > 1:
                    w1load(1)
                for pi_, (hs, a, b) in enumerate(hpieces):
                    wk = f"w1tb{pi_ % 2}"
                    wb = w1tb[pi_ % 2]
                    ga = hs * SL + a
                    if first:
                        for hc in range(HC):
                            kb.mm(pc2[0][0:33, 0:b - a],
                                  b1lz[:, hc:hc + 33],
                                  wb[:, hc * (b - a):(hc + 1) * (b - a)],
                                  start=(hc == 0), stop=(hc == HC - 1),
                                  psum="pc20", reads=("b1lz", wk))
                        kb.dma("Q", bhrow[0:1, 0:b - a],
                               b1[None, ga:ga + b - a], stream="bhrow",
                               reads=("bhrow",), writes=("bhrow",))
                        kb.op("V", "tensor_scalar", crow[0:1, 0:b - a],
                              bhrow[0:1, 0:b - a], qtmp[0:1, 1:2], None,
                              ALU.mult, reads=("bhrow", "qtmp_s0"),
                              writes=("crow",))
                        kb.op("V", "tensor_tensor",
                              out=crow[0:1, 0:b - a],
                              in0=pc2[0][0:1, 0:b - a],
                              in1=crow[0:1, 0:b - a], op=ALU.add,
                              reads=("pc20", "crow"), writes=("crow",))
                        kb.op("V", "tensor_tensor",
                              out=crow[0:1, 0:b - a],
                              in0=pc2[0][32:33, 0:b - a],
                              in1=crow[0:1, 0:b - a], op=ALU.add,
                              reads=("pc20", "crow"), writes=("crow",))
                        kb.op("V", "tensor_copy",
                              out=c1row[0:1, ga:ga + b - a],
                              in_=crow[0:1, 0:b - a], reads=("crow",),
                              writes=("c1row",))
                        kb.op("V", "tensor_tensor",
                              out=c1row[32:33, ga:ga + b - a],
                              in0=crow[0:1, 0:b - a],
                              in1=c1row[0:1, ga:ga + b - a],
                              op=ALU.subtract, reads=("crow", "c1row"),
                              writes=("c1row",))
                    for sub in range((b - a) // P):
                        hcg = ga // P + sub
                        bank = (pi_ * ((b - a) // P) + sub) % 3
                        pk = f"pf{bank}"
                        pp = pf[bank]
                        for hc in range(HC):
                            kb.mm(pp[0:P, 0:Wt * P],
                                  wb[:, hc * (b - a) + sub * P:
                                     hc * (b - a) + (sub + 1) * P],
                                  xfT[tb][hc][:, 0:Wt * P],
                                  start=(hc == 0), stop=False,
                                  psum=pk, reads=(f"xfTg{tb}_{hc}", wk))
                        kb.mm(pp[0:P, 0:Wt * P],
                              c1row[0:33, hcg * P:(hcg + 1) * P],
                              onesK[0:33, 0:Wt * P],
                              start=False, stop=True, psum=pk,
                              reads=("onesK", "c1row"))
                        kb.op("S", "activation",
                              hTt[:, hcg * FCH * P:hcg * FCH * P + Wt * P],
                              pp[0:P, 0:Wt * P],
                              AF.Gelu, scale=rc128[:, 0:1],
                              reads=(pk, "rc128"),
                              writes=(f"hTt{hcg}",))
                        kb.op("V", "tensor_tensor",
                              out=sqr[hcg % 3][:, 0:Wt * P],
                              in0=hTt[:, hcg * FCH * P:
                                      hcg * FCH * P + Wt * P],
                              in1=hTt[:, hcg * FCH * P:
                                      hcg * FCH * P + Wt * P],
                              op=ALU.mult,
                              reads=(f"hTt{hcg}",),
                              writes=(f"sqr{hcg % 3}",))

                        def mkstats(hcg=hcg, Wt=Wt):
                            def em():
                                kb.mm(pst[0][0:33, 0:Wt * P],
                                      onesZ[:, 0:33],
                                      hTt[:, hcg * FCH * P:
                                          hcg * FCH * P + Wt * P],
                                      start=(hcg == 0),
                                      stop=(hcg == HIDC - 1),
                                      psum="pst0",
                                      reads=("onesZ", f"hTt{hcg}"))
                                kb.mm(pst[1][0:33, 0:Wt * P],
                                      onesZ[:, 0:33],
                                      sqr[hcg % 3][:, 0:Wt * P],
                                      start=(hcg == 0),
                                      stop=(hcg == HIDC - 1),
                                      psum="pst1",
                                      reads=("onesZ", f"sqr{hcg % 3}"))
                            return em
                        pend_stats.append(mkstats())
                        if len(pend_stats) > 2:
                            pend_stats.pop(0)()
                    if pi_ + 2 < len(hpieces) and sub == (b - a) // P - 1:
                        w1load(pi_ + 2)
                for f_ in pend_stats:
                    f_()
                pend_stats = []
                kb.op("V", "tensor_scalar", murow[0:1, 0:Wt * P],
                      pst[0][0:1, 0:Wt * P], -1.0 / HID, None, ALU.mult,
                      reads=("pst0",), writes=("murow",))
                kb.op("V", "tensor_scalar", varrow[0:1, 0:Wt * P],
                      pst[1][0:1, 0:Wt * P], 1.0 / HID, None, ALU.mult,
                      reads=("pst1",), writes=("varrow",))
                kb.op("V", "tensor_tensor", out=mu2row[0:1, 0:Wt * P],
                      in0=murow[0:1, 0:Wt * P], in1=murow[0:1, 0:Wt * P],
                      op=ALU.mult, reads=("murow",), writes=("mu2row",))
                kb.op("V", "tensor_tensor", out=varrow[0:1, 0:Wt * P],
                      in0=varrow[0:1, 0:Wt * P], in1=mu2row[0:1, 0:Wt * P],
                      op=ALU.subtract, reads=("varrow", "mu2row"),
                      writes=("varrow",))
                kb.op("S", "activation", varrow[0:1, 0:Wt * P],
                      varrow[0:1, 0:Wt * P], AF.Ln, bias=LN_EPS,
                      reads=("varrow",), writes=("varrow",))
                kb.op("S", "activation", varrow[0:1, 0:Wt * P],
                      varrow[0:1, 0:Wt * P], AF.Exp, scale=-0.5,
                      reads=("varrow",), writes=("varrow",))
                kb.op("V", "tensor_scalar", varrow[0:1, 0:Wt * P],
                      varrow[0:1, 0:Wt * P], qtmp[0:1, 6:7], None,
                      ALU.mult, reads=("varrow", "qtmp_r1"),
                      writes=("varrow",))
                for i in range(Wt):
                    kb.op("T", "transpose", pmu[0:P, 0:1],
                          varrow[0:1, i * P:(i + 1) * P],
                          identf[0:1, 0:1],
                          reads=("varrow", "identf"), writes=("pmu",))
                    kb.op("V", "tensor_copy", out=rsD[:, i:i + 1],
                          in_=pmu[0:P, 0:1], reads=("pmu",),
                          writes=("rsD",))
                for a, b in _bank_pieces(0, H):
                    slab2 = (a // SEG2) * (HIDC * SEG2)
                    kb.dma("Q", w2tb[:, 0:HIDC * (b - a)],
                           t2Ts[:, slab2:slab2 + HIDC * (b - a)],
                           stream="w2tb", reads=("w2tb", "t2Ts_f"),
                           writes=("w2tb",))
                    if first:
                        kb.dma("Q", bhrow[0:1, 0:b - a], b2[None, a:b],
                               stream="bhrow", reads=("bhrow",),
                               writes=("bhrow",))
                        kb.op("V", "tensor_scalar", crow[0:1, 0:b - a],
                              bhrow[0:1, 0:b - a], qtmp[0:1, 5:6], None,
                              ALU.mult, reads=("bhrow", "qtmp_s1"),
                              writes=("crow",))
                        kb.op("V", "tensor_tensor", out=crow[0:1, 0:b - a],
                              in0=c2pre[0:1, a:b],
                              in1=crow[0:1, 0:b - a],
                              op=ALU.add, reads=("c2pre", "crow"),
                              writes=("crow",))
                        kb.mm(pc2[0][0:P, 0:b - a], onesf[0:1, 0:P],
                              crow[0:1, 0:b - a], start=True, stop=True,
                              psum="pc20", reads=("crow", "onesf"))
                        kb.op("V", "tensor_scalar", c2b[:, a:b],
                              pc2[0][0:P, 0:b - a], rc128[:, 1:2], None,
                              ALU.mult, reads=("pc20", "rc128"),
                              writes=("c2b",))
                        for hc in range(HIDC):
                            kb.mm(pmu[0:33, 0:b - a], onesZ[:, 0:33],
                                  w2tb[:, hc * (b - a):(hc + 1) * (b - a)],
                                  start=(hc == 0), stop=(hc == HIDC - 1),
                                  psum="pmu", reads=("onesZ", "w2tb"))
                        kb.op("V", "tensor_copy", out=c2g[0:1, a:b],
                              in_=pmu[0:1, 0:b - a],
                              reads=("pmu",), writes=("c2g",))
                    for i in range(Wt):
                        pk2 = f"pc2{i % 2}"
                        pq = pc2[i % 2]
                        for hc in range(HIDC):
                            kb.mm(pq[0:P, 0:b - a],
                                  hTt[:, hc * FCH * P + i * P:
                                      hc * FCH * P + (i + 1) * P],
                                  w2tb[:, hc * (b - a):(hc + 1) * (b - a)],
                                  start=(hc == 0), stop=False,
                                  psum=pk2, reads=(f"hTt{hc}", "w2tb"))
                        kb.mm(pq[0:P, 0:b - a],
                              murow[0:1, i * P:(i + 1) * P], c2g[0:1, a:b],
                              start=False, stop=True, psum=pk2,
                              reads=("murow", "c2g"))
                        kb.op("V", "tensor_scalar", pq[0:P, 0:b - a],
                              pq[0:P, 0:b - a], rsD[:, i:i + 1], None,
                              ALU.mult, reads=(pk2, "rsD"),
                              writes=(pk2,))
                        kb.op("V", "tensor_tensor", out=ffo[i][:, a:b],
                              in0=pq[0:P, 0:b - a], in1=c2b[:, a:b],
                              op=ALU.add, reads=(pk2, "c2b"),
                              writes=(f"ffo{i}",))
                for i in range(Wt):
                    jo = ct0 + i
                    kb.dma("Q", x2e[i % 2][:, :],
                           x2s[jo * P:(jo + 1) * P, :],
                           stream=f"x2e{i % 2}",
                           reads=(f"x2e{i % 2}", f"x2s_f{jo}"),
                           writes=(f"x2e{i % 2}",))
                    for cb in range(H // BNW):
                        kb.op("V", "bn_stats", bnsC[:, cb * 6:(cb + 1) * 6],
                              ffo[i][:, cb * BNW:(cb + 1) * BNW],
                              reads=(f"ffo{i}",), writes=("bnsC",))
                    kb.op("V", "bn_aggr", mvD[:, :],
                          bnsC[:, 0:6 * (H // BNW)],
                          reads=("bnsC",), writes=("mvD",))
                    rsqrt_sc(kb, srD, "srD", mvD[:, 1:2], ("mvD",))
                    kb.op("V", "tensor_scalar", oto[:, :], ffo[i][:, :],
                          mvD[:, 0:1], srD[:, 0:1], ALU.subtract, ALU.mult,
                          reads=(f"ffo{i}", "mvD", "srD"), writes=("oto",))
                    kb.op("V", "tensor_tensor", out=oto[:, :],
                          in0=oto[:, :], in1=x2e[i % 2][:, :], op=ALU.add,
                          reads=("oto", f"x2e{i % 2}"), writes=("oto",))
                    kb.dma("G", out[jo * P:(jo + 1) * P, :], oto[:, :],
                           stream="out", reads=("oto",))
                first = False
            kb.fence_dma("G")
            kb.fence_dma("Q")
            kb.fence_dma("S")

        if cfg.get("STOP", "") in ("0", "A", "A2", "B"):
            def phase_c(kb, role):
                kb.fence_dma("G")
        with nc.If(pid < cfg["ROLE_SPLIT"]):
            phase_c(kbA, 0)
        with nc.Else():
            phase_c(kbB, 1)
        stC.close()

    return nc



_CACHE = {}


def _get_program(key, cfg):
    if key not in _CACHE:
        _CACHE[key] = build_program(cfg)
    return _CACHE[key]


def _ntabs(T, R):
    t = np.arange(1, T + 1, dtype=np.float64)[:, None]
    i = np.arange(R, dtype=np.float64)
    theta = (1.0 / 10000.0) ** (i / R)
    ang = t * theta
    n1 = np.round(ang / (2 * np.pi)).astype(np.float32)
    n2 = np.round((ang + np.pi / 2) / (2 * np.pi)).astype(np.float32)
    return n1, n2


def make_in_maps(cfg, inputs):
    T = cfg["T"]
    n1, n2 = _ntabs(T, 32)
    shared = {
        "Wq": inputs["L0_attn_Wq"], "bq": inputs["L0_attn_bq"],
        "Wk": inputs["L0_attn_Wk"], "bk": inputs["L0_attn_bk"],
        "Wv": inputs["L0_attn_Wv"], "bv": inputs["L0_attn_bv"],
        "Wo": inputs["L0_attn_Wo"], "bo": inputs["L0_attn_bo"],
        "g1": inputs["L0_ffn_ln1_g"], "b1ln": inputs["L0_ffn_ln1_b"],
        "w1": inputs["L0_ffn_w1"], "b1": inputs["L0_ffn_b1"],
        "g2": inputs["L0_ffn_ln2_g"], "b2ln": inputs["L0_ffn_ln2_b"],
        "w2": inputs["L0_ffn_w2"], "b2": inputs["L0_ffn_b2"],
        "ntab1": n1, "ntab2": n2,
    }
    shared = {k: np.ascontiguousarray(np.asarray(v, dtype=np.float32))
              for k, v in shared.items()}
    x = np.ascontiguousarray(np.asarray(inputs["L0_attn_x"],
                                        dtype=np.float32))
    nseq = x.shape[0]
    in_maps = []
    for c in range(cfg["NCORES"]):
        seq = c % nseq
        m = dict(shared)
        m["xs"] = x[seq]
        in_maps.append(m)
    return in_maps


def assemble_out(cfg, results, xshape):
    T, QA_T = cfg["T"], cfg["QA_T"]
    QA = QA_T * P
    nseq = xshape[0]
    full = np.empty(xshape, dtype=np.float32)
    for c in range(cfg["NCORES"]):
        seq = c % nseq
        o = np.asarray(results[c]["out"])
        if c < cfg["ROLE_SPLIT"]:
            full[seq, 0:QA] = o[0:QA]
        else:
            full[seq, QA:T] = o[0:T - QA]
    return full


def kernel(**inputs):
    cfg = full_cfg()
    nc = _get_program("full", cfg)
    in_maps = make_in_maps(cfg, inputs)
    res = run_bass_kernel_spmd(nc, in_maps,
                               core_ids=list(range(cfg["NCORES"])))
    x = np.asarray(inputs["L0_attn_x"])
    return assemble_out(cfg, res.results, x.shape)


# version 25
# speedup vs baseline: 1.1299x; 1.1299x over previous
import bisect
import math
from contextlib import ExitStack

import numpy as np

import concourse.bass as bass
import concourse.mybir as mybir
from concourse.bass_utils import run_bass_kernel_spmd

F32 = mybir.dt.float32
BF16 = mybir.dt.bfloat16
AF = mybir.ActivationFunctionType
ALU = mybir.AluOpType
AX = mybir.AxisListType

TWO_PI = 2.0 * math.pi
C1 = float(np.float32(TWO_PI))
C2 = float(np.float32(TWO_PI - C1))
C3 = TWO_PI - C1 - C2
NEG_BIG = -8.0e10
LN_EPS = 1e-5
P = 128


def full_cfg():
    return dict(T=4096, H=1024, NH=16, HID=4096, QA_T=17, BCH=4, FCH=3,
                NCORES=8, ROLE_SPLIT=4, MAXW=512)


def mini_cfg():
    return dict(T=512, H=256, NH=4, HID=512, QA_T=2, BCH=2, FCH=1,
                NCORES=2, ROLE_SPLIT=1)


def med_cfg():
    return dict(T=1024, H=256, NH=4, HID=2048, QA_T=3, BCH=2, FCH=2,
                NCORES=2, ROLE_SPLIT=1, MAXW=512)


def med2_cfg():
    return dict(T=1024, H=1024, NH=16, HID=2048, QA_T=5, BCH=4, FCH=2,
                NCORES=2, ROLE_SPLIT=1, MAXW=1024)



class _Eng:
    def __init__(self, name, eng, semname):
        self.name = name
        self.eng = eng
        self.semname = semname
        self.pos = 0
        self.inc_pos = []
        self.inc_cnt = []
        self.count = 0
        self.handles = {}
        self.waited = {}


class Kb:

    def __init__(self, nc, sems, streams):
        self.nc = nc
        self.sems = sems
        self.streams = streams
        self.E = {
            "T": _Eng("T", nc.tensor, "et"),
            "V": _Eng("V", nc.vector, "ev"),
            "S": _Eng("S", nc.scalar, "es"),
            "G": _Eng("G", nc.gpsimd, "eg"),
            "Q": _Eng("Q", nc.sync, "eq"),
        }
        self.dmacnt = {}
        self.res = {}

    def clone_counters_from(self, other):
        for n, e in self.E.items():
            oe = other.E[n]
            e.pos = oe.pos
            e.inc_pos = list(oe.inc_pos)
            e.inc_cnt = list(oe.inc_cnt)
            e.count = oe.count
            e.handles = dict(oe.handles)
            e.waited = dict(oe.waited)
        self.dmacnt = dict(other.dmacnt)
        self.res = {k: {"w": v["w"], "r": dict(v["r"])}
                    for k, v in other.res.items()}

    def _resolve(self, tok):
        if tok[0] == "D":
            st = tok[1]
            return (self.streams[st], self.dmacnt[st])
        _, en, pos = tok
        e = self.E[en]
        i = bisect.bisect_left(e.inc_pos, pos)
        if i < len(e.inc_pos):
            return (e.semname, e.inc_cnt[i])
        ins = e.handles.get(pos)
        assert ins is not None, f"no handle for {en}@{pos}"
        e.count += 1
        ins.then_inc(self.sems[e.semname], 1)
        e.inc_pos.append(pos)
        e.inc_cnt.append(e.count)
        return (e.semname, e.count)

    def _wait_deps(self, e, reads, writes):
        toks = []
        for r in reads:
            ent = self.res.get(r)
            if ent is not None and ent["w"] is not None:
                toks.append(ent["w"])
        for w in writes:
            ent = self.res.get(w)
            if ent is not None:
                if ent["w"] is not None:
                    toks.append(ent["w"])
                toks.extend(ent["r"].values())
        need = {}
        for t in toks:
            s, v = self._resolve(t)
            if need.get(s, 0) < v:
                need[s] = v
        for s, v in need.items():
            if e.waited.get(s, 0) < v:
                e.eng.wait_ge(self.sems[s], v)
                e.waited[s] = v

    def _record(self, tok, reads, writes):
        for w in writes:
            self.res[w] = {"w": tok, "r": {}}
        rk = tok[1]
        for r in reads:
            ent = self.res.setdefault(r, {"w": None, "r": {}})
            ent["r"][rk] = tok

    def op(self, en, method, *args, reads=(), writes=(), **kw):
        e = self.E[en]
        self._wait_deps(e, reads, writes)
        ins = getattr(e.eng, method)(*args, **kw)
        e.pos += 1
        e.handles[e.pos] = ins
        self._record(("E", en, e.pos), reads, writes)
        return ins

    def mm(self, out, lhsT, rhs, *, start, stop, psum, reads=()):
        e = self.E["T"]
        self._wait_deps(e, reads, (psum,) if start else ())
        ins = e.eng.matmul(out, lhsT, rhs, start=start, stop=stop)
        e.pos += 1
        e.handles[e.pos] = ins
        tok = ("E", "T", e.pos)
        for r in reads:
            ent = self.res.setdefault(r, {"w": None, "r": {}})
            ent["r"]["T"] = tok
        if start:
            self.res[psum] = {"w": tok, "r": {}}
        else:
            self.res[psum]["w"] = tok
        return ins

    def dma(self, qn, out, in_, *, stream, reads=(), writes=(),
            transpose=False, slow=False):
        e = self.E[qn]
        if stream not in self.streams:
            self.streams[stream] = f"d{len(self.streams)}"
        semn = self.streams[stream]
        self._wait_deps(e, reads, writes)
        if transpose:
            ins = e.eng.dma_start(out=out, in_=in_, transpose=True)
        else:
            ins = e.eng.dma_start(out=out, in_=in_,
                                  allow_slow_non_contiguous=slow)
        self.dmacnt[stream] = self.dmacnt.get(stream, 0) + 16
        ins.then_inc(self.sems[semn], 16)
        self._record(("D", stream, self.dmacnt[stream]), reads, writes)

    def mark(self, key, stream):
        self.res[key] = {"w": ("D", stream, self.dmacnt.get(stream, 0)),
                         "r": {}}

    def barrier_reset(self):
        for ent in self.res.values():
            if ent["w"] is not None and ent["w"][0] == "E":
                ent["w"] = None
            ent["r"] = {k: v for k, v in ent["r"].items() if v[0] == "D"}

    def fence_dma(self, en="G"):
        e = self.E[en]
        for st, v in self.dmacnt.items():
            semn = self.streams[st]
            if v and e.waited.get(semn, 0) < v:
                e.eng.wait_ge(self.sems[semn], v)
                e.waited[semn] = v


def _bank_pieces(a, b):
    out = []
    while a < b:
        nxt = min(b, (a // 512 + 1) * 512)
        out.append((a, nxt))
        a = nxt
    return out



def build_program(cfg):
    TD = "S" if cfg.get("TDUAL", False) else "Q"
    TD2 = "Q"
    T, H, NH, HID = cfg["T"], cfg["H"], cfg["NH"], cfg["HID"]
    QA_T, BCH, FCH = cfg["QA_T"], cfg["BCH"], cfg["FCH"]
    NT = T // P
    HC = H // P
    HIDC = HID // P
    HS = H // NH
    R = HS // 2
    assert HS == 64 and R == 32 and H == NH * 64
    QA = QA_T * P
    MAXOWN = max(QA_T, NT - QA_T)
    SL = min(1024, HID)
    NSL = HID // SL
    SEG1 = min(512, HID)
    SEG2 = min(512, H)
    BNW = min(512, H)
    BNW2 = min(512, HID)
    SCALE = 1.0 / math.sqrt(HS)
    NHP = NH // 2
    assert NHP == HC
    MAXW = cfg.get("MAXW", 2048)
    GRP = 4
    W_B = BCH * P
    assert W_B <= 512

    nc = bass.Bass("TRN2", target_bir_lowering=False)

    dt = lambda n, s, d, k: nc.dram_tensor(n, s, d, kind=k).ap()
    xs = dt("xs", [T, H], F32, "ExternalInput")
    WSRC = {w: dt("W" + w, [H, H], F32, "ExternalInput") for w in "qkvo"}
    BSRC = {w: dt("b" + w, [H], F32, "ExternalInput") for w in "qkvo"}
    g1 = dt("g1", [H], F32, "ExternalInput")
    b1ln = dt("b1ln", [H], F32, "ExternalInput")
    w1 = dt("w1", [HID, H], F32, "ExternalInput")
    b1 = dt("b1", [HID], F32, "ExternalInput")
    g2 = dt("g2", [HID], F32, "ExternalInput")
    b2ln = dt("b2ln", [HID], F32, "ExternalInput")
    w2 = dt("w2", [H, HID], F32, "ExternalInput")
    b2 = dt("b2", [H], F32, "ExternalInput")
    ntab1 = dt("ntab1", [T, R], F32, "ExternalInput")
    ntab2 = dt("ntab2", [T, R], F32, "ExternalInput")
    out = dt("out", [MAXOWN * P, H], F32, "ExternalOutput")

    xhsc = dt("xhsc", [T, H], BF16, "Internal")
    krsc = dt("krsc", [T, NH * 64], BF16, "Internal")
    qrsc = dt("qrsc", [MAXOWN * P, NH * 64], BF16, "Internal")
    vsc = dt("vsc", [NH * P, NT * 65], BF16, "Internal")
    x2s = dt("x2s", [MAXOWN * P, H], F32, "Internal")
    xfsc = dt("xfsc", [MAXOWN * P, H], BF16, "Internal")
    hsc = dt("hsc", [MAXOWN * P, HID], BF16, "Internal")
    t1sc = dt("t1sc", [HID, H], BF16, "Internal")
    t2sc = dt("t2sc", [H, HID], BF16, "Internal")
    t1Ts = dt("t1Ts", [P, H // P * HID], BF16, "Internal")
    t2Ts = dt("t2Ts", [P, HID // P * H], BF16, "Internal")
    mkdr = dt("mkdr", [16, T], BF16, "Internal")
    wqsc = dt("wqsc", [4 * H, H], BF16, "Internal")

    LN1E4_OVER_R = math.log(1e-4) / R
    WOFF = {"q": 0, "k": H, "v": 2 * H, "o": 3 * H}

    with ExitStack() as gst:
        GE = gst.enter_context
        sems = {n: GE(nc.semaphore(n))
                for n in ("et", "ev", "es", "eg", "eq")}
        for i in range(88):
            sems[f"d{i}"] = GE(nc.semaphore(f"d{i}"))
        streams = {}
        pid = nc.partition_id(list(mybir.ALL_ENGINES))

        cidx = GE(nc.sbuf_tensor("cidx", [P, P], F32))
        pidx = GE(nc.sbuf_tensor("pidx", [P, 1], F32))
        tri01 = GE(nc.sbuf_tensor("tri01", [P, P], BF16))
        identf = GE(nc.sbuf_tensor("identf", [P, P], F32))
        onesf = GE(nc.sbuf_tensor("onesf", [16, P], F32))
        onescol = GE(nc.sbuf_tensor("onescol", [P, 1], F32))
        sc128 = GE(nc.sbuf_tensor("sc128", [P, 2], F32))
        rc128 = GE(nc.sbuf_tensor("rc128", [P, 2], F32))
        qtmp = GE(nc.sbuf_tensor("qtmp", [1, 8], F32))
        qacc = GE(nc.sbuf_tensor("qacc", [P, 3], F32))
        g1c = GE(nc.sbuf_tensor("g1c", [P, HC], F32))
        g2c = GE(nc.sbuf_tensor("g2c", [P, HIDC], F32))
        b1lz = GE(nc.sbuf_tensor("b1lz", [P, 32 + HC], BF16))
        b2lz = GE(nc.sbuf_tensor("b2lz", [P, 32 + HIDC], BF16))
        onesK = GE(nc.sbuf_tensor("onesK", [33, 512], BF16))
        onesZ = GE(nc.sbuf_tensor("onesZ", [P, 33], BF16))
        c2pre = GE(nc.sbuf_tensor("c2pre", [1, H], F32))
        mu1 = GE(nc.sbuf_tensor("mu1", [P, MAXOWN], F32))
        rr1 = GE(nc.sbuf_tensor("rr1", [P, MAXOWN], F32))
        brows = {w: GE(nc.sbuf_tensor(f"br_{w}", [33, H], BF16))
                 for w in "qkv"}
        boB = GE(nc.sbuf_tensor("boB", [P, H], BF16))
        constt = GE(nc.sbuf_tensor("constt", [P, 2], F32))
        nc.const_aps.register(F32, LN_EPS, constt[:, 0:1])
        nc.const_aps.register(F32, 1e-30, constt[:, 1:2])

        kbA = Kb(nc, sems, streams)
        kbB = Kb(nc, sems, streams)

        def rsqrt_sc(kb, dst, dst_key, src_ap, src_keys):
            kb.op("S", "activation", dst[:, 1:2], src_ap, AF.Ln,
                  bias=LN_EPS, reads=src_keys, writes=(dst_key,))
            kb.op("S", "activation", dst[:, 0:1], dst[:, 1:2], AF.Exp,
                  scale=-0.5, reads=(dst_key,), writes=(dst_key,))

        stTab = ExitStack()
        TE = stTab.enter_context
        cosT = TE(nc.sbuf_tensor("cosT", [P, R * NT], BF16, side="right"))
        sinT = TE(nc.sbuf_tensor("sinT", [P, R * NT], BF16, side="right"))
        wTq = [TE(nc.sbuf_tensor(f"wqT{hc}", [P, H], BF16, side="right"))
               for hc in range(HC)]
        wTk = [TE(nc.sbuf_tensor(f"wkT{hc}", [P, H], BF16, side="right"))
               for hc in range(HC)]
        wTv = [TE(nc.sbuf_tensor(f"wvT{hc}", [P, H], BF16, side="right"))
               for hc in range(HC)]
        wT = {"q": wTq, "k": wTk, "v": wTv}

        stLN = ExitStack()
        EL = stLN.enter_context
        xt = [EL(nc.sbuf_tensor(f"A_xt{i}", [P, H], F32, side="right"))
              for i in range(2)]
        bns = EL(nc.sbuf_tensor("A_bns", [P, 6 * (H // BNW)], F32,
                                side="right"))
        mv = EL(nc.sbuf_tensor("A_mv", [P, 2], F32, side="right"))
        sr = EL(nc.sbuf_tensor("A_sr", [P, 2], F32, side="right"))
        xh = EL(nc.sbuf_tensor("A_xh", [P, H], BF16, side="right"))

        def emit_ln_tile(kb, j):
            jb = j % 2
            xkey = f"A_xt{jb}"
            kb.dma("Q", xt[jb][:, :], xs[j * P:(j + 1) * P, :],
                   stream=xkey, reads=(xkey,), writes=(xkey,))
            for c in range(H // BNW):
                kb.op("V", "bn_stats", bns[:, c * 6:(c + 1) * 6],
                      xt[jb][:, c * BNW:(c + 1) * BNW],
                      reads=(xkey,), writes=("A_bns",))
            kb.op("V", "bn_aggr", mv[:, :], bns[:, :],
                  reads=("A_bns",), writes=("A_mv",))
            rsqrt_sc(kb, sr, "A_sr", mv[:, 1:2], ("A_mv",))
            kb.op("V", "tensor_scalar", xh[:, :], xt[jb][:, :],
                  mv[:, 0:1], sr[:, 0:1], ALU.subtract, ALU.mult,
                  reads=(xkey, "A_mv", "A_sr"), writes=("A_xh",))
            kb.dma("G", xhsc[j * P:(j + 1) * P, :], xh[:, :],
                   stream="xhsc", reads=("A_xh",),
                   writes=(f"xhsc_f{j}",))


        with ExitStack() as st0:
            E0 = st0.enter_context
            theta = E0(nc.sbuf_tensor("theta", [1, R], F32))
            irow = E0(nc.sbuf_tensor("irow", [1, R], F32))
            thetab = E0(nc.sbuf_tensor("thetab", [P, R], F32))
            tcol = E0(nc.sbuf_tensor("tcol", [P, 1], F32))
            angt = E0(nc.sbuf_tensor("angt", [P, R], F32))
            angc = E0(nc.sbuf_tensor("angc", [P, R], F32))
            n1t = E0(nc.sbuf_tensor("n1t", [P, R], F32))
            n2t = E0(nc.sbuf_tensor("n2t", [P, R], F32))
            wstg = [E0(nc.sbuf_tensor(f"wstg{i}", [P, H], BF16))
                    for i in range(2)]
            brow = E0(nc.sbuf_tensor("brow", [1, H], F32))
            lrow = E0(nc.sbuf_tensor("lrow", [P, max(HC, HIDC)], F32))
            pp0 = E0(nc.psum_tensor("pp0", [P, 512], F32))

            kb = kbA
            kb.op("G", "iota", cidx[:, :], [[1, P]], channel_multiplier=0,
                  allow_small_or_imprecise_dtypes=True, writes=("cidx",))
            kb.op("G", "iota", pidx[:, :], [[1, 1]], channel_multiplier=1,
                  allow_small_or_imprecise_dtypes=True, writes=("pidx",))
            kb.op("G", "iota", irow[0:1, :], [[1, R]], channel_multiplier=0,
                  allow_small_or_imprecise_dtypes=True, writes=("irow",))
            kb.op("V", "tensor_scalar", identf[:, :], cidx[:, :], pidx[:, :],
                  None, ALU.is_equal, reads=("cidx", "pidx"),
                  writes=("identf",))
            kb.op("V", "tensor_scalar", tri01[:, :], cidx[:, :], pidx[:, :],
                  None, ALU.is_ge,
                  reads=("cidx", "pidx"), writes=("tri01",))
            kb.op("V", "memset", onesf[:, :], 1.0, writes=("onesf",))
            kb.op("V", "memset", onesK[:, :], 0.0, writes=("onesK",))
            kb.op("V", "memset", onesK[0:1, :], 1.0, reads=("onesK",),
                  writes=("onesK",))
            kb.op("V", "memset", onesK[32:33, :], 1.0, reads=("onesK",),
                  writes=("onesK",))
            kb.op("V", "memset", onesZ[:, :], 0.0, writes=("onesZ",))
            kb.op("V", "memset", onesZ[:, 0:1], 1.0, reads=("onesZ",),
                  writes=("onesZ",))
            kb.op("V", "memset", onescol[:, :], 1.0, writes=("onescol",))
            kb.op("V", "memset", qacc[:, :], 0.0, writes=("qaccs",))
            kb.op("V", "memset", constt[:, 0:1], LN_EPS, writes=("constt",))
            kb.op("V", "memset", constt[:, 1:2], 1e-30, reads=("constt",),
                  writes=("constt",))
            kb.op("S", "activation", theta[0:1, :], irow[0:1, :], AF.Exp,
                  scale=LN1E4_OVER_R, reads=("irow",), writes=("theta",))
            kb.mm(pp0[0:P, 0:R], onesf[0:1, 0:P], theta[0:1, :],
                  start=True, stop=True, psum="pp0", reads=("onesf", "theta"))
            kb.op("V", "tensor_copy", out=thetab[:, :], in_=pp0[0:P, 0:R],
                  reads=("pp0",), writes=("thetab",))

            for j in range(NT):
                cs = slice(j * R, (j + 1) * R)
                kb.dma("Q", n1t[:, :], ntab1[j * P:(j + 1) * P, :],
                       stream="n1t", reads=("n1t",), writes=("n1t",))
                kb.dma("Q", n2t[:, :], ntab2[j * P:(j + 1) * P, :],
                       stream="n2t", reads=("n2t",), writes=("n2t",))
                kb.op("V", "tensor_scalar", tcol[:, :], pidx[:, :],
                      float(j * P + 1), None, ALU.add,
                      reads=("pidx",), writes=("tcol",))
                kb.op("V", "tensor_scalar", angt[:, :], thetab[:, :],
                      tcol[:, :], None, ALU.mult,
                      reads=("thetab", "tcol"), writes=("angt",))
                kb.op("V", "tensor_scalar", angc[:, :], angt[:, :],
                      math.pi / 2.0, None, ALU.add,
                      reads=("angt",), writes=("angc",))
                for c in (C1, C2, C3):
                    kb.op("V", "scalar_tensor_tensor", angt[:, :], n1t[:, :],
                          -c, angt[:, :], ALU.mult, ALU.add,
                          reads=("n1t", "angt"), writes=("angt",))
                    kb.op("V", "scalar_tensor_tensor", angc[:, :], n2t[:, :],
                          -c, angc[:, :], ALU.mult, ALU.add,
                          reads=("n2t", "angc"), writes=("angc",))
                kb.op("S", "activation", sinT[:, cs], angt[:, :], AF.Sin,
                      reads=("angt",), writes=(f"sinT{j}",))
                kb.op("S", "activation", cosT[:, cs], angc[:, :], AF.Sin,
                      reads=("angc",), writes=(f"cosT{j}",))

            for w in "qkvo":
                for oc in range(HC):
                    sg = wstg[oc % 2]
                    sk = f"wstg{oc % 2}"
                    kb.dma("G", sg[:, :], WSRC[w][oc * P:(oc + 1) * P, :],
                           stream=sk, reads=(sk,), writes=(sk,))
                    kb.dma("G",
                           wqsc[WOFF[w] + oc * P:WOFF[w] + (oc + 1) * P, :],
                           sg[:, :], stream="wqsc", reads=(sk,))
            kb.mark("wqsc_f", "wqsc")
            for w in "qkv":
                for hc in range(HC):
                    kb.dma("Q", wT[w][hc][:, :],
                           wqsc[WOFF[w]:WOFF[w] + H, hc * P:(hc + 1) * P],
                           stream=f"wT{hc % 4}", transpose=True,
                           reads=(f"w{w}T{hc}", "wqsc_f"),
                           writes=(f"w{w}T{hc}",))
            for w in "qkv":
                kb.op("V", "memset", brows[w][:, :], 0.0,
                      writes=(f"br_{w}",))
                kb.dma("Q", brow[0:1, :], BSRC[w][None, :],
                       stream="brow", reads=("brow",), writes=("brow",))
                kb.op("V", "tensor_copy", out=brows[w][0:1, :],
                      in_=brow[0:1, :], reads=("brow", f"br_{w}"),
                      writes=(f"br_{w}",))
                kb.op("V", "tensor_tensor", out=brows[w][32:33, :],
                      in0=brow[0:1, :], in1=brows[w][0:1, :],
                      op=ALU.subtract, reads=("brow", f"br_{w}"),
                      writes=(f"br_{w}",))
            for w, bt, bkey in (("o", boB, "boB"),):
                kb.dma("Q", brow[0:1, :], BSRC[w][None, :],
                       stream="brow2", reads=("brow",), writes=("brow",))
                for a, b in _bank_pieces(0, H):
                    kb.mm(pp0[0:P, 0:b - a], onesf[0:1, 0:P],
                          brow[0:1, a:b], start=True, stop=True,
                          psum="pp0", reads=("brow", "onesf"))
                    kb.op("V", "tensor_copy", out=bt[:, a:b],
                          in_=pp0[0:P, 0:b - a], reads=("pp0",),
                          writes=(bkey,))

            kb.dma("Q", g1c[:, :], g1.rearrange("(c p) -> p c", p=P),
                   stream="g1c", writes=("g1c",), slow=True)
            kb.dma("Q", g2c[:, :], g2.rearrange("(c p) -> p c", p=P),
                   stream="g2c", writes=("g2c",), slow=True)
            kb.dma("Q", lrow[:, 0:HC], b1ln.rearrange("(c p) -> p c", p=P),
                   stream="lrow", reads=("lrow",), writes=("lrow",),
                   slow=True)
            kb.op("V", "memset", b1lz[:, :], 0.0, writes=("b1lz",))
            kb.op("V", "tensor_copy", out=b1lz[:, 0:HC], in_=lrow[:, 0:HC],
                  reads=("lrow", "b1lz"), writes=("b1lz",))
            kb.op("V", "tensor_tensor", out=b1lz[:, 32:32 + HC],
                  in0=lrow[:, 0:HC],
                  in1=b1lz[:, 0:HC], op=ALU.subtract,
                  reads=("lrow", "b1lz"), writes=("b1lz",))
            kb.dma("Q", lrow[:, 0:HIDC], b2ln.rearrange("(c p) -> p c", p=P),
                   stream="lrow", reads=("lrow",), writes=("lrow",),
                   slow=True)
            kb.op("V", "memset", b2lz[:, :], 0.0, writes=("b2lz",))
            kb.op("V", "tensor_copy", out=b2lz[:, 0:HIDC],
                  in_=lrow[:, 0:HIDC],
                  reads=("lrow", "b2lz"), writes=("b2lz",))
            kb.op("V", "tensor_tensor", out=b2lz[:, 32:32 + HIDC],
                  in0=lrow[:, 0:HIDC],
                  in1=b2lz[:, 0:HIDC], op=ALU.subtract,
                  reads=("lrow", "b2lz"), writes=("b2lz",))
            for j in range(QA_T):
                emit_ln_tile(kb, j)
            kb.fence_dma("V")
            kb.fence_dma("S")
            kb.fence_dma("G")

        nc.all_engine_barrier()
        kbA.barrier_reset()
        kbB.clone_counters_from(kbA)

        stAB = ExitStack()
        AE = stAB.enter_context
        krt = [AE(nc.sbuf_tensor(f"krt{c}", [P, T], BF16)) for c in range(NHP)]
        qrt = [AE(nc.sbuf_tensor(f"qrt{c}", [P, MAXOWN * P], BF16))
               for c in range(NHP)]
        mkacc = AE(nc.sbuf_tensor("mkacc", [P, NH], F32))

        stAL = ExitStack()
        EA = stAL.enter_context
        xhTg = [[EA(nc.sbuf_tensor(f"A_xhT{g}_{hc}", [P, GRP * P], BF16,
                                   side="right")) for hc in range(HC)]
                for g in range(2)]
        rt1 = EA(nc.sbuf_tensor("A_rt1", [P, H // 2], F32, side="right"))
        rt2 = EA(nc.sbuf_tensor("A_rt2", [P, H // 2], F32, side="right"))
        krb = EA(nc.sbuf_tensor("A_krb", [P, NH * 64], BF16, side="right"))
        sq16 = EA(nc.sbuf_tensor("A_sq16", [P, NH], F32, side="right"))
        sqQ = EA(nc.sbuf_tensor("A_sqQ", [P, NH], F32, side="right"))
        sq2 = EA(nc.sbuf_tensor("A_sq2", [P, NH], F32, side="right"))
        vst = [EA(nc.sbuf_tensor(f"A_vst{i}", [P, NH * 65], BF16,
                                 side="right")) for i in range(2)]
        ppr = [EA(nc.psum_tensor(f"A_ppr{i}", [P, 512], F32)) for i in range(4)]

        def phase_a(kb, role):
            own_t0 = 0 if role == 0 else QA_T
            own_nt = QA_T if role == 0 else NT - QA_T
            kv_nt = QA_T if role == 0 else NT

            kb.op("V", "memset", mkacc[:, :], 0.0, writes=("mkacc",))
            kb.op("G", "memset", krb[:, :], 0.0, writes=("A_krb",))
            for i in range(2):
                kb.op("G", "memset",
                      vst[i].rearrange("p (h c) -> p h c", h=NH)[:, :, 64:65],
                      1.0, writes=(f"A_vst{i}_ones",))

            def proj(w, gslot, jj, psum0):
                for a, b in _bank_pieces(0, H):
                    i = psum0 + (a // 512) % 2
                    pk = f"A_ppr{i}"
                    pr = ppr[i]
                    for hc in range(HC):
                        kb.mm(pr[0:P, 0:b - a],
                              xhTg[gslot][hc][:, jj * P:(jj + 1) * P],
                              wT[w][hc][:, a:b],
                              start=(hc == 0), stop=False, psum=pk,
                              reads=(f"A_xhTg{gslot}_{hc}", f"w{w}T{hc}"))
                    kb.mm(pr[0:P, 0:b - a], onesK[0:33, 0:P],
                          brows[w][0:33, a:b], start=False, stop=True,
                          psum=pk, reads=("onesK", f"br_{w}"))

            def rope_psum(kb, psum0, dst3, dstkey, j):
                if cfg.get("NOROPE", False):
                    for a, b in _bank_pieces(0, H):
                        i = psum0 + (a // 512) % 2
                        pk = f"A_ppr{i}"
                        kb.op("V", "tensor_copy",
                              out=rt1[:, a // 2:a // 2 + (b - a) // 2],
                              in_=ppr[i][0:P, 0:(b - a) // 2],
                              reads=(pk,), writes=("A_rt1",))
                    kb.op("V", "tensor_copy", out=dst3,
                          in_=rt1.rearrange("p (h r) -> p h r", h=NH),
                          reads=("A_rt1",), writes=(dstkey,))
                    return
                cs = slice(j * R, (j + 1) * R)
                cb = cosT[:, cs].rearrange("p (o r) -> p o r", o=1)\
                    .broadcast_to((P, NH, R))
                sb = sinT[:, cs].rearrange("p (o r) -> p o r", o=1)\
                    .broadcast_to((P, NH, R))
                for a, b in _bank_pieces(0, H):
                    i = psum0 + (a // 512) % 2
                    pk = f"A_ppr{i}"
                    hper = (b - a) // 64
                    h0 = a // 64
                    se = ppr[i][0:P, 0:b - a].rearrange(
                        "p (h r two) -> p h r two", h=hper, two=2)
                    hs_ = slice(h0, h0 + hper)
                    kb.op("V", "tensor_tensor",
                          out=rt1.rearrange("p (h r) -> p h r", h=NH)[:, hs_],
                          in0=se[:, :, :, 0], in1=cb[:, hs_], op=ALU.mult,
                          reads=(pk, f"cosT{j}"), writes=("A_rt1",))
                    kb.op("V", "tensor_tensor",
                          out=rt2.rearrange("p (h r) -> p h r", h=NH)[:, hs_],
                          in0=se[:, :, :, 1], in1=sb[:, hs_], op=ALU.mult,
                          reads=(pk, f"sinT{j}"), writes=("A_rt2",))
                kb.op("V", "tensor_tensor", out=dst3,
                      in0=rt1.rearrange("p (h r) -> p h r", h=NH),
                      in1=rt2.rearrange("p (h r) -> p h r", h=NH),
                      op=ALU.subtract,
                      reads=("A_rt1", "A_rt2"), writes=(dstkey,))

            groups = [(g0, min(GRP, kv_nt - g0))
                      for g0 in range(0, kv_nt, GRP)]
            if cfg.get("AGRP1", False):
                groups = groups[:1]
            for gi, (g0, gn) in enumerate(groups):
                gslot = gi % 2
                for jj in range(gn):
                    j = g0 + jj
                    if j >= QA_T:
                        emit_ln_tile(kb, j)
                for hc in range(HC):
                    kb.dma(TD2 if hc % 2 == 0 else TD,
                           xhTg[gslot][hc][:, 0:gn * P],
                           xhsc[g0 * P:(g0 + gn) * P, hc * P:(hc + 1) * P],
                           stream=f"xhT{gslot}_{hc % 2}", transpose=True,
                           reads=tuple([f"A_xhTg{gslot}_{hc}"] +
                                       [f"xhsc_f{g0 + t}" for t in range(gn)]),
                           writes=(f"A_xhTg{gslot}_{hc}",))
                if cfg.get("NOPROJ", False):
                    continue
                for jj in range(gn):
                    j = g0 + jj
                    jb = j % 2
                    proj("k", gslot, jj, 0)
                    krz = krb.rearrange("p (h z) -> p h z", z=64)
                    rope_psum(kb, 0, krz[:, :, 0:R], "A_krb", j)
                    kb.op("V", "tensor_tensor",
                          out=rt2.rearrange("p (h r) -> p h r", h=NH),
                          in0=krz[:, :, 0:R],
                          in1=krz[:, :, 0:R], op=ALU.mult,
                          reads=("A_krb",), writes=("A_rt2",))
                    kb.op("V", "tensor_reduce", sq16[:, :],
                          rt2.rearrange("p (h r) -> p h r", h=NH),
                          axis=AX.X, op=ALU.add,
                          reads=("A_rt2",), writes=("A_sq16",))
                    kb.op("V", "tensor_tensor", out=mkacc[:, :],
                          in0=mkacc[:, :], in1=sq16[:, :], op=ALU.max,
                          reads=("mkacc", "A_sq16"), writes=("mkacc",))
                    kb.dma("G", krsc[j * P:(j + 1) * P, :], krb[:, :],
                           stream="krsc",
                           reads=("A_krb", f"krsc_f{j}"),
                           writes=(f"krsc_f{j}",))
                    if cfg.get("ONLYK", False):
                        continue
                    vb = vst[jb]
                    for a, b in _bank_pieces(0, H):
                        i = 2 + (a // 512) % 2
                        pk = f"A_ppr{i}"
                        pr = ppr[i]
                        for hc in range(HC):
                            kb.mm(pr[0:P, 0:b - a],
                                  xhTg[gslot][hc][:, jj * P:(jj + 1) * P],
                                  wT["v"][hc][:, a:b],
                                  start=(hc == 0), stop=False,
                                  psum=pk,
                                  reads=(f"A_xhTg{gslot}_{hc}", f"wvT{hc}"))
                        kb.mm(pr[0:P, 0:b - a], onesK[0:33, 0:P],
                              brows["v"][0:33, a:b], start=False, stop=True,
                              psum=pk, reads=("onesK", "br_v"))
                        h0 = a // 64
                        nh_ = (b - a) // 64
                        kb.op("V", "tensor_copy",
                              out=vb.rearrange("p (h c) -> p h c", h=NH)[
                                  :, h0:h0 + nh_, 0:64],
                              in_=pr[0:P, 0:b - a].rearrange(
                                  "p (h c) -> p h c", c=64),
                              reads=(pk,), writes=(f"A_vst{jb}",))
                    if not cfg.get("NOV", False):
                        kb.dma("G",
                               vsc.rearrange("(h p) m -> p h m", p=P)[
                                   :, :, j * 65:(j + 1) * 65],
                               vb.rearrange("p (h c) -> p h c", h=NH),
                               stream=f"vsc{jb}",
                               reads=(f"A_vst{jb}", f"A_vst{jb}_ones"),
                               writes=(f"vsc_f{j}",))
                    if own_t0 <= j < own_t0 + own_nt:
                        jo = j - own_t0
                        proj("q", gslot, jj, 0)
                        krz = krb.rearrange("p (h z) -> p h z", z=64)
                        rope_psum(kb, 0, krz[:, :, 0:R], "A_krb", j)
                        kb.op("V", "tensor_tensor",
                              out=rt2.rearrange("p (h r) -> p h r", h=NH),
                              in0=krz[:, :, 0:R],
                              in1=krz[:, :, 0:R], op=ALU.mult,
                              reads=("A_krb",), writes=("A_rt2",))
                        kb.op("V", "tensor_reduce", sqQ[:, :],
                              rt2.rearrange("p (h r) -> p h r", h=NH),
                              axis=AX.X, op=ALU.add,
                              reads=("A_rt2",), writes=("A_sqQ",))
                        kb.op("S", "activation", sq2[:, :], sqQ[:, :],
                              AF.Ln, bias=1e-30,
                              reads=("A_sqQ",), writes=("A_sq2",))
                        with nc.allow_low_precision(
                                reason="softmax shift bound; slack is huge"):
                            kb.op("S", "activation", krz[:, :, 32:33],
                                  sq2.rearrange("p (h o) -> p h o", o=1),
                                  AF.Exp, scale=0.5,
                                  reads=("A_sq2",), writes=("A_krb",))
                        kb.dma("G", qrsc[jo * P:(jo + 1) * P, :], krb[:, :],
                               stream="qrsc",
                               reads=("A_krb", f"qrsc_f{jo}"),
                               writes=(f"qrsc_f{jo}",))
            kb.fence_dma("V")
            kb.fence_dma("S")

        if cfg.get("STOP", "") == "0":
            def phase_a(kb, role):
                kb.fence_dma("G")
        with nc.If(pid < cfg["ROLE_SPLIT"]):
            phase_a(kbA, 0)
        with nc.Else():
            phase_a(kbB, 1)
        nc.all_engine_barrier()
        kbA.barrier_reset()
        kbB.barrier_reset()
        stAL.close()

        stA2 = ExitStack()
        EA2 = stA2.enter_context
        mkt = EA2(nc.sbuf_tensor("A2_mkt", [16, P], F32, side="right"))
        mk1 = EA2(nc.sbuf_tensor("A2_mk1", [16, 4], F32, side="right"))
        mkd = EA2(nc.sbuf_tensor("A2_mkd", [16, 16], F32, side="right"))
        mkb = EA2(nc.sbuf_tensor("A2_mkb", [P, NH], F32, side="right"))
        mks = EA2(nc.sbuf_tensor("A2_mks", [16, T], BF16, side="right"))
        pps2 = EA2(nc.psum_tensor("A2_pps", [P, 512], F32))

        def phase_a2(kb, role):
            own_t0 = 0 if role == 0 else QA_T
            own_nt = QA_T if role == 0 else NT - QA_T
            kv_nt = QA_T if role == 0 else NT
            pps = pps2
            kb.op("T", "transpose", pps[0:NH, 0:P], mkacc[:, :], identf[:, :],
                  reads=("mkacc", "identf"), writes=("A2_pps",))
            kb.op("V", "tensor_copy", out=mkt[0:NH, :], in_=pps[0:NH, 0:P],
                  reads=("A2_pps",), writes=("A2_mkt",))
            kb.op("V", "tensor_reduce", mk1[0:NH, 0:1], mkt[0:NH, :],
                  axis=AX.X, op=ALU.max, reads=("A2_mkt",), writes=("A2_mk1",))
            kb.op("S", "activation", mk1[0:NH, 1:2], mk1[0:NH, 0:1],
                  AF.Ln, bias=1e-30, reads=("A2_mk1",), writes=("A2_l",))
            kb.op("S", "activation", mk1[0:NH, 2:3], mk1[0:NH, 1:2],
                  AF.Exp, scale=0.5, reads=("A2_l",), writes=("A2_s",))
            kb.op("V", "tensor_scalar", mk1[0:NH, 3:4], mk1[0:NH, 2:3],
                  -1.0, None, ALU.mult, reads=("A2_s",), writes=("A2_n",))
            kb.op("V", "tensor_scalar", mkd[0:NH, 0:NH],
                  identf[0:NH, 0:NH], mk1[0:NH, 3:4], None, ALU.mult,
                  reads=("identf", "A2_n"), writes=("A2_mkd",))
            kb.mm(pps[0:P, 0:NH], onesf[0:NH, :], mkd[0:NH, 0:NH],
                  start=True, stop=True, psum="A2_pps",
                  reads=("onesf", "A2_mkd"))
            kb.op("V", "tensor_copy", out=mkb[:, :], in_=pps[0:P, 0:NH],
                  reads=("A2_pps",), writes=("A2_mkb",))
            kb.op("V", "memset", mks[0:NH, 0:kv_nt * P], 1.0,
                  writes=("A2_mks",))
            kb.op("V", "tensor_scalar", mks[0:NH, 0:kv_nt * P],
                  mks[0:NH, 0:kv_nt * P], mk1[0:NH, 3:4], None, ALU.mult,
                  reads=("A2_mks", "A2_n"), writes=("A2_mks",))
            kb.dma("G", mkdr[0:NH, 0:kv_nt * P], mks[0:NH, 0:kv_nt * P],
                   stream="mkdr", reads=("A2_mks",))
            kb.mark("mkdr_f", "mkdr")
            for c in range(NHP):
                kb.dma(TD2 if c % 2 == 0 else TD,
                       krt[c][0:P, 0:kv_nt * P],
                       krsc[0:kv_nt * P, c * P:(c + 1) * P],
                       stream=f"krtL{c % 4}", transpose=True,
                       reads=tuple([f"krt_{c}"] +
                                   [f"krsc_f{j}" for j in range(kv_nt)]),
                       writes=(f"krt_{c}",))
                kb.dma(TD if c % 2 == 0 else TD2,
                       qrt[c][0:P, 0:own_nt * P],
                       qrsc[0:own_nt * P, c * P:(c + 1) * P],
                       stream=f"qrtL{c % 4}", transpose=True,
                       reads=tuple([f"qrt_{c}"] +
                                   [f"qrsc_f{j}" for j in range(own_nt)]),
                       writes=(f"qrt_{c}",))
            for c in range(NHP):
                for b0, h in ((32, 2 * c), (96, 2 * c + 1)):
                    kb.dma("Q", krt[c][b0:b0 + 1, 0:kv_nt * P],
                           mkdr[h:h + 1, 0:kv_nt * P],
                           stream=f"mkl{c % 4}",
                           reads=(f"krt_{c}", "mkdr_f"),
                           writes=(f"krt_{c}",))
            kb.fence_dma("Q")
            kb.fence_dma("S")

        if cfg.get("STOP", "") in ("0", "A"):
            def phase_a2(kb, role):
                kb.fence_dma("G")
        with nc.If(pid < cfg["ROLE_SPLIT"]):
            phase_a2(kbA, 0)
        with nc.Else():
            phase_a2(kbB, 1)
        nc.all_engine_barrier()
        kbA.barrier_reset()
        kbB.barrier_reset()
        stA2.close()
        stLN.close()
        stTab.close()

        stB = ExitStack()
        EB = stB.enter_context
        vbuf = [EB(nc.sbuf_tensor(f"vbuf{i}", [P, NT * 65], BF16,
                                  side="right")) for i in range(2)]
        ptp = [EB(nc.sbuf_tensor(f"ptp{i}", [P, 2 * W_B], BF16, side="right"))
               for i in range(2)]
        yts = [EB(nc.sbuf_tensor(f"yts{c}", [P, W_B], BF16, side="right"))
               for c in range(NHP)]
        rrow = EB(nc.sbuf_tensor("rrow", [1, max(W_B, H)], F32, side="right"))
        rsb = EB(nc.sbuf_tensor("rsb", [64, W_B], F32, side="right"))
        atp = EB(nc.sbuf_tensor("atp", [P, H], F32, side="right"))
        x2t = EB(nc.sbuf_tensor("x2t", [P, H], F32, side="right"))
        xre = EB(nc.sbuf_tensor("xre", [P, H], F32, side="right"))
        bnsB = EB(nc.sbuf_tensor("bnsB", [P, 6 * (H // BNW)], F32,
                                 side="right"))
        mvB = EB(nc.sbuf_tensor("mvB", [P, 2], F32, side="right"))
        srB = EB(nc.sbuf_tensor("srB", [P, 2], F32, side="right"))
        mvC = EB(nc.sbuf_tensor("mvC", [P, 6], F32, side="right"))
        qst = [EB(nc.sbuf_tensor(f"qst{i}", [P, MAXW], F32, side="right"))
               for i in range(3)]
        qg = EB(nc.sbuf_tensor("qg", [P, MAXW], BF16, side="right"))
        qt16 = EB(nc.sbuf_tensor("qt16", [P, MAXW], BF16, side="right"))
        tstg = EB(nc.sbuf_tensor("tstg", [P, max(H, HID)], BF16,
                                 side="right"))
        woT = [EB(nc.sbuf_tensor(f"woT{hc}", [P, H], BF16, side="right"))
               for hc in range(HC)]
        psc = [EB(nc.psum_tensor(f"psc{i}", [P, 2 * W_B], F32))
               for i in range(2)]
        pyt = [EB(nc.psum_tensor(f"pyt{i}", [P, W_B], F32))
               for i in range(2)]
        c2preB = EB(nc.sbuf_tensor("c2preB", [33, H], F32, side="right"))
        pmo = EB(nc.psum_tensor("pmo", [P, 512], F32))
        wT["o"] = woT

        def phase_b(kb, role):
            own_t0 = 0 if role == 0 else QA_T
            own_nt = QA_T if role == 0 else NT - QA_T
            kb.op("V", "memset", c2preB[:, :], 0.0, writes=("c2preB",))

            for hc in range(HC):
                kb.dma("Q", woT[hc][:, :],
                       wqsc[WOFF["o"]:WOFF["o"] + H, hc * P:(hc + 1) * P],
                       stream=f"wT{hc % 4}", transpose=True,
                       reads=(f"woT{hc}", "wqsc_f"), writes=(f"woT{hc}",))

            mats = [(0, w1, HID // P, H, max(1, H // MAXW), t1sc),
                    (1, w2, H // P, HID, max(1, HID // MAXW), t2sc)]
            qsteps = []

            def q_ld(src, jt, cc, cw, slot):
                kb.dma("Q", qst[slot][:, 0:cw],
                       src[jt * P:(jt + 1) * P, cc * cw:(cc + 1) * cw],
                       stream=f"qst{slot}", reads=(f"qst{slot}",),
                       writes=(f"qst{slot}",))

            def q_reduce(mi, cw, slot):
                kb.op("V", "tensor_reduce", qacc[:, 0:1],
                      qst[slot][:, 0:cw],
                      axis=AX.X, op=ALU.add, apply_absolute_value=True,
                      reads=(f"qst{slot}",), writes=("qacc1",))
                kb.op("V", "tensor_tensor", out=qacc[:, 1 + mi:2 + mi],
                      in0=qacc[:, 1 + mi:2 + mi], in1=qacc[:, 0:1],
                      op=ALU.add, reads=("qacc1", "qaccs"), writes=("qaccs",))

            def q_fin():
                for mi, src, ntile, width, ncc, dstT in mats:
                    n_el = float(ntile * P * width)
                    kb.mm(pmo[0:1, 0:1], qacc[:, 1 + mi:2 + mi],
                          onescol[:, 0:1], start=True, stop=True, psum="pmo",
                          reads=("qaccs", "onescol"))
                    kb.op("V", "tensor_scalar", qtmp[0:1, 4 * mi:4 * mi + 1],
                          pmo[0:1, 0:1], 1.0 / n_el, 1e-5, ALU.mult, ALU.max,
                          reads=("pmo",), writes=(f"qtmp_m{mi}",))
                    kb.op("V", "reciprocal",
                          out=qtmp[0:1, 4 * mi + 1:4 * mi + 2],
                          in_=qtmp[0:1, 4 * mi:4 * mi + 1],
                          reads=(f"qtmp_m{mi}",), writes=(f"qtmp_s{mi}",))
                    kb.op("V", "reciprocal",
                          out=qtmp[0:1, 4 * mi + 2:4 * mi + 3],
                          in_=qtmp[0:1, 4 * mi + 1:4 * mi + 2],
                          reads=(f"qtmp_s{mi}",), writes=(f"qtmp_r{mi}",))
                    kb.mm(pmo[0:P, 0:1], onesf[0:1, 0:P],
                          qtmp[0:1, 4 * mi + 1:4 * mi + 2],
                          start=True, stop=True, psum="pmo",
                          reads=(f"qtmp_s{mi}", "onesf"))
                    kb.op("V", "tensor_copy", out=sc128[:, mi:mi + 1],
                          in_=pmo[0:P, 0:1], reads=("pmo",), writes=("sc128",))
                    kb.mm(pmo[0:P, 0:1], onesf[0:1, 0:P],
                          qtmp[0:1, 4 * mi + 2:4 * mi + 3],
                          start=True, stop=True, psum="pmo",
                          reads=(f"qtmp_r{mi}", "onesf"))
                    kb.op("V", "tensor_copy", out=rc128[:, mi:mi + 1],
                          in_=pmo[0:P, 0:1], reads=("pmo",), writes=("rc128",))

            def q_tern(mi, jt, cc, cw, dstT, slot):
                kb.op("V", "tensor_scalar", qg[:, 0:cw],
                      qst[slot][:, 0:cw],
                      sc128[:, mi:mi + 1], 0.5, ALU.mult, ALU.is_gt,
                      reads=(f"qst{slot}", "sc128"), writes=("qg",))
                kb.op("V", "tensor_scalar", qt16[:, 0:cw],
                      qst[slot][:, 0:cw],
                      sc128[:, mi:mi + 1], -0.5, ALU.mult, ALU.is_lt,
                      reads=(f"qst{slot}", "sc128"), writes=("qt16",))
                kb.op("V", "tensor_tensor", out=qt16[:, 0:cw],
                      in0=qg[:, 0:cw], in1=qt16[:, 0:cw], op=ALU.subtract,
                      reads=("qg", "qt16"), writes=("qt16",))
                kb.dma("G", dstT[jt * P:(jt + 1) * P,
                               cc * cw:(cc + 1) * cw],
                       qt16[:, 0:cw], stream=f"tsc{mi}", reads=("qt16",))

            def q_stage(mi, c):
                if mi == 0:
                    kb.dma("Q", tstg[:, 0:HID],
                           t1sc[0:HID, c * P:(c + 1) * P],
                           stream="tstg", transpose=True,
                           reads=("tstg", "tsc0_f"), writes=("tstg",))
                    kb.dma("G",
                           t1Ts.rearrange("p (s c m) -> p s c m",
                                          c=HC, m=SEG1)[:, :, c, :],
                           tstg[:, 0:HID].rearrange("p (s m) -> p s m",
                                                    m=SEG1),
                           stream="t1Ts", reads=("tstg",))
                else:
                    kb.dma("Q", tstg[:, 0:H],
                           t2sc[0:H, c * P:(c + 1) * P],
                           stream="tstg", transpose=True,
                           reads=("tstg", "tsc1_f"), writes=("tstg",))
                    for pidx, (a2, b2_) in enumerate(_bank_pieces(0, H)):
                        kb.mm(pmo[0:33, 0:b2_ - a2],
                              b2lz[:, c:c + 33], tstg[:, a2:b2_],
                              start=True, stop=True, psum="pmo",
                              reads=("b2lz", "tstg"))
                        kb.op("V", "tensor_tensor",
                              out=c2preB[0:33, a2:b2_],
                              in0=pmo[0:33, 0:b2_ - a2],
                              in1=c2preB[0:33, a2:b2_], op=ALU.add,
                              reads=("pmo", "c2preB"),
                              writes=("c2preB",))
                    kb.op("V", "tensor_scalar", tstg[:, 0:H],
                          tstg[:, 0:H], g2c[:, c:c + 1], None, ALU.mult,
                          reads=("tstg", "g2c"), writes=("tstg",))
                    kb.dma("G",
                           t2Ts.rearrange("p (s c m) -> p s c m",
                                          c=HIDC, m=SEG2)[:, :, c, :],
                           tstg[:, 0:H].rearrange("p (s m) -> p s m",
                                                  m=SEG2),
                           stream="t2Ts", reads=("tstg",))

            def c2pre_fin():
                for a2, b2_ in _bank_pieces(0, H):
                    kb.op("V", "tensor_copy", out=rrow[0:1, 0:b2_ - a2],
                          in_=c2preB[32:33, a2:b2_],
                          reads=("c2preB",), writes=("rrow",))
                    kb.op("V", "tensor_tensor", out=c2pre[0:1, a2:b2_],
                          in0=c2preB[0:1, a2:b2_],
                          in1=rrow[0:1, 0:b2_ - a2], op=ALU.add,
                          reads=("c2preB", "rrow"),
                          writes=("c2pre",))

            nsl_ = [0]

            def mkld(src, jt, cc, cw):
                slot = nsl_[0] % 3
                nsl_[0] += 1
                return (lambda: q_ld(src, jt, cc, cw, slot)), slot

            for mi, src, ntile, width, ncc, dstT in mats:
                for jt in range(ntile):
                    for cc in range(ncc):
                        ld, slot = mkld(src, jt, cc, width // ncc)
                        qsteps.append(
                            (ld, lambda mi=mi, cw=width // ncc,
                             slot=slot: q_reduce(mi, cw, slot)))
            qsteps.append((None, lambda: q_fin()))
            for mi, src, ntile, width, ncc, dstT in mats:
                for jt in range(ntile):
                    for cc in range(ncc):
                        ld, slot = mkld(src, jt, cc, width // ncc)
                        qsteps.append(
                            (ld, lambda mi=mi, jt=jt, cc=cc,
                             cw=width // ncc, dstT=dstT, slot=slot:
                             q_tern(mi, jt, cc, cw, dstT, slot)))
            qsteps.append((None, lambda: kb.mark("tsc0_f", "tsc0")))
            qsteps.append((None, lambda: kb.mark("tsc1_f", "tsc1")))
            for c in range(HC):
                qsteps.append((None, lambda c=c: q_stage(0, c)))
            for c in range(HIDC):
                qsteps.append((None, lambda c=c: q_stage(1, c)))
            qsteps.append((None, lambda: c2pre_fin()))
            qsteps.append((None, lambda: kb.mark("t1Ts_f", "t1Ts")))
            qsteps.append((None, lambda: kb.mark("t2Ts_f", "t2Ts")))
            qi = [0]
            li = [0]

            def run_q(n):
                while n > 0 and qi[0] < len(qsteps):
                    while li[0] < len(qsteps) and li[0] <= qi[0] + 2:
                        ld = qsteps[li[0]][0]
                        if ld is not None:
                            ld()
                        li[0] += 1
                    qsteps[qi[0]][1]()
                    qi[0] += 1
                    n -= 1

            def emit_avs(pend, h, Wq_, ktmax):
                pairs, tslot = pend
                for (kt, lt0, half) in pairs:
                    col0 = lt0 * P
                    kb.mm(pyt[h % 2][0:65, col0:Wq_],
                          vbuf[h % 2][:, kt * 65:(kt + 1) * 65],
                          ptp[tslot][:, half * W_B + col0:half * W_B + Wq_],
                          start=(kt == 0), stop=(kt == ktmax),
                          psum=f"pyt{h % 2}",
                          reads=(f"vbuf{h % 2}", f"ptp{tslot}"))

            chunks = []
            t0 = 0
            while t0 < own_nt:
                w = min(BCH, own_nt - t0)
                chunks.append((t0, w))
                t0 += w

            pend_norm = [None]
            nq_total = None
            for (ct0, Wt) in chunks:
                glob0 = own_t0 + ct0
                ktmax = glob0 + Wt - 1
                nkt = ktmax + 1
                Wq_ = Wt * P
                kb.dma("Q", vbuf[0][:, 0:nkt * 65],
                       vsc[0:P, 0:nkt * 65], stream="vbuf0",
                       reads=tuple(["vbuf0"] +
                                   [f"vsc_f{j}" for j in range(nkt)]),
                       writes=("vbuf0",))
                for h in range(NH):
                    if h + 1 < NH:
                        kb.dma("Q", vbuf[(h + 1) % 2][:, 0:nkt * 65],
                               vsc[(h + 1) * P:(h + 2) * P, 0:nkt * 65],
                               stream=f"vbuf{(h + 1) % 2}",
                               reads=tuple([f"vbuf{(h + 1) % 2}"] +
                                           [f"vsc_f{j}" for j in range(nkt)]),
                               writes=(f"vbuf{(h + 1) % 2}",))
                    c = h // 2
                    b0 = 64 * (h % 2)
                    kkeys = (f"krt_{c}", f"qrt_{c}")
                    pend = None
                    fl = 0
                    kt = 0
                    while kt < nkt:
                        lt0 = max(0, kt - glob0)
                        pairable = (kt + 1 < nkt and kt + 1 <= glob0
                                    and Wq_ == W_B)
                        group = [(kt, lt0, 0)]
                        if pairable:
                            group.append((kt + 1, 0, 1))
                        tslot = fl % 2
                        pk = f"psc{tslot}"
                        for (k2, l2, half) in group:
                            col0 = l2 * P
                            kb.mm(psc[tslot][0:P,
                                             half * W_B + col0:
                                             half * W_B + Wq_],
                                  krt[c][b0:b0 + 33, k2 * P:(k2 + 1) * P],
                                  qrt[c][b0:b0 + 33,
                                         ct0 * P + col0:ct0 * P + Wq_],
                                  start=True, stop=True, psum=pk,
                                  reads=kkeys)
                        lo = group[0][1] * P
                        hi = (len(group) - 1) * W_B + Wq_
                        kb.op("S", "activation",
                              ptp[tslot][:, lo:hi], psc[tslot][0:P, lo:hi],
                              AF.Exp, scale=SCALE,
                              reads=(pk,), writes=(f"ptp{tslot}",))
                        for (k2, l2, half) in group:
                            if k2 >= glob0:
                                col0 = l2 * P
                                kb.op("V", "tensor_tensor",
                                      out=ptp[tslot][:,
                                                     half * W_B + col0:
                                                     half * W_B + col0 + P],
                                      in0=ptp[tslot][:,
                                                     half * W_B + col0:
                                                     half * W_B + col0 + P],
                                      in1=tri01[:, :], op=ALU.mult,
                                      reads=(f"ptp{tslot}", "tri01"),
                                      writes=(f"ptp{tslot}",))
                        if pend is not None:
                            emit_avs(pend, h, Wq_, ktmax)
                        if fl == 0 and pend_norm[0] is not None:
                            pend_norm[0]()
                            pend_norm[0] = None
                        pend = (group, tslot)
                        fl += 1
                        kt += len(group)
                    if pend is not None:
                        emit_avs(pend, h, Wq_, ktmax)

                    def mknorm(h=h, c=c, b0=b0, Wq_=Wq_):
                        def donorm():
                            kb.op("S", "activation", rrow[0:1, 0:Wq_],
                                  pyt[h % 2][64:65, 0:Wq_], AF.Ln,
                                  reads=(f"pyt{h % 2}",), writes=("rrow",))
                            kb.op("S", "activation", rrow[0:1, 0:Wq_],
                                  rrow[0:1, 0:Wq_], AF.Exp, scale=-1.0,
                                  reads=("rrow",), writes=("rrow",))
                            kb.mm(pmo[0:64, 0:Wq_], onesf[0:1, 0:64],
                                  rrow[0:1, 0:Wq_], start=True, stop=True,
                                  psum="pmo", reads=("rrow", "onesf"))
                            kb.op("V", "tensor_copy", out=rsb[0:64, 0:Wq_],
                                  in_=pmo[0:64, 0:Wq_], reads=("pmo",),
                                  writes=("rsb",))
                            kb.op("V", "tensor_tensor",
                                  out=yts[c][b0:b0 + 64, 0:Wq_],
                                  in0=pyt[h % 2][0:64, 0:Wq_],
                                  in1=rsb[0:64, 0:Wq_],
                                  op=ALU.mult,
                                  reads=(f"pyt{h % 2}", "rsb"),
                                  writes=(f"yts{c}",))
                        return donorm
                    pend_norm[0] = mknorm()
                    run_q(2)
                if pend_norm[0] is not None:
                    pend_norm[0]()
                    pend_norm[0] = None
                run_q(2)
                for lt in range(Wt):
                    jo = ct0 + lt
                    for a, b in _bank_pieces(0, H):
                        for hc in range(HC):
                            kb.mm(pmo[0:P, 0:b - a],
                                  yts[hc][0:P, lt * P:(lt + 1) * P],
                                  wT["o"][hc][:, a:b],
                                  start=(hc == 0), stop=(hc == HC - 1),
                                  psum="pmo",
                                  reads=(f"yts{hc}", f"woT{hc}"))
                        kb.op("V", "tensor_tensor", out=atp[:, a:b],
                              in0=pmo[0:P, 0:b - a], in1=boB[:, a:b],
                              op=ALU.add,
                              reads=("pmo", "boB"), writes=("atp",))
                    for cb in range(H // BNW):
                        kb.op("V", "bn_stats", bnsB[:, cb * 6:(cb + 1) * 6],
                              atp[:, cb * BNW:(cb + 1) * BNW],
                              reads=("atp",), writes=("bnsB",))
                    kb.op("V", "bn_aggr", mvB[:, :], bnsB[:, :],
                          reads=("bnsB",), writes=("mvB",))
                    rsqrt_sc(kb, srB, "srB", mvB[:, 1:2], ("mvB",))
                    kb.dma("Q", xre[:, :],
                           xs[(own_t0 + jo) * P:(own_t0 + jo + 1) * P, :],
                           stream="xre", reads=("xre",), writes=("xre",))
                    kb.op("V", "tensor_scalar", atp[:, :], atp[:, :],
                          mvB[:, 0:1], srB[:, 0:1], ALU.subtract, ALU.mult,
                          reads=("atp", "mvB", "srB"), writes=("atp",))
                    kb.op("V", "tensor_tensor", out=x2t[:, :], in0=atp[:, :],
                          in1=xre[:, :], op=ALU.add,
                          reads=("atp", "xre"), writes=("x2t",))
                    kb.dma("G", x2s[jo * P:(jo + 1) * P, :], x2t[:, :],
                           stream=f"x2s{jo % 2}", reads=("x2t",),
                           writes=(f"x2s_f{jo}",))
                    for cb in range(H // BNW):
                        kb.op("V", "bn_stats", bnsB[:, cb * 6:(cb + 1) * 6],
                              x2t[:, cb * BNW:(cb + 1) * BNW],
                              reads=("x2t",), writes=("bnsB",))
                    kb.op("V", "bn_aggr", mvC[:, 0:2], bnsB[:, :],
                          reads=("bnsB",), writes=("mvC",))
                    kb.op("S", "activation", mvC[:, 2:3], mvC[:, 1:2], AF.Ln,
                          bias=LN_EPS, reads=("mvC",), writes=("mvC_l",))
                    kb.op("S", "activation", mvC[:, 3:4], mvC[:, 2:3],
                          AF.Exp, scale=-0.5, reads=("mvC_l",),
                          writes=("mvC_r1",))
                    kb.op("V", "tensor_tensor", out=mvC[:, 4:5],
                          in0=mvC[:, 3:4], in1=mvC[:, 3:4], op=ALU.mult,
                          reads=("mvC_r1",), writes=("mvC_v2",))
                    kb.op("V", "tensor_tensor", out=mvC[:, 4:5],
                          in0=mvC[:, 4:5], in1=mvC[:, 1:2], op=ALU.mult,
                          reads=("mvC_v2", "mvC"), writes=("mvC_v2",))
                    kb.op("S", "activation", mvC[:, 5:6], mvC[:, 4:5], AF.Ln,
                          bias=LN_EPS, reads=("mvC_v2",), writes=("mvC_l2",))
                    kb.op("S", "activation", mvC[:, 5:6], mvC[:, 5:6],
                          AF.Exp, scale=-0.5, reads=("mvC_l2",),
                          writes=("mvC_r2",))
                    kb.op("V", "tensor_copy", out=mu1[:, jo:jo + 1],
                          in_=mvC[:, 0:1], reads=("mvC",), writes=("mu1",))
                    kb.op("V", "tensor_tensor", out=rr1[:, jo:jo + 1],
                          in0=mvC[:, 3:4], in1=mvC[:, 5:6], op=ALU.mult,
                          reads=("mvC_r1", "mvC_r2"), writes=("rr1",))
                    run_q(3)
            run_q(10 ** 9)
            kb.fence_dma("V")
            kb.fence_dma("S")

        if cfg.get("STOP", "") in ("0", "A", "A2"):
            def phase_b(kb, role):
                kb.fence_dma("G")
        with nc.If(pid < cfg["ROLE_SPLIT"]):
            phase_b(kbA, 0)
        with nc.Else():
            phase_b(kbB, 1)
        nc.all_engine_barrier()
        kbA.barrier_reset()
        kbB.barrier_reset()
        stB.close()
        stAB.close()

        stC = ExitStack()
        EC = stC.enter_context
        x2r = [EC(nc.sbuf_tensor(f"x2r{i}", [P, H], F32)) for i in range(2)]
        x2e = [EC(nc.sbuf_tensor(f"x2e{i}", [P, H], F32)) for i in range(2)]
        xfT = [[EC(nc.sbuf_tensor(f"xfT{tb}_{hc}", [P, FCH * P], BF16))
                for hc in range(HC)] for tb in range(2)]
        w1tb = [EC(nc.sbuf_tensor(f"w1tb{i}", [P, HC * 512], BF16))
                for i in range(2)]
        w2tb = EC(nc.sbuf_tensor("w2tb", [P, HIDC * 512], BF16))
        hTt = EC(nc.sbuf_tensor("hTt", [P, HIDC * FCH * P], BF16))
        sqr = [EC(nc.sbuf_tensor(f"sqr{i}", [P, FCH * P], BF16))
               for i in range(3)]
        varrow = EC(nc.sbuf_tensor("varrow", [1, FCH * P], F32))
        mu2row = EC(nc.sbuf_tensor("mu2row", [1, FCH * P], F32))
        c1row = EC(nc.sbuf_tensor("c1row", [33, HID], BF16))
        c2b = EC(nc.sbuf_tensor("c2b", [P, H], F32))
        c2g = EC(nc.sbuf_tensor("c2g", [1, H], F32))
        crow = EC(nc.sbuf_tensor("crow", [1, 512], F32))
        bhrow = EC(nc.sbuf_tensor("bhrow", [1, 512], F32))
        bnsC = EC(nc.sbuf_tensor("bnsC", [P, FCH * 6 * (HID // BNW2)],
                                  F32))
        mvD = EC(nc.sbuf_tensor("mvD", [P, 2], F32))
        srD = EC(nc.sbuf_tensor("srD", [P, 2], F32))
        rsD = EC(nc.sbuf_tensor("rsD", [P, FCH], F32))
        murow = EC(nc.sbuf_tensor("murow", [1, FCH * P], F32))
        xfC = EC(nc.sbuf_tensor("xfC", [P, H], BF16))
        ffo = [EC(nc.sbuf_tensor(f"ffo{i}", [P, H], F32)) for i in range(FCH)]
        oto = EC(nc.sbuf_tensor("oto", [P, H], F32))
        pf = [EC(nc.psum_tensor(f"pf{i}", [P, 512], F32)) for i in range(3)]
        pst = [EC(nc.psum_tensor(f"pst{i}", [P, 512], F32))
               for i in range(2)]
        pc2 = [EC(nc.psum_tensor(f"pc2{i}", [P, 512], F32)) for i in range(2)]
        pmu = EC(nc.psum_tensor("pmu", [P, 512], F32))

        def phase_c(kb, role):
            own_t0 = 0 if role == 0 else QA_T
            own_nt = QA_T if role == 0 else NT - QA_T
            kb.op("V", "memset", c1row[:, :], 0.0, writes=("c1row",))
            first = True

            chunks = []
            t0 = 0
            while t0 < own_nt:
                w = min(FCH, own_nt - t0)
                chunks.append((t0, w))
                t0 += w
            def stage_chunk(ct0, Wt, tb):
                for i in range(Wt):
                    jo = ct0 + i
                    jb = i % 2
                    kb.dma("Q", x2r[jb][:, :], x2s[jo * P:(jo + 1) * P, :],
                           stream=f"x2r{jb}",
                           reads=(f"x2r{jb}", f"x2s_f{jo}"),
                           writes=(f"x2r{jb}",))
                    kb.op("V", "tensor_scalar", xfC[:, 0:H],
                          x2r[jb][:, :],
                          mu1[:, jo:jo + 1], rr1[:, jo:jo + 1],
                          ALU.subtract, ALU.mult,
                          reads=(f"x2r{jb}", "mu1", "rr1"),
                          writes=("xfC",))
                    kb.dma("G", xfsc[jo * P:(jo + 1) * P, :], xfC[:, 0:H],
                           stream="xfsc", reads=("xfC",),
                           writes=(f"xfsc_f{jo}",))
                for hc in range(HC):
                    kb.dma("Q",
                           xfT[tb][hc][:, 0:Wt * P],
                           xfsc[ct0 * P:(ct0 + Wt) * P, hc * P:(hc + 1) * P],
                           stream=f"xfT{hc % 4}", transpose=True,
                           reads=tuple([f"xfTg{tb}_{hc}"] +
                                       [f"xfsc_f{ct0 + i}"
                                        for i in range(Wt)]),
                           writes=(f"xfTg{tb}_{hc}",))
                for hc in range(HC):
                    kb.op("V", "tensor_scalar", xfT[tb][hc][:, 0:Wt * P],
                          xfT[tb][hc][:, 0:Wt * P], g1c[:, hc:hc + 1],
                          None, ALU.mult,
                          reads=(f"xfTg{tb}_{hc}", "g1c"),
                          writes=(f"xfTg{tb}_{hc}",))

            stage_chunk(chunks[0][0], chunks[0][1], 0)
            for ci, (ct0, Wt) in enumerate(chunks):
                tb = ci % 2
                if ci + 1 < len(chunks):
                    stage_chunk(chunks[ci + 1][0], chunks[ci + 1][1],
                                (ci + 1) % 2)
                hpieces = []
                for hs in range(NSL):
                    for a, b in _bank_pieces(0, SL):
                        hpieces.append((hs, a, b))
                pend_stats = []
                for pi_, (hs, a, b) in enumerate(hpieces):
                    wk = f"w1tb{pi_ % 2}"
                    wb = w1tb[pi_ % 2]
                    ga = hs * SL + a
                    slab = (ga // SEG1) * (HC * SEG1)
                    kb.dma("Q", wb[:, 0:HC * (b - a)],
                           t1Ts[:, slab:slab + HC * (b - a)],
                           stream=wk, reads=(wk, "t1Ts_f"), writes=(wk,))
                    if first:
                        for hc in range(HC):
                            kb.mm(pc2[0][0:33, 0:b - a],
                                  b1lz[:, hc:hc + 33],
                                  wb[:, hc * (b - a):(hc + 1) * (b - a)],
                                  start=(hc == 0), stop=(hc == HC - 1),
                                  psum="pc20", reads=("b1lz", wk))
                        kb.dma("Q", bhrow[0:1, 0:b - a],
                               b1[None, ga:ga + b - a], stream="bhrow",
                               reads=("bhrow",), writes=("bhrow",))
                        kb.op("V", "tensor_scalar", crow[0:1, 0:b - a],
                              bhrow[0:1, 0:b - a], qtmp[0:1, 1:2], None,
                              ALU.mult, reads=("bhrow", "qtmp_s0"),
                              writes=("crow",))
                        kb.op("V", "tensor_tensor",
                              out=crow[0:1, 0:b - a],
                              in0=pc2[0][0:1, 0:b - a],
                              in1=crow[0:1, 0:b - a], op=ALU.add,
                              reads=("pc20", "crow"), writes=("crow",))
                        kb.op("V", "tensor_tensor",
                              out=crow[0:1, 0:b - a],
                              in0=pc2[0][32:33, 0:b - a],
                              in1=crow[0:1, 0:b - a], op=ALU.add,
                              reads=("pc20", "crow"), writes=("crow",))
                        kb.op("V", "tensor_copy",
                              out=c1row[0:1, ga:ga + b - a],
                              in_=crow[0:1, 0:b - a], reads=("crow",),
                              writes=("c1row",))
                        kb.op("V", "tensor_tensor",
                              out=c1row[32:33, ga:ga + b - a],
                              in0=crow[0:1, 0:b - a],
                              in1=c1row[0:1, ga:ga + b - a],
                              op=ALU.subtract, reads=("crow", "c1row"),
                              writes=("c1row",))
                    for sub in range((b - a) // P):
                        hcg = ga // P + sub
                        bank = (pi_ * ((b - a) // P) + sub) % 3
                        pk = f"pf{bank}"
                        pp = pf[bank]
                        for hc in range(HC):
                            kb.mm(pp[0:P, 0:Wt * P],
                                  wb[:, hc * (b - a) + sub * P:
                                     hc * (b - a) + (sub + 1) * P],
                                  xfT[tb][hc][:, 0:Wt * P],
                                  start=(hc == 0), stop=False,
                                  psum=pk, reads=(f"xfTg{tb}_{hc}", wk))
                        kb.mm(pp[0:P, 0:Wt * P],
                              c1row[0:33, hcg * P:(hcg + 1) * P],
                              onesK[0:33, 0:Wt * P],
                              start=False, stop=True, psum=pk,
                              reads=("onesK", "c1row"))
                        kb.op("S", "activation",
                              hTt[:, hcg * FCH * P:hcg * FCH * P + Wt * P],
                              pp[0:P, 0:Wt * P],
                              AF.Gelu, scale=rc128[:, 0:1],
                              reads=(pk, "rc128"),
                              writes=(f"hTt{hcg}",))
                        kb.op("V", "tensor_tensor",
                              out=sqr[hcg % 3][:, 0:Wt * P],
                              in0=hTt[:, hcg * FCH * P:
                                      hcg * FCH * P + Wt * P],
                              in1=hTt[:, hcg * FCH * P:
                                      hcg * FCH * P + Wt * P],
                              op=ALU.mult,
                              reads=(f"hTt{hcg}",),
                              writes=(f"sqr{hcg % 3}",))

                        def mkstats(hcg=hcg, Wt=Wt):
                            def em():
                                kb.mm(pst[0][0:33, 0:Wt * P],
                                      onesZ[:, 0:33],
                                      hTt[:, hcg * FCH * P:
                                          hcg * FCH * P + Wt * P],
                                      start=(hcg == 0),
                                      stop=(hcg == HIDC - 1),
                                      psum="pst0",
                                      reads=("onesZ", f"hTt{hcg}"))
                                kb.mm(pst[1][0:33, 0:Wt * P],
                                      onesZ[:, 0:33],
                                      sqr[hcg % 3][:, 0:Wt * P],
                                      start=(hcg == 0),
                                      stop=(hcg == HIDC - 1),
                                      psum="pst1",
                                      reads=("onesZ", f"sqr{hcg % 3}"))
                            return em
                        pend_stats.append(mkstats())
                        if len(pend_stats) > 2:
                            pend_stats.pop(0)()
                for f_ in pend_stats:
                    f_()
                pend_stats = []
                kb.op("V", "tensor_scalar", murow[0:1, 0:Wt * P],
                      pst[0][0:1, 0:Wt * P], -1.0 / HID, None, ALU.mult,
                      reads=("pst0",), writes=("murow",))
                kb.op("V", "tensor_scalar", varrow[0:1, 0:Wt * P],
                      pst[1][0:1, 0:Wt * P], 1.0 / HID, None, ALU.mult,
                      reads=("pst1",), writes=("varrow",))
                kb.op("V", "tensor_tensor", out=mu2row[0:1, 0:Wt * P],
                      in0=murow[0:1, 0:Wt * P], in1=murow[0:1, 0:Wt * P],
                      op=ALU.mult, reads=("murow",), writes=("mu2row",))
                kb.op("V", "tensor_tensor", out=varrow[0:1, 0:Wt * P],
                      in0=varrow[0:1, 0:Wt * P], in1=mu2row[0:1, 0:Wt * P],
                      op=ALU.subtract, reads=("varrow", "mu2row"),
                      writes=("varrow",))
                kb.op("S", "activation", varrow[0:1, 0:Wt * P],
                      varrow[0:1, 0:Wt * P], AF.Ln, bias=LN_EPS,
                      reads=("varrow",), writes=("varrow",))
                kb.op("S", "activation", varrow[0:1, 0:Wt * P],
                      varrow[0:1, 0:Wt * P], AF.Exp, scale=-0.5,
                      reads=("varrow",), writes=("varrow",))
                kb.op("V", "tensor_scalar", varrow[0:1, 0:Wt * P],
                      varrow[0:1, 0:Wt * P], qtmp[0:1, 6:7], None,
                      ALU.mult, reads=("varrow", "qtmp_r1"),
                      writes=("varrow",))
                for i in range(Wt):
                    kb.op("T", "transpose", pmu[0:P, 0:1],
                          varrow[0:1, i * P:(i + 1) * P],
                          identf[0:1, 0:1],
                          reads=("varrow", "identf"), writes=("pmu",))
                    kb.op("V", "tensor_copy", out=rsD[:, i:i + 1],
                          in_=pmu[0:P, 0:1], reads=("pmu",),
                          writes=("rsD",))
                for a, b in _bank_pieces(0, H):
                    slab2 = (a // SEG2) * (HIDC * SEG2)
                    kb.dma("Q", w2tb[:, 0:HIDC * (b - a)],
                           t2Ts[:, slab2:slab2 + HIDC * (b - a)],
                           stream="w2tb", reads=("w2tb", "t2Ts_f"),
                           writes=("w2tb",))
                    if first:
                        kb.dma("Q", bhrow[0:1, 0:b - a], b2[None, a:b],
                               stream="bhrow", reads=("bhrow",),
                               writes=("bhrow",))
                        kb.op("V", "tensor_scalar", crow[0:1, 0:b - a],
                              bhrow[0:1, 0:b - a], qtmp[0:1, 5:6], None,
                              ALU.mult, reads=("bhrow", "qtmp_s1"),
                              writes=("crow",))
                        kb.op("V", "tensor_tensor", out=crow[0:1, 0:b - a],
                              in0=c2pre[0:1, a:b],
                              in1=crow[0:1, 0:b - a],
                              op=ALU.add, reads=("c2pre", "crow"),
                              writes=("crow",))
                        kb.mm(pc2[0][0:P, 0:b - a], onesf[0:1, 0:P],
                              crow[0:1, 0:b - a], start=True, stop=True,
                              psum="pc20", reads=("crow", "onesf"))
                        kb.op("V", "tensor_scalar", c2b[:, a:b],
                              pc2[0][0:P, 0:b - a], rc128[:, 1:2], None,
                              ALU.mult, reads=("pc20", "rc128"),
                              writes=("c2b",))
                        for hc in range(HIDC):
                            kb.mm(pmu[0:33, 0:b - a], onesZ[:, 0:33],
                                  w2tb[:, hc * (b - a):(hc + 1) * (b - a)],
                                  start=(hc == 0), stop=(hc == HIDC - 1),
                                  psum="pmu", reads=("onesZ", "w2tb"))
                        kb.op("V", "tensor_copy", out=c2g[0:1, a:b],
                              in_=pmu[0:1, 0:b - a],
                              reads=("pmu",), writes=("c2g",))
                    for i in range(Wt):
                        pk2 = f"pc2{i % 2}"
                        pq = pc2[i % 2]
                        for hc in range(HIDC):
                            kb.mm(pq[0:P, 0:b - a],
                                  hTt[:, hc * FCH * P + i * P:
                                      hc * FCH * P + (i + 1) * P],
                                  w2tb[:, hc * (b - a):(hc + 1) * (b - a)],
                                  start=(hc == 0), stop=False,
                                  psum=pk2, reads=(f"hTt{hc}", "w2tb"))
                        kb.mm(pq[0:P, 0:b - a],
                              murow[0:1, i * P:(i + 1) * P], c2g[0:1, a:b],
                              start=False, stop=True, psum=pk2,
                              reads=("murow", "c2g"))
                        kb.op("V", "tensor_scalar", pq[0:P, 0:b - a],
                              pq[0:P, 0:b - a], rsD[:, i:i + 1], None,
                              ALU.mult, reads=(pk2, "rsD"),
                              writes=(pk2,))
                        kb.op("V", "tensor_tensor", out=ffo[i][:, a:b],
                              in0=pq[0:P, 0:b - a], in1=c2b[:, a:b],
                              op=ALU.add, reads=(pk2, "c2b"),
                              writes=(f"ffo{i}",))
                for i in range(Wt):
                    jo = ct0 + i
                    kb.dma("Q", x2e[i % 2][:, :],
                           x2s[jo * P:(jo + 1) * P, :],
                           stream=f"x2e{i % 2}",
                           reads=(f"x2e{i % 2}", f"x2s_f{jo}"),
                           writes=(f"x2e{i % 2}",))
                    for cb in range(H // BNW):
                        kb.op("V", "bn_stats", bnsC[:, cb * 6:(cb + 1) * 6],
                              ffo[i][:, cb * BNW:(cb + 1) * BNW],
                              reads=(f"ffo{i}",), writes=("bnsC",))
                    kb.op("V", "bn_aggr", mvD[:, :],
                          bnsC[:, 0:6 * (H // BNW)],
                          reads=("bnsC",), writes=("mvD",))
                    rsqrt_sc(kb, srD, "srD", mvD[:, 1:2], ("mvD",))
                    kb.op("V", "tensor_scalar", oto[:, :], ffo[i][:, :],
                          mvD[:, 0:1], srD[:, 0:1], ALU.subtract, ALU.mult,
                          reads=(f"ffo{i}", "mvD", "srD"), writes=("oto",))
                    kb.op("V", "tensor_tensor", out=oto[:, :],
                          in0=oto[:, :], in1=x2e[i % 2][:, :], op=ALU.add,
                          reads=("oto", f"x2e{i % 2}"), writes=("oto",))
                    kb.dma("G", out[jo * P:(jo + 1) * P, :], oto[:, :],
                           stream="out", reads=("oto",))
                first = False
            kb.fence_dma("G")
            kb.fence_dma("Q")
            kb.fence_dma("S")

        if cfg.get("STOP", "") in ("0", "A", "A2", "B"):
            def phase_c(kb, role):
                kb.fence_dma("G")
        with nc.If(pid < cfg["ROLE_SPLIT"]):
            phase_c(kbA, 0)
        with nc.Else():
            phase_c(kbB, 1)
        stC.close()

    return nc



_CACHE = {}


def _get_program(key, cfg):
    if key not in _CACHE:
        _CACHE[key] = build_program(cfg)
    return _CACHE[key]


def _ntabs(T, R):
    t = np.arange(1, T + 1, dtype=np.float64)[:, None]
    i = np.arange(R, dtype=np.float64)
    theta = (1.0 / 10000.0) ** (i / R)
    ang = t * theta
    n1 = np.round(ang / (2 * np.pi)).astype(np.float32)
    n2 = np.round((ang + np.pi / 2) / (2 * np.pi)).astype(np.float32)
    return n1, n2


def make_in_maps(cfg, inputs):
    T = cfg["T"]
    n1, n2 = _ntabs(T, 32)
    shared = {
        "Wq": inputs["L0_attn_Wq"], "bq": inputs["L0_attn_bq"],
        "Wk": inputs["L0_attn_Wk"], "bk": inputs["L0_attn_bk"],
        "Wv": inputs["L0_attn_Wv"], "bv": inputs["L0_attn_bv"],
        "Wo": inputs["L0_attn_Wo"], "bo": inputs["L0_attn_bo"],
        "g1": inputs["L0_ffn_ln1_g"], "b1ln": inputs["L0_ffn_ln1_b"],
        "w1": inputs["L0_ffn_w1"], "b1": inputs["L0_ffn_b1"],
        "g2": inputs["L0_ffn_ln2_g"], "b2ln": inputs["L0_ffn_ln2_b"],
        "w2": inputs["L0_ffn_w2"], "b2": inputs["L0_ffn_b2"],
        "ntab1": n1, "ntab2": n2,
    }
    shared = {k: np.ascontiguousarray(np.asarray(v, dtype=np.float32))
              for k, v in shared.items()}
    x = np.ascontiguousarray(np.asarray(inputs["L0_attn_x"],
                                        dtype=np.float32))
    nseq = x.shape[0]
    in_maps = []
    for c in range(cfg["NCORES"]):
        seq = c % nseq
        m = dict(shared)
        m["xs"] = x[seq]
        in_maps.append(m)
    return in_maps


def assemble_out(cfg, results, xshape):
    T, QA_T = cfg["T"], cfg["QA_T"]
    QA = QA_T * P
    nseq = xshape[0]
    full = np.empty(xshape, dtype=np.float32)
    for c in range(cfg["NCORES"]):
        seq = c % nseq
        o = np.asarray(results[c]["out"])
        if c < cfg["ROLE_SPLIT"]:
            full[seq, 0:QA] = o[0:QA]
        else:
            full[seq, QA:T] = o[0:T - QA]
    return full


def kernel(**inputs):
    cfg = full_cfg()
    nc = _get_program("full", cfg)
    in_maps = make_in_maps(cfg, inputs)
    res = run_bass_kernel_spmd(nc, in_maps,
                               core_ids=list(range(cfg["NCORES"])))
    x = np.asarray(inputs["L0_attn_x"])
    return assemble_out(cfg, res.results, x.shape)


# version 26
# speedup vs baseline: 1.2016x; 1.0635x over previous
import bisect
import math
from contextlib import ExitStack

import numpy as np

import concourse.bass as bass
import concourse.mybir as mybir
from concourse.bass_utils import run_bass_kernel_spmd

F32 = mybir.dt.float32
BF16 = mybir.dt.bfloat16
AF = mybir.ActivationFunctionType
ALU = mybir.AluOpType
AX = mybir.AxisListType

TWO_PI = 2.0 * math.pi
C1 = float(np.float32(TWO_PI))
C2 = float(np.float32(TWO_PI - C1))
C3 = TWO_PI - C1 - C2
NEG_BIG = -8.0e10
LN_EPS = 1e-5
P = 128


def full_cfg():
    return dict(T=4096, H=1024, NH=16, HID=4096, QA_T=15, BCH=4, FCH=3,
                NCORES=8, ROLE_SPLIT=4, MAXW=512)


def mini_cfg():
    return dict(T=512, H=256, NH=4, HID=512, QA_T=2, BCH=2, FCH=1,
                NCORES=2, ROLE_SPLIT=1)


def med_cfg():
    return dict(T=1024, H=256, NH=4, HID=2048, QA_T=3, BCH=2, FCH=2,
                NCORES=2, ROLE_SPLIT=1, MAXW=512)


def med2_cfg():
    return dict(T=1024, H=1024, NH=16, HID=2048, QA_T=5, BCH=4, FCH=2,
                NCORES=2, ROLE_SPLIT=1, MAXW=1024)



class _Eng:
    def __init__(self, name, eng, semname):
        self.name = name
        self.eng = eng
        self.semname = semname
        self.pos = 0
        self.inc_pos = []
        self.inc_cnt = []
        self.count = 0
        self.handles = {}
        self.waited = {}


class Kb:

    def __init__(self, nc, sems, streams):
        self.nc = nc
        self.sems = sems
        self.streams = streams
        self.E = {
            "T": _Eng("T", nc.tensor, "et"),
            "V": _Eng("V", nc.vector, "ev"),
            "S": _Eng("S", nc.scalar, "es"),
            "G": _Eng("G", nc.gpsimd, "eg"),
            "Q": _Eng("Q", nc.sync, "eq"),
        }
        self.dmacnt = {}
        self.res = {}

    def clone_counters_from(self, other):
        for n, e in self.E.items():
            oe = other.E[n]
            e.pos = oe.pos
            e.inc_pos = list(oe.inc_pos)
            e.inc_cnt = list(oe.inc_cnt)
            e.count = oe.count
            e.handles = dict(oe.handles)
            e.waited = dict(oe.waited)
        self.dmacnt = dict(other.dmacnt)
        self.res = {k: {"w": v["w"], "r": dict(v["r"])}
                    for k, v in other.res.items()}

    def _resolve(self, tok):
        if tok[0] == "D":
            st = tok[1]
            return (self.streams[st], self.dmacnt[st])
        _, en, pos = tok
        e = self.E[en]
        i = bisect.bisect_left(e.inc_pos, pos)
        if i < len(e.inc_pos):
            return (e.semname, e.inc_cnt[i])
        ins = e.handles.get(pos)
        assert ins is not None, f"no handle for {en}@{pos}"
        e.count += 1
        ins.then_inc(self.sems[e.semname], 1)
        e.inc_pos.append(pos)
        e.inc_cnt.append(e.count)
        return (e.semname, e.count)

    def _wait_deps(self, e, reads, writes):
        toks = []
        for r in reads:
            ent = self.res.get(r)
            if ent is not None and ent["w"] is not None:
                toks.append(ent["w"])
        for w in writes:
            ent = self.res.get(w)
            if ent is not None:
                if ent["w"] is not None:
                    toks.append(ent["w"])
                toks.extend(ent["r"].values())
        need = {}
        for t in toks:
            s, v = self._resolve(t)
            if need.get(s, 0) < v:
                need[s] = v
        for s, v in need.items():
            if e.waited.get(s, 0) < v:
                e.eng.wait_ge(self.sems[s], v)
                e.waited[s] = v

    def _record(self, tok, reads, writes):
        for w in writes:
            self.res[w] = {"w": tok, "r": {}}
        rk = tok[1]
        for r in reads:
            ent = self.res.setdefault(r, {"w": None, "r": {}})
            ent["r"][rk] = tok

    def op(self, en, method, *args, reads=(), writes=(), **kw):
        e = self.E[en]
        self._wait_deps(e, reads, writes)
        ins = getattr(e.eng, method)(*args, **kw)
        e.pos += 1
        e.handles[e.pos] = ins
        self._record(("E", en, e.pos), reads, writes)
        return ins

    def mm(self, out, lhsT, rhs, *, start, stop, psum, reads=()):
        e = self.E["T"]
        self._wait_deps(e, reads, (psum,) if start else ())
        ins = e.eng.matmul(out, lhsT, rhs, start=start, stop=stop)
        e.pos += 1
        e.handles[e.pos] = ins
        tok = ("E", "T", e.pos)
        for r in reads:
            ent = self.res.setdefault(r, {"w": None, "r": {}})
            ent["r"]["T"] = tok
        if start:
            self.res[psum] = {"w": tok, "r": {}}
        else:
            self.res[psum]["w"] = tok
        return ins

    def dma(self, qn, out, in_, *, stream, reads=(), writes=(),
            transpose=False, slow=False):
        e = self.E[qn]
        if stream not in self.streams:
            self.streams[stream] = f"d{len(self.streams)}"
        semn = self.streams[stream]
        self._wait_deps(e, reads, writes)
        if transpose:
            ins = e.eng.dma_start(out=out, in_=in_, transpose=True)
        else:
            ins = e.eng.dma_start(out=out, in_=in_,
                                  allow_slow_non_contiguous=slow)
        self.dmacnt[stream] = self.dmacnt.get(stream, 0) + 16
        ins.then_inc(self.sems[semn], 16)
        self._record(("D", stream, self.dmacnt[stream]), reads, writes)

    def mark(self, key, stream):
        self.res[key] = {"w": ("D", stream, self.dmacnt.get(stream, 0)),
                         "r": {}}

    def barrier_reset(self):
        for ent in self.res.values():
            if ent["w"] is not None and ent["w"][0] == "E":
                ent["w"] = None
            ent["r"] = {k: v for k, v in ent["r"].items() if v[0] == "D"}

    def fence_dma(self, en="G"):
        e = self.E[en]
        for st, v in self.dmacnt.items():
            semn = self.streams[st]
            if v and e.waited.get(semn, 0) < v:
                e.eng.wait_ge(self.sems[semn], v)
                e.waited[semn] = v


def _bank_pieces(a, b):
    out = []
    while a < b:
        nxt = min(b, (a // 512 + 1) * 512)
        out.append((a, nxt))
        a = nxt
    return out



def build_program(cfg):
    TD = "S" if cfg.get("TDUAL", False) else "Q"
    TD2 = "Q"
    T, H, NH, HID = cfg["T"], cfg["H"], cfg["NH"], cfg["HID"]
    QA_T, BCH, FCH = cfg["QA_T"], cfg["BCH"], cfg["FCH"]
    NT = T // P
    HC = H // P
    HIDC = HID // P
    HS = H // NH
    R = HS // 2
    assert HS == 64 and R == 32 and H == NH * 64
    QA = QA_T * P
    MAXOWN = max(QA_T, NT - QA_T)
    SL = min(1024, HID)
    NSL = HID // SL
    SEG1 = min(512, HID)
    SEG2 = min(512, H)
    BNW = min(512, H)
    BNW2 = min(512, HID)
    SCALE = 1.0 / math.sqrt(HS)
    NHP = NH // 2
    assert NHP == HC
    MAXW = cfg.get("MAXW", 2048)
    GRP = 4
    W_B = BCH * P
    assert W_B <= 512

    nc = bass.Bass("TRN2", target_bir_lowering=False)

    dt = lambda n, s, d, k: nc.dram_tensor(n, s, d, kind=k).ap()
    xs = dt("xs", [T, H], F32, "ExternalInput")
    WSRC = {w: dt("W" + w, [H, H], F32, "ExternalInput") for w in "qkvo"}
    BSRC = {w: dt("b" + w, [H], F32, "ExternalInput") for w in "qkvo"}
    g1 = dt("g1", [H], F32, "ExternalInput")
    b1ln = dt("b1ln", [H], F32, "ExternalInput")
    w1 = dt("w1", [HID, H], F32, "ExternalInput")
    b1 = dt("b1", [HID], F32, "ExternalInput")
    g2 = dt("g2", [HID], F32, "ExternalInput")
    b2ln = dt("b2ln", [HID], F32, "ExternalInput")
    w2 = dt("w2", [H, HID], F32, "ExternalInput")
    b2 = dt("b2", [H], F32, "ExternalInput")
    ntab1 = dt("ntab1", [T, R], F32, "ExternalInput")
    ntab2 = dt("ntab2", [T, R], F32, "ExternalInput")
    out = dt("out", [MAXOWN * P, H], F32, "ExternalOutput")

    xhsc = dt("xhsc", [T, H], BF16, "Internal")
    krsc = dt("krsc", [T, NH * 64], BF16, "Internal")
    qrsc = dt("qrsc", [MAXOWN * P, NH * 64], BF16, "Internal")
    vsc = dt("vsc", [NH * P, NT * 65], BF16, "Internal")
    x2s = dt("x2s", [MAXOWN * P, H], F32, "Internal")
    xfsc = dt("xfsc", [MAXOWN * P, H], BF16, "Internal")
    hsc = dt("hsc", [MAXOWN * P, HID], BF16, "Internal")
    t1sc = dt("t1sc", [HID, H], BF16, "Internal")
    t2sc = dt("t2sc", [H, HID], BF16, "Internal")
    t1Ts = dt("t1Ts", [P, H // P * HID], BF16, "Internal")
    t2Ts = dt("t2Ts", [P, HID // P * H], BF16, "Internal")
    mkdr = dt("mkdr", [16, T], BF16, "Internal")
    wqsc = dt("wqsc", [4 * H, H], BF16, "Internal")

    LN1E4_OVER_R = math.log(1e-4) / R
    WOFF = {"q": 0, "k": H, "v": 2 * H, "o": 3 * H}

    with ExitStack() as gst:
        GE = gst.enter_context
        sems = {n: GE(nc.semaphore(n))
                for n in ("et", "ev", "es", "eg", "eq")}
        for i in range(88):
            sems[f"d{i}"] = GE(nc.semaphore(f"d{i}"))
        streams = {}
        pid = nc.partition_id(list(mybir.ALL_ENGINES))

        cidx = GE(nc.sbuf_tensor("cidx", [P, P], F32))
        pidx = GE(nc.sbuf_tensor("pidx", [P, 1], F32))
        tri01 = GE(nc.sbuf_tensor("tri01", [P, P], BF16))
        identf = GE(nc.sbuf_tensor("identf", [P, P], F32))
        onesf = GE(nc.sbuf_tensor("onesf", [16, P], F32))
        onescol = GE(nc.sbuf_tensor("onescol", [P, 1], F32))
        sc128 = GE(nc.sbuf_tensor("sc128", [P, 2], F32))
        rc128 = GE(nc.sbuf_tensor("rc128", [P, 2], F32))
        qtmp = GE(nc.sbuf_tensor("qtmp", [1, 8], F32))
        qacc = GE(nc.sbuf_tensor("qacc", [P, 3], F32))
        g1c = GE(nc.sbuf_tensor("g1c", [P, HC], F32))
        g2c = GE(nc.sbuf_tensor("g2c", [P, HIDC], F32))
        b1lz = GE(nc.sbuf_tensor("b1lz", [P, 32 + HC], BF16))
        b2lz = GE(nc.sbuf_tensor("b2lz", [P, 32 + HIDC], BF16))
        onesK = GE(nc.sbuf_tensor("onesK", [33, 512], BF16))
        onesZ = GE(nc.sbuf_tensor("onesZ", [P, 33], BF16))
        c2pre = GE(nc.sbuf_tensor("c2pre", [1, H], F32))
        mu1 = GE(nc.sbuf_tensor("mu1", [P, MAXOWN], F32))
        rr1 = GE(nc.sbuf_tensor("rr1", [P, MAXOWN], F32))
        brows = {w: GE(nc.sbuf_tensor(f"br_{w}", [33, H], BF16))
                 for w in "qkv"}
        boB = GE(nc.sbuf_tensor("boB", [P, H], BF16))
        constt = GE(nc.sbuf_tensor("constt", [P, 2], F32))
        nc.const_aps.register(F32, LN_EPS, constt[:, 0:1])
        nc.const_aps.register(F32, 1e-30, constt[:, 1:2])

        kbA = Kb(nc, sems, streams)
        kbB = Kb(nc, sems, streams)

        def rsqrt_sc(kb, dst, dst_key, src_ap, src_keys):
            kb.op("S", "activation", dst[:, 1:2], src_ap, AF.Ln,
                  bias=LN_EPS, reads=src_keys, writes=(dst_key,))
            kb.op("S", "activation", dst[:, 0:1], dst[:, 1:2], AF.Exp,
                  scale=-0.5, reads=(dst_key,), writes=(dst_key,))

        stTab = ExitStack()
        TE = stTab.enter_context
        cosT = TE(nc.sbuf_tensor("cosT", [P, R * NT], BF16, side="right"))
        sinT = TE(nc.sbuf_tensor("sinT", [P, R * NT], BF16, side="right"))
        wTq = [TE(nc.sbuf_tensor(f"wqT{hc}", [P, H], BF16, side="right"))
               for hc in range(HC)]
        wTk = [TE(nc.sbuf_tensor(f"wkT{hc}", [P, H], BF16, side="right"))
               for hc in range(HC)]
        wTv = [TE(nc.sbuf_tensor(f"wvT{hc}", [P, H], BF16, side="right"))
               for hc in range(HC)]
        wT = {"q": wTq, "k": wTk, "v": wTv}

        stLN = ExitStack()
        EL = stLN.enter_context
        xt = [EL(nc.sbuf_tensor(f"A_xt{i}", [P, H], F32, side="right"))
              for i in range(2)]
        bns = EL(nc.sbuf_tensor("A_bns", [P, 6 * (H // BNW)], F32,
                                side="right"))
        mv = EL(nc.sbuf_tensor("A_mv", [P, 2], F32, side="right"))
        sr = EL(nc.sbuf_tensor("A_sr", [P, 2], F32, side="right"))
        xh = EL(nc.sbuf_tensor("A_xh", [P, H], BF16, side="right"))

        def emit_ln_tile(kb, j):
            jb = j % 2
            xkey = f"A_xt{jb}"
            kb.dma("Q", xt[jb][:, :], xs[j * P:(j + 1) * P, :],
                   stream=xkey, reads=(xkey,), writes=(xkey,))
            for c in range(H // BNW):
                kb.op("V", "bn_stats", bns[:, c * 6:(c + 1) * 6],
                      xt[jb][:, c * BNW:(c + 1) * BNW],
                      reads=(xkey,), writes=("A_bns",))
            kb.op("V", "bn_aggr", mv[:, :], bns[:, :],
                  reads=("A_bns",), writes=("A_mv",))
            rsqrt_sc(kb, sr, "A_sr", mv[:, 1:2], ("A_mv",))
            kb.op("V", "tensor_scalar", xh[:, :], xt[jb][:, :],
                  mv[:, 0:1], sr[:, 0:1], ALU.subtract, ALU.mult,
                  reads=(xkey, "A_mv", "A_sr"), writes=("A_xh",))
            kb.dma("G", xhsc[j * P:(j + 1) * P, :], xh[:, :],
                   stream="xhsc", reads=("A_xh",),
                   writes=(f"xhsc_f{j}",))


        with ExitStack() as st0:
            E0 = st0.enter_context
            theta = E0(nc.sbuf_tensor("theta", [1, R], F32))
            irow = E0(nc.sbuf_tensor("irow", [1, R], F32))
            thetab = E0(nc.sbuf_tensor("thetab", [P, R], F32))
            tcol = E0(nc.sbuf_tensor("tcol", [P, 1], F32))
            angt = E0(nc.sbuf_tensor("angt", [P, R], F32))
            angc = E0(nc.sbuf_tensor("angc", [P, R], F32))
            n1t = E0(nc.sbuf_tensor("n1t", [P, R], F32))
            n2t = E0(nc.sbuf_tensor("n2t", [P, R], F32))
            wstg = [E0(nc.sbuf_tensor(f"wstg{i}", [P, H], BF16))
                    for i in range(2)]
            brow = E0(nc.sbuf_tensor("brow", [1, H], F32))
            lrow = E0(nc.sbuf_tensor("lrow", [P, max(HC, HIDC)], F32))
            pp0 = E0(nc.psum_tensor("pp0", [P, 512], F32))

            kb = kbA
            kb.op("G", "iota", cidx[:, :], [[1, P]], channel_multiplier=0,
                  allow_small_or_imprecise_dtypes=True, writes=("cidx",))
            kb.op("G", "iota", pidx[:, :], [[1, 1]], channel_multiplier=1,
                  allow_small_or_imprecise_dtypes=True, writes=("pidx",))
            kb.op("G", "iota", irow[0:1, :], [[1, R]], channel_multiplier=0,
                  allow_small_or_imprecise_dtypes=True, writes=("irow",))
            kb.op("V", "tensor_scalar", identf[:, :], cidx[:, :], pidx[:, :],
                  None, ALU.is_equal, reads=("cidx", "pidx"),
                  writes=("identf",))
            kb.op("V", "tensor_scalar", tri01[:, :], cidx[:, :], pidx[:, :],
                  None, ALU.is_ge,
                  reads=("cidx", "pidx"), writes=("tri01",))
            kb.op("V", "memset", onesf[:, :], 1.0, writes=("onesf",))
            kb.op("V", "memset", onesK[:, :], 0.0, writes=("onesK",))
            kb.op("V", "memset", onesK[0:1, :], 1.0, reads=("onesK",),
                  writes=("onesK",))
            kb.op("V", "memset", onesK[32:33, :], 1.0, reads=("onesK",),
                  writes=("onesK",))
            kb.op("V", "memset", onesZ[:, :], 0.0, writes=("onesZ",))
            kb.op("V", "memset", onesZ[:, 0:1], 1.0, reads=("onesZ",),
                  writes=("onesZ",))
            kb.op("V", "memset", onescol[:, :], 1.0, writes=("onescol",))
            kb.op("V", "memset", qacc[:, :], 0.0, writes=("qaccs",))
            kb.op("V", "memset", constt[:, 0:1], LN_EPS, writes=("constt",))
            kb.op("V", "memset", constt[:, 1:2], 1e-30, reads=("constt",),
                  writes=("constt",))
            kb.op("S", "activation", theta[0:1, :], irow[0:1, :], AF.Exp,
                  scale=LN1E4_OVER_R, reads=("irow",), writes=("theta",))
            kb.mm(pp0[0:P, 0:R], onesf[0:1, 0:P], theta[0:1, :],
                  start=True, stop=True, psum="pp0", reads=("onesf", "theta"))
            kb.op("V", "tensor_copy", out=thetab[:, :], in_=pp0[0:P, 0:R],
                  reads=("pp0",), writes=("thetab",))

            for j in range(NT):
                cs = slice(j * R, (j + 1) * R)
                kb.dma("Q", n1t[:, :], ntab1[j * P:(j + 1) * P, :],
                       stream="n1t", reads=("n1t",), writes=("n1t",))
                kb.dma("Q", n2t[:, :], ntab2[j * P:(j + 1) * P, :],
                       stream="n2t", reads=("n2t",), writes=("n2t",))
                kb.op("V", "tensor_scalar", tcol[:, :], pidx[:, :],
                      float(j * P + 1), None, ALU.add,
                      reads=("pidx",), writes=("tcol",))
                kb.op("V", "tensor_scalar", angt[:, :], thetab[:, :],
                      tcol[:, :], None, ALU.mult,
                      reads=("thetab", "tcol"), writes=("angt",))
                kb.op("V", "tensor_scalar", angc[:, :], angt[:, :],
                      math.pi / 2.0, None, ALU.add,
                      reads=("angt",), writes=("angc",))
                for c in (C1, C2, C3):
                    kb.op("V", "scalar_tensor_tensor", angt[:, :], n1t[:, :],
                          -c, angt[:, :], ALU.mult, ALU.add,
                          reads=("n1t", "angt"), writes=("angt",))
                    kb.op("V", "scalar_tensor_tensor", angc[:, :], n2t[:, :],
                          -c, angc[:, :], ALU.mult, ALU.add,
                          reads=("n2t", "angc"), writes=("angc",))
                kb.op("S", "activation", sinT[:, cs], angt[:, :], AF.Sin,
                      reads=("angt",), writes=(f"sinT{j}",))
                kb.op("S", "activation", cosT[:, cs], angc[:, :], AF.Sin,
                      reads=("angc",), writes=(f"cosT{j}",))

            for w in "qkvo":
                for oc in range(HC):
                    sg = wstg[oc % 2]
                    sk = f"wstg{oc % 2}"
                    kb.dma("G", sg[:, :], WSRC[w][oc * P:(oc + 1) * P, :],
                           stream=sk, reads=(sk,), writes=(sk,))
                    kb.dma("G",
                           wqsc[WOFF[w] + oc * P:WOFF[w] + (oc + 1) * P, :],
                           sg[:, :], stream="wqsc", reads=(sk,))
            kb.mark("wqsc_f", "wqsc")
            for w in "qkv":
                for hc in range(HC):
                    kb.dma("Q", wT[w][hc][:, :],
                           wqsc[WOFF[w]:WOFF[w] + H, hc * P:(hc + 1) * P],
                           stream=f"wT{hc % 4}", transpose=True,
                           reads=(f"w{w}T{hc}", "wqsc_f"),
                           writes=(f"w{w}T{hc}",))
            for w in "qkv":
                kb.op("V", "memset", brows[w][:, :], 0.0,
                      writes=(f"br_{w}",))
                kb.dma("Q", brow[0:1, :], BSRC[w][None, :],
                       stream="brow", reads=("brow",), writes=("brow",))
                kb.op("V", "tensor_copy", out=brows[w][0:1, :],
                      in_=brow[0:1, :], reads=("brow", f"br_{w}"),
                      writes=(f"br_{w}",))
                kb.op("V", "tensor_tensor", out=brows[w][32:33, :],
                      in0=brow[0:1, :], in1=brows[w][0:1, :],
                      op=ALU.subtract, reads=("brow", f"br_{w}"),
                      writes=(f"br_{w}",))
            for w, bt, bkey in (("o", boB, "boB"),):
                kb.dma("Q", brow[0:1, :], BSRC[w][None, :],
                       stream="brow2", reads=("brow",), writes=("brow",))
                for a, b in _bank_pieces(0, H):
                    kb.mm(pp0[0:P, 0:b - a], onesf[0:1, 0:P],
                          brow[0:1, a:b], start=True, stop=True,
                          psum="pp0", reads=("brow", "onesf"))
                    kb.op("V", "tensor_copy", out=bt[:, a:b],
                          in_=pp0[0:P, 0:b - a], reads=("pp0",),
                          writes=(bkey,))

            kb.dma("Q", g1c[:, :], g1.rearrange("(c p) -> p c", p=P),
                   stream="g1c", writes=("g1c",), slow=True)
            kb.dma("Q", g2c[:, :], g2.rearrange("(c p) -> p c", p=P),
                   stream="g2c", writes=("g2c",), slow=True)
            kb.dma("Q", lrow[:, 0:HC], b1ln.rearrange("(c p) -> p c", p=P),
                   stream="lrow", reads=("lrow",), writes=("lrow",),
                   slow=True)
            kb.op("V", "memset", b1lz[:, :], 0.0, writes=("b1lz",))
            kb.op("V", "tensor_copy", out=b1lz[:, 0:HC], in_=lrow[:, 0:HC],
                  reads=("lrow", "b1lz"), writes=("b1lz",))
            kb.op("V", "tensor_tensor", out=b1lz[:, 32:32 + HC],
                  in0=lrow[:, 0:HC],
                  in1=b1lz[:, 0:HC], op=ALU.subtract,
                  reads=("lrow", "b1lz"), writes=("b1lz",))
            kb.dma("Q", lrow[:, 0:HIDC], b2ln.rearrange("(c p) -> p c", p=P),
                   stream="lrow", reads=("lrow",), writes=("lrow",),
                   slow=True)
            kb.op("V", "memset", b2lz[:, :], 0.0, writes=("b2lz",))
            kb.op("V", "tensor_copy", out=b2lz[:, 0:HIDC],
                  in_=lrow[:, 0:HIDC],
                  reads=("lrow", "b2lz"), writes=("b2lz",))
            kb.op("V", "tensor_tensor", out=b2lz[:, 32:32 + HIDC],
                  in0=lrow[:, 0:HIDC],
                  in1=b2lz[:, 0:HIDC], op=ALU.subtract,
                  reads=("lrow", "b2lz"), writes=("b2lz",))
            for j in range(QA_T):
                emit_ln_tile(kb, j)
            kb.fence_dma("V")
            kb.fence_dma("S")
            kb.fence_dma("G")

        nc.all_engine_barrier()
        kbA.barrier_reset()
        kbB.clone_counters_from(kbA)

        stAB = ExitStack()
        AE = stAB.enter_context
        krt = [AE(nc.sbuf_tensor(f"krt{c}", [P, T], BF16)) for c in range(NHP)]
        qrt = [AE(nc.sbuf_tensor(f"qrt{c}", [P, MAXOWN * P], BF16))
               for c in range(NHP)]
        mkacc = AE(nc.sbuf_tensor("mkacc", [P, NH], F32))

        stAL = ExitStack()
        EA = stAL.enter_context
        xhTg = [[EA(nc.sbuf_tensor(f"A_xhT{g}_{hc}", [P, GRP * P], BF16,
                                   side="right")) for hc in range(HC)]
                for g in range(2)]
        rt1 = EA(nc.sbuf_tensor("A_rt1", [P, H // 2], F32, side="right"))
        rt2 = EA(nc.sbuf_tensor("A_rt2", [P, H // 2], F32, side="right"))
        krb = EA(nc.sbuf_tensor("A_krb", [P, NH * 64], BF16, side="right"))
        sq16 = EA(nc.sbuf_tensor("A_sq16", [P, NH], F32, side="right"))
        sqQ = EA(nc.sbuf_tensor("A_sqQ", [P, NH], F32, side="right"))
        sq2 = EA(nc.sbuf_tensor("A_sq2", [P, NH], F32, side="right"))
        vst = [EA(nc.sbuf_tensor(f"A_vst{i}", [P, NH * 65], BF16,
                                 side="right")) for i in range(2)]
        ppr = [EA(nc.psum_tensor(f"A_ppr{i}", [P, 512], F32)) for i in range(4)]

        def phase_a(kb, role):
            own_t0 = 0 if role == 0 else QA_T
            own_nt = QA_T if role == 0 else NT - QA_T
            kv_nt = QA_T if role == 0 else NT

            kb.op("V", "memset", mkacc[:, :], 0.0, writes=("mkacc",))
            kb.op("G", "memset", krb[:, :], 0.0, writes=("A_krb",))
            for i in range(2):
                kb.op("G", "memset",
                      vst[i].rearrange("p (h c) -> p h c", h=NH)[:, :, 64:65],
                      1.0, writes=(f"A_vst{i}_ones",))

            def proj(w, gslot, jj, psum0):
                for a, b in _bank_pieces(0, H):
                    i = psum0 + (a // 512) % 2
                    pk = f"A_ppr{i}"
                    pr = ppr[i]
                    for hc in range(HC):
                        kb.mm(pr[0:P, 0:b - a],
                              xhTg[gslot][hc][:, jj * P:(jj + 1) * P],
                              wT[w][hc][:, a:b],
                              start=(hc == 0), stop=False, psum=pk,
                              reads=(f"A_xhTg{gslot}_{hc}", f"w{w}T{hc}"))
                    kb.mm(pr[0:P, 0:b - a], onesK[0:33, 0:P],
                          brows[w][0:33, a:b], start=False, stop=True,
                          psum=pk, reads=("onesK", f"br_{w}"))

            def rope_psum(kb, psum0, dst3, dstkey, j):
                if cfg.get("NOROPE", False):
                    for a, b in _bank_pieces(0, H):
                        i = psum0 + (a // 512) % 2
                        pk = f"A_ppr{i}"
                        kb.op("V", "tensor_copy",
                              out=rt1[:, a // 2:a // 2 + (b - a) // 2],
                              in_=ppr[i][0:P, 0:(b - a) // 2],
                              reads=(pk,), writes=("A_rt1",))
                    kb.op("V", "tensor_copy", out=dst3,
                          in_=rt1.rearrange("p (h r) -> p h r", h=NH),
                          reads=("A_rt1",), writes=(dstkey,))
                    return
                cs = slice(j * R, (j + 1) * R)
                cb = cosT[:, cs].rearrange("p (o r) -> p o r", o=1)\
                    .broadcast_to((P, NH, R))
                sb = sinT[:, cs].rearrange("p (o r) -> p o r", o=1)\
                    .broadcast_to((P, NH, R))
                for a, b in _bank_pieces(0, H):
                    i = psum0 + (a // 512) % 2
                    pk = f"A_ppr{i}"
                    hper = (b - a) // 64
                    h0 = a // 64
                    se = ppr[i][0:P, 0:b - a].rearrange(
                        "p (h r two) -> p h r two", h=hper, two=2)
                    hs_ = slice(h0, h0 + hper)
                    kb.op("V", "tensor_tensor",
                          out=rt1.rearrange("p (h r) -> p h r", h=NH)[:, hs_],
                          in0=se[:, :, :, 0], in1=cb[:, hs_], op=ALU.mult,
                          reads=(pk, f"cosT{j}"), writes=("A_rt1",))
                    kb.op("V", "tensor_tensor",
                          out=rt2.rearrange("p (h r) -> p h r", h=NH)[:, hs_],
                          in0=se[:, :, :, 1], in1=sb[:, hs_], op=ALU.mult,
                          reads=(pk, f"sinT{j}"), writes=("A_rt2",))
                kb.op("V", "tensor_tensor", out=dst3,
                      in0=rt1.rearrange("p (h r) -> p h r", h=NH),
                      in1=rt2.rearrange("p (h r) -> p h r", h=NH),
                      op=ALU.subtract,
                      reads=("A_rt1", "A_rt2"), writes=(dstkey,))

            groups = [(g0, min(GRP, kv_nt - g0))
                      for g0 in range(0, kv_nt, GRP)]
            if cfg.get("AGRP1", False):
                groups = groups[:1]
            for gi, (g0, gn) in enumerate(groups):
                gslot = gi % 2
                for jj in range(gn):
                    j = g0 + jj
                    if j >= QA_T:
                        emit_ln_tile(kb, j)
                for hc in range(HC):
                    kb.dma(TD2 if hc % 2 == 0 else TD,
                           xhTg[gslot][hc][:, 0:gn * P],
                           xhsc[g0 * P:(g0 + gn) * P, hc * P:(hc + 1) * P],
                           stream=f"xhT{gslot}_{hc % 2}", transpose=True,
                           reads=tuple([f"A_xhTg{gslot}_{hc}"] +
                                       [f"xhsc_f{g0 + t}" for t in range(gn)]),
                           writes=(f"A_xhTg{gslot}_{hc}",))
                if cfg.get("NOPROJ", False):
                    continue
                for jj in range(gn):
                    j = g0 + jj
                    jb = j % 2
                    proj("k", gslot, jj, 0)
                    krz = krb.rearrange("p (h z) -> p h z", z=64)
                    rope_psum(kb, 0, krz[:, :, 0:R], "A_krb", j)
                    kb.op("V", "tensor_tensor",
                          out=rt2.rearrange("p (h r) -> p h r", h=NH),
                          in0=krz[:, :, 0:R],
                          in1=krz[:, :, 0:R], op=ALU.mult,
                          reads=("A_krb",), writes=("A_rt2",))
                    kb.op("V", "tensor_reduce", sq16[:, :],
                          rt2.rearrange("p (h r) -> p h r", h=NH),
                          axis=AX.X, op=ALU.add,
                          reads=("A_rt2",), writes=("A_sq16",))
                    kb.op("V", "tensor_tensor", out=mkacc[:, :],
                          in0=mkacc[:, :], in1=sq16[:, :], op=ALU.max,
                          reads=("mkacc", "A_sq16"), writes=("mkacc",))
                    kb.dma("G", krsc[j * P:(j + 1) * P, :], krb[:, :],
                           stream="krsc",
                           reads=("A_krb", f"krsc_f{j}"),
                           writes=(f"krsc_f{j}",))
                    if cfg.get("ONLYK", False):
                        continue
                    vb = vst[jb]
                    for a, b in _bank_pieces(0, H):
                        i = 2 + (a // 512) % 2
                        pk = f"A_ppr{i}"
                        pr = ppr[i]
                        for hc in range(HC):
                            kb.mm(pr[0:P, 0:b - a],
                                  xhTg[gslot][hc][:, jj * P:(jj + 1) * P],
                                  wT["v"][hc][:, a:b],
                                  start=(hc == 0), stop=False,
                                  psum=pk,
                                  reads=(f"A_xhTg{gslot}_{hc}", f"wvT{hc}"))
                        kb.mm(pr[0:P, 0:b - a], onesK[0:33, 0:P],
                              brows["v"][0:33, a:b], start=False, stop=True,
                              psum=pk, reads=("onesK", "br_v"))
                        h0 = a // 64
                        nh_ = (b - a) // 64
                        kb.op("V", "tensor_copy",
                              out=vb.rearrange("p (h c) -> p h c", h=NH)[
                                  :, h0:h0 + nh_, 0:64],
                              in_=pr[0:P, 0:b - a].rearrange(
                                  "p (h c) -> p h c", c=64),
                              reads=(pk,), writes=(f"A_vst{jb}",))
                    if not cfg.get("NOV", False):
                        kb.dma("G",
                               vsc.rearrange("(h p) m -> p h m", p=P)[
                                   :, :, j * 65:(j + 1) * 65],
                               vb.rearrange("p (h c) -> p h c", h=NH),
                               stream=f"vsc{jb}",
                               reads=(f"A_vst{jb}", f"A_vst{jb}_ones"),
                               writes=(f"vsc_f{j}",))
                    if own_t0 <= j < own_t0 + own_nt:
                        jo = j - own_t0
                        proj("q", gslot, jj, 0)
                        krz = krb.rearrange("p (h z) -> p h z", z=64)
                        rope_psum(kb, 0, krz[:, :, 0:R], "A_krb", j)
                        kb.op("V", "tensor_tensor",
                              out=rt2.rearrange("p (h r) -> p h r", h=NH),
                              in0=krz[:, :, 0:R],
                              in1=krz[:, :, 0:R], op=ALU.mult,
                              reads=("A_krb",), writes=("A_rt2",))
                        kb.op("V", "tensor_reduce", sqQ[:, :],
                              rt2.rearrange("p (h r) -> p h r", h=NH),
                              axis=AX.X, op=ALU.add,
                              reads=("A_rt2",), writes=("A_sqQ",))
                        kb.op("S", "activation", sq2[:, :], sqQ[:, :],
                              AF.Ln, bias=1e-30,
                              reads=("A_sqQ",), writes=("A_sq2",))
                        with nc.allow_low_precision(
                                reason="softmax shift bound; slack is huge"):
                            kb.op("S", "activation", krz[:, :, 32:33],
                                  sq2.rearrange("p (h o) -> p h o", o=1),
                                  AF.Exp, scale=0.5,
                                  reads=("A_sq2",), writes=("A_krb",))
                        kb.dma("G", qrsc[jo * P:(jo + 1) * P, :], krb[:, :],
                               stream="qrsc",
                               reads=("A_krb", f"qrsc_f{jo}"),
                               writes=(f"qrsc_f{jo}",))
            kb.fence_dma("V")
            kb.fence_dma("S")

        if cfg.get("STOP", "") == "0":
            def phase_a(kb, role):
                kb.fence_dma("G")
        with nc.If(pid < cfg["ROLE_SPLIT"]):
            phase_a(kbA, 0)
        with nc.Else():
            phase_a(kbB, 1)
        nc.all_engine_barrier()
        kbA.barrier_reset()
        kbB.barrier_reset()
        stAL.close()

        stA2 = ExitStack()
        EA2 = stA2.enter_context
        mkt = EA2(nc.sbuf_tensor("A2_mkt", [16, P], F32, side="right"))
        mk1 = EA2(nc.sbuf_tensor("A2_mk1", [16, 4], F32, side="right"))
        mkd = EA2(nc.sbuf_tensor("A2_mkd", [16, 16], F32, side="right"))
        mkb = EA2(nc.sbuf_tensor("A2_mkb", [P, NH], F32, side="right"))
        mks = EA2(nc.sbuf_tensor("A2_mks", [16, T], BF16, side="right"))
        pps2 = EA2(nc.psum_tensor("A2_pps", [P, 512], F32))

        def phase_a2(kb, role):
            own_t0 = 0 if role == 0 else QA_T
            own_nt = QA_T if role == 0 else NT - QA_T
            kv_nt = QA_T if role == 0 else NT
            pps = pps2
            kb.op("T", "transpose", pps[0:NH, 0:P], mkacc[:, :], identf[:, :],
                  reads=("mkacc", "identf"), writes=("A2_pps",))
            kb.op("V", "tensor_copy", out=mkt[0:NH, :], in_=pps[0:NH, 0:P],
                  reads=("A2_pps",), writes=("A2_mkt",))
            kb.op("V", "tensor_reduce", mk1[0:NH, 0:1], mkt[0:NH, :],
                  axis=AX.X, op=ALU.max, reads=("A2_mkt",), writes=("A2_mk1",))
            kb.op("S", "activation", mk1[0:NH, 1:2], mk1[0:NH, 0:1],
                  AF.Ln, bias=1e-30, reads=("A2_mk1",), writes=("A2_l",))
            kb.op("S", "activation", mk1[0:NH, 2:3], mk1[0:NH, 1:2],
                  AF.Exp, scale=0.5, reads=("A2_l",), writes=("A2_s",))
            kb.op("V", "tensor_scalar", mk1[0:NH, 3:4], mk1[0:NH, 2:3],
                  -1.0, None, ALU.mult, reads=("A2_s",), writes=("A2_n",))
            kb.op("V", "tensor_scalar", mkd[0:NH, 0:NH],
                  identf[0:NH, 0:NH], mk1[0:NH, 3:4], None, ALU.mult,
                  reads=("identf", "A2_n"), writes=("A2_mkd",))
            kb.mm(pps[0:P, 0:NH], onesf[0:NH, :], mkd[0:NH, 0:NH],
                  start=True, stop=True, psum="A2_pps",
                  reads=("onesf", "A2_mkd"))
            kb.op("V", "tensor_copy", out=mkb[:, :], in_=pps[0:P, 0:NH],
                  reads=("A2_pps",), writes=("A2_mkb",))
            kb.op("V", "memset", mks[0:NH, 0:kv_nt * P], 1.0,
                  writes=("A2_mks",))
            kb.op("V", "tensor_scalar", mks[0:NH, 0:kv_nt * P],
                  mks[0:NH, 0:kv_nt * P], mk1[0:NH, 3:4], None, ALU.mult,
                  reads=("A2_mks", "A2_n"), writes=("A2_mks",))
            kb.dma("G", mkdr[0:NH, 0:kv_nt * P], mks[0:NH, 0:kv_nt * P],
                   stream="mkdr", reads=("A2_mks",))
            kb.mark("mkdr_f", "mkdr")
            for c in range(NHP):
                kb.dma(TD2 if c % 2 == 0 else TD,
                       krt[c][0:P, 0:kv_nt * P],
                       krsc[0:kv_nt * P, c * P:(c + 1) * P],
                       stream=f"krtL{c % 4}", transpose=True,
                       reads=tuple([f"krt_{c}"] +
                                   [f"krsc_f{j}" for j in range(kv_nt)]),
                       writes=(f"krt_{c}",))
                kb.dma(TD if c % 2 == 0 else TD2,
                       qrt[c][0:P, 0:own_nt * P],
                       qrsc[0:own_nt * P, c * P:(c + 1) * P],
                       stream=f"qrtL{c % 4}", transpose=True,
                       reads=tuple([f"qrt_{c}"] +
                                   [f"qrsc_f{j}" for j in range(own_nt)]),
                       writes=(f"qrt_{c}",))
            for c in range(NHP):
                for b0, h in ((32, 2 * c), (96, 2 * c + 1)):
                    kb.dma("Q", krt[c][b0:b0 + 1, 0:kv_nt * P],
                           mkdr[h:h + 1, 0:kv_nt * P],
                           stream=f"mkl{c % 4}",
                           reads=(f"krt_{c}", "mkdr_f"),
                           writes=(f"krt_{c}",))
            kb.fence_dma("Q")
            kb.fence_dma("S")

        if cfg.get("STOP", "") in ("0", "A"):
            def phase_a2(kb, role):
                kb.fence_dma("G")
        with nc.If(pid < cfg["ROLE_SPLIT"]):
            phase_a2(kbA, 0)
        with nc.Else():
            phase_a2(kbB, 1)
        nc.all_engine_barrier()
        kbA.barrier_reset()
        kbB.barrier_reset()
        stA2.close()
        stLN.close()
        stTab.close()

        stB = ExitStack()
        EB = stB.enter_context
        vbuf = [EB(nc.sbuf_tensor(f"vbuf{i}", [P, NT * 65], BF16,
                                  side="right")) for i in range(2)]
        ptp = [EB(nc.sbuf_tensor(f"ptp{i}", [P, 2 * W_B], BF16, side="right"))
               for i in range(2)]
        yts = [EB(nc.sbuf_tensor(f"yts{c}", [P, W_B], BF16, side="right"))
               for c in range(NHP)]
        rrow = EB(nc.sbuf_tensor("rrow", [1, max(W_B, H)], F32, side="right"))
        rsb = EB(nc.sbuf_tensor("rsb", [64, W_B], F32, side="right"))
        atp = EB(nc.sbuf_tensor("atp", [P, H], F32, side="right"))
        x2t = EB(nc.sbuf_tensor("x2t", [P, H], F32, side="right"))
        xre = EB(nc.sbuf_tensor("xre", [P, H], F32, side="right"))
        bnsB = EB(nc.sbuf_tensor("bnsB", [P, 6 * (H // BNW)], F32,
                                 side="right"))
        mvB = EB(nc.sbuf_tensor("mvB", [P, 2], F32, side="right"))
        srB = EB(nc.sbuf_tensor("srB", [P, 2], F32, side="right"))
        mvC = EB(nc.sbuf_tensor("mvC", [P, 6], F32, side="right"))
        qst = [EB(nc.sbuf_tensor(f"qst{i}", [P, MAXW], F32, side="right"))
               for i in range(3)]
        qg = EB(nc.sbuf_tensor("qg", [P, MAXW], BF16, side="right"))
        qt16 = EB(nc.sbuf_tensor("qt16", [P, MAXW], BF16, side="right"))
        tstg = EB(nc.sbuf_tensor("tstg", [P, max(H, HID)], BF16,
                                 side="right"))
        woT = [EB(nc.sbuf_tensor(f"woT{hc}", [P, H], BF16, side="right"))
               for hc in range(HC)]
        psc = [EB(nc.psum_tensor(f"psc{i}", [P, 2 * W_B], F32))
               for i in range(2)]
        pyt = [EB(nc.psum_tensor(f"pyt{i}", [P, W_B], F32))
               for i in range(2)]
        c2preB = EB(nc.sbuf_tensor("c2preB", [33, H], F32, side="right"))
        pmo = EB(nc.psum_tensor("pmo", [P, 512], F32))
        wT["o"] = woT

        def phase_b(kb, role):
            own_t0 = 0 if role == 0 else QA_T
            own_nt = QA_T if role == 0 else NT - QA_T
            kb.op("V", "memset", c2preB[:, :], 0.0, writes=("c2preB",))

            for hc in range(HC):
                kb.dma("Q", woT[hc][:, :],
                       wqsc[WOFF["o"]:WOFF["o"] + H, hc * P:(hc + 1) * P],
                       stream=f"wT{hc % 4}", transpose=True,
                       reads=(f"woT{hc}", "wqsc_f"), writes=(f"woT{hc}",))

            mats = [(0, w1, HID // P, H, max(1, H // MAXW), t1sc),
                    (1, w2, H // P, HID, max(1, HID // MAXW), t2sc)]
            qsteps = []

            def q_ld(src, jt, cc, cw, slot):
                kb.dma("Q", qst[slot][:, 0:cw],
                       src[jt * P:(jt + 1) * P, cc * cw:(cc + 1) * cw],
                       stream=f"qst{slot}", reads=(f"qst{slot}",),
                       writes=(f"qst{slot}",))

            def q_reduce(mi, cw, slot):
                kb.op("V", "tensor_reduce", qacc[:, 0:1],
                      qst[slot][:, 0:cw],
                      axis=AX.X, op=ALU.add, apply_absolute_value=True,
                      reads=(f"qst{slot}",), writes=("qacc1",))
                kb.op("V", "tensor_tensor", out=qacc[:, 1 + mi:2 + mi],
                      in0=qacc[:, 1 + mi:2 + mi], in1=qacc[:, 0:1],
                      op=ALU.add, reads=("qacc1", "qaccs"), writes=("qaccs",))

            def q_fin():
                for mi, src, ntile, width, ncc, dstT in mats:
                    n_el = float(ntile * P * width)
                    kb.mm(pmo[0:1, 0:1], qacc[:, 1 + mi:2 + mi],
                          onescol[:, 0:1], start=True, stop=True, psum="pmo",
                          reads=("qaccs", "onescol"))
                    kb.op("V", "tensor_scalar", qtmp[0:1, 4 * mi:4 * mi + 1],
                          pmo[0:1, 0:1], 1.0 / n_el, 1e-5, ALU.mult, ALU.max,
                          reads=("pmo",), writes=(f"qtmp_m{mi}",))
                    kb.op("V", "reciprocal",
                          out=qtmp[0:1, 4 * mi + 1:4 * mi + 2],
                          in_=qtmp[0:1, 4 * mi:4 * mi + 1],
                          reads=(f"qtmp_m{mi}",), writes=(f"qtmp_s{mi}",))
                    kb.op("V", "reciprocal",
                          out=qtmp[0:1, 4 * mi + 2:4 * mi + 3],
                          in_=qtmp[0:1, 4 * mi + 1:4 * mi + 2],
                          reads=(f"qtmp_s{mi}",), writes=(f"qtmp_r{mi}",))
                    kb.mm(pmo[0:P, 0:1], onesf[0:1, 0:P],
                          qtmp[0:1, 4 * mi + 1:4 * mi + 2],
                          start=True, stop=True, psum="pmo",
                          reads=(f"qtmp_s{mi}", "onesf"))
                    kb.op("V", "tensor_copy", out=sc128[:, mi:mi + 1],
                          in_=pmo[0:P, 0:1], reads=("pmo",), writes=("sc128",))
                    kb.mm(pmo[0:P, 0:1], onesf[0:1, 0:P],
                          qtmp[0:1, 4 * mi + 2:4 * mi + 3],
                          start=True, stop=True, psum="pmo",
                          reads=(f"qtmp_r{mi}", "onesf"))
                    kb.op("V", "tensor_copy", out=rc128[:, mi:mi + 1],
                          in_=pmo[0:P, 0:1], reads=("pmo",), writes=("rc128",))

            def q_tern(mi, jt, cc, cw, dstT, slot):
                kb.op("V", "tensor_scalar", qg[:, 0:cw],
                      qst[slot][:, 0:cw],
                      sc128[:, mi:mi + 1], 0.5, ALU.mult, ALU.is_gt,
                      reads=(f"qst{slot}", "sc128"), writes=("qg",))
                kb.op("V", "tensor_scalar", qt16[:, 0:cw],
                      qst[slot][:, 0:cw],
                      sc128[:, mi:mi + 1], -0.5, ALU.mult, ALU.is_lt,
                      reads=(f"qst{slot}", "sc128"), writes=("qt16",))
                kb.op("V", "tensor_tensor", out=qt16[:, 0:cw],
                      in0=qg[:, 0:cw], in1=qt16[:, 0:cw], op=ALU.subtract,
                      reads=("qg", "qt16"), writes=("qt16",))
                kb.dma("G", dstT[jt * P:(jt + 1) * P,
                               cc * cw:(cc + 1) * cw],
                       qt16[:, 0:cw], stream=f"tsc{mi}", reads=("qt16",))

            def q_stage(mi, c):
                if mi == 0:
                    kb.dma("Q", tstg[:, 0:HID],
                           t1sc[0:HID, c * P:(c + 1) * P],
                           stream="tstg", transpose=True,
                           reads=("tstg", "tsc0_f"), writes=("tstg",))
                    kb.dma("G",
                           t1Ts.rearrange("p (s c m) -> p s c m",
                                          c=HC, m=SEG1)[:, :, c, :],
                           tstg[:, 0:HID].rearrange("p (s m) -> p s m",
                                                    m=SEG1),
                           stream="t1Ts", reads=("tstg",))
                else:
                    kb.dma("Q", tstg[:, 0:H],
                           t2sc[0:H, c * P:(c + 1) * P],
                           stream="tstg", transpose=True,
                           reads=("tstg", "tsc1_f"), writes=("tstg",))
                    for pidx, (a2, b2_) in enumerate(_bank_pieces(0, H)):
                        kb.mm(pmo[0:33, 0:b2_ - a2],
                              b2lz[:, c:c + 33], tstg[:, a2:b2_],
                              start=True, stop=True, psum="pmo",
                              reads=("b2lz", "tstg"))
                        kb.op("V", "tensor_tensor",
                              out=c2preB[0:33, a2:b2_],
                              in0=pmo[0:33, 0:b2_ - a2],
                              in1=c2preB[0:33, a2:b2_], op=ALU.add,
                              reads=("pmo", "c2preB"),
                              writes=("c2preB",))
                    kb.op("V", "tensor_scalar", tstg[:, 0:H],
                          tstg[:, 0:H], g2c[:, c:c + 1], None, ALU.mult,
                          reads=("tstg", "g2c"), writes=("tstg",))
                    kb.dma("G",
                           t2Ts.rearrange("p (s c m) -> p s c m",
                                          c=HIDC, m=SEG2)[:, :, c, :],
                           tstg[:, 0:H].rearrange("p (s m) -> p s m",
                                                  m=SEG2),
                           stream="t2Ts", reads=("tstg",))

            def c2pre_fin():
                for a2, b2_ in _bank_pieces(0, H):
                    kb.op("V", "tensor_copy", out=rrow[0:1, 0:b2_ - a2],
                          in_=c2preB[32:33, a2:b2_],
                          reads=("c2preB",), writes=("rrow",))
                    kb.op("V", "tensor_tensor", out=c2pre[0:1, a2:b2_],
                          in0=c2preB[0:1, a2:b2_],
                          in1=rrow[0:1, 0:b2_ - a2], op=ALU.add,
                          reads=("c2preB", "rrow"),
                          writes=("c2pre",))

            nsl_ = [0]

            def mkld(src, jt, cc, cw):
                slot = nsl_[0] % 3
                nsl_[0] += 1
                return (lambda: q_ld(src, jt, cc, cw, slot)), slot

            for mi, src, ntile, width, ncc, dstT in mats:
                for jt in range(ntile):
                    for cc in range(ncc):
                        ld, slot = mkld(src, jt, cc, width // ncc)
                        qsteps.append(
                            (ld, lambda mi=mi, cw=width // ncc,
                             slot=slot: q_reduce(mi, cw, slot)))
            qsteps.append((None, lambda: q_fin()))
            for mi, src, ntile, width, ncc, dstT in mats:
                for jt in range(ntile):
                    for cc in range(ncc):
                        ld, slot = mkld(src, jt, cc, width // ncc)
                        qsteps.append(
                            (ld, lambda mi=mi, jt=jt, cc=cc,
                             cw=width // ncc, dstT=dstT, slot=slot:
                             q_tern(mi, jt, cc, cw, dstT, slot)))
            qsteps.append((None, lambda: kb.mark("tsc0_f", "tsc0")))
            qsteps.append((None, lambda: kb.mark("tsc1_f", "tsc1")))
            for c in range(HC):
                qsteps.append((None, lambda c=c: q_stage(0, c)))
            for c in range(HIDC):
                qsteps.append((None, lambda c=c: q_stage(1, c)))
            qsteps.append((None, lambda: c2pre_fin()))
            qsteps.append((None, lambda: kb.mark("t1Ts_f", "t1Ts")))
            qsteps.append((None, lambda: kb.mark("t2Ts_f", "t2Ts")))
            qi = [0]
            li = [0]

            def run_q(n):
                while n > 0 and qi[0] < len(qsteps):
                    while li[0] < len(qsteps) and li[0] <= qi[0] + 2:
                        ld = qsteps[li[0]][0]
                        if ld is not None:
                            ld()
                        li[0] += 1
                    qsteps[qi[0]][1]()
                    qi[0] += 1
                    n -= 1

            def emit_avs(pend, h, Wq_, ktmax):
                pairs, tslot = pend
                for (kt, lt0, half) in pairs:
                    col0 = lt0 * P
                    kb.mm(pyt[h % 2][0:65, col0:Wq_],
                          vbuf[h % 2][:, kt * 65:(kt + 1) * 65],
                          ptp[tslot][:, half * W_B + col0:half * W_B + Wq_],
                          start=(kt == 0), stop=(kt == ktmax),
                          psum=f"pyt{h % 2}",
                          reads=(f"vbuf{h % 2}", f"ptp{tslot}"))

            chunks = []
            t0 = 0
            while t0 < own_nt:
                w = min(BCH, own_nt - t0)
                chunks.append((t0, w))
                t0 += w

            pend_norm = [None]
            nq_total = None
            for (ct0, Wt) in chunks:
                glob0 = own_t0 + ct0
                ktmax = glob0 + Wt - 1
                nkt = ktmax + 1
                Wq_ = Wt * P
                kb.dma("Q", vbuf[0][:, 0:nkt * 65],
                       vsc[0:P, 0:nkt * 65], stream="vbuf0",
                       reads=tuple(["vbuf0"] +
                                   [f"vsc_f{j}" for j in range(nkt)]),
                       writes=("vbuf0",))
                for h in range(NH):
                    if h + 1 < NH:
                        kb.dma("Q", vbuf[(h + 1) % 2][:, 0:nkt * 65],
                               vsc[(h + 1) * P:(h + 2) * P, 0:nkt * 65],
                               stream=f"vbuf{(h + 1) % 2}",
                               reads=tuple([f"vbuf{(h + 1) % 2}"] +
                                           [f"vsc_f{j}" for j in range(nkt)]),
                               writes=(f"vbuf{(h + 1) % 2}",))
                    c = h // 2
                    b0 = 64 * (h % 2)
                    kkeys = (f"krt_{c}", f"qrt_{c}")
                    pend = None
                    fl = 0
                    kt = 0
                    while kt < nkt:
                        lt0 = max(0, kt - glob0)
                        pairable = (kt + 1 < nkt and kt + 1 <= glob0
                                    and Wq_ == W_B)
                        group = [(kt, lt0, 0)]
                        if pairable:
                            group.append((kt + 1, 0, 1))
                        tslot = fl % 2
                        pk = f"psc{tslot}"
                        for (k2, l2, half) in group:
                            col0 = l2 * P
                            kb.mm(psc[tslot][0:P,
                                             half * W_B + col0:
                                             half * W_B + Wq_],
                                  krt[c][b0:b0 + 33, k2 * P:(k2 + 1) * P],
                                  qrt[c][b0:b0 + 33,
                                         ct0 * P + col0:ct0 * P + Wq_],
                                  start=True, stop=True, psum=pk,
                                  reads=kkeys)
                        lo = group[0][1] * P
                        hi = (len(group) - 1) * W_B + Wq_
                        kb.op("S", "activation",
                              ptp[tslot][:, lo:hi], psc[tslot][0:P, lo:hi],
                              AF.Exp, scale=SCALE,
                              reads=(pk,), writes=(f"ptp{tslot}",))
                        for (k2, l2, half) in group:
                            if k2 >= glob0:
                                col0 = l2 * P
                                kb.op("V", "tensor_tensor",
                                      out=ptp[tslot][:,
                                                     half * W_B + col0:
                                                     half * W_B + col0 + P],
                                      in0=ptp[tslot][:,
                                                     half * W_B + col0:
                                                     half * W_B + col0 + P],
                                      in1=tri01[:, :], op=ALU.mult,
                                      reads=(f"ptp{tslot}", "tri01"),
                                      writes=(f"ptp{tslot}",))
                        if pend is not None:
                            emit_avs(pend, h, Wq_, ktmax)
                        if fl == 0 and pend_norm[0] is not None:
                            pend_norm[0]()
                            pend_norm[0] = None
                        pend = (group, tslot)
                        fl += 1
                        kt += len(group)
                    if pend is not None:
                        emit_avs(pend, h, Wq_, ktmax)

                    def mknorm(h=h, c=c, b0=b0, Wq_=Wq_):
                        def donorm():
                            kb.op("S", "activation", rrow[0:1, 0:Wq_],
                                  pyt[h % 2][64:65, 0:Wq_], AF.Ln,
                                  reads=(f"pyt{h % 2}",), writes=("rrow",))
                            kb.op("S", "activation", rrow[0:1, 0:Wq_],
                                  rrow[0:1, 0:Wq_], AF.Exp, scale=-1.0,
                                  reads=("rrow",), writes=("rrow",))
                            kb.mm(pmo[0:64, 0:Wq_], onesf[0:1, 0:64],
                                  rrow[0:1, 0:Wq_], start=True, stop=True,
                                  psum="pmo", reads=("rrow", "onesf"))
                            kb.op("V", "tensor_copy", out=rsb[0:64, 0:Wq_],
                                  in_=pmo[0:64, 0:Wq_], reads=("pmo",),
                                  writes=("rsb",))
                            kb.op("V", "tensor_tensor",
                                  out=yts[c][b0:b0 + 64, 0:Wq_],
                                  in0=pyt[h % 2][0:64, 0:Wq_],
                                  in1=rsb[0:64, 0:Wq_],
                                  op=ALU.mult,
                                  reads=(f"pyt{h % 2}", "rsb"),
                                  writes=(f"yts{c}",))
                        return donorm
                    pend_norm[0] = mknorm()
                    run_q(2)
                if pend_norm[0] is not None:
                    pend_norm[0]()
                    pend_norm[0] = None
                run_q(2)
                for lt in range(Wt):
                    jo = ct0 + lt
                    for a, b in _bank_pieces(0, H):
                        for hc in range(HC):
                            kb.mm(pmo[0:P, 0:b - a],
                                  yts[hc][0:P, lt * P:(lt + 1) * P],
                                  wT["o"][hc][:, a:b],
                                  start=(hc == 0), stop=(hc == HC - 1),
                                  psum="pmo",
                                  reads=(f"yts{hc}", f"woT{hc}"))
                        kb.op("V", "tensor_tensor", out=atp[:, a:b],
                              in0=pmo[0:P, 0:b - a], in1=boB[:, a:b],
                              op=ALU.add,
                              reads=("pmo", "boB"), writes=("atp",))
                    for cb in range(H // BNW):
                        kb.op("V", "bn_stats", bnsB[:, cb * 6:(cb + 1) * 6],
                              atp[:, cb * BNW:(cb + 1) * BNW],
                              reads=("atp",), writes=("bnsB",))
                    kb.op("V", "bn_aggr", mvB[:, :], bnsB[:, :],
                          reads=("bnsB",), writes=("mvB",))
                    rsqrt_sc(kb, srB, "srB", mvB[:, 1:2], ("mvB",))
                    kb.dma("Q", xre[:, :],
                           xs[(own_t0 + jo) * P:(own_t0 + jo + 1) * P, :],
                           stream="xre", reads=("xre",), writes=("xre",))
                    kb.op("V", "tensor_scalar", atp[:, :], atp[:, :],
                          mvB[:, 0:1], srB[:, 0:1], ALU.subtract, ALU.mult,
                          reads=("atp", "mvB", "srB"), writes=("atp",))
                    kb.op("V", "tensor_tensor", out=x2t[:, :], in0=atp[:, :],
                          in1=xre[:, :], op=ALU.add,
                          reads=("atp", "xre"), writes=("x2t",))
                    kb.dma("G", x2s[jo * P:(jo + 1) * P, :], x2t[:, :],
                           stream=f"x2s{jo % 2}", reads=("x2t",),
                           writes=(f"x2s_f{jo}",))
                    for cb in range(H // BNW):
                        kb.op("V", "bn_stats", bnsB[:, cb * 6:(cb + 1) * 6],
                              x2t[:, cb * BNW:(cb + 1) * BNW],
                              reads=("x2t",), writes=("bnsB",))
                    kb.op("V", "bn_aggr", mvC[:, 0:2], bnsB[:, :],
                          reads=("bnsB",), writes=("mvC",))
                    kb.op("S", "activation", mvC[:, 2:3], mvC[:, 1:2], AF.Ln,
                          bias=LN_EPS, reads=("mvC",), writes=("mvC_l",))
                    kb.op("S", "activation", mvC[:, 3:4], mvC[:, 2:3],
                          AF.Exp, scale=-0.5, reads=("mvC_l",),
                          writes=("mvC_r1",))
                    kb.op("V", "tensor_tensor", out=mvC[:, 4:5],
                          in0=mvC[:, 3:4], in1=mvC[:, 3:4], op=ALU.mult,
                          reads=("mvC_r1",), writes=("mvC_v2",))
                    kb.op("V", "tensor_tensor", out=mvC[:, 4:5],
                          in0=mvC[:, 4:5], in1=mvC[:, 1:2], op=ALU.mult,
                          reads=("mvC_v2", "mvC"), writes=("mvC_v2",))
                    kb.op("S", "activation", mvC[:, 5:6], mvC[:, 4:5], AF.Ln,
                          bias=LN_EPS, reads=("mvC_v2",), writes=("mvC_l2",))
                    kb.op("S", "activation", mvC[:, 5:6], mvC[:, 5:6],
                          AF.Exp, scale=-0.5, reads=("mvC_l2",),
                          writes=("mvC_r2",))
                    kb.op("V", "tensor_copy", out=mu1[:, jo:jo + 1],
                          in_=mvC[:, 0:1], reads=("mvC",), writes=("mu1",))
                    kb.op("V", "tensor_tensor", out=rr1[:, jo:jo + 1],
                          in0=mvC[:, 3:4], in1=mvC[:, 5:6], op=ALU.mult,
                          reads=("mvC_r1", "mvC_r2"), writes=("rr1",))
                    run_q(3)
            run_q(10 ** 9)
            kb.fence_dma("V")
            kb.fence_dma("S")

        if cfg.get("STOP", "") in ("0", "A", "A2"):
            def phase_b(kb, role):
                kb.fence_dma("G")
        with nc.If(pid < cfg["ROLE_SPLIT"]):
            phase_b(kbA, 0)
        with nc.Else():
            phase_b(kbB, 1)
        nc.all_engine_barrier()
        kbA.barrier_reset()
        kbB.barrier_reset()
        stB.close()
        stAB.close()

        stC = ExitStack()
        EC = stC.enter_context
        x2r = [EC(nc.sbuf_tensor(f"x2r{i}", [P, H], F32)) for i in range(2)]
        x2e = [EC(nc.sbuf_tensor(f"x2e{i}", [P, H], F32)) for i in range(2)]
        xfT = [[EC(nc.sbuf_tensor(f"xfT{tb}_{hc}", [P, FCH * P], BF16))
                for hc in range(HC)] for tb in range(2)]
        w1tb = [EC(nc.sbuf_tensor(f"w1tb{i}", [P, HC * 512], BF16))
                for i in range(2)]
        w2tb = EC(nc.sbuf_tensor("w2tb", [P, HIDC * 512], BF16))
        hTt = EC(nc.sbuf_tensor("hTt", [P, HIDC * FCH * P], BF16))
        sqr = [EC(nc.sbuf_tensor(f"sqr{i}", [P, FCH * P], BF16))
               for i in range(3)]
        varrow = EC(nc.sbuf_tensor("varrow", [1, FCH * P], F32))
        mu2row = EC(nc.sbuf_tensor("mu2row", [1, FCH * P], F32))
        c1row = EC(nc.sbuf_tensor("c1row", [33, HID], BF16))
        c2b = EC(nc.sbuf_tensor("c2b", [P, H], F32))
        c2g = EC(nc.sbuf_tensor("c2g", [1, H], F32))
        crow = EC(nc.sbuf_tensor("crow", [1, 512], F32))
        bhrow = EC(nc.sbuf_tensor("bhrow", [1, 512], F32))
        bnsC = EC(nc.sbuf_tensor("bnsC", [P, FCH * 6 * (HID // BNW2)],
                                  F32))
        mvD = EC(nc.sbuf_tensor("mvD", [P, 2], F32))
        srD = EC(nc.sbuf_tensor("srD", [P, 2], F32))
        rsD = EC(nc.sbuf_tensor("rsD", [P, FCH], F32))
        murow = EC(nc.sbuf_tensor("murow", [1, FCH * P], F32))
        xfC = EC(nc.sbuf_tensor("xfC", [P, H], BF16))
        ffo = [EC(nc.sbuf_tensor(f"ffo{i}", [P, H], F32)) for i in range(FCH)]
        oto = EC(nc.sbuf_tensor("oto", [P, H], F32))
        pf = [EC(nc.psum_tensor(f"pf{i}", [P, 512], F32)) for i in range(3)]
        pst = [EC(nc.psum_tensor(f"pst{i}", [P, 512], F32))
               for i in range(2)]
        pc2 = [EC(nc.psum_tensor(f"pc2{i}", [P, 512], F32)) for i in range(2)]
        pmu = EC(nc.psum_tensor("pmu", [P, 512], F32))

        def phase_c(kb, role):
            own_t0 = 0 if role == 0 else QA_T
            own_nt = QA_T if role == 0 else NT - QA_T
            kb.op("V", "memset", c1row[:, :], 0.0, writes=("c1row",))
            first = True

            chunks = []
            t0 = 0
            while t0 < own_nt:
                w = min(FCH, own_nt - t0)
                chunks.append((t0, w))
                t0 += w
            def stage_chunk(ct0, Wt, tb):
                for i in range(Wt):
                    jo = ct0 + i
                    jb = i % 2
                    kb.dma("Q", x2r[jb][:, :], x2s[jo * P:(jo + 1) * P, :],
                           stream=f"x2r{jb}",
                           reads=(f"x2r{jb}", f"x2s_f{jo}"),
                           writes=(f"x2r{jb}",))
                    kb.op("V", "tensor_scalar", xfC[:, 0:H],
                          x2r[jb][:, :],
                          mu1[:, jo:jo + 1], rr1[:, jo:jo + 1],
                          ALU.subtract, ALU.mult,
                          reads=(f"x2r{jb}", "mu1", "rr1"),
                          writes=("xfC",))
                    kb.dma("G", xfsc[jo * P:(jo + 1) * P, :], xfC[:, 0:H],
                           stream="xfsc", reads=("xfC",),
                           writes=(f"xfsc_f{jo}",))
                for hc in range(HC):
                    kb.dma("Q",
                           xfT[tb][hc][:, 0:Wt * P],
                           xfsc[ct0 * P:(ct0 + Wt) * P, hc * P:(hc + 1) * P],
                           stream=f"xfT{hc % 4}", transpose=True,
                           reads=tuple([f"xfTg{tb}_{hc}"] +
                                       [f"xfsc_f{ct0 + i}"
                                        for i in range(Wt)]),
                           writes=(f"xfTg{tb}_{hc}",))
                for hc in range(HC):
                    kb.op("V", "tensor_scalar", xfT[tb][hc][:, 0:Wt * P],
                          xfT[tb][hc][:, 0:Wt * P], g1c[:, hc:hc + 1],
                          None, ALU.mult,
                          reads=(f"xfTg{tb}_{hc}", "g1c"),
                          writes=(f"xfTg{tb}_{hc}",))

            stage_chunk(chunks[0][0], chunks[0][1], 0)
            for ci, (ct0, Wt) in enumerate(chunks):
                tb = ci % 2
                if ci + 1 < len(chunks):
                    stage_chunk(chunks[ci + 1][0], chunks[ci + 1][1],
                                (ci + 1) % 2)
                hpieces = []
                for hs in range(NSL):
                    for a, b in _bank_pieces(0, SL):
                        hpieces.append((hs, a, b))
                pend_stats = []
                for pi_, (hs, a, b) in enumerate(hpieces):
                    wk = f"w1tb{pi_ % 2}"
                    wb = w1tb[pi_ % 2]
                    ga = hs * SL + a
                    slab = (ga // SEG1) * (HC * SEG1)
                    kb.dma("Q", wb[:, 0:HC * (b - a)],
                           t1Ts[:, slab:slab + HC * (b - a)],
                           stream=wk, reads=(wk, "t1Ts_f"), writes=(wk,))
                    if first:
                        for hc in range(HC):
                            kb.mm(pc2[0][0:33, 0:b - a],
                                  b1lz[:, hc:hc + 33],
                                  wb[:, hc * (b - a):(hc + 1) * (b - a)],
                                  start=(hc == 0), stop=(hc == HC - 1),
                                  psum="pc20", reads=("b1lz", wk))
                        kb.dma("Q", bhrow[0:1, 0:b - a],
                               b1[None, ga:ga + b - a], stream="bhrow",
                               reads=("bhrow",), writes=("bhrow",))
                        kb.op("V", "tensor_scalar", crow[0:1, 0:b - a],
                              bhrow[0:1, 0:b - a], qtmp[0:1, 1:2], None,
                              ALU.mult, reads=("bhrow", "qtmp_s0"),
                              writes=("crow",))
                        kb.op("V", "tensor_tensor",
                              out=crow[0:1, 0:b - a],
                              in0=pc2[0][0:1, 0:b - a],
                              in1=crow[0:1, 0:b - a], op=ALU.add,
                              reads=("pc20", "crow"), writes=("crow",))
                        kb.op("V", "tensor_tensor",
                              out=crow[0:1, 0:b - a],
                              in0=pc2[0][32:33, 0:b - a],
                              in1=crow[0:1, 0:b - a], op=ALU.add,
                              reads=("pc20", "crow"), writes=("crow",))
                        kb.op("V", "tensor_copy",
                              out=c1row[0:1, ga:ga + b - a],
                              in_=crow[0:1, 0:b - a], reads=("crow",),
                              writes=("c1row",))
                        kb.op("V", "tensor_tensor",
                              out=c1row[32:33, ga:ga + b - a],
                              in0=crow[0:1, 0:b - a],
                              in1=c1row[0:1, ga:ga + b - a],
                              op=ALU.subtract, reads=("crow", "c1row"),
                              writes=("c1row",))
                    for sub in range((b - a) // P):
                        hcg = ga // P + sub
                        bank = (pi_ * ((b - a) // P) + sub) % 3
                        pk = f"pf{bank}"
                        pp = pf[bank]
                        for hc in range(HC):
                            kb.mm(pp[0:P, 0:Wt * P],
                                  wb[:, hc * (b - a) + sub * P:
                                     hc * (b - a) + (sub + 1) * P],
                                  xfT[tb][hc][:, 0:Wt * P],
                                  start=(hc == 0), stop=False,
                                  psum=pk, reads=(f"xfTg{tb}_{hc}", wk))
                        kb.mm(pp[0:P, 0:Wt * P],
                              c1row[0:33, hcg * P:(hcg + 1) * P],
                              onesK[0:33, 0:Wt * P],
                              start=False, stop=True, psum=pk,
                              reads=("onesK", "c1row"))
                        kb.op("S", "activation",
                              hTt[:, hcg * FCH * P:hcg * FCH * P + Wt * P],
                              pp[0:P, 0:Wt * P],
                              AF.Gelu, scale=rc128[:, 0:1],
                              reads=(pk, "rc128"),
                              writes=(f"hTt{hcg}",))
                        kb.op("V", "tensor_tensor",
                              out=sqr[hcg % 3][:, 0:Wt * P],
                              in0=hTt[:, hcg * FCH * P:
                                      hcg * FCH * P + Wt * P],
                              in1=hTt[:, hcg * FCH * P:
                                      hcg * FCH * P + Wt * P],
                              op=ALU.mult,
                              reads=(f"hTt{hcg}",),
                              writes=(f"sqr{hcg % 3}",))

                        def mkstats(hcg=hcg, Wt=Wt):
                            def em():
                                kb.mm(pst[0][0:33, 0:Wt * P],
                                      onesZ[:, 0:33],
                                      hTt[:, hcg * FCH * P:
                                          hcg * FCH * P + Wt * P],
                                      start=(hcg == 0),
                                      stop=(hcg == HIDC - 1),
                                      psum="pst0",
                                      reads=("onesZ", f"hTt{hcg}"))
                                kb.mm(pst[1][0:33, 0:Wt * P],
                                      onesZ[:, 0:33],
                                      sqr[hcg % 3][:, 0:Wt * P],
                                      start=(hcg == 0),
                                      stop=(hcg == HIDC - 1),
                                      psum="pst1",
                                      reads=("onesZ", f"sqr{hcg % 3}"))
                            return em
                        pend_stats.append(mkstats())
                        if len(pend_stats) > 2:
                            pend_stats.pop(0)()
                for f_ in pend_stats:
                    f_()
                pend_stats = []
                kb.op("V", "tensor_scalar", murow[0:1, 0:Wt * P],
                      pst[0][0:1, 0:Wt * P], -1.0 / HID, None, ALU.mult,
                      reads=("pst0",), writes=("murow",))
                kb.op("V", "tensor_scalar", varrow[0:1, 0:Wt * P],
                      pst[1][0:1, 0:Wt * P], 1.0 / HID, None, ALU.mult,
                      reads=("pst1",), writes=("varrow",))
                kb.op("V", "tensor_tensor", out=mu2row[0:1, 0:Wt * P],
                      in0=murow[0:1, 0:Wt * P], in1=murow[0:1, 0:Wt * P],
                      op=ALU.mult, reads=("murow",), writes=("mu2row",))
                kb.op("V", "tensor_tensor", out=varrow[0:1, 0:Wt * P],
                      in0=varrow[0:1, 0:Wt * P], in1=mu2row[0:1, 0:Wt * P],
                      op=ALU.subtract, reads=("varrow", "mu2row"),
                      writes=("varrow",))
                kb.op("S", "activation", varrow[0:1, 0:Wt * P],
                      varrow[0:1, 0:Wt * P], AF.Ln, bias=LN_EPS,
                      reads=("varrow",), writes=("varrow",))
                kb.op("S", "activation", varrow[0:1, 0:Wt * P],
                      varrow[0:1, 0:Wt * P], AF.Exp, scale=-0.5,
                      reads=("varrow",), writes=("varrow",))
                kb.op("V", "tensor_scalar", varrow[0:1, 0:Wt * P],
                      varrow[0:1, 0:Wt * P], qtmp[0:1, 6:7], None,
                      ALU.mult, reads=("varrow", "qtmp_r1"),
                      writes=("varrow",))
                for i in range(Wt):
                    kb.op("T", "transpose", pmu[0:P, 0:1],
                          varrow[0:1, i * P:(i + 1) * P],
                          identf[0:1, 0:1],
                          reads=("varrow", "identf"), writes=("pmu",))
                    kb.op("V", "tensor_copy", out=rsD[:, i:i + 1],
                          in_=pmu[0:P, 0:1], reads=("pmu",),
                          writes=("rsD",))
                for a, b in _bank_pieces(0, H):
                    slab2 = (a // SEG2) * (HIDC * SEG2)
                    kb.dma("Q", w2tb[:, 0:HIDC * (b - a)],
                           t2Ts[:, slab2:slab2 + HIDC * (b - a)],
                           stream="w2tb", reads=("w2tb", "t2Ts_f"),
                           writes=("w2tb",))
                    if first:
                        kb.dma("Q", bhrow[0:1, 0:b - a], b2[None, a:b],
                               stream="bhrow", reads=("bhrow",),
                               writes=("bhrow",))
                        kb.op("V", "tensor_scalar", crow[0:1, 0:b - a],
                              bhrow[0:1, 0:b - a], qtmp[0:1, 5:6], None,
                              ALU.mult, reads=("bhrow", "qtmp_s1"),
                              writes=("crow",))
                        kb.op("V", "tensor_tensor", out=crow[0:1, 0:b - a],
                              in0=c2pre[0:1, a:b],
                              in1=crow[0:1, 0:b - a],
                              op=ALU.add, reads=("c2pre", "crow"),
                              writes=("crow",))
                        kb.mm(pc2[0][0:P, 0:b - a], onesf[0:1, 0:P],
                              crow[0:1, 0:b - a], start=True, stop=True,
                              psum="pc20", reads=("crow", "onesf"))
                        kb.op("V", "tensor_scalar", c2b[:, a:b],
                              pc2[0][0:P, 0:b - a], rc128[:, 1:2], None,
                              ALU.mult, reads=("pc20", "rc128"),
                              writes=("c2b",))
                        for hc in range(HIDC):
                            kb.mm(pmu[0:33, 0:b - a], onesZ[:, 0:33],
                                  w2tb[:, hc * (b - a):(hc + 1) * (b - a)],
                                  start=(hc == 0), stop=(hc == HIDC - 1),
                                  psum="pmu", reads=("onesZ", "w2tb"))
                        kb.op("V", "tensor_copy", out=c2g[0:1, a:b],
                              in_=pmu[0:1, 0:b - a],
                              reads=("pmu",), writes=("c2g",))
                    for i in range(Wt):
                        pk2 = f"pc2{i % 2}"
                        pq = pc2[i % 2]
                        for hc in range(HIDC):
                            kb.mm(pq[0:P, 0:b - a],
                                  hTt[:, hc * FCH * P + i * P:
                                      hc * FCH * P + (i + 1) * P],
                                  w2tb[:, hc * (b - a):(hc + 1) * (b - a)],
                                  start=(hc == 0), stop=False,
                                  psum=pk2, reads=(f"hTt{hc}", "w2tb"))
                        kb.mm(pq[0:P, 0:b - a],
                              murow[0:1, i * P:(i + 1) * P], c2g[0:1, a:b],
                              start=False, stop=True, psum=pk2,
                              reads=("murow", "c2g"))
                        kb.op("V", "tensor_scalar", pq[0:P, 0:b - a],
                              pq[0:P, 0:b - a], rsD[:, i:i + 1], None,
                              ALU.mult, reads=(pk2, "rsD"),
                              writes=(pk2,))
                        kb.op("V", "tensor_tensor", out=ffo[i][:, a:b],
                              in0=pq[0:P, 0:b - a], in1=c2b[:, a:b],
                              op=ALU.add, reads=(pk2, "c2b"),
                              writes=(f"ffo{i}",))
                for i in range(Wt):
                    jo = ct0 + i
                    kb.dma("Q", x2e[i % 2][:, :],
                           x2s[jo * P:(jo + 1) * P, :],
                           stream=f"x2e{i % 2}",
                           reads=(f"x2e{i % 2}", f"x2s_f{jo}"),
                           writes=(f"x2e{i % 2}",))
                    for cb in range(H // BNW):
                        kb.op("V", "bn_stats", bnsC[:, cb * 6:(cb + 1) * 6],
                              ffo[i][:, cb * BNW:(cb + 1) * BNW],
                              reads=(f"ffo{i}",), writes=("bnsC",))
                    kb.op("V", "bn_aggr", mvD[:, :],
                          bnsC[:, 0:6 * (H // BNW)],
                          reads=("bnsC",), writes=("mvD",))
                    rsqrt_sc(kb, srD, "srD", mvD[:, 1:2], ("mvD",))
                    kb.op("V", "tensor_scalar", oto[:, :], ffo[i][:, :],
                          mvD[:, 0:1], srD[:, 0:1], ALU.subtract, ALU.mult,
                          reads=(f"ffo{i}", "mvD", "srD"), writes=("oto",))
                    kb.op("V", "tensor_tensor", out=oto[:, :],
                          in0=oto[:, :], in1=x2e[i % 2][:, :], op=ALU.add,
                          reads=("oto", f"x2e{i % 2}"), writes=("oto",))
                    kb.dma("G", out[jo * P:(jo + 1) * P, :], oto[:, :],
                           stream="out", reads=("oto",))
                first = False
            kb.fence_dma("G")
            kb.fence_dma("Q")
            kb.fence_dma("S")

        if cfg.get("STOP", "") in ("0", "A", "A2", "B"):
            def phase_c(kb, role):
                kb.fence_dma("G")
        with nc.If(pid < cfg["ROLE_SPLIT"]):
            phase_c(kbA, 0)
        with nc.Else():
            phase_c(kbB, 1)
        stC.close()

    return nc



_CACHE = {}


def _get_program(key, cfg):
    if key not in _CACHE:
        _CACHE[key] = build_program(cfg)
    return _CACHE[key]


def _ntabs(T, R):
    t = np.arange(1, T + 1, dtype=np.float64)[:, None]
    i = np.arange(R, dtype=np.float64)
    theta = (1.0 / 10000.0) ** (i / R)
    ang = t * theta
    n1 = np.round(ang / (2 * np.pi)).astype(np.float32)
    n2 = np.round((ang + np.pi / 2) / (2 * np.pi)).astype(np.float32)
    return n1, n2


def make_in_maps(cfg, inputs):
    T = cfg["T"]
    n1, n2 = _ntabs(T, 32)
    shared = {
        "Wq": inputs["L0_attn_Wq"], "bq": inputs["L0_attn_bq"],
        "Wk": inputs["L0_attn_Wk"], "bk": inputs["L0_attn_bk"],
        "Wv": inputs["L0_attn_Wv"], "bv": inputs["L0_attn_bv"],
        "Wo": inputs["L0_attn_Wo"], "bo": inputs["L0_attn_bo"],
        "g1": inputs["L0_ffn_ln1_g"], "b1ln": inputs["L0_ffn_ln1_b"],
        "w1": inputs["L0_ffn_w1"], "b1": inputs["L0_ffn_b1"],
        "g2": inputs["L0_ffn_ln2_g"], "b2ln": inputs["L0_ffn_ln2_b"],
        "w2": inputs["L0_ffn_w2"], "b2": inputs["L0_ffn_b2"],
        "ntab1": n1, "ntab2": n2,
    }
    shared = {k: np.ascontiguousarray(np.asarray(v, dtype=np.float32))
              for k, v in shared.items()}
    x = np.ascontiguousarray(np.asarray(inputs["L0_attn_x"],
                                        dtype=np.float32))
    nseq = x.shape[0]
    in_maps = []
    for c in range(cfg["NCORES"]):
        seq = c % nseq
        m = dict(shared)
        m["xs"] = x[seq]
        in_maps.append(m)
    return in_maps


def assemble_out(cfg, results, xshape):
    T, QA_T = cfg["T"], cfg["QA_T"]
    QA = QA_T * P
    nseq = xshape[0]
    full = np.empty(xshape, dtype=np.float32)
    for c in range(cfg["NCORES"]):
        seq = c % nseq
        o = np.asarray(results[c]["out"])
        if c < cfg["ROLE_SPLIT"]:
            full[seq, 0:QA] = o[0:QA]
        else:
            full[seq, QA:T] = o[0:T - QA]
    return full


def kernel(**inputs):
    cfg = full_cfg()
    nc = _get_program("full", cfg)
    in_maps = make_in_maps(cfg, inputs)
    res = run_bass_kernel_spmd(nc, in_maps,
                               core_ids=list(range(cfg["NCORES"])))
    x = np.asarray(inputs["L0_attn_x"])
    return assemble_out(cfg, res.results, x.shape)


# version 31
# speedup vs baseline: 1.4348x; 1.1941x over previous
import bisect
import math
from contextlib import ExitStack

import numpy as np

import concourse.bass as bass
import concourse.mybir as mybir
from concourse.bass_utils import run_bass_kernel_spmd

F32 = mybir.dt.float32
BF16 = mybir.dt.bfloat16
AF = mybir.ActivationFunctionType
ALU = mybir.AluOpType
AX = mybir.AxisListType

TWO_PI = 2.0 * math.pi
C1 = float(np.float32(TWO_PI))
C2 = float(np.float32(TWO_PI - C1))
C3 = TWO_PI - C1 - C2
NEG_BIG = -8.0e10
LN_EPS = 1e-5
P = 128


def full_cfg():
    return dict(T=4096, H=1024, NH=16, HID=4096, QA_T=12, BCH=4, FCH=3,
                NCORES=8, ROLE_SPLIT=4, MAXW=512)


def mini_cfg():
    return dict(T=512, H=256, NH=4, HID=512, QA_T=2, BCH=2, FCH=1,
                NCORES=2, ROLE_SPLIT=1)


def med_cfg():
    return dict(T=1024, H=256, NH=4, HID=2048, QA_T=3, BCH=2, FCH=2,
                NCORES=2, ROLE_SPLIT=1, MAXW=512)


def med2_cfg():
    return dict(T=1024, H=1024, NH=16, HID=2048, QA_T=5, BCH=4, FCH=2,
                NCORES=2, ROLE_SPLIT=1, MAXW=1024)



class _Eng:
    def __init__(self, name, eng, semname):
        self.name = name
        self.eng = eng
        self.semname = semname
        self.pos = 0
        self.inc_pos = []
        self.inc_cnt = []
        self.count = 0
        self.handles = {}
        self.waited = {}


class Kb:

    def __init__(self, nc, sems, streams):
        self.nc = nc
        self.sems = sems
        self.streams = streams
        self.E = {
            "T": _Eng("T", nc.tensor, "et"),
            "V": _Eng("V", nc.vector, "ev"),
            "S": _Eng("S", nc.scalar, "es"),
            "G": _Eng("G", nc.gpsimd, "eg"),
            "Q": _Eng("Q", nc.sync, "eq"),
        }
        self.dmacnt = {}
        self.res = {}

    def clone_counters_from(self, other):
        for n, e in self.E.items():
            oe = other.E[n]
            e.pos = oe.pos
            e.inc_pos = list(oe.inc_pos)
            e.inc_cnt = list(oe.inc_cnt)
            e.count = oe.count
            e.handles = dict(oe.handles)
            e.waited = dict(oe.waited)
        self.dmacnt = dict(other.dmacnt)
        self.res = {k: {"w": v["w"], "r": dict(v["r"])}
                    for k, v in other.res.items()}

    def _resolve(self, tok):
        if tok[0] == "D":
            st = tok[1]
            return (self.streams[st], self.dmacnt[st])
        _, en, pos = tok
        e = self.E[en]
        i = bisect.bisect_left(e.inc_pos, pos)
        if i < len(e.inc_pos):
            return (e.semname, e.inc_cnt[i])
        ins = e.handles.get(pos)
        assert ins is not None, f"no handle for {en}@{pos}"
        e.count += 1
        ins.then_inc(self.sems[e.semname], 1)
        e.inc_pos.append(pos)
        e.inc_cnt.append(e.count)
        return (e.semname, e.count)

    def _wait_deps(self, e, reads, writes):
        toks = []
        for r in reads:
            ent = self.res.get(r)
            if ent is not None and ent["w"] is not None:
                toks.append(ent["w"])
        for w in writes:
            ent = self.res.get(w)
            if ent is not None:
                if ent["w"] is not None:
                    toks.append(ent["w"])
                toks.extend(ent["r"].values())
        need = {}
        for t in toks:
            s, v = self._resolve(t)
            if need.get(s, 0) < v:
                need[s] = v
        for s, v in need.items():
            if e.waited.get(s, 0) < v:
                e.eng.wait_ge(self.sems[s], v)
                e.waited[s] = v

    def _record(self, tok, reads, writes):
        for w in writes:
            self.res[w] = {"w": tok, "r": {}}
        rk = tok[1]
        for r in reads:
            ent = self.res.setdefault(r, {"w": None, "r": {}})
            ent["r"][rk] = tok

    def op(self, en, method, *args, reads=(), writes=(), **kw):
        e = self.E[en]
        self._wait_deps(e, reads, writes)
        ins = getattr(e.eng, method)(*args, **kw)
        e.pos += 1
        e.handles[e.pos] = ins
        self._record(("E", en, e.pos), reads, writes)
        return ins

    def mm(self, out, lhsT, rhs, *, start, stop, psum, reads=()):
        e = self.E["T"]
        self._wait_deps(e, reads, (psum,) if start else ())
        ins = e.eng.matmul(out, lhsT, rhs, start=start, stop=stop)
        e.pos += 1
        e.handles[e.pos] = ins
        tok = ("E", "T", e.pos)
        for r in reads:
            ent = self.res.setdefault(r, {"w": None, "r": {}})
            ent["r"]["T"] = tok
        if start:
            self.res[psum] = {"w": tok, "r": {}}
        else:
            self.res[psum]["w"] = tok
        return ins

    def dma(self, qn, out, in_, *, stream, reads=(), writes=(),
            transpose=False, slow=False):
        e = self.E[qn]
        if stream not in self.streams:
            self.streams[stream] = f"d{len(self.streams)}"
        semn = self.streams[stream]
        self._wait_deps(e, reads, writes)
        if transpose:
            ins = e.eng.dma_start(out=out, in_=in_, transpose=True)
        else:
            ins = e.eng.dma_start(out=out, in_=in_,
                                  allow_slow_non_contiguous=slow)
        self.dmacnt[stream] = self.dmacnt.get(stream, 0) + 16
        ins.then_inc(self.sems[semn], 16)
        self._record(("D", stream, self.dmacnt[stream]), reads, writes)

    def mark(self, key, stream):
        self.res[key] = {"w": ("D", stream, self.dmacnt.get(stream, 0)),
                         "r": {}}

    def barrier_reset(self):
        for ent in self.res.values():
            if ent["w"] is not None and ent["w"][0] == "E":
                ent["w"] = None
            ent["r"] = {k: v for k, v in ent["r"].items() if v[0] == "D"}

    def fence_dma(self, en="G"):
        e = self.E[en]
        for st, v in self.dmacnt.items():
            semn = self.streams[st]
            if v and e.waited.get(semn, 0) < v:
                e.eng.wait_ge(self.sems[semn], v)
                e.waited[semn] = v


def _bank_pieces(a, b):
    out = []
    while a < b:
        nxt = min(b, (a // 512 + 1) * 512)
        out.append((a, nxt))
        a = nxt
    return out



def build_program(cfg):
    TD = "S" if cfg.get("TDUAL", False) else "Q"
    TD2 = "Q"
    T, H, NH, HID = cfg["T"], cfg["H"], cfg["NH"], cfg["HID"]
    QA_T, BCH, FCH = cfg["QA_T"], cfg["BCH"], cfg["FCH"]
    NT = T // P
    HC = H // P
    HIDC = HID // P
    HS = H // NH
    R = HS // 2
    assert HS == 64 and R == 32 and H == NH * 64
    QA = QA_T * P
    MAXOWN = max(QA_T, NT - QA_T)
    SL = min(1024, HID)
    NSL = HID // SL
    SEG1 = min(512, HID)
    SEG2 = min(512, H)
    BNW = min(512, H)
    BNW2 = min(512, HID)
    SCALE = 1.0 / math.sqrt(HS)
    NHP = NH // 2
    assert NHP == HC
    MAXW = cfg.get("MAXW", 2048)
    GRP = 4
    W_B = BCH * P
    assert W_B <= 512

    nc = bass.Bass("TRN2", target_bir_lowering=False)

    dt = lambda n, s, d, k: nc.dram_tensor(n, s, d, kind=k).ap()
    xs = dt("xs", [T, H], F32, "ExternalInput")
    WSRC = {w: dt("W" + w, [H, H], F32, "ExternalInput") for w in "qkvo"}
    BSRC = {w: dt("b" + w, [H], F32, "ExternalInput") for w in "qkvo"}
    g1 = dt("g1", [H], F32, "ExternalInput")
    b1ln = dt("b1ln", [H], F32, "ExternalInput")
    w1 = dt("w1", [HID, H], F32, "ExternalInput")
    b1 = dt("b1", [HID], F32, "ExternalInput")
    g2 = dt("g2", [HID], F32, "ExternalInput")
    b2ln = dt("b2ln", [HID], F32, "ExternalInput")
    w2 = dt("w2", [H, HID], F32, "ExternalInput")
    b2 = dt("b2", [H], F32, "ExternalInput")
    ntab1 = dt("ntab1", [T, R], F32, "ExternalInput")
    ntab2 = dt("ntab2", [T, R], F32, "ExternalInput")
    out = dt("out", [MAXOWN * P, H], F32, "ExternalOutput")

    xhsc = dt("xhsc", [T, H], BF16, "Internal")
    krsc = dt("krsc", [T, NH * 64], BF16, "Internal")
    qrsc = dt("qrsc", [MAXOWN * P, NH * 64], BF16, "Internal")
    vsc = dt("vsc", [NH * P, NT * 65], BF16, "Internal")
    x2s = dt("x2s", [MAXOWN * P, H], F32, "Internal")
    xfsc = dt("xfsc", [MAXOWN * P, H], BF16, "Internal")
    hsc = dt("hsc", [MAXOWN * P, HID], BF16, "Internal")
    t1sc = dt("t1sc", [HID, H], BF16, "Internal")
    t2sc = dt("t2sc", [H, HID], BF16, "Internal")
    t1Ts = dt("t1Ts", [P, H // P * HID], BF16, "Internal")
    t2Ts = dt("t2Ts", [P, HID // P * H], BF16, "Internal")
    mkdr = dt("mkdr", [16, T], BF16, "Internal")
    wqsc = dt("wqsc", [4 * H, H], BF16, "Internal")

    LN1E4_OVER_R = math.log(1e-4) / R
    WOFF = {"q": 0, "k": H, "v": 2 * H, "o": 3 * H}

    with ExitStack() as gst:
        GE = gst.enter_context
        sems = {n: GE(nc.semaphore(n))
                for n in ("et", "ev", "es", "eg", "eq")}
        for i in range(88):
            sems[f"d{i}"] = GE(nc.semaphore(f"d{i}"))
        streams = {}
        pid = nc.partition_id(list(mybir.ALL_ENGINES))

        cidx = GE(nc.sbuf_tensor("cidx", [P, P], F32))
        pidx = GE(nc.sbuf_tensor("pidx", [P, 1], F32))
        tri01 = GE(nc.sbuf_tensor("tri01", [P, P], BF16))
        identf = GE(nc.sbuf_tensor("identf", [P, P], F32))
        onesf = GE(nc.sbuf_tensor("onesf", [16, P], F32))
        onescol = GE(nc.sbuf_tensor("onescol", [P, 1], F32))
        sc128 = GE(nc.sbuf_tensor("sc128", [P, 2], F32))
        rc128 = GE(nc.sbuf_tensor("rc128", [P, 2], F32))
        qtmp = GE(nc.sbuf_tensor("qtmp", [1, 8], F32))
        qacc = GE(nc.sbuf_tensor("qacc", [P, 3], F32))
        g1c = GE(nc.sbuf_tensor("g1c", [P, HC], F32))
        g2c = GE(nc.sbuf_tensor("g2c", [P, HIDC], F32))
        b1lz = GE(nc.sbuf_tensor("b1lz", [P, 32 + HC], BF16))
        b2lz = GE(nc.sbuf_tensor("b2lz", [P, 32 + HIDC], BF16))
        onesK = GE(nc.sbuf_tensor("onesK", [33, max(FCH * P, P)], BF16))
        onesZ = GE(nc.sbuf_tensor("onesZ", [P, 33], BF16))
        c2pre = GE(nc.sbuf_tensor("c2pre", [1, H], F32))
        mu1 = GE(nc.sbuf_tensor("mu1", [P, MAXOWN], F32))
        rr1 = GE(nc.sbuf_tensor("rr1", [P, MAXOWN], F32))
        brows = {w: GE(nc.sbuf_tensor(f"br_{w}", [33, H], BF16))
                 for w in "qkv"}
        boB = GE(nc.sbuf_tensor("boB", [P, H], BF16))
        constt = GE(nc.sbuf_tensor("constt", [P, 2], F32))
        nc.const_aps.register(F32, LN_EPS, constt[:, 0:1])
        nc.const_aps.register(F32, 1e-30, constt[:, 1:2])

        kbA = Kb(nc, sems, streams)
        kbB = Kb(nc, sems, streams)

        def rsqrt_sc(kb, dst, dst_key, src_ap, src_keys):
            kb.op("S", "activation", dst[:, 1:2], src_ap, AF.Ln,
                  bias=LN_EPS, reads=src_keys, writes=(dst_key,))
            kb.op("S", "activation", dst[:, 0:1], dst[:, 1:2], AF.Exp,
                  scale=-0.5, reads=(dst_key,), writes=(dst_key,))

        stTab = ExitStack()
        TE = stTab.enter_context
        cosT = TE(nc.sbuf_tensor("cosT", [P, R * NT], BF16, side="right"))
        sinT = TE(nc.sbuf_tensor("sinT", [P, R * NT], BF16, side="right"))
        wTq = [TE(nc.sbuf_tensor(f"wqT{hc}", [P, H], BF16, side="right"))
               for hc in range(HC)]
        wTk = [TE(nc.sbuf_tensor(f"wkT{hc}", [P, H], BF16, side="right"))
               for hc in range(HC)]
        wTv = [TE(nc.sbuf_tensor(f"wvT{hc}", [P, H], BF16, side="right"))
               for hc in range(HC)]
        wT = {"q": wTq, "k": wTk, "v": wTv}

        stLN = ExitStack()
        EL = stLN.enter_context
        xt = [EL(nc.sbuf_tensor(f"A_xt{i}", [P, H], F32, side="right"))
              for i in range(2)]
        bns = EL(nc.sbuf_tensor("A_bns", [P, 6 * (H // BNW)], F32,
                                side="right"))
        mv = EL(nc.sbuf_tensor("A_mv", [P, 2], F32, side="right"))
        sr = EL(nc.sbuf_tensor("A_sr", [P, 2], F32, side="right"))
        xh = EL(nc.sbuf_tensor("A_xh", [P, H], BF16, side="right"))

        def emit_ln_tile(kb, j):
            jb = j % 2
            xkey = f"A_xt{jb}"
            kb.dma("Q", xt[jb][:, :], xs[j * P:(j + 1) * P, :],
                   stream=xkey, reads=(xkey,), writes=(xkey,))
            for c in range(H // BNW):
                kb.op("V", "bn_stats", bns[:, c * 6:(c + 1) * 6],
                      xt[jb][:, c * BNW:(c + 1) * BNW],
                      reads=(xkey,), writes=("A_bns",))
            kb.op("V", "bn_aggr", mv[:, :], bns[:, :],
                  reads=("A_bns",), writes=("A_mv",))
            rsqrt_sc(kb, sr, "A_sr", mv[:, 1:2], ("A_mv",))
            kb.op("V", "tensor_scalar", xh[:, :], xt[jb][:, :],
                  mv[:, 0:1], sr[:, 0:1], ALU.subtract, ALU.mult,
                  reads=(xkey, "A_mv", "A_sr"), writes=("A_xh",))
            kb.dma("G", xhsc[j * P:(j + 1) * P, :], xh[:, :],
                   stream="xhsc", reads=("A_xh",),
                   writes=(f"xhsc_f{j}",))


        with ExitStack() as st0:
            E0 = st0.enter_context
            theta = E0(nc.sbuf_tensor("theta", [1, R], F32))
            irow = E0(nc.sbuf_tensor("irow", [1, R], F32))
            thetab = E0(nc.sbuf_tensor("thetab", [P, R], F32))
            tcol = E0(nc.sbuf_tensor("tcol", [P, 1], F32))
            angt = E0(nc.sbuf_tensor("angt", [P, R], F32))
            angc = E0(nc.sbuf_tensor("angc", [P, R], F32))
            n1t = E0(nc.sbuf_tensor("n1t", [P, R], F32))
            n2t = E0(nc.sbuf_tensor("n2t", [P, R], F32))
            wstg = [E0(nc.sbuf_tensor(f"wstg{i}", [P, H], BF16))
                    for i in range(4)]
            brow = E0(nc.sbuf_tensor("brow", [1, H], F32))
            lrow = E0(nc.sbuf_tensor("lrow", [P, max(HC, HIDC)], F32))
            pp0 = E0(nc.psum_tensor("pp0", [P, 512], F32))

            kb = kbA
            kb.op("G", "iota", cidx[:, :], [[1, P]], channel_multiplier=0,
                  allow_small_or_imprecise_dtypes=True, writes=("cidx",))
            kb.op("G", "iota", pidx[:, :], [[1, 1]], channel_multiplier=1,
                  allow_small_or_imprecise_dtypes=True, writes=("pidx",))
            kb.op("G", "iota", irow[0:1, :], [[1, R]], channel_multiplier=0,
                  allow_small_or_imprecise_dtypes=True, writes=("irow",))
            kb.op("V", "tensor_scalar", identf[:, :], cidx[:, :], pidx[:, :],
                  None, ALU.is_equal, reads=("cidx", "pidx"),
                  writes=("identf",))
            kb.op("V", "tensor_scalar", tri01[:, :], cidx[:, :], pidx[:, :],
                  None, ALU.is_ge,
                  reads=("cidx", "pidx"), writes=("tri01",))
            kb.op("V", "memset", onesf[:, :], 1.0, writes=("onesf",))
            kb.op("V", "memset", onesK[:, :], 0.0, writes=("onesK",))
            kb.op("V", "memset", onesK[0:1, :], 1.0, reads=("onesK",),
                  writes=("onesK",))
            kb.op("V", "memset", onesK[32:33, :], 1.0, reads=("onesK",),
                  writes=("onesK",))
            kb.op("V", "memset", onesZ[:, :], 0.0, writes=("onesZ",))
            kb.op("V", "memset", onesZ[:, 0:1], 1.0, reads=("onesZ",),
                  writes=("onesZ",))
            kb.op("V", "memset", onescol[:, :], 1.0, writes=("onescol",))
            kb.op("V", "memset", qacc[:, :], 0.0, writes=("qaccs",))
            kb.op("V", "memset", constt[:, 0:1], LN_EPS, writes=("constt",))
            kb.op("V", "memset", constt[:, 1:2], 1e-30, reads=("constt",),
                  writes=("constt",))
            kb.op("S", "activation", theta[0:1, :], irow[0:1, :], AF.Exp,
                  scale=LN1E4_OVER_R, reads=("irow",), writes=("theta",))
            kb.mm(pp0[0:P, 0:R], onesf[0:1, 0:P], theta[0:1, :],
                  start=True, stop=True, psum="pp0", reads=("onesf", "theta"))
            kb.op("V", "tensor_copy", out=thetab[:, :], in_=pp0[0:P, 0:R],
                  reads=("pp0",), writes=("thetab",))

            for j in range(NT):
                cs = slice(j * R, (j + 1) * R)
                kb.dma("Q", n1t[:, :], ntab1[j * P:(j + 1) * P, :],
                       stream="n1t", reads=("n1t",), writes=("n1t",))
                kb.dma("Q", n2t[:, :], ntab2[j * P:(j + 1) * P, :],
                       stream="n2t", reads=("n2t",), writes=("n2t",))
                kb.op("V", "tensor_scalar", tcol[:, :], pidx[:, :],
                      float(j * P + 1), None, ALU.add,
                      reads=("pidx",), writes=("tcol",))
                kb.op("V", "tensor_scalar", angt[:, :], thetab[:, :],
                      tcol[:, :], None, ALU.mult,
                      reads=("thetab", "tcol"), writes=("angt",))
                kb.op("V", "tensor_scalar", angc[:, :], angt[:, :],
                      math.pi / 2.0, None, ALU.add,
                      reads=("angt",), writes=("angc",))
                for c in (C1, C2, C3):
                    kb.op("V", "scalar_tensor_tensor", angt[:, :], n1t[:, :],
                          -c, angt[:, :], ALU.mult, ALU.add,
                          reads=("n1t", "angt"), writes=("angt",))
                    kb.op("V", "scalar_tensor_tensor", angc[:, :], n2t[:, :],
                          -c, angc[:, :], ALU.mult, ALU.add,
                          reads=("n2t", "angc"), writes=("angc",))
                kb.op("S", "activation", sinT[:, cs], angt[:, :], AF.Sin,
                      reads=("angt",), writes=(f"sinT{j}",))
                kb.op("S", "activation", cosT[:, cs], angc[:, :], AF.Sin,
                      reads=("angc",), writes=(f"cosT{j}",))

            for w in "qkvo":
                for oc in range(HC):
                    sg = wstg[oc % 4]
                    sk = f"wstg{oc % 4}"
                    kb.dma("G", sg[:, :], WSRC[w][oc * P:(oc + 1) * P, :],
                           stream=sk, reads=(sk,), writes=(sk,))
                    kb.dma("G",
                           wqsc[WOFF[w] + oc * P:WOFF[w] + (oc + 1) * P, :],
                           sg[:, :], stream="wqsc", reads=(sk,))
                kb.mark(f"wqsc_{w}", "wqsc")
                if w in "qkv":
                    for hc in range(HC):
                        kb.dma("Q", wT[w][hc][:, :],
                               wqsc[WOFF[w]:WOFF[w] + H,
                                    hc * P:(hc + 1) * P],
                               stream=f"wT{hc % 4}", transpose=True,
                               reads=(f"w{w}T{hc}", f"wqsc_{w}"),
                               writes=(f"w{w}T{hc}",))
            kb.mark("wqsc_f", "wqsc")
            for w in "qkv":
                kb.op("V", "memset", brows[w][:, :], 0.0,
                      writes=(f"br_{w}",))
                kb.dma("Q", brow[0:1, :], BSRC[w][None, :],
                       stream="brow", reads=("brow",), writes=("brow",))
                kb.op("V", "tensor_copy", out=brows[w][0:1, :],
                      in_=brow[0:1, :], reads=("brow", f"br_{w}"),
                      writes=(f"br_{w}",))
                kb.op("V", "tensor_tensor", out=brows[w][32:33, :],
                      in0=brow[0:1, :], in1=brows[w][0:1, :],
                      op=ALU.subtract, reads=("brow", f"br_{w}"),
                      writes=(f"br_{w}",))
            for w, bt, bkey in (("o", boB, "boB"),):
                kb.dma("Q", brow[0:1, :], BSRC[w][None, :],
                       stream="brow2", reads=("brow",), writes=("brow",))
                for a, b in _bank_pieces(0, H):
                    kb.mm(pp0[0:P, 0:b - a], onesf[0:1, 0:P],
                          brow[0:1, a:b], start=True, stop=True,
                          psum="pp0", reads=("brow", "onesf"))
                    kb.op("V", "tensor_copy", out=bt[:, a:b],
                          in_=pp0[0:P, 0:b - a], reads=("pp0",),
                          writes=(bkey,))

            kb.dma("Q", g1c[:, :], g1.rearrange("(c p) -> p c", p=P),
                   stream="g1c", writes=("g1c",), slow=True)
            kb.dma("Q", g2c[:, :], g2.rearrange("(c p) -> p c", p=P),
                   stream="g2c", writes=("g2c",), slow=True)
            kb.dma("Q", lrow[:, 0:HC], b1ln.rearrange("(c p) -> p c", p=P),
                   stream="lrow", reads=("lrow",), writes=("lrow",),
                   slow=True)
            kb.op("V", "memset", b1lz[:, :], 0.0, writes=("b1lz",))
            kb.op("V", "tensor_copy", out=b1lz[:, 0:HC], in_=lrow[:, 0:HC],
                  reads=("lrow", "b1lz"), writes=("b1lz",))
            kb.op("V", "tensor_tensor", out=b1lz[:, 32:32 + HC],
                  in0=lrow[:, 0:HC],
                  in1=b1lz[:, 0:HC], op=ALU.subtract,
                  reads=("lrow", "b1lz"), writes=("b1lz",))
            kb.dma("Q", lrow[:, 0:HIDC], b2ln.rearrange("(c p) -> p c", p=P),
                   stream="lrow", reads=("lrow",), writes=("lrow",),
                   slow=True)
            kb.op("V", "memset", b2lz[:, :], 0.0, writes=("b2lz",))
            kb.op("V", "tensor_copy", out=b2lz[:, 0:HIDC],
                  in_=lrow[:, 0:HIDC],
                  reads=("lrow", "b2lz"), writes=("b2lz",))
            kb.op("V", "tensor_tensor", out=b2lz[:, 32:32 + HIDC],
                  in0=lrow[:, 0:HIDC],
                  in1=b2lz[:, 0:HIDC], op=ALU.subtract,
                  reads=("lrow", "b2lz"), writes=("b2lz",))
            for j in range(QA_T):
                emit_ln_tile(kb, j)
            kb.fence_dma("V")
            kb.fence_dma("S")
            kb.fence_dma("G")

        nc.all_engine_barrier()
        kbA.barrier_reset()
        kbB.clone_counters_from(kbA)

        stAB = ExitStack()
        AE = stAB.enter_context
        krt = [AE(nc.sbuf_tensor(f"krt{c}", [P, T], BF16)) for c in range(NHP)]
        qrt = [AE(nc.sbuf_tensor(f"qrt{c}", [P, MAXOWN * P], BF16))
               for c in range(NHP)]
        mkacc = AE(nc.sbuf_tensor("mkacc", [P, NH], F32))

        stAL = ExitStack()
        EA = stAL.enter_context
        xhTg = [[EA(nc.sbuf_tensor(f"A_xhT{g}_{hc}", [P, GRP * P], BF16,
                                   side="right")) for hc in range(HC)]
                for g in range(2)]
        rt1 = EA(nc.sbuf_tensor("A_rt1", [P, H // 2], F32, side="right"))
        rt2 = EA(nc.sbuf_tensor("A_rt2", [P, H // 2], F32, side="right"))
        krb = EA(nc.sbuf_tensor("A_krb", [P, NH * 64], BF16, side="right"))
        sq16 = EA(nc.sbuf_tensor("A_sq16", [P, NH], F32, side="right"))
        sqQ = EA(nc.sbuf_tensor("A_sqQ", [P, NH], F32, side="right"))
        sq2 = EA(nc.sbuf_tensor("A_sq2", [P, NH], F32, side="right"))
        vst = [EA(nc.sbuf_tensor(f"A_vst{i}", [P, NH * 65], BF16,
                                 side="right")) for i in range(2)]
        ppr = [EA(nc.psum_tensor(f"A_ppr{i}", [P, 512], F32)) for i in range(4)]

        def phase_a(kb, role):
            own_t0 = 0 if role == 0 else QA_T
            own_nt = QA_T if role == 0 else NT - QA_T
            kv_nt = QA_T if role == 0 else NT

            kb.op("V", "memset", mkacc[:, :], 0.0, writes=("mkacc",))
            kb.op("G", "memset", krb[:, :], 0.0, writes=("A_krb",))
            for i in range(2):
                kb.op("G", "memset",
                      vst[i].rearrange("p (h c) -> p h c", h=NH)[:, :, 64:65],
                      1.0, writes=(f"A_vst{i}_ones",))

            def proj(w, gslot, jj, psum0):
                for a, b in _bank_pieces(0, H):
                    i = psum0 + (a // 512) % 2
                    pk = f"A_ppr{i}"
                    pr = ppr[i]
                    for hc in range(HC):
                        kb.mm(pr[0:P, 0:b - a],
                              xhTg[gslot][hc][:, jj * P:(jj + 1) * P],
                              wT[w][hc][:, a:b],
                              start=(hc == 0), stop=False, psum=pk,
                              reads=(f"A_xhTg{gslot}_{hc}", f"w{w}T{hc}"))
                    kb.mm(pr[0:P, 0:b - a], onesK[0:33, 0:P],
                          brows[w][0:33, a:b], start=False, stop=True,
                          psum=pk, reads=("onesK", f"br_{w}"))

            def rope_psum(kb, psum0, dst3, dstkey, j):
                if cfg.get("NOROPE", False):
                    for a, b in _bank_pieces(0, H):
                        i = psum0 + (a // 512) % 2
                        pk = f"A_ppr{i}"
                        kb.op("V", "tensor_copy",
                              out=rt1[:, a // 2:a // 2 + (b - a) // 2],
                              in_=ppr[i][0:P, 0:(b - a) // 2],
                              reads=(pk,), writes=("A_rt1",))
                    kb.op("V", "tensor_copy", out=dst3,
                          in_=rt1.rearrange("p (h r) -> p h r", h=NH),
                          reads=("A_rt1",), writes=(dstkey,))
                    return
                cs = slice(j * R, (j + 1) * R)
                cb = cosT[:, cs].rearrange("p (o r) -> p o r", o=1)\
                    .broadcast_to((P, NH, R))
                sb = sinT[:, cs].rearrange("p (o r) -> p o r", o=1)\
                    .broadcast_to((P, NH, R))
                for a, b in _bank_pieces(0, H):
                    i = psum0 + (a // 512) % 2
                    pk = f"A_ppr{i}"
                    hper = (b - a) // 64
                    h0 = a // 64
                    se = ppr[i][0:P, 0:b - a].rearrange(
                        "p (h r two) -> p h r two", h=hper, two=2)
                    hs_ = slice(h0, h0 + hper)
                    kb.op("V", "tensor_tensor",
                          out=rt1.rearrange("p (h r) -> p h r", h=NH)[:, hs_],
                          in0=se[:, :, :, 0], in1=cb[:, hs_], op=ALU.mult,
                          reads=(pk, f"cosT{j}"), writes=("A_rt1",))
                    kb.op("V", "tensor_tensor",
                          out=rt2.rearrange("p (h r) -> p h r", h=NH)[:, hs_],
                          in0=se[:, :, :, 1], in1=sb[:, hs_], op=ALU.mult,
                          reads=(pk, f"sinT{j}"), writes=("A_rt2",))
                kb.op("V", "tensor_tensor", out=dst3,
                      in0=rt1.rearrange("p (h r) -> p h r", h=NH),
                      in1=rt2.rearrange("p (h r) -> p h r", h=NH),
                      op=ALU.subtract,
                      reads=("A_rt1", "A_rt2"), writes=(dstkey,))

            groups = [(g0, min(GRP, kv_nt - g0))
                      for g0 in range(0, kv_nt, GRP)]
            if cfg.get("AGRP1", False):
                groups = groups[:1]
            for gi, (g0, gn) in enumerate(groups):
                gslot = gi % 2
                for jj in range(gn):
                    j = g0 + jj
                    if j >= QA_T:
                        emit_ln_tile(kb, j)
                for hc in range(HC):
                    kb.dma(TD2 if hc % 2 == 0 else TD,
                           xhTg[gslot][hc][:, 0:gn * P],
                           xhsc[g0 * P:(g0 + gn) * P, hc * P:(hc + 1) * P],
                           stream=f"xhT{gslot}_{hc % 2}", transpose=True,
                           reads=tuple([f"A_xhTg{gslot}_{hc}"] +
                                       [f"xhsc_f{g0 + t}" for t in range(gn)]),
                           writes=(f"A_xhTg{gslot}_{hc}",))
                if cfg.get("NOPROJ", False):
                    continue
                for jj in range(gn):
                    j = g0 + jj
                    jb = j % 2
                    proj("k", gslot, jj, 0)
                    krz = krb.rearrange("p (h z) -> p h z", z=64)
                    rope_psum(kb, 0, krz[:, :, 0:R], "A_krb", j)
                    kb.op("V", "tensor_tensor",
                          out=rt2.rearrange("p (h r) -> p h r", h=NH),
                          in0=krz[:, :, 0:R],
                          in1=krz[:, :, 0:R], op=ALU.mult,
                          reads=("A_krb",), writes=("A_rt2",))
                    kb.op("V", "tensor_reduce", sq16[:, :],
                          rt2.rearrange("p (h r) -> p h r", h=NH),
                          axis=AX.X, op=ALU.add,
                          reads=("A_rt2",), writes=("A_sq16",))
                    kb.op("V", "tensor_tensor", out=mkacc[:, :],
                          in0=mkacc[:, :], in1=sq16[:, :], op=ALU.max,
                          reads=("mkacc", "A_sq16"), writes=("mkacc",))
                    kb.dma("G", krsc[j * P:(j + 1) * P, :], krb[:, :],
                           stream="krsc",
                           reads=("A_krb", f"krsc_f{j}"),
                           writes=(f"krsc_f{j}",))
                    if cfg.get("ONLYK", False):
                        continue
                    vb = vst[jb]
                    for a, b in _bank_pieces(0, H):
                        i = 2 + (a // 512) % 2
                        pk = f"A_ppr{i}"
                        pr = ppr[i]
                        for hc in range(HC):
                            kb.mm(pr[0:P, 0:b - a],
                                  xhTg[gslot][hc][:, jj * P:(jj + 1) * P],
                                  wT["v"][hc][:, a:b],
                                  start=(hc == 0), stop=False,
                                  psum=pk,
                                  reads=(f"A_xhTg{gslot}_{hc}", f"wvT{hc}"))
                        kb.mm(pr[0:P, 0:b - a], onesK[0:33, 0:P],
                              brows["v"][0:33, a:b], start=False, stop=True,
                              psum=pk, reads=("onesK", "br_v"))
                        h0 = a // 64
                        nh_ = (b - a) // 64
                        kb.op("V", "tensor_copy",
                              out=vb.rearrange("p (h c) -> p h c", h=NH)[
                                  :, h0:h0 + nh_, 0:64],
                              in_=pr[0:P, 0:b - a].rearrange(
                                  "p (h c) -> p h c", c=64),
                              reads=(pk,), writes=(f"A_vst{jb}",))
                    if not cfg.get("NOV", False):
                        kb.dma("G",
                               vsc.rearrange("(h p) m -> p h m", p=P)[
                                   :, :, j * 65:(j + 1) * 65],
                               vb.rearrange("p (h c) -> p h c", h=NH),
                               stream=f"vsc{jb}",
                               reads=(f"A_vst{jb}", f"A_vst{jb}_ones"),
                               writes=(f"vsc_f{j}",))
                    if own_t0 <= j < own_t0 + own_nt:
                        jo = j - own_t0
                        proj("q", gslot, jj, 0)
                        krz = krb.rearrange("p (h z) -> p h z", z=64)
                        rope_psum(kb, 0, krz[:, :, 0:R], "A_krb", j)
                        kb.op("V", "tensor_tensor",
                              out=rt2.rearrange("p (h r) -> p h r", h=NH),
                              in0=krz[:, :, 0:R],
                              in1=krz[:, :, 0:R], op=ALU.mult,
                              reads=("A_krb",), writes=("A_rt2",))
                        kb.op("V", "tensor_reduce", sqQ[:, :],
                              rt2.rearrange("p (h r) -> p h r", h=NH),
                              axis=AX.X, op=ALU.add,
                              reads=("A_rt2",), writes=("A_sqQ",))
                        kb.op("S", "activation", sq2[:, :], sqQ[:, :],
                              AF.Ln, bias=1e-30,
                              reads=("A_sqQ",), writes=("A_sq2",))
                        with nc.allow_low_precision(
                                reason="softmax shift bound; slack is huge"):
                            kb.op("S", "activation", krz[:, :, 32:33],
                                  sq2.rearrange("p (h o) -> p h o", o=1),
                                  AF.Exp, scale=0.5,
                                  reads=("A_sq2",), writes=("A_krb",))
                        kb.dma("G", qrsc[jo * P:(jo + 1) * P, :], krb[:, :],
                               stream="qrsc",
                               reads=("A_krb", f"qrsc_f{jo}"),
                               writes=(f"qrsc_f{jo}",))
            kb.fence_dma("V")
            kb.fence_dma("S")

        if cfg.get("STOP", "") == "0":
            def phase_a(kb, role):
                kb.fence_dma("G")
        with nc.If(pid < cfg["ROLE_SPLIT"]):
            phase_a(kbA, 0)
        with nc.Else():
            phase_a(kbB, 1)
        nc.all_engine_barrier()
        kbA.barrier_reset()
        kbB.barrier_reset()
        stAL.close()

        stA2 = ExitStack()
        EA2 = stA2.enter_context
        mkt = EA2(nc.sbuf_tensor("A2_mkt", [16, P], F32, side="right"))
        mk1 = EA2(nc.sbuf_tensor("A2_mk1", [16, 4], F32, side="right"))
        mkd = EA2(nc.sbuf_tensor("A2_mkd", [16, 16], F32, side="right"))
        mkb = EA2(nc.sbuf_tensor("A2_mkb", [P, NH], F32, side="right"))
        mks = EA2(nc.sbuf_tensor("A2_mks", [16, T], BF16, side="right"))
        pps2 = EA2(nc.psum_tensor("A2_pps", [P, 512], F32))

        def phase_a2(kb, role):
            own_t0 = 0 if role == 0 else QA_T
            own_nt = QA_T if role == 0 else NT - QA_T
            kv_nt = QA_T if role == 0 else NT
            pps = pps2
            kb.op("T", "transpose", pps[0:NH, 0:P], mkacc[:, :], identf[:, :],
                  reads=("mkacc", "identf"), writes=("A2_pps",))
            kb.op("V", "tensor_copy", out=mkt[0:NH, :], in_=pps[0:NH, 0:P],
                  reads=("A2_pps",), writes=("A2_mkt",))
            kb.op("V", "tensor_reduce", mk1[0:NH, 0:1], mkt[0:NH, :],
                  axis=AX.X, op=ALU.max, reads=("A2_mkt",), writes=("A2_mk1",))
            kb.op("S", "activation", mk1[0:NH, 1:2], mk1[0:NH, 0:1],
                  AF.Ln, bias=1e-30, reads=("A2_mk1",), writes=("A2_l",))
            kb.op("S", "activation", mk1[0:NH, 2:3], mk1[0:NH, 1:2],
                  AF.Exp, scale=0.5, reads=("A2_l",), writes=("A2_s",))
            kb.op("V", "tensor_scalar", mk1[0:NH, 3:4], mk1[0:NH, 2:3],
                  -1.0, None, ALU.mult, reads=("A2_s",), writes=("A2_n",))
            kb.op("V", "tensor_scalar", mkd[0:NH, 0:NH],
                  identf[0:NH, 0:NH], mk1[0:NH, 3:4], None, ALU.mult,
                  reads=("identf", "A2_n"), writes=("A2_mkd",))
            kb.mm(pps[0:P, 0:NH], onesf[0:NH, :], mkd[0:NH, 0:NH],
                  start=True, stop=True, psum="A2_pps",
                  reads=("onesf", "A2_mkd"))
            kb.op("V", "tensor_copy", out=mkb[:, :], in_=pps[0:P, 0:NH],
                  reads=("A2_pps",), writes=("A2_mkb",))
            kb.op("V", "memset", mks[0:NH, 0:kv_nt * P], 1.0,
                  writes=("A2_mks",))
            kb.op("V", "tensor_scalar", mks[0:NH, 0:kv_nt * P],
                  mks[0:NH, 0:kv_nt * P], mk1[0:NH, 3:4], None, ALU.mult,
                  reads=("A2_mks", "A2_n"), writes=("A2_mks",))
            kb.dma("G", mkdr[0:NH, 0:kv_nt * P], mks[0:NH, 0:kv_nt * P],
                   stream="mkdr", reads=("A2_mks",))
            kb.mark("mkdr_f", "mkdr")
            for c in range(NHP):
                kb.dma(TD2 if c % 2 == 0 else TD,
                       krt[c][0:P, 0:kv_nt * P],
                       krsc[0:kv_nt * P, c * P:(c + 1) * P],
                       stream=f"krtL{c % 4}", transpose=True,
                       reads=tuple([f"krt_{c}"] +
                                   [f"krsc_f{j}" for j in range(kv_nt)]),
                       writes=(f"krt_{c}",))
                kb.dma(TD if c % 2 == 0 else TD2,
                       qrt[c][0:P, 0:own_nt * P],
                       qrsc[0:own_nt * P, c * P:(c + 1) * P],
                       stream=f"qrtL{c % 4}", transpose=True,
                       reads=tuple([f"qrt_{c}"] +
                                   [f"qrsc_f{j}" for j in range(own_nt)]),
                       writes=(f"qrt_{c}",))
            for c in range(NHP):
                for b0, h in ((32, 2 * c), (96, 2 * c + 1)):
                    kb.dma("Q", krt[c][b0:b0 + 1, 0:kv_nt * P],
                           mkdr[h:h + 1, 0:kv_nt * P],
                           stream=f"mkl{c % 4}",
                           reads=(f"krt_{c}", "mkdr_f"),
                           writes=(f"krt_{c}",))
            kb.fence_dma("Q")
            kb.fence_dma("S")

        if cfg.get("STOP", "") in ("0", "A"):
            def phase_a2(kb, role):
                kb.fence_dma("G")
        with nc.If(pid < cfg["ROLE_SPLIT"]):
            phase_a2(kbA, 0)
        with nc.Else():
            phase_a2(kbB, 1)
        nc.all_engine_barrier()
        kbA.barrier_reset()
        kbB.barrier_reset()
        stA2.close()
        stLN.close()
        stTab.close()

        stB = ExitStack()
        EB = stB.enter_context
        vbuf = [EB(nc.sbuf_tensor(f"vbuf{i}", [P, NT * 65], BF16,
                                  side="right")) for i in range(2)]
        ptp = [EB(nc.sbuf_tensor(f"ptp{i}", [P, 2 * W_B], BF16, side="right"))
               for i in range(2)]
        yts = [EB(nc.sbuf_tensor(f"yts{c}", [P, W_B], BF16, side="right"))
               for c in range(NHP)]
        rrow = EB(nc.sbuf_tensor("rrow", [1, max(W_B, H)], F32, side="right"))
        rsb = EB(nc.sbuf_tensor("rsb", [64, W_B], F32, side="right"))
        atp = EB(nc.sbuf_tensor("atp", [P, H], F32, side="right"))
        x2t = EB(nc.sbuf_tensor("x2t", [P, H], F32, side="right"))
        xre = EB(nc.sbuf_tensor("xre", [P, H], F32, side="right"))
        bnsB = EB(nc.sbuf_tensor("bnsB", [P, 6 * (H // BNW)], F32,
                                 side="right"))
        mvB = EB(nc.sbuf_tensor("mvB", [P, 2], F32, side="right"))
        srB = EB(nc.sbuf_tensor("srB", [P, 2], F32, side="right"))
        mvC = EB(nc.sbuf_tensor("mvC", [P, 6], F32, side="right"))
        qst = [EB(nc.sbuf_tensor(f"qst{i}", [P, MAXW], F32, side="right"))
               for i in range(3)]
        qg = EB(nc.sbuf_tensor("qg", [P, MAXW], BF16, side="right"))
        qt16 = EB(nc.sbuf_tensor("qt16", [P, MAXW], BF16, side="right"))
        tstg = EB(nc.sbuf_tensor("tstg", [P, max(H, HID)], BF16,
                                 side="right"))
        woT = [EB(nc.sbuf_tensor(f"woT{hc}", [P, H], BF16, side="right"))
               for hc in range(HC)]
        psc = [EB(nc.psum_tensor(f"psc{i}", [P, 2 * W_B], F32))
               for i in range(2)]
        pyt = [EB(nc.psum_tensor(f"pyt{i}", [P, W_B], F32))
               for i in range(2)]
        c2preB = EB(nc.sbuf_tensor("c2preB", [33, H], F32, side="right"))
        pmo = EB(nc.psum_tensor("pmo", [P, 512], F32))
        wT["o"] = woT

        def phase_b(kb, role):
            own_t0 = 0 if role == 0 else QA_T
            own_nt = QA_T if role == 0 else NT - QA_T
            kb.op("V", "memset", c2preB[:, :], 0.0, writes=("c2preB",))

            for hc in range(HC):
                kb.dma("Q", woT[hc][:, :],
                       wqsc[WOFF["o"]:WOFF["o"] + H, hc * P:(hc + 1) * P],
                       stream=f"wT{hc % 4}", transpose=True,
                       reads=(f"woT{hc}", "wqsc_f"), writes=(f"woT{hc}",))

            mats = [(0, w1, HID // P, H, max(1, H // MAXW), t1sc),
                    (1, w2, H // P, HID, max(1, HID // MAXW), t2sc)]
            qsteps = []

            def q_ld(src, jt, cc, cw, slot):
                kb.dma("Q", qst[slot][:, 0:cw],
                       src[jt * P:(jt + 1) * P, cc * cw:(cc + 1) * cw],
                       stream=f"qst{slot}", reads=(f"qst{slot}",),
                       writes=(f"qst{slot}",))

            def q_reduce(mi, cw, slot):
                kb.op("V", "tensor_reduce", qacc[:, 0:1],
                      qst[slot][:, 0:cw],
                      axis=AX.X, op=ALU.add, apply_absolute_value=True,
                      reads=(f"qst{slot}",), writes=("qacc1",))
                kb.op("V", "tensor_tensor", out=qacc[:, 1 + mi:2 + mi],
                      in0=qacc[:, 1 + mi:2 + mi], in1=qacc[:, 0:1],
                      op=ALU.add, reads=("qacc1", "qaccs"), writes=("qaccs",))

            def q_fin():
                for mi, src, ntile, width, ncc, dstT in mats:
                    n_el = float(ntile * P * width)
                    kb.mm(pmo[0:1, 0:1], qacc[:, 1 + mi:2 + mi],
                          onescol[:, 0:1], start=True, stop=True, psum="pmo",
                          reads=("qaccs", "onescol"))
                    kb.op("V", "tensor_scalar", qtmp[0:1, 4 * mi:4 * mi + 1],
                          pmo[0:1, 0:1], 1.0 / n_el, 1e-5, ALU.mult, ALU.max,
                          reads=("pmo",), writes=(f"qtmp_m{mi}",))
                    kb.op("V", "reciprocal",
                          out=qtmp[0:1, 4 * mi + 1:4 * mi + 2],
                          in_=qtmp[0:1, 4 * mi:4 * mi + 1],
                          reads=(f"qtmp_m{mi}",), writes=(f"qtmp_s{mi}",))
                    kb.op("V", "reciprocal",
                          out=qtmp[0:1, 4 * mi + 2:4 * mi + 3],
                          in_=qtmp[0:1, 4 * mi + 1:4 * mi + 2],
                          reads=(f"qtmp_s{mi}",), writes=(f"qtmp_r{mi}",))
                    kb.mm(pmo[0:P, 0:1], onesf[0:1, 0:P],
                          qtmp[0:1, 4 * mi + 1:4 * mi + 2],
                          start=True, stop=True, psum="pmo",
                          reads=(f"qtmp_s{mi}", "onesf"))
                    kb.op("V", "tensor_copy", out=sc128[:, mi:mi + 1],
                          in_=pmo[0:P, 0:1], reads=("pmo",), writes=("sc128",))
                    kb.mm(pmo[0:P, 0:1], onesf[0:1, 0:P],
                          qtmp[0:1, 4 * mi + 2:4 * mi + 3],
                          start=True, stop=True, psum="pmo",
                          reads=(f"qtmp_r{mi}", "onesf"))
                    kb.op("V", "tensor_copy", out=rc128[:, mi:mi + 1],
                          in_=pmo[0:P, 0:1], reads=("pmo",), writes=("rc128",))

            def q_tern(mi, jt, cc, cw, dstT, slot):
                kb.op("V", "tensor_scalar", qg[:, 0:cw],
                      qst[slot][:, 0:cw],
                      sc128[:, mi:mi + 1], 0.5, ALU.mult, ALU.is_gt,
                      reads=(f"qst{slot}", "sc128"), writes=("qg",))
                kb.op("V", "tensor_scalar", qt16[:, 0:cw],
                      qst[slot][:, 0:cw],
                      sc128[:, mi:mi + 1], -0.5, ALU.mult, ALU.is_lt,
                      reads=(f"qst{slot}", "sc128"), writes=("qt16",))
                kb.op("V", "tensor_tensor", out=qt16[:, 0:cw],
                      in0=qg[:, 0:cw], in1=qt16[:, 0:cw], op=ALU.subtract,
                      reads=("qg", "qt16"), writes=("qt16",))
                kb.dma("G", dstT[jt * P:(jt + 1) * P,
                               cc * cw:(cc + 1) * cw],
                       qt16[:, 0:cw], stream=f"tsc{mi}", reads=("qt16",))

            def q_stage(mi, c):
                if mi == 0:
                    kb.dma("Q", tstg[:, 0:HID],
                           t1sc[0:HID, c * P:(c + 1) * P],
                           stream="tstg", transpose=True,
                           reads=("tstg", "tsc0_f"), writes=("tstg",))
                    kb.dma("G",
                           t1Ts.rearrange("p (s c m) -> p s c m",
                                          c=HC, m=SEG1)[:, :, c, :],
                           tstg[:, 0:HID].rearrange("p (s m) -> p s m",
                                                    m=SEG1),
                           stream="t1Ts", reads=("tstg",))
                else:
                    kb.dma("Q", tstg[:, 0:H],
                           t2sc[0:H, c * P:(c + 1) * P],
                           stream="tstg", transpose=True,
                           reads=("tstg", "tsc1_f"), writes=("tstg",))
                    for pidx, (a2, b2_) in enumerate(_bank_pieces(0, H)):
                        kb.mm(pmo[0:33, 0:b2_ - a2],
                              b2lz[:, c:c + 33], tstg[:, a2:b2_],
                              start=True, stop=True, psum="pmo",
                              reads=("b2lz", "tstg"))
                        kb.op("V", "tensor_tensor",
                              out=c2preB[0:33, a2:b2_],
                              in0=pmo[0:33, 0:b2_ - a2],
                              in1=c2preB[0:33, a2:b2_], op=ALU.add,
                              reads=("pmo", "c2preB"),
                              writes=("c2preB",))
                    kb.op("V", "tensor_scalar", tstg[:, 0:H],
                          tstg[:, 0:H], g2c[:, c:c + 1], None, ALU.mult,
                          reads=("tstg", "g2c"), writes=("tstg",))
                    kb.dma("G",
                           t2Ts.rearrange("p (s c m) -> p s c m",
                                          c=HIDC, m=SEG2)[:, :, c, :],
                           tstg[:, 0:H].rearrange("p (s m) -> p s m",
                                                  m=SEG2),
                           stream="t2Ts", reads=("tstg",))

            def c2pre_fin():
                for a2, b2_ in _bank_pieces(0, H):
                    kb.op("V", "tensor_copy", out=rrow[0:1, 0:b2_ - a2],
                          in_=c2preB[32:33, a2:b2_],
                          reads=("c2preB",), writes=("rrow",))
                    kb.op("V", "tensor_tensor", out=c2pre[0:1, a2:b2_],
                          in0=c2preB[0:1, a2:b2_],
                          in1=rrow[0:1, 0:b2_ - a2], op=ALU.add,
                          reads=("c2preB", "rrow"),
                          writes=("c2pre",))

            nsl_ = [0]

            def mkld(src, jt, cc, cw):
                slot = nsl_[0] % 3
                nsl_[0] += 1
                return (lambda: q_ld(src, jt, cc, cw, slot)), slot

            for mi, src, ntile, width, ncc, dstT in mats:
                for jt in range(ntile):
                    for cc in range(ncc):
                        ld, slot = mkld(src, jt, cc, width // ncc)
                        qsteps.append(
                            (ld, lambda mi=mi, cw=width // ncc,
                             slot=slot: q_reduce(mi, cw, slot)))
            qsteps.append((None, lambda: q_fin()))
            for mi, src, ntile, width, ncc, dstT in mats:
                for jt in range(ntile):
                    for cc in range(ncc):
                        ld, slot = mkld(src, jt, cc, width // ncc)
                        qsteps.append(
                            (ld, lambda mi=mi, jt=jt, cc=cc,
                             cw=width // ncc, dstT=dstT, slot=slot:
                             q_tern(mi, jt, cc, cw, dstT, slot)))
            qsteps.append((None, lambda: kb.mark("tsc0_f", "tsc0")))
            qsteps.append((None, lambda: kb.mark("tsc1_f", "tsc1")))
            for c in range(HC):
                qsteps.append((None, lambda c=c: q_stage(0, c)))
            for c in range(HIDC):
                qsteps.append((None, lambda c=c: q_stage(1, c)))
            qsteps.append((None, lambda: c2pre_fin()))
            qsteps.append((None, lambda: kb.mark("t1Ts_f", "t1Ts")))
            qsteps.append((None, lambda: kb.mark("t2Ts_f", "t2Ts")))
            qi = [0]
            li = [0]

            def run_q(n):
                while n > 0 and qi[0] < len(qsteps):
                    while li[0] < len(qsteps) and li[0] <= qi[0] + 2:
                        ld = qsteps[li[0]][0]
                        if ld is not None:
                            ld()
                        li[0] += 1
                    qsteps[qi[0]][1]()
                    qi[0] += 1
                    n -= 1

            def emit_avs(pend, h, Wq_, ktmax):
                pairs, tslot = pend
                for (kt, lt0, half) in pairs:
                    col0 = lt0 * P
                    kb.mm(pyt[h % 2][0:65, col0:Wq_],
                          vbuf[h % 2][:, kt * 65:(kt + 1) * 65],
                          ptp[tslot][:, half * W_B + col0:half * W_B + Wq_],
                          start=(kt == 0), stop=(kt == ktmax),
                          psum=f"pyt{h % 2}",
                          reads=(f"vbuf{h % 2}", f"ptp{tslot}"))

            chunks = []
            t0 = 0
            while t0 < own_nt:
                w = min(BCH, own_nt - t0)
                chunks.append((t0, w))
                t0 += w

            pend_norm = [None]
            nq_total = None
            for (ct0, Wt) in chunks:
                glob0 = own_t0 + ct0
                ktmax = glob0 + Wt - 1
                nkt = ktmax + 1
                Wq_ = Wt * P
                kb.dma("Q", vbuf[0][:, 0:nkt * 65],
                       vsc[0:P, 0:nkt * 65], stream="vbuf0",
                       reads=tuple(["vbuf0"] +
                                   [f"vsc_f{j}" for j in range(nkt)]),
                       writes=("vbuf0",))
                for h in range(NH):
                    if h + 1 < NH:
                        kb.dma("Q", vbuf[(h + 1) % 2][:, 0:nkt * 65],
                               vsc[(h + 1) * P:(h + 2) * P, 0:nkt * 65],
                               stream=f"vbuf{(h + 1) % 2}",
                               reads=tuple([f"vbuf{(h + 1) % 2}"] +
                                           [f"vsc_f{j}" for j in range(nkt)]),
                               writes=(f"vbuf{(h + 1) % 2}",))
                    c = h // 2
                    b0 = 64 * (h % 2)
                    kkeys = (f"krt_{c}", f"qrt_{c}")
                    pend = None
                    fl = 0
                    kt = 0
                    while kt < nkt:
                        lt0 = max(0, kt - glob0)
                        pairable = (kt + 1 < nkt and kt + 1 <= glob0
                                    and Wq_ == W_B)
                        group = [(kt, lt0, 0)]
                        if pairable:
                            group.append((kt + 1, 0, 1))
                        tslot = fl % 2
                        pk = f"psc{tslot}"
                        for (k2, l2, half) in group:
                            col0 = l2 * P
                            kb.mm(psc[tslot][0:P,
                                             half * W_B + col0:
                                             half * W_B + Wq_],
                                  krt[c][b0:b0 + 33, k2 * P:(k2 + 1) * P],
                                  qrt[c][b0:b0 + 33,
                                         ct0 * P + col0:ct0 * P + Wq_],
                                  start=True, stop=True, psum=pk,
                                  reads=kkeys)
                        lo = group[0][1] * P
                        hi = (len(group) - 1) * W_B + Wq_
                        kb.op("S", "activation",
                              ptp[tslot][:, lo:hi], psc[tslot][0:P, lo:hi],
                              AF.Exp, scale=SCALE,
                              reads=(pk,), writes=(f"ptp{tslot}",))
                        for (k2, l2, half) in group:
                            if k2 >= glob0:
                                col0 = l2 * P
                                kb.op("V", "tensor_tensor",
                                      out=ptp[tslot][:,
                                                     half * W_B + col0:
                                                     half * W_B + col0 + P],
                                      in0=ptp[tslot][:,
                                                     half * W_B + col0:
                                                     half * W_B + col0 + P],
                                      in1=tri01[:, :], op=ALU.mult,
                                      reads=(f"ptp{tslot}", "tri01"),
                                      writes=(f"ptp{tslot}",))
                        if pend is not None:
                            emit_avs(pend, h, Wq_, ktmax)
                        if fl == 0 and pend_norm[0] is not None:
                            pend_norm[0]()
                            pend_norm[0] = None
                        pend = (group, tslot)
                        fl += 1
                        kt += len(group)
                    if pend is not None:
                        emit_avs(pend, h, Wq_, ktmax)

                    def mknorm(h=h, c=c, b0=b0, Wq_=Wq_):
                        def donorm():
                            kb.op("S", "activation", rrow[0:1, 0:Wq_],
                                  pyt[h % 2][64:65, 0:Wq_], AF.Ln,
                                  reads=(f"pyt{h % 2}",), writes=("rrow",))
                            kb.op("S", "activation", rrow[0:1, 0:Wq_],
                                  rrow[0:1, 0:Wq_], AF.Exp, scale=-1.0,
                                  reads=("rrow",), writes=("rrow",))
                            kb.mm(pmo[0:64, 0:Wq_], onesf[0:1, 0:64],
                                  rrow[0:1, 0:Wq_], start=True, stop=True,
                                  psum="pmo", reads=("rrow", "onesf"))
                            kb.op("V", "tensor_copy", out=rsb[0:64, 0:Wq_],
                                  in_=pmo[0:64, 0:Wq_], reads=("pmo",),
                                  writes=("rsb",))
                            kb.op("V", "tensor_tensor",
                                  out=yts[c][b0:b0 + 64, 0:Wq_],
                                  in0=pyt[h % 2][0:64, 0:Wq_],
                                  in1=rsb[0:64, 0:Wq_],
                                  op=ALU.mult,
                                  reads=(f"pyt{h % 2}", "rsb"),
                                  writes=(f"yts{c}",))
                        return donorm
                    pend_norm[0] = mknorm()
                    run_q(2)
                if pend_norm[0] is not None:
                    pend_norm[0]()
                    pend_norm[0] = None
                run_q(2)
                for lt in range(Wt):
                    jo = ct0 + lt
                    for a, b in _bank_pieces(0, H):
                        for hc in range(HC):
                            kb.mm(pmo[0:P, 0:b - a],
                                  yts[hc][0:P, lt * P:(lt + 1) * P],
                                  wT["o"][hc][:, a:b],
                                  start=(hc == 0), stop=(hc == HC - 1),
                                  psum="pmo",
                                  reads=(f"yts{hc}", f"woT{hc}"))
                        kb.op("V", "tensor_tensor", out=atp[:, a:b],
                              in0=pmo[0:P, 0:b - a], in1=boB[:, a:b],
                              op=ALU.add,
                              reads=("pmo", "boB"), writes=("atp",))
                    for cb in range(H // BNW):
                        kb.op("V", "bn_stats", bnsB[:, cb * 6:(cb + 1) * 6],
                              atp[:, cb * BNW:(cb + 1) * BNW],
                              reads=("atp",), writes=("bnsB",))
                    kb.op("V", "bn_aggr", mvB[:, :], bnsB[:, :],
                          reads=("bnsB",), writes=("mvB",))
                    rsqrt_sc(kb, srB, "srB", mvB[:, 1:2], ("mvB",))
                    kb.dma("Q", xre[:, :],
                           xs[(own_t0 + jo) * P:(own_t0 + jo + 1) * P, :],
                           stream="xre", reads=("xre",), writes=("xre",))
                    kb.op("V", "tensor_scalar", atp[:, :], atp[:, :],
                          mvB[:, 0:1], srB[:, 0:1], ALU.subtract, ALU.mult,
                          reads=("atp", "mvB", "srB"), writes=("atp",))
                    kb.op("V", "tensor_tensor", out=x2t[:, :], in0=atp[:, :],
                          in1=xre[:, :], op=ALU.add,
                          reads=("atp", "xre"), writes=("x2t",))
                    kb.dma("G", x2s[jo * P:(jo + 1) * P, :], x2t[:, :],
                           stream=f"x2s{jo % 2}", reads=("x2t",),
                           writes=(f"x2s_f{jo}",))
                    for cb in range(H // BNW):
                        kb.op("V", "bn_stats", bnsB[:, cb * 6:(cb + 1) * 6],
                              x2t[:, cb * BNW:(cb + 1) * BNW],
                              reads=("x2t",), writes=("bnsB",))
                    kb.op("V", "bn_aggr", mvC[:, 0:2], bnsB[:, :],
                          reads=("bnsB",), writes=("mvC",))
                    kb.op("S", "activation", mvC[:, 2:3], mvC[:, 1:2], AF.Ln,
                          bias=LN_EPS, reads=("mvC",), writes=("mvC_l",))
                    kb.op("S", "activation", mvC[:, 3:4], mvC[:, 2:3],
                          AF.Exp, scale=-0.5, reads=("mvC_l",),
                          writes=("mvC_r1",))
                    kb.op("V", "tensor_tensor", out=mvC[:, 4:5],
                          in0=mvC[:, 3:4], in1=mvC[:, 3:4], op=ALU.mult,
                          reads=("mvC_r1",), writes=("mvC_v2",))
                    kb.op("V", "tensor_tensor", out=mvC[:, 4:5],
                          in0=mvC[:, 4:5], in1=mvC[:, 1:2], op=ALU.mult,
                          reads=("mvC_v2", "mvC"), writes=("mvC_v2",))
                    kb.op("S", "activation", mvC[:, 5:6], mvC[:, 4:5], AF.Ln,
                          bias=LN_EPS, reads=("mvC_v2",), writes=("mvC_l2",))
                    kb.op("S", "activation", mvC[:, 5:6], mvC[:, 5:6],
                          AF.Exp, scale=-0.5, reads=("mvC_l2",),
                          writes=("mvC_r2",))
                    kb.op("V", "tensor_copy", out=mu1[:, jo:jo + 1],
                          in_=mvC[:, 0:1], reads=("mvC",), writes=("mu1",))
                    kb.op("V", "tensor_tensor", out=rr1[:, jo:jo + 1],
                          in0=mvC[:, 3:4], in1=mvC[:, 5:6], op=ALU.mult,
                          reads=("mvC_r1", "mvC_r2"), writes=("rr1",))
                    run_q(3)
            run_q(10 ** 9)
            kb.fence_dma("V")
            kb.fence_dma("S")

        if cfg.get("STOP", "") in ("0", "A", "A2"):
            def phase_b(kb, role):
                kb.fence_dma("G")
        with nc.If(pid < cfg["ROLE_SPLIT"]):
            phase_b(kbA, 0)
        with nc.Else():
            phase_b(kbB, 1)
        nc.all_engine_barrier()
        kbA.barrier_reset()
        kbB.barrier_reset()
        stB.close()
        stAB.close()

        stC = ExitStack()
        EC = stC.enter_context
        x2r = [EC(nc.sbuf_tensor(f"x2r{i}", [P, H], F32)) for i in range(2)]
        x2e = [EC(nc.sbuf_tensor(f"x2e{i}", [P, H], F32)) for i in range(2)]
        xfT = [[EC(nc.sbuf_tensor(f"xfT{tb}_{hc}", [P, FCH * P], BF16))
                for hc in range(HC)] for tb in range(2)]
        w1tb = [EC(nc.sbuf_tensor(f"w1tb{i}", [P, HC * 512], BF16))
                for i in range(2)]
        w2tb = EC(nc.sbuf_tensor("w2tb", [P, HIDC * 512], BF16))
        hTt = EC(nc.sbuf_tensor("hTt", [P, HIDC * FCH * P], BF16))
        sqr = [EC(nc.sbuf_tensor(f"sqr{i}", [P, FCH * P], BF16))
               for i in range(3)]
        varrow = EC(nc.sbuf_tensor("varrow", [1, FCH * P], F32))
        mu2row = EC(nc.sbuf_tensor("mu2row", [1, FCH * P], F32))
        c1row = EC(nc.sbuf_tensor("c1row", [33, HID], BF16))
        c2b = EC(nc.sbuf_tensor("c2b", [P, H], F32))
        c2g = EC(nc.sbuf_tensor("c2g", [1, H], F32))
        crow = EC(nc.sbuf_tensor("crow", [1, 512], F32))
        bhrow = EC(nc.sbuf_tensor("bhrow", [1, 512], F32))
        bnsC = EC(nc.sbuf_tensor("bnsC", [P, FCH * 6 * (HID // BNW2)],
                                  F32))
        mvD = EC(nc.sbuf_tensor("mvD", [P, 2], F32))
        srD = EC(nc.sbuf_tensor("srD", [P, 2], F32))
        rsD = EC(nc.sbuf_tensor("rsD", [P, FCH], F32))
        murow = EC(nc.sbuf_tensor("murow", [1, FCH * P], F32))
        xfC = EC(nc.sbuf_tensor("xfC", [P, H], BF16))
        ffo = [EC(nc.sbuf_tensor(f"ffo{i}", [P, H], F32)) for i in range(FCH)]
        oto = EC(nc.sbuf_tensor("oto", [P, H], F32))
        pf = [EC(nc.psum_tensor(f"pf{i}", [P, 512], F32)) for i in range(3)]
        pst = [EC(nc.psum_tensor(f"pst{i}", [P, 512], F32))
               for i in range(2)]
        pc2 = [EC(nc.psum_tensor(f"pc2{i}", [P, 512], F32)) for i in range(2)]
        pmu = EC(nc.psum_tensor("pmu", [P, 512], F32))

        def phase_c(kb, role):
            own_t0 = 0 if role == 0 else QA_T
            own_nt = QA_T if role == 0 else NT - QA_T
            kb.op("V", "memset", c1row[:, :], 0.0, writes=("c1row",))
            first = True

            chunks = []
            t0 = 0
            while t0 < own_nt:
                w = min(FCH, own_nt - t0)
                chunks.append((t0, w))
                t0 += w
            def stage_chunk(ct0, Wt, tb):
                for i in range(Wt):
                    jo = ct0 + i
                    jb = i % 2
                    kb.dma("Q", x2r[jb][:, :], x2s[jo * P:(jo + 1) * P, :],
                           stream=f"x2r{jb}",
                           reads=(f"x2r{jb}", f"x2s_f{jo}"),
                           writes=(f"x2r{jb}",))
                    kb.op("V", "tensor_scalar", xfC[:, 0:H],
                          x2r[jb][:, :],
                          mu1[:, jo:jo + 1], rr1[:, jo:jo + 1],
                          ALU.subtract, ALU.mult,
                          reads=(f"x2r{jb}", "mu1", "rr1"),
                          writes=("xfC",))
                    kb.dma("G", xfsc[jo * P:(jo + 1) * P, :], xfC[:, 0:H],
                           stream="xfsc", reads=("xfC",),
                           writes=(f"xfsc_f{jo}",))
                for hc in range(HC):
                    kb.dma("Q",
                           xfT[tb][hc][:, 0:Wt * P],
                           xfsc[ct0 * P:(ct0 + Wt) * P, hc * P:(hc + 1) * P],
                           stream=f"xfT{hc % 4}", transpose=True,
                           reads=tuple([f"xfTg{tb}_{hc}"] +
                                       [f"xfsc_f{ct0 + i}"
                                        for i in range(Wt)]),
                           writes=(f"xfTg{tb}_{hc}",))
                for hc in range(HC):
                    kb.op("V", "tensor_scalar", xfT[tb][hc][:, 0:Wt * P],
                          xfT[tb][hc][:, 0:Wt * P], g1c[:, hc:hc + 1],
                          None, ALU.mult,
                          reads=(f"xfTg{tb}_{hc}", "g1c"),
                          writes=(f"xfTg{tb}_{hc}",))

            stage_chunk(chunks[0][0], chunks[0][1], 0)
            for ci, (ct0, Wt) in enumerate(chunks):
                tb = ci % 2
                if ci + 1 < len(chunks):
                    stage_chunk(chunks[ci + 1][0], chunks[ci + 1][1],
                                (ci + 1) % 2)
                hpieces = []
                for hs in range(NSL):
                    for a, b in _bank_pieces(0, SL):
                        hpieces.append((hs, a, b))
                pend_stats = []
                for pi_, (hs, a, b) in enumerate(hpieces):
                    wk = f"w1tb{pi_ % 2}"
                    wb = w1tb[pi_ % 2]
                    ga = hs * SL + a
                    slab = (ga // SEG1) * (HC * SEG1)
                    kb.dma("Q", wb[:, 0:HC * (b - a)],
                           t1Ts[:, slab:slab + HC * (b - a)],
                           stream=wk, reads=(wk, "t1Ts_f"), writes=(wk,))
                    if first:
                        for hc in range(HC):
                            kb.mm(pc2[0][0:33, 0:b - a],
                                  b1lz[:, hc:hc + 33],
                                  wb[:, hc * (b - a):(hc + 1) * (b - a)],
                                  start=(hc == 0), stop=(hc == HC - 1),
                                  psum="pc20", reads=("b1lz", wk))
                        kb.dma("Q", bhrow[0:1, 0:b - a],
                               b1[None, ga:ga + b - a], stream="bhrow",
                               reads=("bhrow",), writes=("bhrow",))
                        kb.op("V", "tensor_scalar", crow[0:1, 0:b - a],
                              bhrow[0:1, 0:b - a], qtmp[0:1, 1:2], None,
                              ALU.mult, reads=("bhrow", "qtmp_s0"),
                              writes=("crow",))
                        kb.op("V", "tensor_tensor",
                              out=crow[0:1, 0:b - a],
                              in0=pc2[0][0:1, 0:b - a],
                              in1=crow[0:1, 0:b - a], op=ALU.add,
                              reads=("pc20", "crow"), writes=("crow",))
                        kb.op("V", "tensor_tensor",
                              out=crow[0:1, 0:b - a],
                              in0=pc2[0][32:33, 0:b - a],
                              in1=crow[0:1, 0:b - a], op=ALU.add,
                              reads=("pc20", "crow"), writes=("crow",))
                        kb.op("V", "tensor_copy",
                              out=c1row[0:1, ga:ga + b - a],
                              in_=crow[0:1, 0:b - a], reads=("crow",),
                              writes=("c1row",))
                        kb.op("V", "tensor_tensor",
                              out=c1row[32:33, ga:ga + b - a],
                              in0=crow[0:1, 0:b - a],
                              in1=c1row[0:1, ga:ga + b - a],
                              op=ALU.subtract, reads=("crow", "c1row"),
                              writes=("c1row",))
                    for sub in range((b - a) // P):
                        hcg = ga // P + sub
                        bank = (pi_ * ((b - a) // P) + sub) % 3
                        pk = f"pf{bank}"
                        pp = pf[bank]
                        for hc in range(HC):
                            kb.mm(pp[0:P, 0:Wt * P],
                                  wb[:, hc * (b - a) + sub * P:
                                     hc * (b - a) + (sub + 1) * P],
                                  xfT[tb][hc][:, 0:Wt * P],
                                  start=(hc == 0), stop=False,
                                  psum=pk, reads=(f"xfTg{tb}_{hc}", wk))
                        kb.mm(pp[0:P, 0:Wt * P],
                              c1row[0:33, hcg * P:(hcg + 1) * P],
                              onesK[0:33, 0:Wt * P],
                              start=False, stop=True, psum=pk,
                              reads=("onesK", "c1row"))
                        kb.op("S", "activation",
                              hTt[:, hcg * FCH * P:hcg * FCH * P + Wt * P],
                              pp[0:P, 0:Wt * P],
                              AF.Gelu, scale=rc128[:, 0:1],
                              reads=(pk, "rc128"),
                              writes=(f"hTt{hcg}",))
                        kb.op("V", "tensor_tensor",
                              out=sqr[hcg % 3][:, 0:Wt * P],
                              in0=hTt[:, hcg * FCH * P:
                                      hcg * FCH * P + Wt * P],
                              in1=hTt[:, hcg * FCH * P:
                                      hcg * FCH * P + Wt * P],
                              op=ALU.mult,
                              reads=(f"hTt{hcg}",),
                              writes=(f"sqr{hcg % 3}",))

                        def mkstats(hcg=hcg, Wt=Wt):
                            def em():
                                kb.mm(pst[0][0:33, 0:Wt * P],
                                      onesZ[:, 0:33],
                                      hTt[:, hcg * FCH * P:
                                          hcg * FCH * P + Wt * P],
                                      start=(hcg == 0),
                                      stop=(hcg == HIDC - 1),
                                      psum="pst0",
                                      reads=("onesZ", f"hTt{hcg}"))
                                kb.mm(pst[1][0:33, 0:Wt * P],
                                      onesZ[:, 0:33],
                                      sqr[hcg % 3][:, 0:Wt * P],
                                      start=(hcg == 0),
                                      stop=(hcg == HIDC - 1),
                                      psum="pst1",
                                      reads=("onesZ", f"sqr{hcg % 3}"))
                            return em
                        pend_stats.append(mkstats())
                        if len(pend_stats) > 2:
                            pend_stats.pop(0)()
                for f_ in pend_stats:
                    f_()
                pend_stats = []
                kb.op("V", "tensor_scalar", murow[0:1, 0:Wt * P],
                      pst[0][0:1, 0:Wt * P], -1.0 / HID, None, ALU.mult,
                      reads=("pst0",), writes=("murow",))
                kb.op("V", "tensor_scalar", varrow[0:1, 0:Wt * P],
                      pst[1][0:1, 0:Wt * P], 1.0 / HID, None, ALU.mult,
                      reads=("pst1",), writes=("varrow",))
                kb.op("V", "tensor_tensor", out=mu2row[0:1, 0:Wt * P],
                      in0=murow[0:1, 0:Wt * P], in1=murow[0:1, 0:Wt * P],
                      op=ALU.mult, reads=("murow",), writes=("mu2row",))
                kb.op("V", "tensor_tensor", out=varrow[0:1, 0:Wt * P],
                      in0=varrow[0:1, 0:Wt * P], in1=mu2row[0:1, 0:Wt * P],
                      op=ALU.subtract, reads=("varrow", "mu2row"),
                      writes=("varrow",))
                kb.op("S", "activation", varrow[0:1, 0:Wt * P],
                      varrow[0:1, 0:Wt * P], AF.Ln, bias=LN_EPS,
                      reads=("varrow",), writes=("varrow",))
                kb.op("S", "activation", varrow[0:1, 0:Wt * P],
                      varrow[0:1, 0:Wt * P], AF.Exp, scale=-0.5,
                      reads=("varrow",), writes=("varrow",))
                kb.op("V", "tensor_scalar", varrow[0:1, 0:Wt * P],
                      varrow[0:1, 0:Wt * P], qtmp[0:1, 6:7], None,
                      ALU.mult, reads=("varrow", "qtmp_r1"),
                      writes=("varrow",))
                for i in range(Wt):
                    kb.op("T", "transpose", pmu[0:P, 0:1],
                          varrow[0:1, i * P:(i + 1) * P],
                          identf[0:1, 0:1],
                          reads=("varrow", "identf"), writes=("pmu",))
                    kb.op("V", "tensor_copy", out=rsD[:, i:i + 1],
                          in_=pmu[0:P, 0:1], reads=("pmu",),
                          writes=("rsD",))
                for a, b in _bank_pieces(0, H):
                    slab2 = (a // SEG2) * (HIDC * SEG2)
                    kb.dma("Q", w2tb[:, 0:HIDC * (b - a)],
                           t2Ts[:, slab2:slab2 + HIDC * (b - a)],
                           stream="w2tb", reads=("w2tb", "t2Ts_f"),
                           writes=("w2tb",))
                    if first:
                        kb.dma("Q", bhrow[0:1, 0:b - a], b2[None, a:b],
                               stream="bhrow", reads=("bhrow",),
                               writes=("bhrow",))
                        kb.op("V", "tensor_scalar", crow[0:1, 0:b - a],
                              bhrow[0:1, 0:b - a], qtmp[0:1, 5:6], None,
                              ALU.mult, reads=("bhrow", "qtmp_s1"),
                              writes=("crow",))
                        kb.op("V", "tensor_tensor", out=crow[0:1, 0:b - a],
                              in0=c2pre[0:1, a:b],
                              in1=crow[0:1, 0:b - a],
                              op=ALU.add, reads=("c2pre", "crow"),
                              writes=("crow",))
                        kb.mm(pc2[0][0:P, 0:b - a], onesf[0:1, 0:P],
                              crow[0:1, 0:b - a], start=True, stop=True,
                              psum="pc20", reads=("crow", "onesf"))
                        kb.op("V", "tensor_scalar", c2b[:, a:b],
                              pc2[0][0:P, 0:b - a], rc128[:, 1:2], None,
                              ALU.mult, reads=("pc20", "rc128"),
                              writes=("c2b",))
                        for hc in range(HIDC):
                            kb.mm(pmu[0:33, 0:b - a], onesZ[:, 0:33],
                                  w2tb[:, hc * (b - a):(hc + 1) * (b - a)],
                                  start=(hc == 0), stop=(hc == HIDC - 1),
                                  psum="pmu", reads=("onesZ", "w2tb"))
                        kb.op("V", "tensor_copy", out=c2g[0:1, a:b],
                              in_=pmu[0:1, 0:b - a],
                              reads=("pmu",), writes=("c2g",))
                    for i in range(Wt):
                        pk2 = f"pc2{i % 2}"
                        pq = pc2[i % 2]
                        for hc in range(HIDC):
                            kb.mm(pq[0:P, 0:b - a],
                                  hTt[:, hc * FCH * P + i * P:
                                      hc * FCH * P + (i + 1) * P],
                                  w2tb[:, hc * (b - a):(hc + 1) * (b - a)],
                                  start=(hc == 0), stop=False,
                                  psum=pk2, reads=(f"hTt{hc}", "w2tb"))
                        kb.mm(pq[0:P, 0:b - a],
                              murow[0:1, i * P:(i + 1) * P], c2g[0:1, a:b],
                              start=False, stop=True, psum=pk2,
                              reads=("murow", "c2g"))
                        kb.op("V", "tensor_scalar", pq[0:P, 0:b - a],
                              pq[0:P, 0:b - a], rsD[:, i:i + 1], None,
                              ALU.mult, reads=(pk2, "rsD"),
                              writes=(pk2,))
                        kb.op("V", "tensor_tensor", out=ffo[i][:, a:b],
                              in0=pq[0:P, 0:b - a], in1=c2b[:, a:b],
                              op=ALU.add, reads=(pk2, "c2b"),
                              writes=(f"ffo{i}",))
                for i in range(Wt):
                    jo = ct0 + i
                    kb.dma("Q", x2e[i % 2][:, :],
                           x2s[jo * P:(jo + 1) * P, :],
                           stream=f"x2e{i % 2}",
                           reads=(f"x2e{i % 2}", f"x2s_f{jo}"),
                           writes=(f"x2e{i % 2}",))
                    for cb in range(H // BNW):
                        kb.op("V", "bn_stats", bnsC[:, cb * 6:(cb + 1) * 6],
                              ffo[i][:, cb * BNW:(cb + 1) * BNW],
                              reads=(f"ffo{i}",), writes=("bnsC",))
                    kb.op("V", "bn_aggr", mvD[:, :],
                          bnsC[:, 0:6 * (H // BNW)],
                          reads=("bnsC",), writes=("mvD",))
                    rsqrt_sc(kb, srD, "srD", mvD[:, 1:2], ("mvD",))
                    kb.op("V", "tensor_scalar", oto[:, :], ffo[i][:, :],
                          mvD[:, 0:1], srD[:, 0:1], ALU.subtract, ALU.mult,
                          reads=(f"ffo{i}", "mvD", "srD"), writes=("oto",))
                    kb.op("V", "tensor_tensor", out=oto[:, :],
                          in0=oto[:, :], in1=x2e[i % 2][:, :], op=ALU.add,
                          reads=("oto", f"x2e{i % 2}"), writes=("oto",))
                    kb.dma("G", out[jo * P:(jo + 1) * P, :], oto[:, :],
                           stream="out", reads=("oto",))
                first = False
            kb.fence_dma("G")
            kb.fence_dma("Q")
            kb.fence_dma("S")

        if cfg.get("STOP", "") in ("0", "A", "A2", "B"):
            def phase_c(kb, role):
                kb.fence_dma("G")
        with nc.If(pid < cfg["ROLE_SPLIT"]):
            phase_c(kbA, 0)
        with nc.Else():
            phase_c(kbB, 1)
        stC.close()

    return nc



_CACHE = {}


def _get_program(key, cfg):
    if key not in _CACHE:
        _CACHE[key] = build_program(cfg)
    return _CACHE[key]


def _ntabs(T, R):
    t = np.arange(1, T + 1, dtype=np.float64)[:, None]
    i = np.arange(R, dtype=np.float64)
    theta = (1.0 / 10000.0) ** (i / R)
    ang = t * theta
    n1 = np.round(ang / (2 * np.pi)).astype(np.float32)
    n2 = np.round((ang + np.pi / 2) / (2 * np.pi)).astype(np.float32)
    return n1, n2


def make_in_maps(cfg, inputs):
    T = cfg["T"]
    n1, n2 = _ntabs(T, 32)
    shared = {
        "Wq": inputs["L0_attn_Wq"], "bq": inputs["L0_attn_bq"],
        "Wk": inputs["L0_attn_Wk"], "bk": inputs["L0_attn_bk"],
        "Wv": inputs["L0_attn_Wv"], "bv": inputs["L0_attn_bv"],
        "Wo": inputs["L0_attn_Wo"], "bo": inputs["L0_attn_bo"],
        "g1": inputs["L0_ffn_ln1_g"], "b1ln": inputs["L0_ffn_ln1_b"],
        "w1": inputs["L0_ffn_w1"], "b1": inputs["L0_ffn_b1"],
        "g2": inputs["L0_ffn_ln2_g"], "b2ln": inputs["L0_ffn_ln2_b"],
        "w2": inputs["L0_ffn_w2"], "b2": inputs["L0_ffn_b2"],
        "ntab1": n1, "ntab2": n2,
    }
    shared = {k: np.ascontiguousarray(np.asarray(v, dtype=np.float32))
              for k, v in shared.items()}
    x = np.ascontiguousarray(np.asarray(inputs["L0_attn_x"],
                                        dtype=np.float32))
    nseq = x.shape[0]
    in_maps = []
    for c in range(cfg["NCORES"]):
        seq = c % nseq
        m = dict(shared)
        m["xs"] = x[seq]
        in_maps.append(m)
    return in_maps


def assemble_out(cfg, results, xshape):
    T, QA_T = cfg["T"], cfg["QA_T"]
    QA = QA_T * P
    nseq = xshape[0]
    full = np.empty(xshape, dtype=np.float32)
    for c in range(cfg["NCORES"]):
        seq = c % nseq
        o = np.asarray(results[c]["out"])
        if c < cfg["ROLE_SPLIT"]:
            full[seq, 0:QA] = o[0:QA]
        else:
            full[seq, QA:T] = o[0:T - QA]
    return full


def kernel(**inputs):
    cfg = full_cfg()
    nc = _get_program("full", cfg)
    in_maps = make_in_maps(cfg, inputs)
    res = run_bass_kernel_spmd(nc, in_maps,
                               core_ids=list(range(cfg["NCORES"])))
    x = np.asarray(inputs["L0_attn_x"])
    return assemble_out(cfg, res.results, x.shape)


# version 33 (final)
# speedup vs baseline: 1.4880x; 1.0371x over previous
import bisect
import math
from contextlib import ExitStack

import numpy as np

import concourse.bass as bass
import concourse.mybir as mybir
from concourse.bass_utils import run_bass_kernel_spmd

F32 = mybir.dt.float32
BF16 = mybir.dt.bfloat16
AF = mybir.ActivationFunctionType
ALU = mybir.AluOpType
AX = mybir.AxisListType

TWO_PI = 2.0 * math.pi
C1 = float(np.float32(TWO_PI))
C2 = float(np.float32(TWO_PI - C1))
C3 = TWO_PI - C1 - C2
NEG_BIG = -8.0e10
LN_EPS = 1e-5
P = 128


def full_cfg():
    return dict(T=4096, H=1024, NH=16, HID=4096, QA_T=11, BCH=4, FCH=3,
                NCORES=8, ROLE_SPLIT=4, MAXW=512)


def mini_cfg():
    return dict(T=512, H=256, NH=4, HID=512, QA_T=2, BCH=2, FCH=1,
                NCORES=2, ROLE_SPLIT=1)


def med_cfg():
    return dict(T=1024, H=256, NH=4, HID=2048, QA_T=3, BCH=2, FCH=2,
                NCORES=2, ROLE_SPLIT=1, MAXW=512)


def med2_cfg():
    return dict(T=1024, H=1024, NH=16, HID=2048, QA_T=5, BCH=4, FCH=2,
                NCORES=2, ROLE_SPLIT=1, MAXW=1024)



class _Eng:
    def __init__(self, name, eng, semname):
        self.name = name
        self.eng = eng
        self.semname = semname
        self.pos = 0
        self.inc_pos = []
        self.inc_cnt = []
        self.count = 0
        self.handles = {}
        self.waited = {}


class Kb:

    def __init__(self, nc, sems, streams):
        self.nc = nc
        self.sems = sems
        self.streams = streams
        self.E = {
            "T": _Eng("T", nc.tensor, "et"),
            "V": _Eng("V", nc.vector, "ev"),
            "S": _Eng("S", nc.scalar, "es"),
            "G": _Eng("G", nc.gpsimd, "eg"),
            "Q": _Eng("Q", nc.sync, "eq"),
        }
        self.dmacnt = {}
        self.res = {}

    def clone_counters_from(self, other):
        for n, e in self.E.items():
            oe = other.E[n]
            e.pos = oe.pos
            e.inc_pos = list(oe.inc_pos)
            e.inc_cnt = list(oe.inc_cnt)
            e.count = oe.count
            e.handles = dict(oe.handles)
            e.waited = dict(oe.waited)
        self.dmacnt = dict(other.dmacnt)
        self.res = {k: {"w": v["w"], "r": dict(v["r"])}
                    for k, v in other.res.items()}

    def _resolve(self, tok):
        if tok[0] == "D":
            st = tok[1]
            return (self.streams[st], self.dmacnt[st])
        _, en, pos = tok
        e = self.E[en]
        i = bisect.bisect_left(e.inc_pos, pos)
        if i < len(e.inc_pos):
            return (e.semname, e.inc_cnt[i])
        ins = e.handles.get(pos)
        assert ins is not None, f"no handle for {en}@{pos}"
        e.count += 1
        ins.then_inc(self.sems[e.semname], 1)
        e.inc_pos.append(pos)
        e.inc_cnt.append(e.count)
        return (e.semname, e.count)

    def _wait_deps(self, e, reads, writes):
        toks = []
        for r in reads:
            ent = self.res.get(r)
            if ent is not None and ent["w"] is not None:
                toks.append(ent["w"])
        for w in writes:
            ent = self.res.get(w)
            if ent is not None:
                if ent["w"] is not None:
                    toks.append(ent["w"])
                toks.extend(ent["r"].values())
        need = {}
        for t in toks:
            s, v = self._resolve(t)
            if need.get(s, 0) < v:
                need[s] = v
        for s, v in need.items():
            if e.waited.get(s, 0) < v:
                e.eng.wait_ge(self.sems[s], v)
                e.waited[s] = v

    def _record(self, tok, reads, writes):
        for w in writes:
            self.res[w] = {"w": tok, "r": {}}
        rk = tok[1]
        for r in reads:
            ent = self.res.setdefault(r, {"w": None, "r": {}})
            ent["r"][rk] = tok

    def op(self, en, method, *args, reads=(), writes=(), **kw):
        e = self.E[en]
        self._wait_deps(e, reads, writes)
        ins = getattr(e.eng, method)(*args, **kw)
        e.pos += 1
        e.handles[e.pos] = ins
        self._record(("E", en, e.pos), reads, writes)
        return ins

    def mm(self, out, lhsT, rhs, *, start, stop, psum, reads=()):
        e = self.E["T"]
        self._wait_deps(e, reads, (psum,) if start else ())
        ins = e.eng.matmul(out, lhsT, rhs, start=start, stop=stop)
        e.pos += 1
        e.handles[e.pos] = ins
        tok = ("E", "T", e.pos)
        for r in reads:
            ent = self.res.setdefault(r, {"w": None, "r": {}})
            ent["r"]["T"] = tok
        if start:
            self.res[psum] = {"w": tok, "r": {}}
        else:
            self.res[psum]["w"] = tok
        return ins

    def dma(self, qn, out, in_, *, stream, reads=(), writes=(),
            transpose=False, slow=False):
        e = self.E[qn]
        if stream not in self.streams:
            self.streams[stream] = f"d{len(self.streams)}"
        semn = self.streams[stream]
        self._wait_deps(e, reads, writes)
        if transpose:
            ins = e.eng.dma_start(out=out, in_=in_, transpose=True)
        else:
            ins = e.eng.dma_start(out=out, in_=in_,
                                  allow_slow_non_contiguous=slow)
        self.dmacnt[stream] = self.dmacnt.get(stream, 0) + 16
        ins.then_inc(self.sems[semn], 16)
        self._record(("D", stream, self.dmacnt[stream]), reads, writes)

    def mark(self, key, stream):
        self.res[key] = {"w": ("D", stream, self.dmacnt.get(stream, 0)),
                         "r": {}}

    def barrier_reset(self):
        for ent in self.res.values():
            if ent["w"] is not None and ent["w"][0] == "E":
                ent["w"] = None
            ent["r"] = {k: v for k, v in ent["r"].items() if v[0] == "D"}

    def fence_dma(self, en="G"):
        e = self.E[en]
        for st, v in self.dmacnt.items():
            semn = self.streams[st]
            if v and e.waited.get(semn, 0) < v:
                e.eng.wait_ge(self.sems[semn], v)
                e.waited[semn] = v


def _bank_pieces(a, b):
    out = []
    while a < b:
        nxt = min(b, (a // 512 + 1) * 512)
        out.append((a, nxt))
        a = nxt
    return out



def build_program(cfg):
    TD = "S" if cfg.get("TDUAL", False) else "Q"
    TD2 = "Q"
    T, H, NH, HID = cfg["T"], cfg["H"], cfg["NH"], cfg["HID"]
    QA_T, BCH, FCH = cfg["QA_T"], cfg["BCH"], cfg["FCH"]
    NT = T // P
    HC = H // P
    HIDC = HID // P
    HS = H // NH
    R = HS // 2
    assert HS == 64 and R == 32 and H == NH * 64
    QA = QA_T * P
    MAXOWN = max(QA_T, NT - QA_T)
    SL = min(1024, HID)
    NSL = HID // SL
    SEG1 = min(512, HID)
    SEG2 = min(512, H)
    BNW = min(512, H)
    BNW2 = min(512, HID)
    SCALE = 1.0 / math.sqrt(HS)
    NHP = NH // 2
    assert NHP == HC
    MAXW = cfg.get("MAXW", 2048)
    GRP = 4
    W_B = BCH * P
    assert W_B <= 512

    nc = bass.Bass("TRN2", target_bir_lowering=False)

    dt = lambda n, s, d, k: nc.dram_tensor(n, s, d, kind=k).ap()
    xs = dt("xs", [T, H], F32, "ExternalInput")
    WSRC = {w: dt("W" + w, [H, H], F32, "ExternalInput") for w in "qkvo"}
    BSRC = {w: dt("b" + w, [H], F32, "ExternalInput") for w in "qkvo"}
    g1 = dt("g1", [H], F32, "ExternalInput")
    b1ln = dt("b1ln", [H], F32, "ExternalInput")
    w1 = dt("w1", [HID, H], F32, "ExternalInput")
    b1 = dt("b1", [HID], F32, "ExternalInput")
    g2 = dt("g2", [HID], F32, "ExternalInput")
    b2ln = dt("b2ln", [HID], F32, "ExternalInput")
    w2 = dt("w2", [H, HID], F32, "ExternalInput")
    b2 = dt("b2", [H], F32, "ExternalInput")
    ntab1 = dt("ntab1", [T, R], F32, "ExternalInput")
    ntab2 = dt("ntab2", [T, R], F32, "ExternalInput")
    out = dt("out", [MAXOWN * P, H], F32, "ExternalOutput")

    xhsc = dt("xhsc", [T, H], BF16, "Internal")
    krsc = dt("krsc", [T, NH * 64], BF16, "Internal")
    qrsc = dt("qrsc", [MAXOWN * P, NH * 64], BF16, "Internal")
    vsc = dt("vsc", [NH * P, NT * 65], BF16, "Internal")
    x2s = dt("x2s", [MAXOWN * P, H], F32, "Internal")
    xfsc = dt("xfsc", [MAXOWN * P, H], BF16, "Internal")
    hsc = dt("hsc", [MAXOWN * P, HID], BF16, "Internal")
    t1sc = dt("t1sc", [HID, H], BF16, "Internal")
    t2sc = dt("t2sc", [H, HID], BF16, "Internal")
    t1Ts = dt("t1Ts", [P, H // P * HID], BF16, "Internal")
    t2Ts = dt("t2Ts", [P, HID // P * H], BF16, "Internal")
    mkdr = dt("mkdr", [16, T], BF16, "Internal")
    wqsc = dt("wqsc", [4 * H, H], BF16, "Internal")

    LN1E4_OVER_R = math.log(1e-4) / R
    WOFF = {"q": 0, "k": H, "v": 2 * H, "o": 3 * H}

    with ExitStack() as gst:
        GE = gst.enter_context
        sems = {n: GE(nc.semaphore(n))
                for n in ("et", "ev", "es", "eg", "eq")}
        for i in range(88):
            sems[f"d{i}"] = GE(nc.semaphore(f"d{i}"))
        streams = {}
        pid = nc.partition_id(list(mybir.ALL_ENGINES))

        cidx = GE(nc.sbuf_tensor("cidx", [P, P], F32))
        pidx = GE(nc.sbuf_tensor("pidx", [P, 1], F32))
        tri01 = GE(nc.sbuf_tensor("tri01", [P, P], BF16))
        identf = GE(nc.sbuf_tensor("identf", [P, P], F32))
        onesf = GE(nc.sbuf_tensor("onesf", [16, P], F32))
        onescol = GE(nc.sbuf_tensor("onescol", [P, 1], F32))
        sc128 = GE(nc.sbuf_tensor("sc128", [P, 2], F32))
        rc128 = GE(nc.sbuf_tensor("rc128", [P, 2], F32))
        qtmp = GE(nc.sbuf_tensor("qtmp", [1, 8], F32))
        qacc = GE(nc.sbuf_tensor("qacc", [P, 3], F32))
        g1c = GE(nc.sbuf_tensor("g1c", [P, HC], F32))
        g2c = GE(nc.sbuf_tensor("g2c", [P, HIDC], F32))
        b1lz = GE(nc.sbuf_tensor("b1lz", [P, 32 + HC], BF16))
        b2lz = GE(nc.sbuf_tensor("b2lz", [P, 32 + HIDC], BF16))
        onesK = GE(nc.sbuf_tensor("onesK", [33, max(FCH * P, P)], BF16))
        onesZ = GE(nc.sbuf_tensor("onesZ", [P, 33], BF16))
        c2pre = GE(nc.sbuf_tensor("c2pre", [1, H], F32))
        mu1 = GE(nc.sbuf_tensor("mu1", [P, MAXOWN], F32))
        rr1 = GE(nc.sbuf_tensor("rr1", [P, MAXOWN], F32))
        brows = {w: GE(nc.sbuf_tensor(f"br_{w}", [33, H], BF16))
                 for w in "qkv"}
        boB = GE(nc.sbuf_tensor("boB", [P, H], BF16))
        constt = GE(nc.sbuf_tensor("constt", [P, 2], F32))
        nc.const_aps.register(F32, LN_EPS, constt[:, 0:1])
        nc.const_aps.register(F32, 1e-30, constt[:, 1:2])

        kbA = Kb(nc, sems, streams)
        kbB = Kb(nc, sems, streams)

        def rsqrt_sc(kb, dst, dst_key, src_ap, src_keys):
            kb.op("S", "activation", dst[:, 1:2], src_ap, AF.Ln,
                  bias=LN_EPS, reads=src_keys, writes=(dst_key,))
            kb.op("S", "activation", dst[:, 0:1], dst[:, 1:2], AF.Exp,
                  scale=-0.5, reads=(dst_key,), writes=(dst_key,))

        stTab = ExitStack()
        TE = stTab.enter_context
        cosT = TE(nc.sbuf_tensor("cosT", [P, R * NT], BF16, side="right"))
        sinT = TE(nc.sbuf_tensor("sinT", [P, R * NT], BF16, side="right"))
        wTq = [TE(nc.sbuf_tensor(f"wqT{hc}", [P, H], BF16, side="right"))
               for hc in range(HC)]
        wTk = [TE(nc.sbuf_tensor(f"wkT{hc}", [P, H], BF16, side="right"))
               for hc in range(HC)]
        wTv = [TE(nc.sbuf_tensor(f"wvT{hc}", [P, H], BF16, side="right"))
               for hc in range(HC)]
        wT = {"q": wTq, "k": wTk, "v": wTv}

        stLN = ExitStack()
        EL = stLN.enter_context
        xt = [EL(nc.sbuf_tensor(f"A_xt{i}", [P, H], F32, side="right"))
              for i in range(2)]
        bns = EL(nc.sbuf_tensor("A_bns", [P, 6 * (H // BNW)], F32,
                                side="right"))
        mv = EL(nc.sbuf_tensor("A_mv", [P, 2], F32, side="right"))
        sr = EL(nc.sbuf_tensor("A_sr", [P, 2], F32, side="right"))
        xh = EL(nc.sbuf_tensor("A_xh", [P, H], BF16, side="right"))

        def emit_ln_tile(kb, j):
            jb = j % 2
            xkey = f"A_xt{jb}"
            kb.dma("Q", xt[jb][:, :], xs[j * P:(j + 1) * P, :],
                   stream=xkey, reads=(xkey,), writes=(xkey,))
            for c in range(H // BNW):
                kb.op("V", "bn_stats", bns[:, c * 6:(c + 1) * 6],
                      xt[jb][:, c * BNW:(c + 1) * BNW],
                      reads=(xkey,), writes=("A_bns",))
            kb.op("V", "bn_aggr", mv[:, :], bns[:, :],
                  reads=("A_bns",), writes=("A_mv",))
            rsqrt_sc(kb, sr, "A_sr", mv[:, 1:2], ("A_mv",))
            kb.op("V", "tensor_scalar", xh[:, :], xt[jb][:, :],
                  mv[:, 0:1], sr[:, 0:1], ALU.subtract, ALU.mult,
                  reads=(xkey, "A_mv", "A_sr"), writes=("A_xh",))
            kb.dma("G", xhsc[j * P:(j + 1) * P, :], xh[:, :],
                   stream="xhsc", reads=("A_xh",),
                   writes=(f"xhsc_f{j}",))


        with ExitStack() as st0:
            E0 = st0.enter_context
            theta = E0(nc.sbuf_tensor("theta", [1, R], F32))
            irow = E0(nc.sbuf_tensor("irow", [1, R], F32))
            thetab = E0(nc.sbuf_tensor("thetab", [P, R], F32))
            tcol = E0(nc.sbuf_tensor("tcol", [P, 1], F32))
            angt = E0(nc.sbuf_tensor("angt", [P, R], F32))
            angc = E0(nc.sbuf_tensor("angc", [P, R], F32))
            n1t = E0(nc.sbuf_tensor("n1t", [P, R], F32))
            n2t = E0(nc.sbuf_tensor("n2t", [P, R], F32))
            wstg = [E0(nc.sbuf_tensor(f"wstg{i}", [P, H], BF16))
                    for i in range(4)]
            brow = E0(nc.sbuf_tensor("brow", [1, H], F32))
            lrow = E0(nc.sbuf_tensor("lrow", [P, max(HC, HIDC)], F32))
            pp0 = E0(nc.psum_tensor("pp0", [P, 512], F32))

            kb = kbA
            kb.op("G", "iota", cidx[:, :], [[1, P]], channel_multiplier=0,
                  allow_small_or_imprecise_dtypes=True, writes=("cidx",))
            kb.op("G", "iota", pidx[:, :], [[1, 1]], channel_multiplier=1,
                  allow_small_or_imprecise_dtypes=True, writes=("pidx",))
            kb.op("G", "iota", irow[0:1, :], [[1, R]], channel_multiplier=0,
                  allow_small_or_imprecise_dtypes=True, writes=("irow",))
            kb.op("V", "tensor_scalar", identf[:, :], cidx[:, :], pidx[:, :],
                  None, ALU.is_equal, reads=("cidx", "pidx"),
                  writes=("identf",))
            kb.op("V", "tensor_scalar", tri01[:, :], cidx[:, :], pidx[:, :],
                  None, ALU.is_ge,
                  reads=("cidx", "pidx"), writes=("tri01",))
            kb.op("V", "memset", onesf[:, :], 1.0, writes=("onesf",))
            kb.op("V", "memset", onesK[:, :], 0.0, writes=("onesK",))
            kb.op("V", "memset", onesK[0:1, :], 1.0, reads=("onesK",),
                  writes=("onesK",))
            kb.op("V", "memset", onesK[32:33, :], 1.0, reads=("onesK",),
                  writes=("onesK",))
            kb.op("V", "memset", onesZ[:, :], 0.0, writes=("onesZ",))
            kb.op("V", "memset", onesZ[:, 0:1], 1.0, reads=("onesZ",),
                  writes=("onesZ",))
            kb.op("V", "memset", onescol[:, :], 1.0, writes=("onescol",))
            kb.op("V", "memset", qacc[:, :], 0.0, writes=("qaccs",))
            kb.op("V", "memset", constt[:, 0:1], LN_EPS, writes=("constt",))
            kb.op("V", "memset", constt[:, 1:2], 1e-30, reads=("constt",),
                  writes=("constt",))
            kb.op("S", "activation", theta[0:1, :], irow[0:1, :], AF.Exp,
                  scale=LN1E4_OVER_R, reads=("irow",), writes=("theta",))
            kb.mm(pp0[0:P, 0:R], onesf[0:1, 0:P], theta[0:1, :],
                  start=True, stop=True, psum="pp0", reads=("onesf", "theta"))
            kb.op("V", "tensor_copy", out=thetab[:, :], in_=pp0[0:P, 0:R],
                  reads=("pp0",), writes=("thetab",))

            for j in range(NT):
                cs = slice(j * R, (j + 1) * R)
                kb.dma("Q", n1t[:, :], ntab1[j * P:(j + 1) * P, :],
                       stream="n1t", reads=("n1t",), writes=("n1t",))
                kb.dma("Q", n2t[:, :], ntab2[j * P:(j + 1) * P, :],
                       stream="n2t", reads=("n2t",), writes=("n2t",))
                kb.op("V", "tensor_scalar", tcol[:, :], pidx[:, :],
                      float(j * P + 1), None, ALU.add,
                      reads=("pidx",), writes=("tcol",))
                kb.op("V", "tensor_scalar", angt[:, :], thetab[:, :],
                      tcol[:, :], None, ALU.mult,
                      reads=("thetab", "tcol"), writes=("angt",))
                kb.op("V", "tensor_scalar", angc[:, :], angt[:, :],
                      math.pi / 2.0, None, ALU.add,
                      reads=("angt",), writes=("angc",))
                for c in (C1, C2, C3):
                    kb.op("V", "scalar_tensor_tensor", angt[:, :], n1t[:, :],
                          -c, angt[:, :], ALU.mult, ALU.add,
                          reads=("n1t", "angt"), writes=("angt",))
                    kb.op("V", "scalar_tensor_tensor", angc[:, :], n2t[:, :],
                          -c, angc[:, :], ALU.mult, ALU.add,
                          reads=("n2t", "angc"), writes=("angc",))
                kb.op("S", "activation", sinT[:, cs], angt[:, :], AF.Sin,
                      reads=("angt",), writes=(f"sinT{j}",))
                kb.op("S", "activation", cosT[:, cs], angc[:, :], AF.Sin,
                      reads=("angc",), writes=(f"cosT{j}",))

            for w in "qkvo":
                for oc in range(HC):
                    sg = wstg[oc % 4]
                    sk = f"wstg{oc % 4}"
                    kb.dma("G", sg[:, :], WSRC[w][oc * P:(oc + 1) * P, :],
                           stream=sk, reads=(sk,), writes=(sk,))
                    kb.dma("G",
                           wqsc[WOFF[w] + oc * P:WOFF[w] + (oc + 1) * P, :],
                           sg[:, :], stream="wqsc", reads=(sk,))
                kb.mark(f"wqsc_{w}", "wqsc")
                if w in "qkv":
                    for hc in range(HC):
                        kb.dma("Q", wT[w][hc][:, :],
                               wqsc[WOFF[w]:WOFF[w] + H,
                                    hc * P:(hc + 1) * P],
                               stream=f"wT{hc % 4}", transpose=True,
                               reads=(f"w{w}T{hc}", f"wqsc_{w}"),
                               writes=(f"w{w}T{hc}",))
            kb.mark("wqsc_f", "wqsc")
            for w in "qkv":
                kb.op("V", "memset", brows[w][:, :], 0.0,
                      writes=(f"br_{w}",))
                kb.dma("Q", brow[0:1, :], BSRC[w][None, :],
                       stream="brow", reads=("brow",), writes=("brow",))
                kb.op("V", "tensor_copy", out=brows[w][0:1, :],
                      in_=brow[0:1, :], reads=("brow", f"br_{w}"),
                      writes=(f"br_{w}",))
                kb.op("V", "tensor_tensor", out=brows[w][32:33, :],
                      in0=brow[0:1, :], in1=brows[w][0:1, :],
                      op=ALU.subtract, reads=("brow", f"br_{w}"),
                      writes=(f"br_{w}",))
            for w, bt, bkey in (("o", boB, "boB"),):
                kb.dma("Q", brow[0:1, :], BSRC[w][None, :],
                       stream="brow2", reads=("brow",), writes=("brow",))
                for a, b in _bank_pieces(0, H):
                    kb.mm(pp0[0:P, 0:b - a], onesf[0:1, 0:P],
                          brow[0:1, a:b], start=True, stop=True,
                          psum="pp0", reads=("brow", "onesf"))
                    kb.op("V", "tensor_copy", out=bt[:, a:b],
                          in_=pp0[0:P, 0:b - a], reads=("pp0",),
                          writes=(bkey,))

            kb.dma("Q", g1c[:, :], g1.rearrange("(c p) -> p c", p=P),
                   stream="g1c", writes=("g1c",), slow=True)
            kb.dma("Q", g2c[:, :], g2.rearrange("(c p) -> p c", p=P),
                   stream="g2c", writes=("g2c",), slow=True)
            kb.dma("Q", lrow[:, 0:HC], b1ln.rearrange("(c p) -> p c", p=P),
                   stream="lrow", reads=("lrow",), writes=("lrow",),
                   slow=True)
            kb.op("V", "memset", b1lz[:, :], 0.0, writes=("b1lz",))
            kb.op("V", "tensor_copy", out=b1lz[:, 0:HC], in_=lrow[:, 0:HC],
                  reads=("lrow", "b1lz"), writes=("b1lz",))
            kb.op("V", "tensor_tensor", out=b1lz[:, 32:32 + HC],
                  in0=lrow[:, 0:HC],
                  in1=b1lz[:, 0:HC], op=ALU.subtract,
                  reads=("lrow", "b1lz"), writes=("b1lz",))
            kb.dma("Q", lrow[:, 0:HIDC], b2ln.rearrange("(c p) -> p c", p=P),
                   stream="lrow", reads=("lrow",), writes=("lrow",),
                   slow=True)
            kb.op("V", "memset", b2lz[:, :], 0.0, writes=("b2lz",))
            kb.op("V", "tensor_copy", out=b2lz[:, 0:HIDC],
                  in_=lrow[:, 0:HIDC],
                  reads=("lrow", "b2lz"), writes=("b2lz",))
            kb.op("V", "tensor_tensor", out=b2lz[:, 32:32 + HIDC],
                  in0=lrow[:, 0:HIDC],
                  in1=b2lz[:, 0:HIDC], op=ALU.subtract,
                  reads=("lrow", "b2lz"), writes=("b2lz",))
            for j in range(QA_T):
                emit_ln_tile(kb, j)
            kb.fence_dma("V")
            kb.fence_dma("S")
            kb.fence_dma("G")

        nc.all_engine_barrier()
        kbA.barrier_reset()
        kbB.clone_counters_from(kbA)

        stAB = ExitStack()
        AE = stAB.enter_context
        krt = [AE(nc.sbuf_tensor(f"krt{c}", [P, T], BF16)) for c in range(NHP)]
        qrt = [AE(nc.sbuf_tensor(f"qrt{c}", [P, MAXOWN * P], BF16))
               for c in range(NHP)]
        mkacc = AE(nc.sbuf_tensor("mkacc", [P, NH], F32))

        stAL = ExitStack()
        EA = stAL.enter_context
        xhTg = [[EA(nc.sbuf_tensor(f"A_xhT{g}_{hc}", [P, GRP * P], BF16,
                                   side="right")) for hc in range(HC)]
                for g in range(2)]
        rt1 = EA(nc.sbuf_tensor("A_rt1", [P, H // 2], BF16, side="right"))
        rt2 = EA(nc.sbuf_tensor("A_rt2", [P, H // 2], BF16, side="right"))
        krb = EA(nc.sbuf_tensor("A_krb", [P, NH * 64], BF16, side="right"))
        sq16 = EA(nc.sbuf_tensor("A_sq16", [P, NH], F32, side="right"))
        sqQ = EA(nc.sbuf_tensor("A_sqQ", [P, NH], F32, side="right"))
        sq2 = EA(nc.sbuf_tensor("A_sq2", [P, NH], F32, side="right"))
        vst = [EA(nc.sbuf_tensor(f"A_vst{i}", [P, NH * 65], BF16,
                                 side="right")) for i in range(2)]
        ppr = [EA(nc.psum_tensor(f"A_ppr{i}", [P, 512], F32)) for i in range(4)]

        def phase_a(kb, role):
            own_t0 = 0 if role == 0 else QA_T
            own_nt = QA_T if role == 0 else NT - QA_T
            kv_nt = QA_T if role == 0 else NT

            kb.op("V", "memset", mkacc[:, :], 0.0, writes=("mkacc",))
            kb.op("G", "memset", krb[:, :], 0.0, writes=("A_krb",))
            for i in range(2):
                kb.op("G", "memset",
                      vst[i].rearrange("p (h c) -> p h c", h=NH)[:, :, 64:65],
                      1.0, writes=(f"A_vst{i}_ones",))

            def proj(w, gslot, jj, psum0):
                for a, b in _bank_pieces(0, H):
                    i = psum0 + (a // 512) % 2
                    pk = f"A_ppr{i}"
                    pr = ppr[i]
                    for hc in range(HC):
                        kb.mm(pr[0:P, 0:b - a],
                              xhTg[gslot][hc][:, jj * P:(jj + 1) * P],
                              wT[w][hc][:, a:b],
                              start=(hc == 0), stop=False, psum=pk,
                              reads=(f"A_xhTg{gslot}_{hc}", f"w{w}T{hc}"))
                    kb.mm(pr[0:P, 0:b - a], onesK[0:33, 0:P],
                          brows[w][0:33, a:b], start=False, stop=True,
                          psum=pk, reads=("onesK", f"br_{w}"))

            def rope_psum(kb, psum0, dst3, dstkey, j):
                if cfg.get("NOROPE", False):
                    for a, b in _bank_pieces(0, H):
                        i = psum0 + (a // 512) % 2
                        pk = f"A_ppr{i}"
                        kb.op("V", "tensor_copy",
                              out=rt1[:, a // 2:a // 2 + (b - a) // 2],
                              in_=ppr[i][0:P, 0:(b - a) // 2],
                              reads=(pk,), writes=("A_rt1",))
                    kb.op("V", "tensor_copy", out=dst3,
                          in_=rt1.rearrange("p (h r) -> p h r", h=NH),
                          reads=("A_rt1",), writes=(dstkey,))
                    return
                cs = slice(j * R, (j + 1) * R)
                cb = cosT[:, cs].rearrange("p (o r) -> p o r", o=1)\
                    .broadcast_to((P, NH, R))
                sb = sinT[:, cs].rearrange("p (o r) -> p o r", o=1)\
                    .broadcast_to((P, NH, R))
                for a, b in _bank_pieces(0, H):
                    i = psum0 + (a // 512) % 2
                    pk = f"A_ppr{i}"
                    hper = (b - a) // 64
                    h0 = a // 64
                    se = ppr[i][0:P, 0:b - a].rearrange(
                        "p (h r two) -> p h r two", h=hper, two=2)
                    hs_ = slice(h0, h0 + hper)
                    kb.op("V", "tensor_tensor",
                          out=rt1.rearrange("p (h r) -> p h r", h=NH)[:, hs_],
                          in0=se[:, :, :, 0], in1=cb[:, hs_], op=ALU.mult,
                          reads=(pk, f"cosT{j}"), writes=("A_rt1",))
                    kb.op("V", "tensor_tensor",
                          out=rt2.rearrange("p (h r) -> p h r", h=NH)[:, hs_],
                          in0=se[:, :, :, 1], in1=sb[:, hs_], op=ALU.mult,
                          reads=(pk, f"sinT{j}"), writes=("A_rt2",))
                kb.op("V", "tensor_tensor", out=dst3,
                      in0=rt1.rearrange("p (h r) -> p h r", h=NH),
                      in1=rt2.rearrange("p (h r) -> p h r", h=NH),
                      op=ALU.subtract,
                      reads=("A_rt1", "A_rt2"), writes=(dstkey,))

            groups = [(g0, min(GRP, kv_nt - g0))
                      for g0 in range(0, kv_nt, GRP)]
            if cfg.get("AGRP1", False):
                groups = groups[:1]
            for gi, (g0, gn) in enumerate(groups):
                gslot = gi % 2
                for jj in range(gn):
                    j = g0 + jj
                    if j >= QA_T:
                        emit_ln_tile(kb, j)
                for hc in range(HC):
                    kb.dma(TD2 if hc % 2 == 0 else TD,
                           xhTg[gslot][hc][:, 0:gn * P],
                           xhsc[g0 * P:(g0 + gn) * P, hc * P:(hc + 1) * P],
                           stream=f"xhT{gslot}_{hc % 2}", transpose=True,
                           reads=tuple([f"A_xhTg{gslot}_{hc}"] +
                                       [f"xhsc_f{g0 + t}" for t in range(gn)]),
                           writes=(f"A_xhTg{gslot}_{hc}",))
                if cfg.get("NOPROJ", False):
                    continue
                for jj in range(gn):
                    j = g0 + jj
                    jb = j % 2
                    proj("k", gslot, jj, 0)
                    krz = krb.rearrange("p (h z) -> p h z", z=64)
                    rope_psum(kb, 0, krz[:, :, 0:R], "A_krb", j)
                    kb.op("V", "tensor_tensor",
                          out=rt2.rearrange("p (h r) -> p h r", h=NH),
                          in0=krz[:, :, 0:R],
                          in1=krz[:, :, 0:R], op=ALU.mult,
                          reads=("A_krb",), writes=("A_rt2",))
                    kb.op("V", "tensor_reduce", sq16[:, :],
                          rt2.rearrange("p (h r) -> p h r", h=NH),
                          axis=AX.X, op=ALU.add,
                          reads=("A_rt2",), writes=("A_sq16",))
                    kb.op("V", "tensor_tensor", out=mkacc[:, :],
                          in0=mkacc[:, :], in1=sq16[:, :], op=ALU.max,
                          reads=("mkacc", "A_sq16"), writes=("mkacc",))
                    kb.dma("G", krsc[j * P:(j + 1) * P, :], krb[:, :],
                           stream="krsc",
                           reads=("A_krb", f"krsc_f{j}"),
                           writes=(f"krsc_f{j}",))
                    if cfg.get("ONLYK", False):
                        continue
                    vb = vst[jb]
                    for a, b in _bank_pieces(0, H):
                        i = 2 + (a // 512) % 2
                        pk = f"A_ppr{i}"
                        pr = ppr[i]
                        for hc in range(HC):
                            kb.mm(pr[0:P, 0:b - a],
                                  xhTg[gslot][hc][:, jj * P:(jj + 1) * P],
                                  wT["v"][hc][:, a:b],
                                  start=(hc == 0), stop=False,
                                  psum=pk,
                                  reads=(f"A_xhTg{gslot}_{hc}", f"wvT{hc}"))
                        kb.mm(pr[0:P, 0:b - a], onesK[0:33, 0:P],
                              brows["v"][0:33, a:b], start=False, stop=True,
                              psum=pk, reads=("onesK", "br_v"))
                        h0 = a // 64
                        nh_ = (b - a) // 64
                        kb.op("V", "tensor_copy",
                              out=vb.rearrange("p (h c) -> p h c", h=NH)[
                                  :, h0:h0 + nh_, 0:64],
                              in_=pr[0:P, 0:b - a].rearrange(
                                  "p (h c) -> p h c", c=64),
                              reads=(pk,), writes=(f"A_vst{jb}",))
                    if not cfg.get("NOV", False):
                        kb.dma("G",
                               vsc.rearrange("(h p) m -> p h m", p=P)[
                                   :, :, j * 65:(j + 1) * 65],
                               vb.rearrange("p (h c) -> p h c", h=NH),
                               stream=f"vsc{jb}",
                               reads=(f"A_vst{jb}", f"A_vst{jb}_ones"),
                               writes=(f"vsc_f{j}",))
                    if own_t0 <= j < own_t0 + own_nt:
                        jo = j - own_t0
                        proj("q", gslot, jj, 0)
                        krz = krb.rearrange("p (h z) -> p h z", z=64)
                        rope_psum(kb, 0, krz[:, :, 0:R], "A_krb", j)
                        kb.op("V", "tensor_tensor",
                              out=rt2.rearrange("p (h r) -> p h r", h=NH),
                              in0=krz[:, :, 0:R],
                              in1=krz[:, :, 0:R], op=ALU.mult,
                              reads=("A_krb",), writes=("A_rt2",))
                        kb.op("V", "tensor_reduce", sqQ[:, :],
                              rt2.rearrange("p (h r) -> p h r", h=NH),
                              axis=AX.X, op=ALU.add,
                              reads=("A_rt2",), writes=("A_sqQ",))
                        kb.op("S", "activation", sq2[:, :], sqQ[:, :],
                              AF.Ln, bias=1e-30,
                              reads=("A_sqQ",), writes=("A_sq2",))
                        with nc.allow_low_precision(
                                reason="softmax shift bound; slack is huge"):
                            kb.op("S", "activation", krz[:, :, 32:33],
                                  sq2.rearrange("p (h o) -> p h o", o=1),
                                  AF.Exp, scale=0.5,
                                  reads=("A_sq2",), writes=("A_krb",))
                        kb.dma("G", qrsc[jo * P:(jo + 1) * P, :], krb[:, :],
                               stream="qrsc",
                               reads=("A_krb", f"qrsc_f{jo}"),
                               writes=(f"qrsc_f{jo}",))
            kb.fence_dma("V")
            kb.fence_dma("S")

        if cfg.get("STOP", "") == "0":
            def phase_a(kb, role):
                kb.fence_dma("G")
        with nc.If(pid < cfg["ROLE_SPLIT"]):
            phase_a(kbA, 0)
        with nc.Else():
            phase_a(kbB, 1)
        nc.all_engine_barrier()
        kbA.barrier_reset()
        kbB.barrier_reset()
        stAL.close()

        stA2 = ExitStack()
        EA2 = stA2.enter_context
        mkt = EA2(nc.sbuf_tensor("A2_mkt", [16, P], F32, side="right"))
        mk1 = EA2(nc.sbuf_tensor("A2_mk1", [16, 4], F32, side="right"))
        mkd = EA2(nc.sbuf_tensor("A2_mkd", [16, 16], F32, side="right"))
        mkb = EA2(nc.sbuf_tensor("A2_mkb", [P, NH], F32, side="right"))
        mks = EA2(nc.sbuf_tensor("A2_mks", [16, T], BF16, side="right"))
        pps2 = EA2(nc.psum_tensor("A2_pps", [P, 512], F32))

        def phase_a2(kb, role):
            own_t0 = 0 if role == 0 else QA_T
            own_nt = QA_T if role == 0 else NT - QA_T
            kv_nt = QA_T if role == 0 else NT
            pps = pps2
            kb.op("T", "transpose", pps[0:NH, 0:P], mkacc[:, :], identf[:, :],
                  reads=("mkacc", "identf"), writes=("A2_pps",))
            kb.op("V", "tensor_copy", out=mkt[0:NH, :], in_=pps[0:NH, 0:P],
                  reads=("A2_pps",), writes=("A2_mkt",))
            kb.op("V", "tensor_reduce", mk1[0:NH, 0:1], mkt[0:NH, :],
                  axis=AX.X, op=ALU.max, reads=("A2_mkt",), writes=("A2_mk1",))
            kb.op("S", "activation", mk1[0:NH, 1:2], mk1[0:NH, 0:1],
                  AF.Ln, bias=1e-30, reads=("A2_mk1",), writes=("A2_l",))
            kb.op("S", "activation", mk1[0:NH, 2:3], mk1[0:NH, 1:2],
                  AF.Exp, scale=0.5, reads=("A2_l",), writes=("A2_s",))
            kb.op("V", "tensor_scalar", mk1[0:NH, 3:4], mk1[0:NH, 2:3],
                  -1.0, None, ALU.mult, reads=("A2_s",), writes=("A2_n",))
            kb.op("V", "tensor_scalar", mkd[0:NH, 0:NH],
                  identf[0:NH, 0:NH], mk1[0:NH, 3:4], None, ALU.mult,
                  reads=("identf", "A2_n"), writes=("A2_mkd",))
            kb.mm(pps[0:P, 0:NH], onesf[0:NH, :], mkd[0:NH, 0:NH],
                  start=True, stop=True, psum="A2_pps",
                  reads=("onesf", "A2_mkd"))
            kb.op("V", "tensor_copy", out=mkb[:, :], in_=pps[0:P, 0:NH],
                  reads=("A2_pps",), writes=("A2_mkb",))
            kb.op("V", "memset", mks[0:NH, 0:kv_nt * P], 1.0,
                  writes=("A2_mks",))
            kb.op("V", "tensor_scalar", mks[0:NH, 0:kv_nt * P],
                  mks[0:NH, 0:kv_nt * P], mk1[0:NH, 3:4], None, ALU.mult,
                  reads=("A2_mks", "A2_n"), writes=("A2_mks",))
            kb.dma("G", mkdr[0:NH, 0:kv_nt * P], mks[0:NH, 0:kv_nt * P],
                   stream="mkdr", reads=("A2_mks",))
            kb.mark("mkdr_f", "mkdr")
            for c in range(NHP):
                kb.dma(TD2 if c % 2 == 0 else TD,
                       krt[c][0:P, 0:kv_nt * P],
                       krsc[0:kv_nt * P, c * P:(c + 1) * P],
                       stream=f"krtL{c % 4}", transpose=True,
                       reads=tuple([f"krt_{c}"] +
                                   [f"krsc_f{j}" for j in range(kv_nt)]),
                       writes=(f"krt_{c}",))
                kb.dma(TD if c % 2 == 0 else TD2,
                       qrt[c][0:P, 0:own_nt * P],
                       qrsc[0:own_nt * P, c * P:(c + 1) * P],
                       stream=f"qrtL{c % 4}", transpose=True,
                       reads=tuple([f"qrt_{c}"] +
                                   [f"qrsc_f{j}" for j in range(own_nt)]),
                       writes=(f"qrt_{c}",))
            for c in range(NHP):
                for b0, h in ((32, 2 * c), (96, 2 * c + 1)):
                    kb.dma("Q", krt[c][b0:b0 + 1, 0:kv_nt * P],
                           mkdr[h:h + 1, 0:kv_nt * P],
                           stream=f"mkl{c % 4}",
                           reads=(f"krt_{c}", "mkdr_f"),
                           writes=(f"krt_{c}",))
            kb.fence_dma("Q")
            kb.fence_dma("S")

        if cfg.get("STOP", "") in ("0", "A"):
            def phase_a2(kb, role):
                kb.fence_dma("G")
        with nc.If(pid < cfg["ROLE_SPLIT"]):
            phase_a2(kbA, 0)
        with nc.Else():
            phase_a2(kbB, 1)
        nc.all_engine_barrier()
        kbA.barrier_reset()
        kbB.barrier_reset()
        stA2.close()
        stLN.close()
        stTab.close()

        stB = ExitStack()
        EB = stB.enter_context
        vbuf = [EB(nc.sbuf_tensor(f"vbuf{i}", [P, NT * 65], BF16,
                                  side="right")) for i in range(2)]
        ptp = [EB(nc.sbuf_tensor(f"ptp{i}", [P, 2 * W_B], BF16, side="right"))
               for i in range(2)]
        yts = [EB(nc.sbuf_tensor(f"yts{c}", [P, W_B], BF16, side="right"))
               for c in range(NHP)]
        rrow = EB(nc.sbuf_tensor("rrow", [1, max(W_B, H)], F32, side="right"))
        rsb = EB(nc.sbuf_tensor("rsb", [64, W_B], F32, side="right"))
        atp = EB(nc.sbuf_tensor("atp", [P, H], F32, side="right"))
        x2t = EB(nc.sbuf_tensor("x2t", [P, H], F32, side="right"))
        xre = EB(nc.sbuf_tensor("xre", [P, H], F32, side="right"))
        bnsB = EB(nc.sbuf_tensor("bnsB", [P, 6 * (H // BNW)], F32,
                                 side="right"))
        mvB = EB(nc.sbuf_tensor("mvB", [P, 2], F32, side="right"))
        srB = EB(nc.sbuf_tensor("srB", [P, 2], F32, side="right"))
        mvC = EB(nc.sbuf_tensor("mvC", [P, 6], F32, side="right"))
        qst = [EB(nc.sbuf_tensor(f"qst{i}", [P, MAXW], F32, side="right"))
               for i in range(3)]
        qg = EB(nc.sbuf_tensor("qg", [P, MAXW], BF16, side="right"))
        qt16 = EB(nc.sbuf_tensor("qt16", [P, MAXW], BF16, side="right"))
        tstg = EB(nc.sbuf_tensor("tstg", [P, max(H, HID)], BF16,
                                 side="right"))
        woT = [EB(nc.sbuf_tensor(f"woT{hc}", [P, H], BF16, side="right"))
               for hc in range(HC)]
        psc = [EB(nc.psum_tensor(f"psc{i}", [P, 2 * W_B], F32))
               for i in range(2)]
        pyt = [EB(nc.psum_tensor(f"pyt{i}", [P, W_B], F32))
               for i in range(2)]
        c2preB = EB(nc.sbuf_tensor("c2preB", [33, H], F32, side="right"))
        pmo = EB(nc.psum_tensor("pmo", [P, 512], F32))
        wT["o"] = woT

        def phase_b(kb, role):
            own_t0 = 0 if role == 0 else QA_T
            own_nt = QA_T if role == 0 else NT - QA_T
            kb.op("V", "memset", c2preB[:, :], 0.0, writes=("c2preB",))

            for hc in range(HC):
                kb.dma("Q", woT[hc][:, :],
                       wqsc[WOFF["o"]:WOFF["o"] + H, hc * P:(hc + 1) * P],
                       stream=f"wT{hc % 4}", transpose=True,
                       reads=(f"woT{hc}", "wqsc_f"), writes=(f"woT{hc}",))

            mats = [(0, w1, HID // P, H, max(1, H // MAXW), t1sc),
                    (1, w2, H // P, HID, max(1, HID // MAXW), t2sc)]
            qsteps = []

            def q_ld(src, jt, cc, cw, slot):
                kb.dma("Q", qst[slot][:, 0:cw],
                       src[jt * P:(jt + 1) * P, cc * cw:(cc + 1) * cw],
                       stream=f"qst{slot}", reads=(f"qst{slot}",),
                       writes=(f"qst{slot}",))

            def q_reduce(mi, cw, slot):
                kb.op("V", "tensor_reduce", qacc[:, 0:1],
                      qst[slot][:, 0:cw],
                      axis=AX.X, op=ALU.add, apply_absolute_value=True,
                      reads=(f"qst{slot}",), writes=("qacc1",))
                kb.op("V", "tensor_tensor", out=qacc[:, 1 + mi:2 + mi],
                      in0=qacc[:, 1 + mi:2 + mi], in1=qacc[:, 0:1],
                      op=ALU.add, reads=("qacc1", "qaccs"), writes=("qaccs",))

            def q_fin():
                for mi, src, ntile, width, ncc, dstT in mats:
                    n_el = float(ntile * P * width)
                    kb.mm(pmo[0:1, 0:1], qacc[:, 1 + mi:2 + mi],
                          onescol[:, 0:1], start=True, stop=True, psum="pmo",
                          reads=("qaccs", "onescol"))
                    kb.op("V", "tensor_scalar", qtmp[0:1, 4 * mi:4 * mi + 1],
                          pmo[0:1, 0:1], 1.0 / n_el, 1e-5, ALU.mult, ALU.max,
                          reads=("pmo",), writes=(f"qtmp_m{mi}",))
                    kb.op("V", "reciprocal",
                          out=qtmp[0:1, 4 * mi + 1:4 * mi + 2],
                          in_=qtmp[0:1, 4 * mi:4 * mi + 1],
                          reads=(f"qtmp_m{mi}",), writes=(f"qtmp_s{mi}",))
                    kb.op("V", "reciprocal",
                          out=qtmp[0:1, 4 * mi + 2:4 * mi + 3],
                          in_=qtmp[0:1, 4 * mi + 1:4 * mi + 2],
                          reads=(f"qtmp_s{mi}",), writes=(f"qtmp_r{mi}",))
                    kb.mm(pmo[0:P, 0:1], onesf[0:1, 0:P],
                          qtmp[0:1, 4 * mi + 1:4 * mi + 2],
                          start=True, stop=True, psum="pmo",
                          reads=(f"qtmp_s{mi}", "onesf"))
                    kb.op("V", "tensor_copy", out=sc128[:, mi:mi + 1],
                          in_=pmo[0:P, 0:1], reads=("pmo",), writes=("sc128",))
                    kb.mm(pmo[0:P, 0:1], onesf[0:1, 0:P],
                          qtmp[0:1, 4 * mi + 2:4 * mi + 3],
                          start=True, stop=True, psum="pmo",
                          reads=(f"qtmp_r{mi}", "onesf"))
                    kb.op("V", "tensor_copy", out=rc128[:, mi:mi + 1],
                          in_=pmo[0:P, 0:1], reads=("pmo",), writes=("rc128",))

            def q_tern(mi, jt, cc, cw, dstT, slot):
                kb.op("V", "tensor_scalar", qg[:, 0:cw],
                      qst[slot][:, 0:cw],
                      sc128[:, mi:mi + 1], 0.5, ALU.mult, ALU.is_gt,
                      reads=(f"qst{slot}", "sc128"), writes=("qg",))
                kb.op("V", "tensor_scalar", qt16[:, 0:cw],
                      qst[slot][:, 0:cw],
                      sc128[:, mi:mi + 1], -0.5, ALU.mult, ALU.is_lt,
                      reads=(f"qst{slot}", "sc128"), writes=("qt16",))
                kb.op("V", "tensor_tensor", out=qt16[:, 0:cw],
                      in0=qg[:, 0:cw], in1=qt16[:, 0:cw], op=ALU.subtract,
                      reads=("qg", "qt16"), writes=("qt16",))
                kb.dma("G", dstT[jt * P:(jt + 1) * P,
                               cc * cw:(cc + 1) * cw],
                       qt16[:, 0:cw], stream=f"tsc{mi}", reads=("qt16",))

            def q_stage(mi, c):
                if mi == 0:
                    kb.dma("Q", tstg[:, 0:HID],
                           t1sc[0:HID, c * P:(c + 1) * P],
                           stream="tstg", transpose=True,
                           reads=("tstg", "tsc0_f"), writes=("tstg",))
                    kb.dma("G",
                           t1Ts.rearrange("p (s c m) -> p s c m",
                                          c=HC, m=SEG1)[:, :, c, :],
                           tstg[:, 0:HID].rearrange("p (s m) -> p s m",
                                                    m=SEG1),
                           stream="t1Ts", reads=("tstg",))
                else:
                    kb.dma("Q", tstg[:, 0:H],
                           t2sc[0:H, c * P:(c + 1) * P],
                           stream="tstg", transpose=True,
                           reads=("tstg", "tsc1_f"), writes=("tstg",))
                    for pidx, (a2, b2_) in enumerate(_bank_pieces(0, H)):
                        kb.mm(pmo[0:33, 0:b2_ - a2],
                              b2lz[:, c:c + 33], tstg[:, a2:b2_],
                              start=True, stop=True, psum="pmo",
                              reads=("b2lz", "tstg"))
                        kb.op("V", "tensor_tensor",
                              out=c2preB[0:33, a2:b2_],
                              in0=pmo[0:33, 0:b2_ - a2],
                              in1=c2preB[0:33, a2:b2_], op=ALU.add,
                              reads=("pmo", "c2preB"),
                              writes=("c2preB",))
                    kb.op("V", "tensor_scalar", tstg[:, 0:H],
                          tstg[:, 0:H], g2c[:, c:c + 1], None, ALU.mult,
                          reads=("tstg", "g2c"), writes=("tstg",))
                    kb.dma("G",
                           t2Ts.rearrange("p (s c m) -> p s c m",
                                          c=HIDC, m=SEG2)[:, :, c, :],
                           tstg[:, 0:H].rearrange("p (s m) -> p s m",
                                                  m=SEG2),
                           stream="t2Ts", reads=("tstg",))

            def c2pre_fin():
                for a2, b2_ in _bank_pieces(0, H):
                    kb.op("V", "tensor_copy", out=rrow[0:1, 0:b2_ - a2],
                          in_=c2preB[32:33, a2:b2_],
                          reads=("c2preB",), writes=("rrow",))
                    kb.op("V", "tensor_tensor", out=c2pre[0:1, a2:b2_],
                          in0=c2preB[0:1, a2:b2_],
                          in1=rrow[0:1, 0:b2_ - a2], op=ALU.add,
                          reads=("c2preB", "rrow"),
                          writes=("c2pre",))

            nsl_ = [0]

            def mkld(src, jt, cc, cw):
                slot = nsl_[0] % 3
                nsl_[0] += 1
                return (lambda: q_ld(src, jt, cc, cw, slot)), slot

            for mi, src, ntile, width, ncc, dstT in mats:
                for jt in range(ntile):
                    for cc in range(ncc):
                        ld, slot = mkld(src, jt, cc, width // ncc)
                        qsteps.append(
                            (ld, lambda mi=mi, cw=width // ncc,
                             slot=slot: q_reduce(mi, cw, slot)))
            qsteps.append((None, lambda: q_fin()))
            for mi, src, ntile, width, ncc, dstT in mats:
                for jt in range(ntile):
                    for cc in range(ncc):
                        ld, slot = mkld(src, jt, cc, width // ncc)
                        qsteps.append(
                            (ld, lambda mi=mi, jt=jt, cc=cc,
                             cw=width // ncc, dstT=dstT, slot=slot:
                             q_tern(mi, jt, cc, cw, dstT, slot)))
            qsteps.append((None, lambda: kb.mark("tsc0_f", "tsc0")))
            qsteps.append((None, lambda: kb.mark("tsc1_f", "tsc1")))
            for c in range(HC):
                qsteps.append((None, lambda c=c: q_stage(0, c)))
            for c in range(HIDC):
                qsteps.append((None, lambda c=c: q_stage(1, c)))
            qsteps.append((None, lambda: c2pre_fin()))
            qsteps.append((None, lambda: kb.mark("t1Ts_f", "t1Ts")))
            qsteps.append((None, lambda: kb.mark("t2Ts_f", "t2Ts")))
            qi = [0]
            li = [0]

            def run_q(n):
                while n > 0 and qi[0] < len(qsteps):
                    while li[0] < len(qsteps) and li[0] <= qi[0] + 2:
                        ld = qsteps[li[0]][0]
                        if ld is not None:
                            ld()
                        li[0] += 1
                    qsteps[qi[0]][1]()
                    qi[0] += 1
                    n -= 1

            def emit_avs(pend, h, Wq_, ktmax):
                pairs, tslot = pend
                for (kt, lt0, half) in pairs:
                    col0 = lt0 * P
                    kb.mm(pyt[h % 2][0:65, col0:Wq_],
                          vbuf[h % 2][:, kt * 65:(kt + 1) * 65],
                          ptp[tslot][:, half * W_B + col0:half * W_B + Wq_],
                          start=(kt == 0), stop=(kt == ktmax),
                          psum=f"pyt{h % 2}",
                          reads=(f"vbuf{h % 2}", f"ptp{tslot}"))

            chunks = []
            t0 = 0
            while t0 < own_nt:
                w = min(BCH, own_nt - t0)
                chunks.append((t0, w))
                t0 += w

            pend_norm = [None]
            nq_total = None
            for (ct0, Wt) in chunks:
                glob0 = own_t0 + ct0
                ktmax = glob0 + Wt - 1
                nkt = ktmax + 1
                Wq_ = Wt * P
                kb.dma("Q", vbuf[0][:, 0:nkt * 65],
                       vsc[0:P, 0:nkt * 65], stream="vbuf0",
                       reads=tuple(["vbuf0"] +
                                   [f"vsc_f{j}" for j in range(nkt)]),
                       writes=("vbuf0",))
                for h in range(NH):
                    if h + 1 < NH:
                        kb.dma("Q", vbuf[(h + 1) % 2][:, 0:nkt * 65],
                               vsc[(h + 1) * P:(h + 2) * P, 0:nkt * 65],
                               stream=f"vbuf{(h + 1) % 2}",
                               reads=tuple([f"vbuf{(h + 1) % 2}"] +
                                           [f"vsc_f{j}" for j in range(nkt)]),
                               writes=(f"vbuf{(h + 1) % 2}",))
                    c = h // 2
                    b0 = 64 * (h % 2)
                    kkeys = (f"krt_{c}", f"qrt_{c}")
                    pend = None
                    fl = 0
                    kt = 0
                    while kt < nkt:
                        lt0 = max(0, kt - glob0)
                        pairable = (kt + 1 < nkt and kt + 1 <= glob0
                                    and Wq_ == W_B)
                        group = [(kt, lt0, 0)]
                        if pairable:
                            group.append((kt + 1, 0, 1))
                        tslot = fl % 2
                        pk = f"psc{tslot}"
                        for (k2, l2, half) in group:
                            col0 = l2 * P
                            kb.mm(psc[tslot][0:P,
                                             half * W_B + col0:
                                             half * W_B + Wq_],
                                  krt[c][b0:b0 + 33, k2 * P:(k2 + 1) * P],
                                  qrt[c][b0:b0 + 33,
                                         ct0 * P + col0:ct0 * P + Wq_],
                                  start=True, stop=True, psum=pk,
                                  reads=kkeys)
                        lo = group[0][1] * P
                        hi = (len(group) - 1) * W_B + Wq_
                        kb.op("S", "activation",
                              ptp[tslot][:, lo:hi], psc[tslot][0:P, lo:hi],
                              AF.Exp, scale=SCALE,
                              reads=(pk,), writes=(f"ptp{tslot}",))
                        for (k2, l2, half) in group:
                            if k2 >= glob0:
                                col0 = l2 * P
                                kb.op("V", "tensor_tensor",
                                      out=ptp[tslot][:,
                                                     half * W_B + col0:
                                                     half * W_B + col0 + P],
                                      in0=ptp[tslot][:,
                                                     half * W_B + col0:
                                                     half * W_B + col0 + P],
                                      in1=tri01[:, :], op=ALU.mult,
                                      reads=(f"ptp{tslot}", "tri01"),
                                      writes=(f"ptp{tslot}",))
                        if pend is not None:
                            emit_avs(pend, h, Wq_, ktmax)
                        if fl == 0 and pend_norm[0] is not None:
                            pend_norm[0]()
                            pend_norm[0] = None
                        pend = (group, tslot)
                        fl += 1
                        kt += len(group)
                    if pend is not None:
                        emit_avs(pend, h, Wq_, ktmax)

                    def mknorm(h=h, c=c, b0=b0, Wq_=Wq_):
                        def donorm():
                            kb.op("S", "activation", rrow[0:1, 0:Wq_],
                                  pyt[h % 2][64:65, 0:Wq_], AF.Ln,
                                  reads=(f"pyt{h % 2}",), writes=("rrow",))
                            kb.op("S", "activation", rrow[0:1, 0:Wq_],
                                  rrow[0:1, 0:Wq_], AF.Exp, scale=-1.0,
                                  reads=("rrow",), writes=("rrow",))
                            kb.mm(pmo[0:64, 0:Wq_], onesf[0:1, 0:64],
                                  rrow[0:1, 0:Wq_], start=True, stop=True,
                                  psum="pmo", reads=("rrow", "onesf"))
                            kb.op("V", "tensor_copy", out=rsb[0:64, 0:Wq_],
                                  in_=pmo[0:64, 0:Wq_], reads=("pmo",),
                                  writes=("rsb",))
                            kb.op("V", "tensor_tensor",
                                  out=yts[c][b0:b0 + 64, 0:Wq_],
                                  in0=pyt[h % 2][0:64, 0:Wq_],
                                  in1=rsb[0:64, 0:Wq_],
                                  op=ALU.mult,
                                  reads=(f"pyt{h % 2}", "rsb"),
                                  writes=(f"yts{c}",))
                        return donorm
                    pend_norm[0] = mknorm()
                    run_q(2)
                if pend_norm[0] is not None:
                    pend_norm[0]()
                    pend_norm[0] = None
                run_q(2)
                for lt in range(Wt):
                    jo = ct0 + lt
                    for a, b in _bank_pieces(0, H):
                        for hc in range(HC):
                            kb.mm(pmo[0:P, 0:b - a],
                                  yts[hc][0:P, lt * P:(lt + 1) * P],
                                  wT["o"][hc][:, a:b],
                                  start=(hc == 0), stop=(hc == HC - 1),
                                  psum="pmo",
                                  reads=(f"yts{hc}", f"woT{hc}"))
                        kb.op("V", "tensor_tensor", out=atp[:, a:b],
                              in0=pmo[0:P, 0:b - a], in1=boB[:, a:b],
                              op=ALU.add,
                              reads=("pmo", "boB"), writes=("atp",))
                    for cb in range(H // BNW):
                        kb.op("V", "bn_stats", bnsB[:, cb * 6:(cb + 1) * 6],
                              atp[:, cb * BNW:(cb + 1) * BNW],
                              reads=("atp",), writes=("bnsB",))
                    kb.op("V", "bn_aggr", mvB[:, :], bnsB[:, :],
                          reads=("bnsB",), writes=("mvB",))
                    rsqrt_sc(kb, srB, "srB", mvB[:, 1:2], ("mvB",))
                    kb.dma("Q", xre[:, :],
                           xs[(own_t0 + jo) * P:(own_t0 + jo + 1) * P, :],
                           stream="xre", reads=("xre",), writes=("xre",))
                    kb.op("V", "tensor_scalar", atp[:, :], atp[:, :],
                          mvB[:, 0:1], srB[:, 0:1], ALU.subtract, ALU.mult,
                          reads=("atp", "mvB", "srB"), writes=("atp",))
                    kb.op("V", "tensor_tensor", out=x2t[:, :], in0=atp[:, :],
                          in1=xre[:, :], op=ALU.add,
                          reads=("atp", "xre"), writes=("x2t",))
                    kb.dma("G", x2s[jo * P:(jo + 1) * P, :], x2t[:, :],
                           stream=f"x2s{jo % 2}", reads=("x2t",),
                           writes=(f"x2s_f{jo}",))
                    for cb in range(H // BNW):
                        kb.op("V", "bn_stats", bnsB[:, cb * 6:(cb + 1) * 6],
                              x2t[:, cb * BNW:(cb + 1) * BNW],
                              reads=("x2t",), writes=("bnsB",))
                    kb.op("V", "bn_aggr", mvC[:, 0:2], bnsB[:, :],
                          reads=("bnsB",), writes=("mvC",))
                    kb.op("S", "activation", mvC[:, 2:3], mvC[:, 1:2], AF.Ln,
                          bias=LN_EPS, reads=("mvC",), writes=("mvC_l",))
                    kb.op("S", "activation", mvC[:, 3:4], mvC[:, 2:3],
                          AF.Exp, scale=-0.5, reads=("mvC_l",),
                          writes=("mvC_r1",))
                    kb.op("V", "tensor_tensor", out=mvC[:, 4:5],
                          in0=mvC[:, 3:4], in1=mvC[:, 3:4], op=ALU.mult,
                          reads=("mvC_r1",), writes=("mvC_v2",))
                    kb.op("V", "tensor_tensor", out=mvC[:, 4:5],
                          in0=mvC[:, 4:5], in1=mvC[:, 1:2], op=ALU.mult,
                          reads=("mvC_v2", "mvC"), writes=("mvC_v2",))
                    kb.op("S", "activation", mvC[:, 5:6], mvC[:, 4:5], AF.Ln,
                          bias=LN_EPS, reads=("mvC_v2",), writes=("mvC_l2",))
                    kb.op("S", "activation", mvC[:, 5:6], mvC[:, 5:6],
                          AF.Exp, scale=-0.5, reads=("mvC_l2",),
                          writes=("mvC_r2",))
                    kb.op("V", "tensor_copy", out=mu1[:, jo:jo + 1],
                          in_=mvC[:, 0:1], reads=("mvC",), writes=("mu1",))
                    kb.op("V", "tensor_tensor", out=rr1[:, jo:jo + 1],
                          in0=mvC[:, 3:4], in1=mvC[:, 5:6], op=ALU.mult,
                          reads=("mvC_r1", "mvC_r2"), writes=("rr1",))
                    run_q(3)
            run_q(10 ** 9)
            kb.fence_dma("V")
            kb.fence_dma("S")

        if cfg.get("STOP", "") in ("0", "A", "A2"):
            def phase_b(kb, role):
                kb.fence_dma("G")
        with nc.If(pid < cfg["ROLE_SPLIT"]):
            phase_b(kbA, 0)
        with nc.Else():
            phase_b(kbB, 1)
        nc.all_engine_barrier()
        kbA.barrier_reset()
        kbB.barrier_reset()
        stB.close()
        stAB.close()

        stC = ExitStack()
        EC = stC.enter_context
        x2r = [EC(nc.sbuf_tensor(f"x2r{i}", [P, H], F32)) for i in range(2)]
        x2e = [EC(nc.sbuf_tensor(f"x2e{i}", [P, H], F32)) for i in range(2)]
        xfT = [[EC(nc.sbuf_tensor(f"xfT{tb}_{hc}", [P, FCH * P], BF16))
                for hc in range(HC)] for tb in range(2)]
        w1tb = [EC(nc.sbuf_tensor(f"w1tb{i}", [P, HC * 512], BF16))
                for i in range(2)]
        w2tb = EC(nc.sbuf_tensor("w2tb", [P, HIDC * 512], BF16))
        hTt = EC(nc.sbuf_tensor("hTt", [P, HIDC * FCH * P], BF16))
        sqr = [EC(nc.sbuf_tensor(f"sqr{i}", [P, FCH * P], BF16))
               for i in range(3)]
        varrow = EC(nc.sbuf_tensor("varrow", [1, FCH * P], F32))
        mu2row = EC(nc.sbuf_tensor("mu2row", [1, FCH * P], F32))
        c1row = EC(nc.sbuf_tensor("c1row", [33, HID], BF16))
        c2b = EC(nc.sbuf_tensor("c2b", [P, H], F32))
        c2g = EC(nc.sbuf_tensor("c2g", [1, H], F32))
        crow = EC(nc.sbuf_tensor("crow", [1, 512], F32))
        bhrow = EC(nc.sbuf_tensor("bhrow", [1, 512], F32))
        bnsC = EC(nc.sbuf_tensor("bnsC", [P, FCH * 6 * (HID // BNW2)],
                                  F32))
        mvD = EC(nc.sbuf_tensor("mvD", [P, 2], F32))
        srD = EC(nc.sbuf_tensor("srD", [P, 2], F32))
        rsD = EC(nc.sbuf_tensor("rsD", [P, FCH], F32))
        murow = EC(nc.sbuf_tensor("murow", [1, FCH * P], F32))
        xfC = EC(nc.sbuf_tensor("xfC", [P, H], BF16))
        ffo = [EC(nc.sbuf_tensor(f"ffo{i}", [P, H], F32)) for i in range(FCH)]
        oto = EC(nc.sbuf_tensor("oto", [P, H], F32))
        pf = [EC(nc.psum_tensor(f"pf{i}", [P, 512], F32)) for i in range(3)]
        pst = [EC(nc.psum_tensor(f"pst{i}", [P, 512], F32))
               for i in range(2)]
        pc2 = [EC(nc.psum_tensor(f"pc2{i}", [P, 512], F32)) for i in range(2)]
        pmu = EC(nc.psum_tensor("pmu", [P, 512], F32))

        def phase_c(kb, role):
            own_t0 = 0 if role == 0 else QA_T
            own_nt = QA_T if role == 0 else NT - QA_T
            kb.op("V", "memset", c1row[:, :], 0.0, writes=("c1row",))
            first = True

            chunks = []
            t0 = 0
            while t0 < own_nt:
                w = min(FCH, own_nt - t0)
                chunks.append((t0, w))
                t0 += w
            def stage_chunk(ct0, Wt, tb):
                for i in range(Wt):
                    jo = ct0 + i
                    jb = i % 2
                    kb.dma("Q", x2r[jb][:, :], x2s[jo * P:(jo + 1) * P, :],
                           stream=f"x2r{jb}",
                           reads=(f"x2r{jb}", f"x2s_f{jo}"),
                           writes=(f"x2r{jb}",))
                    kb.op("V", "tensor_scalar", xfC[:, 0:H],
                          x2r[jb][:, :],
                          mu1[:, jo:jo + 1], rr1[:, jo:jo + 1],
                          ALU.subtract, ALU.mult,
                          reads=(f"x2r{jb}", "mu1", "rr1"),
                          writes=("xfC",))
                    kb.dma("G", xfsc[jo * P:(jo + 1) * P, :], xfC[:, 0:H],
                           stream="xfsc", reads=("xfC",),
                           writes=(f"xfsc_f{jo}",))
                for hc in range(HC):
                    kb.dma("Q",
                           xfT[tb][hc][:, 0:Wt * P],
                           xfsc[ct0 * P:(ct0 + Wt) * P, hc * P:(hc + 1) * P],
                           stream=f"xfT{hc % 4}", transpose=True,
                           reads=tuple([f"xfTg{tb}_{hc}"] +
                                       [f"xfsc_f{ct0 + i}"
                                        for i in range(Wt)]),
                           writes=(f"xfTg{tb}_{hc}",))
                for hc in range(HC):
                    kb.op("V", "tensor_scalar", xfT[tb][hc][:, 0:Wt * P],
                          xfT[tb][hc][:, 0:Wt * P], g1c[:, hc:hc + 1],
                          None, ALU.mult,
                          reads=(f"xfTg{tb}_{hc}", "g1c"),
                          writes=(f"xfTg{tb}_{hc}",))

            stage_chunk(chunks[0][0], chunks[0][1], 0)
            for ci, (ct0, Wt) in enumerate(chunks):
                tb = ci % 2
                if ci + 1 < len(chunks):
                    stage_chunk(chunks[ci + 1][0], chunks[ci + 1][1],
                                (ci + 1) % 2)
                hpieces = []
                for hs in range(NSL):
                    for a, b in _bank_pieces(0, SL):
                        hpieces.append((hs, a, b))
                pend_stats = []
                for pi_, (hs, a, b) in enumerate(hpieces):
                    wk = f"w1tb{pi_ % 2}"
                    wb = w1tb[pi_ % 2]
                    ga = hs * SL + a
                    slab = (ga // SEG1) * (HC * SEG1)
                    kb.dma("Q", wb[:, 0:HC * (b - a)],
                           t1Ts[:, slab:slab + HC * (b - a)],
                           stream=wk, reads=(wk, "t1Ts_f"), writes=(wk,))
                    if first:
                        for hc in range(HC):
                            kb.mm(pc2[0][0:33, 0:b - a],
                                  b1lz[:, hc:hc + 33],
                                  wb[:, hc * (b - a):(hc + 1) * (b - a)],
                                  start=(hc == 0), stop=(hc == HC - 1),
                                  psum="pc20", reads=("b1lz", wk))
                        kb.dma("Q", bhrow[0:1, 0:b - a],
                               b1[None, ga:ga + b - a], stream="bhrow",
                               reads=("bhrow",), writes=("bhrow",))
                        kb.op("V", "tensor_scalar", crow[0:1, 0:b - a],
                              bhrow[0:1, 0:b - a], qtmp[0:1, 1:2], None,
                              ALU.mult, reads=("bhrow", "qtmp_s0"),
                              writes=("crow",))
                        kb.op("V", "tensor_tensor",
                              out=crow[0:1, 0:b - a],
                              in0=pc2[0][0:1, 0:b - a],
                              in1=crow[0:1, 0:b - a], op=ALU.add,
                              reads=("pc20", "crow"), writes=("crow",))
                        kb.op("V", "tensor_tensor",
                              out=crow[0:1, 0:b - a],
                              in0=pc2[0][32:33, 0:b - a],
                              in1=crow[0:1, 0:b - a], op=ALU.add,
                              reads=("pc20", "crow"), writes=("crow",))
                        kb.op("V", "tensor_copy",
                              out=c1row[0:1, ga:ga + b - a],
                              in_=crow[0:1, 0:b - a], reads=("crow",),
                              writes=("c1row",))
                        kb.op("V", "tensor_tensor",
                              out=c1row[32:33, ga:ga + b - a],
                              in0=crow[0:1, 0:b - a],
                              in1=c1row[0:1, ga:ga + b - a],
                              op=ALU.subtract, reads=("crow", "c1row"),
                              writes=("c1row",))
                    for sub in range((b - a) // P):
                        hcg = ga // P + sub
                        bank = (pi_ * ((b - a) // P) + sub) % 3
                        pk = f"pf{bank}"
                        pp = pf[bank]
                        for hc in range(HC):
                            kb.mm(pp[0:P, 0:Wt * P],
                                  wb[:, hc * (b - a) + sub * P:
                                     hc * (b - a) + (sub + 1) * P],
                                  xfT[tb][hc][:, 0:Wt * P],
                                  start=(hc == 0), stop=False,
                                  psum=pk, reads=(f"xfTg{tb}_{hc}", wk))
                        kb.mm(pp[0:P, 0:Wt * P],
                              c1row[0:33, hcg * P:(hcg + 1) * P],
                              onesK[0:33, 0:Wt * P],
                              start=False, stop=True, psum=pk,
                              reads=("onesK", "c1row"))
                        kb.op("S", "activation",
                              hTt[:, hcg * FCH * P:hcg * FCH * P + Wt * P],
                              pp[0:P, 0:Wt * P],
                              AF.Gelu, scale=rc128[:, 0:1],
                              reads=(pk, "rc128"),
                              writes=(f"hTt{hcg}",))
                        kb.op("V", "tensor_tensor",
                              out=sqr[hcg % 3][:, 0:Wt * P],
                              in0=hTt[:, hcg * FCH * P:
                                      hcg * FCH * P + Wt * P],
                              in1=hTt[:, hcg * FCH * P:
                                      hcg * FCH * P + Wt * P],
                              op=ALU.mult,
                              reads=(f"hTt{hcg}",),
                              writes=(f"sqr{hcg % 3}",))

                        def mkstats(hcg=hcg, Wt=Wt):
                            def em():
                                kb.mm(pst[0][0:33, 0:Wt * P],
                                      onesZ[:, 0:33],
                                      hTt[:, hcg * FCH * P:
                                          hcg * FCH * P + Wt * P],
                                      start=(hcg == 0),
                                      stop=(hcg == HIDC - 1),
                                      psum="pst0",
                                      reads=("onesZ", f"hTt{hcg}"))
                                kb.mm(pst[1][0:33, 0:Wt * P],
                                      onesZ[:, 0:33],
                                      sqr[hcg % 3][:, 0:Wt * P],
                                      start=(hcg == 0),
                                      stop=(hcg == HIDC - 1),
                                      psum="pst1",
                                      reads=("onesZ", f"sqr{hcg % 3}"))
                            return em
                        pend_stats.append(mkstats())
                        if len(pend_stats) > 2:
                            pend_stats.pop(0)()
                for f_ in pend_stats:
                    f_()
                pend_stats = []
                kb.op("V", "tensor_scalar", murow[0:1, 0:Wt * P],
                      pst[0][0:1, 0:Wt * P], -1.0 / HID, None, ALU.mult,
                      reads=("pst0",), writes=("murow",))
                kb.op("V", "tensor_scalar", varrow[0:1, 0:Wt * P],
                      pst[1][0:1, 0:Wt * P], 1.0 / HID, None, ALU.mult,
                      reads=("pst1",), writes=("varrow",))
                kb.op("V", "tensor_tensor", out=mu2row[0:1, 0:Wt * P],
                      in0=murow[0:1, 0:Wt * P], in1=murow[0:1, 0:Wt * P],
                      op=ALU.mult, reads=("murow",), writes=("mu2row",))
                kb.op("V", "tensor_tensor", out=varrow[0:1, 0:Wt * P],
                      in0=varrow[0:1, 0:Wt * P], in1=mu2row[0:1, 0:Wt * P],
                      op=ALU.subtract, reads=("varrow", "mu2row"),
                      writes=("varrow",))
                kb.op("S", "activation", varrow[0:1, 0:Wt * P],
                      varrow[0:1, 0:Wt * P], AF.Ln, bias=LN_EPS,
                      reads=("varrow",), writes=("varrow",))
                kb.op("S", "activation", varrow[0:1, 0:Wt * P],
                      varrow[0:1, 0:Wt * P], AF.Exp, scale=-0.5,
                      reads=("varrow",), writes=("varrow",))
                kb.op("V", "tensor_scalar", varrow[0:1, 0:Wt * P],
                      varrow[0:1, 0:Wt * P], qtmp[0:1, 6:7], None,
                      ALU.mult, reads=("varrow", "qtmp_r1"),
                      writes=("varrow",))
                for i in range(Wt):
                    kb.op("T", "transpose", pmu[0:P, 0:1],
                          varrow[0:1, i * P:(i + 1) * P],
                          identf[0:1, 0:1],
                          reads=("varrow", "identf"), writes=("pmu",))
                    kb.op("V", "tensor_copy", out=rsD[:, i:i + 1],
                          in_=pmu[0:P, 0:1], reads=("pmu",),
                          writes=("rsD",))
                for a, b in _bank_pieces(0, H):
                    slab2 = (a // SEG2) * (HIDC * SEG2)
                    kb.dma("Q", w2tb[:, 0:HIDC * (b - a)],
                           t2Ts[:, slab2:slab2 + HIDC * (b - a)],
                           stream="w2tb", reads=("w2tb", "t2Ts_f"),
                           writes=("w2tb",))
                    if first:
                        kb.dma("Q", bhrow[0:1, 0:b - a], b2[None, a:b],
                               stream="bhrow", reads=("bhrow",),
                               writes=("bhrow",))
                        kb.op("V", "tensor_scalar", crow[0:1, 0:b - a],
                              bhrow[0:1, 0:b - a], qtmp[0:1, 5:6], None,
                              ALU.mult, reads=("bhrow", "qtmp_s1"),
                              writes=("crow",))
                        kb.op("V", "tensor_tensor", out=crow[0:1, 0:b - a],
                              in0=c2pre[0:1, a:b],
                              in1=crow[0:1, 0:b - a],
                              op=ALU.add, reads=("c2pre", "crow"),
                              writes=("crow",))
                        kb.mm(pc2[0][0:P, 0:b - a], onesf[0:1, 0:P],
                              crow[0:1, 0:b - a], start=True, stop=True,
                              psum="pc20", reads=("crow", "onesf"))
                        kb.op("V", "tensor_scalar", c2b[:, a:b],
                              pc2[0][0:P, 0:b - a], rc128[:, 1:2], None,
                              ALU.mult, reads=("pc20", "rc128"),
                              writes=("c2b",))
                        for hc in range(HIDC):
                            kb.mm(pmu[0:33, 0:b - a], onesZ[:, 0:33],
                                  w2tb[:, hc * (b - a):(hc + 1) * (b - a)],
                                  start=(hc == 0), stop=(hc == HIDC - 1),
                                  psum="pmu", reads=("onesZ", "w2tb"))
                        kb.op("V", "tensor_copy", out=c2g[0:1, a:b],
                              in_=pmu[0:1, 0:b - a],
                              reads=("pmu",), writes=("c2g",))
                    for i in range(Wt):
                        pk2 = f"pc2{i % 2}"
                        pq = pc2[i % 2]
                        for hc in range(HIDC):
                            kb.mm(pq[0:P, 0:b - a],
                                  hTt[:, hc * FCH * P + i * P:
                                      hc * FCH * P + (i + 1) * P],
                                  w2tb[:, hc * (b - a):(hc + 1) * (b - a)],
                                  start=(hc == 0), stop=False,
                                  psum=pk2, reads=(f"hTt{hc}", "w2tb"))
                        kb.mm(pq[0:P, 0:b - a],
                              murow[0:1, i * P:(i + 1) * P], c2g[0:1, a:b],
                              start=False, stop=True, psum=pk2,
                              reads=("murow", "c2g"))
                        kb.op("V", "tensor_scalar", pq[0:P, 0:b - a],
                              pq[0:P, 0:b - a], rsD[:, i:i + 1], None,
                              ALU.mult, reads=(pk2, "rsD"),
                              writes=(pk2,))
                        kb.op("V", "tensor_tensor", out=ffo[i][:, a:b],
                              in0=pq[0:P, 0:b - a], in1=c2b[:, a:b],
                              op=ALU.add, reads=(pk2, "c2b"),
                              writes=(f"ffo{i}",))
                for i in range(Wt):
                    jo = ct0 + i
                    kb.dma("Q", x2e[i % 2][:, :],
                           x2s[jo * P:(jo + 1) * P, :],
                           stream=f"x2e{i % 2}",
                           reads=(f"x2e{i % 2}", f"x2s_f{jo}"),
                           writes=(f"x2e{i % 2}",))
                    for cb in range(H // BNW):
                        kb.op("V", "bn_stats", bnsC[:, cb * 6:(cb + 1) * 6],
                              ffo[i][:, cb * BNW:(cb + 1) * BNW],
                              reads=(f"ffo{i}",), writes=("bnsC",))
                    kb.op("V", "bn_aggr", mvD[:, :],
                          bnsC[:, 0:6 * (H // BNW)],
                          reads=("bnsC",), writes=("mvD",))
                    rsqrt_sc(kb, srD, "srD", mvD[:, 1:2], ("mvD",))
                    kb.op("V", "tensor_scalar", oto[:, :], ffo[i][:, :],
                          mvD[:, 0:1], srD[:, 0:1], ALU.subtract, ALU.mult,
                          reads=(f"ffo{i}", "mvD", "srD"), writes=("oto",))
                    kb.op("V", "tensor_tensor", out=oto[:, :],
                          in0=oto[:, :], in1=x2e[i % 2][:, :], op=ALU.add,
                          reads=("oto", f"x2e{i % 2}"), writes=("oto",))
                    kb.dma("G", out[jo * P:(jo + 1) * P, :], oto[:, :],
                           stream="out", reads=("oto",))
                first = False
            kb.fence_dma("G")
            kb.fence_dma("Q")
            kb.fence_dma("S")

        if cfg.get("STOP", "") in ("0", "A", "A2", "B"):
            def phase_c(kb, role):
                kb.fence_dma("G")
        with nc.If(pid < cfg["ROLE_SPLIT"]):
            phase_c(kbA, 0)
        with nc.Else():
            phase_c(kbB, 1)
        stC.close()

    return nc



_CACHE = {}


def _get_program(key, cfg):
    if key not in _CACHE:
        _CACHE[key] = build_program(cfg)
    return _CACHE[key]


def _ntabs(T, R):
    t = np.arange(1, T + 1, dtype=np.float64)[:, None]
    i = np.arange(R, dtype=np.float64)
    theta = (1.0 / 10000.0) ** (i / R)
    ang = t * theta
    n1 = np.round(ang / (2 * np.pi)).astype(np.float32)
    n2 = np.round((ang + np.pi / 2) / (2 * np.pi)).astype(np.float32)
    return n1, n2


def make_in_maps(cfg, inputs):
    T = cfg["T"]
    n1, n2 = _ntabs(T, 32)
    shared = {
        "Wq": inputs["L0_attn_Wq"], "bq": inputs["L0_attn_bq"],
        "Wk": inputs["L0_attn_Wk"], "bk": inputs["L0_attn_bk"],
        "Wv": inputs["L0_attn_Wv"], "bv": inputs["L0_attn_bv"],
        "Wo": inputs["L0_attn_Wo"], "bo": inputs["L0_attn_bo"],
        "g1": inputs["L0_ffn_ln1_g"], "b1ln": inputs["L0_ffn_ln1_b"],
        "w1": inputs["L0_ffn_w1"], "b1": inputs["L0_ffn_b1"],
        "g2": inputs["L0_ffn_ln2_g"], "b2ln": inputs["L0_ffn_ln2_b"],
        "w2": inputs["L0_ffn_w2"], "b2": inputs["L0_ffn_b2"],
        "ntab1": n1, "ntab2": n2,
    }
    shared = {k: np.ascontiguousarray(np.asarray(v, dtype=np.float32))
              for k, v in shared.items()}
    x = np.ascontiguousarray(np.asarray(inputs["L0_attn_x"],
                                        dtype=np.float32))
    nseq = x.shape[0]
    in_maps = []
    for c in range(cfg["NCORES"]):
        seq = c % nseq
        m = dict(shared)
        m["xs"] = x[seq]
        in_maps.append(m)
    return in_maps


def assemble_out(cfg, results, xshape):
    T, QA_T = cfg["T"], cfg["QA_T"]
    QA = QA_T * P
    nseq = xshape[0]
    full = np.empty(xshape, dtype=np.float32)
    for c in range(cfg["NCORES"]):
        seq = c % nseq
        o = np.asarray(results[c]["out"])
        if c < cfg["ROLE_SPLIT"]:
            full[seq, 0:QA] = o[0:QA]
        else:
            full[seq, QA:T] = o[0:T - QA]
    return full


def kernel(**inputs):
    cfg = full_cfg()
    nc = _get_program("full", cfg)
    in_maps = make_in_maps(cfg, inputs)
    res = run_bass_kernel_spmd(nc, in_maps,
                               core_ids=list(range(cfg["NCORES"])))
    x = np.asarray(inputs["L0_attn_x"])
    return assemble_out(cfg, res.results, x.shape)
